# version 03
# baseline (speedup 1.0000x reference)
import numpy as np
from contextlib import ExitStack

import concourse.bass as bass
import concourse.mybir as mybir
from concourse.bass_utils import run_bass_kernel_spmd

F32 = mybir.dt.float32
BF16 = mybir.dt.bfloat16
AF = mybir.ActivationFunctionType
ALU = mybir.AluOpType
AX = mybir.AxisListType

NCORES = 8
B_PER_CORE = 2
T = 2048
D = 1024
NDT = D // 128
TC = 512
NCH = T // TC

ENGS = ("sync", "tensor", "vector", "scalar", "gpsimd")
ESEM = {"sync": "e_sy", "tensor": "e_pe", "vector": "e_dv",
        "scalar": "e_sc", "gpsimd": "e_po"}


def _tname(x):
    if isinstance(x, tuple):
        return _tname(x[0]) + "#" + str(x[1])
    t = getattr(x, "tensor", None)
    if t is not None:
        return t.name
    return x.name


class Bld:

    def __init__(self, nc):
        self.nc = nc
        self.es = ExitStack()
        self.ops = {e: [] for e in ENGS}
        self.sems = {}
        self.cnt = {}
        self.lastw = {}
        self.readers = {}
        self.regions = {}
        self.clock = {e: {} for e in ENGS}

    def sem(self, name):
        if name not in self.sems:
            self.sems[name] = self.es.enter_context(self.nc.semaphore(name))
            self.cnt[name] = 0
        return self.sems[name]

    def sb(self, name, shape, dt=F32):
        return self.es.enter_context(self.nc.sbuf_tensor(name, shape, dt))

    def ps(self, name, shape, dt=F32):
        return self.es.enter_context(self.nc.psum_tensor(name, shape, dt))

    def _conflicts(self, name):
        root = name.split("#")[0]
        if name == root:
            return [root] + sorted(self.regions.get(root, ()))
        return [name, root]

    def _deps(self, eng, rd, wr, extra):
        deps = {}

        def add(rec):
            if rec is None:
                return
            s, tok, peng = rec
            if eng == "tensor" and peng == "tensor":
                return
            if deps.get(s, 0) < tok:
                deps[s] = tok
        for x in rd:
            for n in self._conflicts(_tname(x)):
                add(self.lastw.get(n))
        for x in wr:
            for n in self._conflicts(_tname(x)):
                add(self.lastw.get(n))
                for rec in self.readers.get(n, ()):
                    add(rec)
        for s, tok in extra:
            if deps.get(s, 0) < tok:
                deps[s] = tok
        ck = self.clock[eng]
        out = [(s, v) for s, v in deps.items() if ck.get(s, 0) < v]
        for s, v in out:
            ck[s] = v
            self.sem(s)
        return out

    def _note_region(self, name):
        root = name.split("#")[0]
        if name != root:
            self.regions.setdefault(root, set()).add(name)

    def auto(self, eng, make, rd=(), wr=(), sem=None, incn=1, extra=()):
        sem = sem or ESEM[eng]
        self.sem(sem)
        wait = self._deps(eng, rd, wr, extra)
        self.cnt[sem] += incn
        tok = self.cnt[sem]
        sems = self.sems

        def fn(e, _make=make, _wait=tuple(wait), _sem=sem, _incn=incn):
            for n, v in _wait:
                e.wait_ge(sems[n], v)
            _make(e).then_inc(sems[_sem], _incn)
        self.ops[eng].append(fn)
        rec = (sem, tok, eng)
        for x in rd:
            n = _tname(x)
            self._note_region(n)
            self.readers.setdefault(n, []).append(rec)
        for x in wr:
            n = _tname(x)
            self._note_region(n)
            self.lastw[n] = rec
            self.readers[n] = []
        return tok

    def dma(self, eng, out, in_, sem=None):
        ap_out = out[0] if isinstance(out, tuple) else out
        ap_in = in_[0] if isinstance(in_, tuple) else in_
        if sem is None:
            is_sb = getattr(ap_out, "tensor", ap_out).__class__.__name__.startswith("SB")
            root = (_tname(out).split("#")[0] if is_sb
                    else "st_" + _tname(in_).split("#")[0])
            sem = f"q_{eng[:2]}_{root}"
        return self.auto(eng, lambda e, o=ap_out, i=ap_in: e.dma_start(out=o, in_=i),
                         rd=[in_], wr=[out], sem=sem, incn=16)

    def mm_group(self, psum, tiles, extra=()):
        n = len(tiles)
        rd = []
        for lh, rh in tiles:
            rd.append(lh)
            rd.append(rh)
        wait = self._deps("tensor", rd, [psum], extra)
        sems = self.sems
        self.sem("e_pe")
        for idx, (lh, rh) in enumerate(tiles):
            last = idx == n - 1
            if last:
                self.cnt["e_pe"] += 1
            tok = self.cnt["e_pe"]

            def fn(e, _lh=lh, _rh=rh, _st=(idx == 0), _sp=last,
                   _wait=tuple(wait if idx == 0 else ()), _ps=psum, _inc=last, _tok=tok):
                for nm, v in _wait:
                    e.wait_ge(sems[nm], v)
                ins = e.matmul(_ps, _lh, _rh, start=_st, stop=_sp)
                if _inc:
                    ins.then_inc(sems["e_pe"], 1)
            self.ops["tensor"].append(fn)
        rec = ("e_pe", self.cnt["e_pe"], "tensor")
        for x in rd:
            n = _tname(x)
            self._note_region(n)
            self.readers.setdefault(n, []).append(rec)
        n_ = _tname(psum)
        self._note_region(n_)
        self.lastw[n_] = rec
        self.readers[n_] = []
        return self.cnt["e_pe"]

    def finalize(self, final_waits):
        nc = self.nc
        with self.es:
            for n, v in final_waits:
                self.sem(n)
            with nc.Block() as block:
                ops, sems = self.ops, self.sems

                @block.sync
                def _(e):
                    for fn in ops["sync"]:
                        fn(e)
                    for n, v in final_waits:
                        e.wait_ge(sems[n], v)

                @block.tensor
                def _(e):
                    for fn in ops["tensor"]:
                        fn(e)

                @block.vector
                def _(e):
                    for fn in ops["vector"]:
                        fn(e)

                @block.scalar
                def _(e):
                    for fn in ops["scalar"]:
                        fn(e)

                @block.gpsimd
                def _(e):
                    for fn in ops["gpsimd"]:
                        fn(e)
        return nc



def build_phase_A():
    nc = bass.Bass("TRN2", target_bir_lowering=False)
    b = Bld(nc)

    x_in = nc.dram_tensor("x", [B_PER_CORE, D, T], F32, kind="ExternalInput")
    w_inT = nc.dram_tensor("w_inT", [D, 2 * D], F32, kind="ExternalInput")
    conv_w = nc.dram_tensor("conv_w", [D, 4], F32, kind="ExternalInput")
    conv_b = nc.dram_tensor("conv_b", [D, 1], F32, kind="ExternalInput")
    w_xpT = nc.dram_tensor("w_xpT", [D, 33], F32, kind="ExternalInput")
    dtwb = nc.dram_tensor("dtwb", [2, D], F32, kind="ExternalInput")
    a_rep = nc.dram_tensor("a_rep", [128, 128], F32, kind="ExternalInput")
    r_sel = nc.dram_tensor("r_sel", [128, 16 * 128], F32, kind="ExternalInput")
    s_sel = nc.dram_tensor("s_sel", [128, 16 * 128], F32, kind="ExternalInput")
    rb16 = nc.dram_tensor("rb16", [16, 128], F32, kind="ExternalInput")
    w_outT = nc.dram_tensor("w_outT", [D, D], F32, kind="ExternalInput")
    y_out = nc.dram_tensor("y", [B_PER_CORE, D, T], F32, kind="ExternalOutput")

    ones_b = b.sb("ones_b", [128, 1], BF16)
    ones_f1 = b.sb("ones_f1", [1, 128], F32)
    WinT = [b.sb(f"WinT{i}", [128, 2 * D], BF16) for i in range(NDT)]
    XpT = [b.sb(f"XpT{i}", [128, 33], BF16) for i in range(NDT)]
    WoT = [b.sb(f"WoT{i}", [128, D], BF16) for i in range(NDT)]
    cw = b.sb("cw", [128, 4 * NDT], F32)
    cb = b.sb("cb", [128, NDT], F32)
    dtwb_s = b.sb("dtwb_s", [2, D], F32)
    a_rep_s = b.sb("a_rep_s", [128, 128], F32)
    Rsel = b.sb("Rsel", [128, 16 * 128], F32)
    Ssel = b.sb("Ssel", [128, 16 * 128], BF16)
    Rb16 = b.sb("Rb16", [16, 128], F32)

    x_t = [b.sb(f"x_t{i}", [128, TC]) for i in range(NDT)]
    xb_t = [b.sb(f"xb_t{i}", [128, TC], BF16) for i in range(NDT)]
    xn_t = [b.sb(f"xn_t{i}", [128, TC], BF16) for i in range(NDT)]
    xin_t = [b.sb(f"xin_t{i}", [128, TC + 3]) for i in range(NDT)]
    marg = b.sb("marg", [128, 3 * NDT])
    xc_t = [b.sb(f"xc_t{i}", [128, TC]) for i in range(NDT)]
    xcb_t = [b.sb(f"xcb_t{i}", [128, TC], BF16) for i in range(NDT)]
    xq_t = xcb_t
    sz_t = [b.sb(f"sz_t{i}", [128, TC]) for i in range(NDT)]
    dt_t = [b.sb(f"dt_t{i}", [128, TC]) for i in range(NDT)]
    ym_t = xn_t
    row_m = b.sb("row_m", [1, TC])
    row_v = b.sb("row_v", [1, TC])
    row_r = b.sb("row_r", [1, TC])
    row_mr = b.sb("row_mr", [1, TC])
    brow = b.sb("brow", [16, TC])
    crow = b.sb("crow", [16, TC])
    s1row = b.sb("s1row", [2, TC])
    sp_a = b.sb("sp_a", [128, TC])
    sp_b = b.sb("sp_b", [128, TC])
    brep = b.sb("brep", [128, TC])
    crep = b.sb("crep", [128, TC])
    dA = [b.sb(f"dA{j}", [128, TC]) for j in range(2)]
    dBx = [b.sb(f"dBx{j}", [128, TC]) for j in range(2)]
    hbuf = [b.sb(f"hbuf{j}", [128, TC]) for j in range(2)]
    hC = [b.sb(f"hC{j}", [128, TC], BF16) for j in range(2)]
    carry = b.sb("carry", [128, 128])

    P = [b.ps(f"P{i}", [128, TC]) for i in range(8)]

    for i in range(NDT):
        b.dma("gpsimd", WinT[i][:], w_inT[i * 128:(i + 1) * 128, :])
        b.dma("gpsimd", XpT[i][:], w_xpT[i * 128:(i + 1) * 128, :])
        b.dma("gpsimd", WoT[i][:], w_outT[i * 128:(i + 1) * 128, :])
        b.dma("sync", (cw[:, 4 * i:4 * i + 4], i), conv_w[i * 128:(i + 1) * 128, :], "q_cw")
        b.dma("sync", (cb[:, i:i + 1], i), conv_b[i * 128:(i + 1) * 128, :], "q_cb")
    b.dma("sync", dtwb_s[:], dtwb[:])
    b.dma("sync", a_rep_s[:], a_rep[:])
    b.dma("sync", Rsel[:], r_sel[:])
    b.dma("gpsimd", Ssel[:], s_sel[:])
    b.dma("sync", Rb16[:], rb16[:])

    b.auto("vector", lambda e: e.memset(ones_b[:], 1.0), wr=[ones_b])
    b.auto("vector", lambda e: e.memset(ones_f1[:], 1.0), wr=[ones_f1])
    b.auto("vector", lambda e: e.memset(s1row[:], 1.0), wr=[s1row])
    b.auto("scalar", lambda e: e.activation(a_rep_s[:], a_rep_s[:], AF.Exp),
           rd=[a_rep_s], wr=[a_rep_s])
    b.auto("vector", lambda e: e.tensor_single_scalar(a_rep_s[:], a_rep_s[:], -1.0, ALU.mult),
           rd=[a_rep_s], wr=[a_rep_s])

    for bi in range(B_PER_CORE):
        b.auto("vector", lambda e: e.memset(marg[:], 0.0), wr=[marg])
        b.auto("vector", lambda e: e.memset(carry[:], 0.0), wr=[carry])
        for c in range(NCH):
            t0 = c * TC
            xsl = slice(t0, t0 + TC)
            for i in range(NDT):
                b.dma("sync", x_t[i][:], x_in[bi, i * 128:(i + 1) * 128, xsl])

            for i in range(NDT):
                b.auto("scalar", lambda e, i=i: e.copy(xb_t[i][:], x_t[i][:]),
                       rd=[x_t[i]], wr=[xb_t[i]])
                b.auto("scalar", lambda e, i=i: e.activation(xq_t[i][:], x_t[i][:], AF.Square),
                       rd=[x_t[i]], wr=[xq_t[i]])

            b.mm_group(P[6][0:1, :], [(ones_b[:], xb_t[i][:]) for i in range(NDT)])
            b.mm_group(P[7][0:1, :], [(ones_b[:], xq_t[i][:]) for i in range(NDT)])

            b.auto("vector", lambda e: e.tensor_single_scalar(row_m[:], P[6][0:1, :], 1.0 / D, ALU.mult),
                   rd=[P[6]], wr=[row_m])
            b.auto("vector", lambda e: e.tensor_single_scalar(row_v[:], P[7][0:1, :], 1.0 / D, ALU.mult),
                   rd=[P[7]], wr=[row_v])
            b.auto("vector", lambda e: e.tensor_mul(row_r[:], row_m[:], row_m[:]),
                   rd=[row_m], wr=[row_r])
            b.auto("vector", lambda e: e.scalar_tensor_tensor(
                row_v[:], row_v[:], 1e-5, row_r[:], ALU.add, ALU.subtract),
                rd=[row_v, row_r], wr=[row_v])
            b.auto("scalar", lambda e: e.activation(row_r[:], row_v[:], AF.Ln),
                   rd=[row_v], wr=[row_r])
            b.auto("scalar", lambda e: e.activation(row_r[:], row_r[:], AF.Exp, scale=-0.5),
                   rd=[row_r], wr=[row_r])
            b.auto("vector", lambda e: e.tensor_mul(row_mr[:], row_m[:], row_r[:]),
                   rd=[row_m, row_r], wr=[row_mr])

            b.mm_group(P[6][:], [(ones_f1[:], row_r[:])])
            b.mm_group(P[7][:], [(ones_f1[:], row_mr[:])])

            for i in range(NDT):
                b.auto("vector", lambda e, i=i: e.scalar_tensor_tensor(
                    sp_a[:], x_t[i][:], 1.0, P[6][:], ALU.mult, ALU.mult),
                    rd=[x_t[i], P[6]], wr=[sp_a])
                b.auto("vector", lambda e, i=i: e.tensor_sub(xn_t[i][:], sp_a[:], P[7][:]),
                       rd=[sp_a, P[7]], wr=[xn_t[i]])

            for m in range(16):
                slot = m % 2
                b.mm_group(P[slot][:],
                           [(WinT[k][:, m * 128:(m + 1) * 128], xn_t[k][:]) for k in range(NDT)])
                if m < 8:
                    b.auto("scalar", lambda e, m=m, slot=slot: e.copy(
                        xin_t[m][:, 3:TC + 3], P[slot][:]), rd=[P[slot]], wr=[xin_t[m]])
                else:
                    b.auto("scalar", lambda e, m=m, slot=slot: e.activation(
                        sz_t[m - 8][:], P[slot][:], AF.Silu), rd=[P[slot]], wr=[sz_t[m - 8]])

            for i in range(NDT):
                b.auto("vector", lambda e, i=i: e.tensor_copy(
                    xin_t[i][:, 0:3], marg[:, 3 * i:3 * i + 3]), rd=[marg], wr=[xin_t[i]])
            for i in range(NDT):
                b.auto("vector", lambda e, i=i: e.tensor_single_scalar(
                    xc_t[i][:], xin_t[i][:, 0:TC], cw[:, 4 * i:4 * i + 1], ALU.mult),
                    rd=[xin_t[i], cw], wr=[xc_t[i]])
                for k in range(1, 4):
                    b.auto("vector", lambda e, i=i, k=k: e.scalar_tensor_tensor(
                        xc_t[i][:], xin_t[i][:, k:TC + k], cw[:, 4 * i + k:4 * i + k + 1],
                        xc_t[i][:], ALU.mult, ALU.add),
                        rd=[xin_t[i], cw, xc_t[i]], wr=[xc_t[i]])
            for i in range(NDT):
                b.auto("vector", lambda e, i=i: e.tensor_copy(
                    marg[:, 3 * i:3 * i + 3], xin_t[i][:, TC:TC + 3]),
                    rd=[xin_t[i]], wr=[marg])

            for i in range(NDT):
                b.auto("scalar", lambda e, i=i: e.activation(
                    xc_t[i][:], xc_t[i][:], AF.Silu, bias=cb[:, i:i + 1]),
                    rd=[xc_t[i], cb], wr=[xc_t[i]])
                b.auto("scalar", lambda e, i=i: e.copy(xcb_t[i][:], xc_t[i][:]),
                       rd=[xc_t[i]], wr=[xcb_t[i]])

            for (slot, c0, c1, dst) in ((0, 0, 16, brow), (1, 16, 32, crow), (2, 32, 33, None)):
                mm_rows = c1 - c0
                b.mm_group(P[slot][0:mm_rows, :],
                           [(XpT[k][:, c0:c1], xcb_t[k][:]) for k in range(NDT)])
                if dst is not None:
                    b.auto("scalar", lambda e, dst=dst, mm_rows=mm_rows, slot=slot:
                           e.copy(dst[:], P[slot][0:mm_rows, :]), rd=[P[slot]], wr=[dst])
                else:
                    b.auto("vector", lambda e: e.tensor_copy(s1row[0:1, :], P[2][0:1, :]),
                           rd=[P[2]], wr=[s1row])

            for i in range(NDT):
                slot = 3 + (i % 2)
                b.mm_group(P[slot][:], [(dtwb_s[:, i * 128:(i + 1) * 128], s1row[:])])
                b.auto("scalar", lambda e, slot=slot: e.activation(sp_a[:], P[slot][:], AF.Abs),
                       rd=[P[slot]], wr=[sp_a])
                b.auto("scalar", lambda e: e.activation(sp_b[:], sp_a[:], AF.Exp, scale=-1.0),
                       rd=[sp_a], wr=[sp_b])
                b.auto("scalar", lambda e: e.activation(sp_b[:], sp_b[:], AF.Ln, bias=1.0),
                       rd=[sp_b], wr=[sp_b])
                b.auto("scalar", lambda e, slot=slot, i=i: e.activation(
                    dt_t[i][:], P[slot][:], AF.Relu), rd=[P[slot]], wr=[dt_t[i]])
                b.auto("vector", lambda e, i=i: e.tensor_add(dt_t[i][:], dt_t[i][:], sp_b[:]),
                       rd=[dt_t[i], sp_b], wr=[dt_t[i]])
                b.auto("vector", lambda e, i=i: e.tensor_mul(xc_t[i][:], dt_t[i][:], xc_t[i][:]),
                       rd=[dt_t[i], xc_t[i]], wr=[xc_t[i]])

            b.mm_group(P[5][:], [(Rb16[:], brow[:])])
            b.auto("scalar", lambda e: e.copy(brep[:], P[5][:]), rd=[P[5]], wr=[brep])
            b.mm_group(P[6][:], [(Rb16[:], crow[:])])
            b.auto("scalar", lambda e: e.copy(crep[:], P[6][:]), rd=[P[6]], wr=[crep])

            for db in range(NDT):
                for k in range(16):
                    tau = db * 16 + k
                    j = k % 2
                    sA, sU = j, 2 + j
                    ksl = slice(k * 128, (k + 1) * 128)
                    b.mm_group(P[sA][:], [(Rsel[:, ksl], dt_t[db][:])])
                    b.mm_group(P[sU][:], [(Rsel[:, ksl], xc_t[db][:])])
                    b.auto("scalar", lambda e, j=j, sA=sA, tau=tau: e.activation(
                        dA[j][:], P[sA][:], AF.Exp, scale=a_rep_s[:, tau:tau + 1]),
                        rd=[P[sA], a_rep_s], wr=[dA[j]])
                    b.auto("vector", lambda e, j=j, sU=sU: e.tensor_tensor(
                        dBx[j][:], P[sU][:], brep[:], ALU.mult),
                        rd=[P[sU], brep], wr=[dBx[j]])
                    b.auto("vector", lambda e, j=j, tau=tau: e.tensor_tensor_scan(
                        hbuf[j][:], dA[j][:], dBx[j][:], carry[:, tau:tau + 1],
                        ALU.mult, ALU.add),
                        rd=[dA[j], dBx[j], carry], wr=[hbuf[j]])
                    b.auto("vector", lambda e, j=j, tau=tau: e.tensor_copy(
                        carry[:, tau:tau + 1], hbuf[j][:, TC - 1:TC]),
                        rd=[hbuf[j]], wr=[carry])
                    b.auto("vector", lambda e, j=j: e.tensor_tensor(
                        hC[j][:], hbuf[j][:], crep[:], ALU.mult),
                        rd=[hbuf[j], crep], wr=[hC[j]])
                    nfirst = (k == 0)
                    nlast = (k == 15)
                    b.auto(
                        "tensor",
                        (lambda e, _ksl=ksl, _j=j, _st=nfirst, _sp=nlast:
                         e.matmul(P[4][:], Ssel[:, _ksl], hC[_j][:], start=_st, stop=_sp)),
                        rd=[Ssel, hC[j]], wr=([P[4]] if (nfirst or nlast) else []))
                b.auto("vector", lambda e, db=db: e.tensor_tensor(
                    ym_t[db][:], P[4][:], sz_t[db][:], ALU.mult),
                    rd=[P[4], sz_t[db]], wr=[ym_t[db]])

            for m in range(NDT):
                slot = 5 + (m % 2)
                b.mm_group(P[slot][:],
                           [(WoT[k][:, m * 128:(m + 1) * 128], ym_t[k][:]) for k in range(NDT)])
                b.auto("vector", lambda e, m=m, slot=slot: e.tensor_tensor(
                    x_t[m][:], P[slot][:], x_t[m][:], ALU.add),
                    rd=[P[slot], x_t[m]], wr=[x_t[m]])
                b.dma("sync", (y_out[bi, m * 128:(m + 1) * 128, xsl], f"{bi}.{c}.{m}"),
                      x_t[m][:])

    return b.finalize([(nm, b.cnt[nm]) for nm in list(b.cnt) if nm.startswith("q_sy_st_")])



def build_phase_preln():
    nc = bass.Bass("TRN2", target_bir_lowering=False)
    b = Bld(nc)
    x_in = nc.dram_tensor("x", [B_PER_CORE, D, T], F32, kind="ExternalInput")
    y_out = nc.dram_tensor("y", [B_PER_CORE, D, T], F32, kind="ExternalOutput")
    ones_b = b.sb("ones_b", [128, 1], BF16)
    ones_f1 = b.sb("ones_f1", [1, 128], F32)
    x_t = [b.sb(f"x_t{i}", [128, TC]) for i in range(NDT)]
    xb_t = [b.sb(f"xb_t{i}", [128, TC], BF16) for i in range(NDT)]
    sq_t = [b.sb(f"sq_t{i}", [128, TC], BF16) for i in range(NDT)]
    t1 = b.sb("t1", [128, TC])
    row_m = b.sb("row_m", [1, TC])
    row_v = b.sb("row_v", [1, TC])
    row_r = b.sb("row_r", [1, TC])
    row_mr = b.sb("row_mr", [1, TC])
    P = [b.ps(f"P{i}", [128, TC]) for i in range(2)]
    b.auto("vector", lambda e: e.memset(ones_b[:], 1.0), wr=[ones_b])
    b.auto("vector", lambda e: e.memset(ones_f1[:], 1.0), wr=[ones_f1])
    for bi in range(B_PER_CORE):
        for c in range(NCH):
            xsl = slice(c * TC, (c + 1) * TC)
            for i in range(NDT):
                b.dma("sync", x_t[i][:], x_in[bi, i * 128:(i + 1) * 128, xsl])
            for i in range(NDT):
                b.auto("scalar", lambda e, i=i: e.copy(xb_t[i][:], x_t[i][:]),
                       rd=[x_t[i]], wr=[xb_t[i]])
                b.auto("scalar", lambda e, i=i: e.activation(sq_t[i][:], x_t[i][:], AF.Square),
                       rd=[x_t[i]], wr=[sq_t[i]])
            b.mm_group(P[0][0:1, :], [(ones_b[:], xb_t[i][:]) for i in range(NDT)])
            b.mm_group(P[1][0:1, :], [(ones_b[:], sq_t[i][:]) for i in range(NDT)])
            b.auto("vector", lambda e: e.tensor_single_scalar(row_m[:], P[0][0:1, :], 1.0 / D, ALU.mult),
                   rd=[P[0]], wr=[row_m])
            b.auto("vector", lambda e: e.tensor_single_scalar(row_v[:], P[1][0:1, :], 1.0 / D, ALU.mult),
                   rd=[P[1]], wr=[row_v])
            b.auto("vector", lambda e: e.tensor_mul(row_r[:], row_m[:], row_m[:]),
                   rd=[row_m], wr=[row_r])
            b.auto("vector", lambda e: e.scalar_tensor_tensor(
                row_v[:], row_v[:], 1e-5, row_r[:], ALU.add, ALU.subtract),
                rd=[row_v, row_r], wr=[row_v])
            b.auto("scalar", lambda e: e.activation(row_r[:], row_v[:], AF.Ln),
                   rd=[row_v], wr=[row_r])
            b.auto("scalar", lambda e: e.activation(row_r[:], row_r[:], AF.Exp, scale=-0.5),
                   rd=[row_r], wr=[row_r])
            b.auto("vector", lambda e: e.tensor_mul(row_mr[:], row_m[:], row_r[:]),
                   rd=[row_m, row_r], wr=[row_mr])
            b.mm_group(P[0][:], [(ones_f1[:], row_r[:])])
            b.mm_group(P[1][:], [(ones_f1[:], row_mr[:])])
            for i in range(NDT):
                b.auto("vector", lambda e, i=i: e.scalar_tensor_tensor(
                    t1[:], x_t[i][:], 1.0, P[0][:], ALU.mult, ALU.mult),
                    rd=[x_t[i], P[0]], wr=[t1])
                b.auto("vector", lambda e, i=i: e.tensor_sub(x_t[i][:], t1[:], P[1][:]),
                       rd=[t1, P[1]], wr=[x_t[i]])
                b.dma("sync", (y_out[bi, i * 128:(i + 1) * 128, xsl], f"{bi}.{c}.{i}"),
                      x_t[i][:])
    return b.finalize([(nm, b.cnt[nm]) for nm in list(b.cnt) if nm.startswith("q_sy_st_")])



def build_phase_addres():
    nc = bass.Bass("TRN2", target_bir_lowering=False)
    b = Bld(nc)
    x_in = nc.dram_tensor("x", [B_PER_CORE, D, T], F32, kind="ExternalInput")
    xr_in = nc.dram_tensor("xr", [B_PER_CORE, D, T], F32, kind="ExternalInput")
    y_out = nc.dram_tensor("y", [B_PER_CORE, D, T], F32, kind="ExternalOutput")
    a_t = [b.sb(f"a_t{i}", [128, T]) for i in range(NDT)]
    r_t = [b.sb(f"r_t{i}", [128, T]) for i in range(NDT)]
    for bi in range(B_PER_CORE):
        for i in range(NDT):
            rsl = slice(i * 128, (i + 1) * 128)
            b.dma("sync", (a_t[i][:], bi), x_in[bi, rsl, :])
            b.dma("gpsimd", (r_t[i][:], bi), xr_in[bi, rsl, :])
            b.auto("vector", lambda e, i=i: e.tensor_add(a_t[i][:], a_t[i][:], r_t[i][:]),
                   rd=[(a_t[i], bi), (r_t[i], bi), a_t[i], r_t[i]], wr=[a_t[i]])
            b.dma("sync", (y_out[bi, rsl, :], f"{bi}.{i}"), a_t[i][:])
    return b.finalize([(nm, b.cnt[nm]) for nm in list(b.cnt) if nm.startswith("q_sy_st_")])



HEADS = 16
DH = 64
WIN = 256
PAIR = 512
NPAIR = T // PAIR
BAND = 384


def build_phase_B_attn():
    nc = bass.Bass("TRN2", target_bir_lowering=False)
    b = Bld(nc)

    x_in = nc.dram_tensor("x", [B_PER_CORE, D, T], F32, kind="ExternalInput")
    wq = nc.dram_tensor("wq", [D, D], F32, kind="ExternalInput")
    wqr = nc.dram_tensor("wqr", [D, D], F32, kind="ExternalInput")
    wk = nc.dram_tensor("wk", [D, D], F32, kind="ExternalInput")
    wkr = nc.dram_tensor("wkr", [D, D], F32, kind="ExternalInput")
    wv = nc.dram_tensor("wv", [D, D], F32, kind="ExternalInput")
    wo = nc.dram_tensor("wo", [D, D], F32, kind="ExternalInput")
    qsc = nc.dram_tensor("qsc", [128, 2], F32, kind="ExternalInput")
    ksc = nc.dram_tensor("ksc", [128, 2], F32, kind="ExternalInput")
    csq_d = nc.dram_tensor("csq", [128, T], F32, kind="ExternalInput")
    ssq_d = nc.dram_tensor("ssq", [128, T], F32, kind="ExternalInput")
    csk_d = nc.dram_tensor("csk", [128, T], F32, kind="ExternalInput")
    ssk_d = nc.dram_tensor("ssk", [128, T], F32, kind="ExternalInput")
    mask_d = nc.dram_tensor("mask3", [128, 3 * BAND], F32, kind="ExternalInput")
    bsel_d = nc.dram_tensor("bsel", [128, 128], F32, kind="ExternalInput")
    psel_d = nc.dram_tensor("psel", [128, 2], F32, kind="ExternalInput")
    pselT_d = nc.dram_tensor("pselT", [2, 128], F32, kind="ExternalInput")
    ident_d = nc.dram_tensor("ident", [128, 128], F32, kind="ExternalInput")
    y_out = nc.dram_tensor("y", [B_PER_CORE, D, T], F32, kind="ExternalOutput")

    WqT = [b.sb(f"WqT{i}", [128, D], BF16) for i in range(NDT)]
    WqrT = [b.sb(f"WqrT{i}", [128, D], BF16) for i in range(NDT)]
    WkT = [b.sb(f"WkT{i}", [128, D], BF16) for i in range(NDT)]
    WkrT = [b.sb(f"WkrT{i}", [128, D], BF16) for i in range(NDT)]
    WvT = [b.sb(f"WvT{i}", [128, D], BF16) for i in range(NDT)]
    WoT = [b.sb(f"WoT{i}", [128, D], BF16) for i in range(NDT)]
    qs_s = b.sb("qs_s", [128, 2])
    ks_s = b.sb("ks_s", [128, 2])
    csq = b.sb("csq_s", [128, T], BF16)
    ssq = b.sb("ssq_s", [128, T], BF16)
    csk = b.sb("csk_s", [128, T], BF16)
    ssk = b.sb("ssk_s", [128, T], BF16)
    mask3 = b.sb("mask3_s", [128, 3 * BAND], BF16)
    bsel = b.sb("bsel_s", [128, 128], BF16)
    psel = b.sb("psel_s", [128, 2], BF16)
    pselT = b.sb("pselT_s", [2, 128])
    ident = b.sb("ident_s", [128, 128], BF16)

    xw = [b.sb(f"xw{i}", [128, PAIR + 1]) for i in range(NDT)]
    xs = [b.sb(f"xs{i}", [128, PAIR], BF16) for i in range(NDT)]
    Kc = [b.sb(f"Kc{i}", [128, 768], BF16) for i in range(NDT)]
    Vc = [b.sb(f"Vc{i}", [128, D], BF16) for i in range(6)]
    Qp = [b.sb(f"Qp{i}", [128, PAIR], BF16) for i in range(NDT)]
    sqq = [b.sb(f"sqq{i}", [128, PAIR], BF16) for i in range(NDT)]
    sqk = b.sb("sqk", [128, PAIR], BF16)
    ao = xs
    rt1 = b.sb("rt1", [128, PAIR])
    rt2 = b.sb("rt2", [128, PAIR])
    rnk = b.sb("rnk", [2, PAIR])
    rsq8 = [b.sb(f"rsq8{i}", [128, HEADS]) for i in range(4)]
    nrm_a = b.sb("nrm_a", [2, PAIR])
    esb = b.sb("esb", [128, BAND])
    atb = b.sb("atb", [128, BAND], BF16)
    attnT = b.sb("attnT", [128, BAND], BF16)
    zrow = b.sb("zrow", [128, 1])
    rz = b.sb("rz", [128, 1])
    ngm = b.sb("ngm", [128, 1])
    bia = b.sb("bia", [128, 1])

    P = [b.ps(f"P{i}", [128, PAIR]) for i in range(6)]
    PT = b.ps("PT", [128, BAND], BF16)
    PV = b.ps("PV", [128, 128])

    for i in range(NDT):
        rsl = slice(i * 128, (i + 1) * 128)
        b.dma("gpsimd", WqT[i][:], wq[rsl, :])
        b.dma("gpsimd", WqrT[i][:], wqr[rsl, :])
        b.dma("gpsimd", WkT[i][:], wk[rsl, :])
        b.dma("gpsimd", WkrT[i][:], wkr[rsl, :])
        b.dma("gpsimd", WvT[i][:], wv[rsl, :])
        b.dma("gpsimd", WoT[i][:], wo[rsl, :])
    b.dma("sync", qs_s[:], qsc[:])
    b.dma("sync", ks_s[:], ksc[:])
    b.dma("gpsimd", csq[:], csq_d[:])
    b.dma("gpsimd", ssq[:], ssq_d[:])
    b.dma("gpsimd", csk[:], csk_d[:])
    b.dma("gpsimd", ssk[:], ssk_d[:])
    b.dma("gpsimd", mask3[:], mask_d[:])
    b.dma("gpsimd", bsel[:], bsel_d[:])
    b.dma("gpsimd", psel[:], psel_d[:])
    b.dma("sync", pselT[:], pselT_d[:])
    b.dma("gpsimd", ident[:], ident_d[:])

    for bi in range(B_PER_CORE):
        for p in range(NPAIR):
            t0 = p * PAIR
            for i in range(NDT):
                rsl = slice(i * 128, (i + 1) * 128)
                if p == 0:
                    b.auto("vector", lambda e, i=i: e.memset(xw[i][:, 0:1], 0.0),
                           wr=[(xw[i], "m")])
                    b.dma("sync", (xw[i][:, 1:PAIR + 1], "b"), x_in[bi, rsl, 0:PAIR])
                else:
                    b.dma("sync", xw[i][:], x_in[bi, rsl, t0 - 1:t0 + PAIR])
            for i in range(NDT):
                src = (xw[i][:, 1:PAIR + 1]) if i < 4 else (xw[i][:, 0:PAIR])
                b.auto("scalar", lambda e, i=i, src=src: e.copy(xs[i][:], src),
                       rd=[xw[i], (xw[i], "m"), (xw[i], "b")], wr=[xs[i]])
            if p == 0:
                for i in range(NDT):
                    b.auto("vector", lambda e, i=i: e.memset(Kc[i][:, 0:256], 0.0),
                           rd=[], wr=[(Kc[i], "prev")])
                for i in range(2):
                    b.auto("vector", lambda e, i=i: e.memset(Vc[i][:], 0.0), wr=[Vc[i]])
            else:
                for i in range(NDT):
                    b.auto("vector", lambda e, i=i: e.tensor_copy(
                        Kc[i][:, 0:256], Kc[i][:, 512:768]),
                        rd=[(Kc[i], "cur")], wr=[(Kc[i], "prev")])
                for i in range(2):
                    b.auto("vector", lambda e, i=i: e.tensor_copy(Vc[i][:], Vc[i + 4][:]),
                           rd=[Vc[i + 4]], wr=[Vc[i]])
            for tt in range(4):
                tsl = slice(tt * 128, (tt + 1) * 128)
                for ch in range(2):
                    b.mm_group(P[ch][:, 0:512],
                               [(xs[k][:, tsl], WvT[k][:, ch * 512:(ch + 1) * 512])
                                for k in range(NDT)])
                    b.auto("vector", lambda e, tt=tt, ch=ch: e.tensor_copy(
                        Vc[2 + tt][:, ch * 512:(ch + 1) * 512], P[ch][:, 0:512]),
                        rd=[P[ch]], wr=[(Vc[2 + tt], ch)])
            for m in range(NDT):
                msl = slice(m * 128, (m + 1) * 128)
                b.mm_group(P[0][:], [(WqT[k][:, msl], xs[k][:]) for k in range(NDT)])
                b.mm_group(P[1][:], [(WqrT[k][:, msl], xs[k][:]) for k in range(NDT)])
                b.mm_group(P[2][:], [(WkT[k][:, msl], xs[k][:]) for k in range(NDT)])
                b.mm_group(P[3][:], [(WkrT[k][:, msl], xs[k][:]) for k in range(NDT)])
                b.auto("scalar", lambda e, m=m: e.activation(sqq[m][:], P[0][:], AF.Square),
                       rd=[P[0]], wr=[sqq[m]])
                b.auto("scalar", lambda e: e.activation(sqk[:], P[2][:], AF.Square),
                       rd=[P[2]], wr=[sqk])
                b.mm_group(P[4][0:2, :], [(psel[:], sqk[:])])
                b.auto("scalar", lambda e: e.activation(nrm_a[:], P[4][0:2, :], AF.Ln),
                       rd=[P[4]], wr=[nrm_a])
                b.auto("scalar", lambda e: e.activation(nrm_a[:], nrm_a[:], AF.Exp, scale=0.5),
                       rd=[nrm_a], wr=[nrm_a])
                b.auto("vector", lambda e: e.tensor_single_scalar(nrm_a[:], nrm_a[:], 1e-12, ALU.max),
                       rd=[nrm_a], wr=[nrm_a])
                b.auto("vector", lambda e: e.reciprocal(rnk[:], nrm_a[:]),
                       rd=[nrm_a], wr=[rnk])
                b.mm_group(P[4][:], [(pselT[:], rnk[:])])
                tq = slice(t0, t0 + PAIR)
                b.auto("vector", lambda e, m=m, tq=tq: e.scalar_tensor_tensor(
                    rt1[:], P[0][:], qs_s[:, 0:1], csq[:, tq], ALU.mult, ALU.mult),
                    rd=[P[0], qs_s, csq], wr=[rt1])
                b.auto("vector", lambda e, m=m, tq=tq: e.scalar_tensor_tensor(
                    rt2[:], P[1][:], qs_s[:, 1:2], ssq[:, tq], ALU.mult, ALU.mult),
                    rd=[P[1], qs_s, ssq], wr=[rt2])
                b.auto("vector", lambda e, m=m: e.tensor_add(Qp[m][:], rt1[:], rt2[:]),
                       rd=[rt1, rt2], wr=[Qp[m]])
                b.auto("vector", lambda e, m=m, tq=tq: e.scalar_tensor_tensor(
                    rt1[:], P[2][:], ks_s[:, 0:1], csk[:, tq], ALU.mult, ALU.mult),
                    rd=[P[2], ks_s, csk], wr=[rt1])
                b.auto("vector", lambda e, m=m, tq=tq: e.scalar_tensor_tensor(
                    rt2[:], P[3][:], ks_s[:, 1:2], ssk[:, tq], ALU.mult, ALU.mult),
                    rd=[P[3], ks_s, ssk], wr=[rt2])
                b.auto("vector", lambda e: e.tensor_add(rt1[:], rt1[:], rt2[:]),
                       rd=[rt1, rt2], wr=[rt1])
                b.auto("vector", lambda e, m=m: e.tensor_tensor(
                    Kc[m][:, 256:768], rt1[:], P[4][:], ALU.mult),
                    rd=[rt1, P[4]], wr=[(Kc[m], "cur")])
            for tt in range(4):
                tsl = slice(tt * 128, (tt + 1) * 128)
                b.mm_group(P[5][:, 0:16],
                           [(sqq[m][:, tsl], bsel[:, m * 16:(m + 1) * 16]) for m in range(NDT)])
                b.auto("scalar", lambda e, tt=tt: e.activation(
                    rsq8[tt][:], P[5][:, 0:16], AF.Ln), rd=[P[5]], wr=[rsq8[tt]])
                b.auto("scalar", lambda e, tt=tt: e.activation(
                    rsq8[tt][:], rsq8[tt][:], AF.Exp, scale=0.5), rd=[rsq8[tt]], wr=[rsq8[tt]])
                b.auto("vector", lambda e, tt=tt: e.tensor_single_scalar(
                    rsq8[tt][:], rsq8[tt][:], 1e-12, ALU.max), rd=[rsq8[tt]], wr=[rsq8[tt]])
                b.auto("vector", lambda e, tt=tt: e.reciprocal(rsq8[tt][:], rsq8[tt][:]),
                       rd=[rsq8[tt]], wr=[rsq8[tt]])
                b.auto("vector", lambda e, tt=tt: e.tensor_single_scalar(
                    rsq8[tt][:], rsq8[tt][:], 8.0, ALU.mult), rd=[rsq8[tt]], wr=[rsq8[tt]])
            for wloc in range(2):
                w_g = 2 * p + wloc
                for qt in range(2):
                    tt = 2 * wloc + qt
                    for h in range(HEADS):
                        m = h // 2
                        hr = slice((h % 2) * 64, (h % 2) * 64 + 64)
                        qs_l = slice(tt * 128, (tt + 1) * 128)
                        koff = 256 * wloc + 128 * qt
                        mv = 0 if w_g > 0 else (1 + qt)
                        msl = slice(mv * BAND, (mv + 1) * BAND)
                        b.mm_group(P[5][:, 0:BAND],
                                   [(Qp[m][hr, qs_l], Kc[m][hr, koff:koff + BAND])])
                        b.auto("vector", lambda e: e.reduce_max(
                            ngm[:], P[5][:, 0:BAND], axis=AX.X, negate=True),
                            rd=[P[5]], wr=[ngm])
                        b.auto("vector", lambda e, tt=tt, h=h: e.tensor_mul(
                            bia[:], ngm[:], rsq8[tt][:, h:h + 1]),
                            rd=[ngm, rsq8[tt]], wr=[bia])
                        b.auto("scalar", lambda e, tt=tt, h=h: e.activation(
                            esb[:], P[5][:, 0:BAND], AF.Exp,
                            bias=bia[:], scale=rsq8[tt][:, h:h + 1]),
                            rd=[P[5], bia, rsq8[tt]], wr=[esb])
                        b.auto("vector", lambda e, msl=msl: e.scalar_tensor_tensor(
                            atb[:], esb[:], 1.0, mask3[:, msl], ALU.mult, ALU.mult,
                            accum_out=zrow[:]),
                            rd=[esb, mask3], wr=[atb, zrow])
                        b.auto("vector", lambda e: e.reciprocal(rz[:], zrow[:]),
                               rd=[zrow], wr=[rz])
                        b.auto("vector", lambda e: e.tensor_single_scalar(
                            atb[:], atb[:], rz[:], ALU.mult), rd=[atb, rz], wr=[atb])
                        for j in range(3):
                            b.auto("tensor", lambda e, j=j: e.transpose(
                                PT[:, j * 128:(j + 1) * 128], atb[:, j * 128:(j + 1) * 128],
                                ident[:]), rd=[atb, ident], wr=[(PT, j)])
                        b.auto("vector", lambda e: e.tensor_copy(attnT[:], PT[:]),
                               rd=[PT, (PT, 0), (PT, 1), (PT, 2)], wr=[attnT])
                        vbase = 2 * wloc + qt
                        b.mm_group(PV[hr, :],
                                   [(Vc[vbase + j][:, h * 64:(h + 1) * 64],
                                     attnT[:, j * 128:(j + 1) * 128]) for j in range(3)])
                        if h % 2 == 1:
                            b.auto("vector", lambda e, m=m, tt=tt: e.tensor_copy(
                                ao[m][:, tt * 128:(tt + 1) * 128], PV[:]),
                                rd=[PV], wr=[(ao[m], tt)])
            for mo in range(NDT):
                msl = slice(mo * 128, (mo + 1) * 128)
                b.mm_group(P[mo % 2][:],
                           [(WoT[k][:, msl], ao[k][:]) for k in range(NDT)])
                b.auto("vector", lambda e, mo=mo: e.tensor_tensor(
                    xw[mo][:, 1:PAIR + 1], P[mo % 2][:], xw[mo][:, 1:PAIR + 1], ALU.add),
                    rd=[P[mo % 2], xw[mo], (xw[mo], "b")], wr=[(xw[mo], "b"), xw[mo]])
                b.dma("sync", (y_out[bi, msl, t0:t0 + PAIR], f"{bi}.{p}.{mo}"),
                      xw[mo][:, 1:PAIR + 1])

    st = [(nm, b.cnt[nm]) for nm in list(b.cnt) if nm.startswith("q_sy_st_")]
    return b.finalize(st)



TC2 = 256
DFF = 2730
NJ = 22


def build_phase_B_ffn():
    nc = bass.Bass("TRN2", target_bir_lowering=False)
    b = Bld(nc)

    x_in = nc.dram_tensor("x", [B_PER_CORE, D, T], F32, kind="ExternalInput")
    w1_d = nc.dram_tensor("w1", [D, 2 * DFF], F32, kind="ExternalInput")
    w2_d = nc.dram_tensor("w2", [DFF, D], F32, kind="ExternalInput")
    g_d = nc.dram_tensor("lng", [128, NDT], F32, kind="ExternalInput")
    bb_d = nc.dram_tensor("lnb", [128, NDT], F32, kind="ExternalInput")
    b1a_d = nc.dram_tensor("b1a", [128, NJ], F32, kind="ExternalInput")
    b1g_d = nc.dram_tensor("b1g", [128, NJ], F32, kind="ExternalInput")
    b2_d = nc.dram_tensor("b2", [128, NDT], F32, kind="ExternalInput")
    y_out = nc.dram_tensor("y", [B_PER_CORE, D, T], F32, kind="ExternalOutput")

    W1 = [b.sb(f"W1_{i}", [128, 2 * DFF], BF16) for i in range(NDT)]
    W2 = [b.sb(f"W2_{j}", [128 if j < NJ - 1 else 42, D], BF16) for j in range(NJ)]
    g_s = b.sb("g_s", [128, NDT])
    bb_s = b.sb("bb_s", [128, NDT])
    b1a_s = b.sb("b1a_s", [128, NJ])
    b1g_s = b.sb("b1g_s", [128, NJ])
    b2_s = b.sb("b2_s", [128, NDT])
    ones_b = b.sb("ones_b", [128, 1], BF16)
    ones_f1 = b.sb("ones_f1", [1, 128])

    xw = [b.sb(f"xw{i}", [128, TC2 + 1]) for i in range(NDT)]
    xs_b = [b.sb(f"xsb{i}", [128, TC2], BF16) for i in range(NDT)]
    sq_b = [b.sb(f"sqb{i}", [128, TC2], BF16) for i in range(NDT)]
    xn = [b.sb(f"xn{i}", [128, TC2], BF16) for i in range(NDT)]
    hjs = [b.sb(f"hj{j}", [128, TC2], BF16) for j in range(NJ)]
    gel = b.sb("gel", [128, TC2])
    row_m = b.sb("row_m", [1, TC2])
    row_v = b.sb("row_v", [1, TC2])
    row_r = b.sb("row_r", [1, TC2])
    row_mr = b.sb("row_mr", [1, TC2])
    t1 = b.sb("t1", [128, TC2])

    PY4 = [b.ps(f"PY{i}", [128, 2 * TC2]) for i in range(4)]
    PY = [PY4[i // 2][:, (i % 2) * TC2:(i % 2 + 1) * TC2] for i in range(NDT)]
    PA = b.ps("PA", [128, TC2])
    PG = b.ps("PG", [128, TC2])
    PB1 = b.ps("PB1", [128, TC2])
    PB2 = b.ps("PB2", [128, TC2])
    PS = PB1[0:1, :]
    PQ = PB2[0:1, :]
    PR = PB1[:]
    PMR = PB2[:]

    for i in range(NDT):
        b.dma("gpsimd", W1[i][:], w1_d[i * 128:(i + 1) * 128, :])
    for j in range(NJ):
        r0 = j * 128
        r1 = min(DFF, r0 + 128)
        b.dma("gpsimd", W2[j][:], w2_d[r0:r1, :])
    b.dma("sync", g_s[:], g_d[:])
    b.dma("sync", bb_s[:], bb_d[:])
    b.dma("sync", b1a_s[:], b1a_d[:])
    b.dma("sync", b1g_s[:], b1g_d[:])
    b.dma("sync", b2_s[:], b2_d[:])
    b.auto("vector", lambda e: e.memset(ones_b[:], 1.0), wr=[ones_b])
    b.auto("vector", lambda e: e.memset(ones_f1[:], 1.0), wr=[ones_f1])

    for bi in range(B_PER_CORE):
        for c in range(T // TC2):
            t0 = c * TC2
            for i in range(NDT):
                rsl = slice(i * 128, (i + 1) * 128)
                if c == 0:
                    b.auto("vector", lambda e, i=i: e.memset(xw[i][:, 0:1], 0.0),
                           wr=[(xw[i], "m")])
                    b.dma("sync", (xw[i][:, 1:TC2 + 1], "b"), x_in[bi, rsl, 0:TC2])
                else:
                    b.dma("sync", xw[i][:], x_in[bi, rsl, t0 - 1:t0 + TC2])
            for i in range(NDT):
                src = (xw[i][:, 1:TC2 + 1]) if i < 4 else (xw[i][:, 0:TC2])
                b.auto("scalar", lambda e, i=i, src=src: e.copy(xs_b[i][:], src),
                       rd=[xw[i], (xw[i], "m"), (xw[i], "b")], wr=[xs_b[i]])
                b.auto("scalar", lambda e, i=i, src=src: e.activation(sq_b[i][:], src, AF.Square),
                       rd=[xw[i], (xw[i], "m"), (xw[i], "b")], wr=[sq_b[i]])
            b.mm_group(PS, [(ones_b[:], xs_b[i][:]) for i in range(NDT)])
            b.mm_group(PQ, [(ones_b[:], sq_b[i][:]) for i in range(NDT)])
            b.auto("vector", lambda e: e.tensor_single_scalar(row_m[:], PS, 1.0 / D, ALU.mult),
                   rd=[PS], wr=[row_m])
            b.auto("vector", lambda e: e.tensor_single_scalar(row_v[:], PQ, 1.0 / D, ALU.mult),
                   rd=[PQ], wr=[row_v])
            b.auto("vector", lambda e: e.tensor_mul(row_r[:], row_m[:], row_m[:]),
                   rd=[row_m], wr=[row_r])
            b.auto("vector", lambda e: e.scalar_tensor_tensor(
                row_v[:], row_v[:], 1e-5, row_r[:], ALU.add, ALU.subtract),
                rd=[row_v, row_r], wr=[row_v])
            b.auto("scalar", lambda e: e.activation(row_r[:], row_v[:], AF.Ln),
                   rd=[row_v], wr=[row_r])
            b.auto("scalar", lambda e: e.activation(row_r[:], row_r[:], AF.Exp, scale=-0.5),
                   rd=[row_r], wr=[row_r])
            b.auto("vector", lambda e: e.tensor_mul(row_mr[:], row_m[:], row_r[:]),
                   rd=[row_m, row_r], wr=[row_mr])
            b.mm_group(PR, [(ones_f1[:], row_r[:])])
            b.mm_group(PMR, [(ones_f1[:], row_mr[:])])
            for i in range(NDT):
                src = (xw[i][:, 1:TC2 + 1]) if i < 4 else (xw[i][:, 0:TC2])
                b.auto("vector", lambda e, i=i, src=src: e.scalar_tensor_tensor(
                    t1[:], src, 1.0, PR, ALU.mult, ALU.mult),
                    rd=[xw[i], PR], wr=[t1])
                b.auto("vector", lambda e, i=i: e.tensor_sub(t1[:], t1[:], PMR),
                       rd=[t1, PMR], wr=[t1])
                b.auto("vector", lambda e, i=i: e.tensor_scalar(
                    xn[i][:], t1[:], g_s[:, i:i + 1], bb_s[:, i:i + 1], ALU.mult, ALU.add),
                    rd=[t1, g_s, bb_s], wr=[xn[i]])
            for j in range(NJ):
                rows = 128 if j < NJ - 1 else 42
                a0 = j * 128
                g0 = DFF + j * 128
                b.mm_group(PA[0:rows, :], [(W1[k][:, a0:a0 + rows], xn[k][:]) for k in range(NDT)])
                b.mm_group(PG[0:rows, :], [(W1[k][:, g0:g0 + rows], xn[k][:]) for k in range(NDT)])
                b.auto("scalar", lambda e, rows=rows, j=j: e.activation(
                    gel[0:rows, :], PG[0:rows, :], AF.Gelu, bias=b1g_s[0:rows, j:j + 1]),
                    rd=[PG, b1g_s], wr=[gel])
                b.auto("vector", lambda e, rows=rows, j=j: e.scalar_tensor_tensor(
                    hjs[j][0:rows, :], PA[0:rows, :], b1a_s[0:rows, j:j + 1], gel[0:rows, :],
                    ALU.add, ALU.mult),
                    rd=[PA, b1a_s, gel], wr=[hjs[j]])
            for mo in range(NDT):
                b.mm_group(PY[mo],
                           [(W2[j][0:(128 if j < NJ - 1 else 42), mo * 128:(mo + 1) * 128],
                             hjs[j][0:(128 if j < NJ - 1 else 42), :]) for j in range(NJ)],)
            for mo in range(NDT):
                b.auto("vector", lambda e, mo=mo: e.scalar_tensor_tensor(
                    xw[mo][:, 1:TC2 + 1], PY[mo], b2_s[:, mo:mo + 1],
                    xw[mo][:, 1:TC2 + 1], ALU.add, ALU.add),
                    rd=[PY4[mo // 2], b2_s, xw[mo], (xw[mo], "b")],
                    wr=[(xw[mo], "b"), xw[mo]])
                b.dma("sync", (y_out[bi, mo * 128:(mo + 1) * 128, t0:t0 + TC2],
                               f"{bi}.{c}.{mo}"), xw[mo][:, 1:TC2 + 1])

    st = [(nm, b.cnt[nm]) for nm in list(b.cnt) if nm.startswith("q_sy_st_")]
    return b.finalize(st)



LBLK = 256
LW = 128
LBAND = 384
NLAT = 64
LHD = 64
NLH = 4


def build_phase_C():
    nc = bass.Bass("TRN2", target_bir_lowering=False)
    b = Bld(nc)

    x_in = nc.dram_tensor("x", [B_PER_CORE, D, T], F32, kind="ExternalInput")
    qkv_w = nc.dram_tensor("qkvw", [D, 3 * D], F32, kind="ExternalInput")
    qkv_b = nc.dram_tensor("qkvb", [128, 3 * NDT], F32, kind="ExternalInput")
    vb_row = nc.dram_tensor("vbrow", [1, D], F32, kind="ExternalInput")
    rq_w = nc.dram_tensor("rqw", [D, 256], F32, kind="ExternalInput")
    rq_b = nc.dram_tensor("rqb", [128, 2], F32, kind="ExternalInput")
    rkv_w = nc.dram_tensor("rkvw", [D, 512], F32, kind="ExternalInput")
    rk_b = nc.dram_tensor("rkb", [128, 2], F32, kind="ExternalInput")
    rvb_row = nc.dram_tensor("rvbrow", [1, 256], F32, kind="ExternalInput")
    rout_w = nc.dram_tensor("routw", [256, D], F32, kind="ExternalInput")
    rout_b = nc.dram_tensor("routb", [128, NDT], F32, kind="ExternalInput")
    out_w = nc.dram_tensor("outw", [D, D], F32, kind="ExternalInput")
    out_b = nc.dram_tensor("outb", [128, NDT], F32, kind="ExternalInput")
    glr_w = nc.dram_tensor("glrw", [128, 2], F32, kind="ExternalInput")
    maskl_d = nc.dram_tensor("maskl", [128, 3 * LBAND], F32, kind="ExternalInput")
    masklat_d = nc.dram_tensor("masklat", [128, 16 * NLAT], F32, kind="ExternalInput")
    ident_d = nc.dram_tensor("ident", [128, 128], F32, kind="ExternalInput")
    y_out = nc.dram_tensor("y", [B_PER_CORE, D, T], F32, kind="ExternalOutput")

    KVW = [b.sb(f"KVW{i}", [128, 2 * D], BF16) for i in range(NDT)]
    WqT = [b.sb(f"WqT{i}", [128, D], BF16) for i in range(NDT)]
    RqW = [b.sb(f"RqW{i}", [128, 256], BF16) for i in range(NDT)]
    RkvW = [b.sb(f"RkvW{i}", [128, 512], BF16) for i in range(NDT)]
    RoW = [b.sb(f"RoW{i}", [128, D], BF16) for i in range(2)]
    OW = [b.sb(f"OW{i}", [128, D], BF16) for i in range(NDT)]
    qkvb_s = b.sb("qkvb_s", [128, 3 * NDT])
    vbr = b.sb("vbr", [1, D], BF16)
    rqb_s = b.sb("rqb_s", [128, 2])
    rkb_s = b.sb("rkb_s", [128, 2])
    rvbr = b.sb("rvbr", [1, 256], BF16)
    routb_s = b.sb("routb_s", [128, NDT])
    outb_s = b.sb("outb_s", [128, NDT])
    glr_s = b.sb("glr_s", [128, 2], BF16)
    maskl = b.sb("maskl_s", [128, 3 * LBAND], BF16)
    masklat = b.sb("masklat_s", [128, 16 * NLAT], BF16)
    ident = b.sb("ident_s", [128, 128], BF16)
    ones_b = b.sb("ones_b", [128, 1], BF16)
    ones_f1 = b.sb("ones_f1", [1, 128])
    ones_r128 = b.sb("ones_r128", [1, 128], BF16)

    KT = [b.sb(f"KT{i}", [128, 640], BF16) for i in range(NDT)]
    VT = [b.sb(f"VT{i}", [128, D], BF16) for i in range(5)]
    localT = [b.sb(f"localT{i}", [128, T], BF16) for i in range(NDT)]
    rqT = [b.sb(f"rqT{i}", [128, T], BF16) for i in range(2)]
    latoT = [b.sb(f"latoT{i}", [128, T], BF16) for i in range(2)]
    pooled = [b.sb(f"pooled{i}", [128, NLAT], BF16) for i in range(NDT)]
    pooled_f = [b.sb(f"pooledf{i}", [128, NLAT]) for i in range(NDT)]
    rkT = [b.sb(f"rkT{i}", [128, NLAT], BF16) for i in range(2)]
    rv_sb = b.sb("rv_sb", [NLAT, 256], BF16)
    x_t = [b.sb(f"x_t{i}", [128, TC]) for i in range(NDT)]
    xb_t = [b.sb(f"xb_t{i}", [128, TC], BF16) for i in range(NDT)]
    qblk = [b.sb(f"qblk{i}", [128, LBLK], BF16) for i in range(NDT)]
    atb = b.sb("atb", [128, LBAND], BF16)
    esb = atb
    attnT = b.sb("attnT", [128, LBAND], BF16)
    zrow = b.sb("zrow", [128, 1])
    rz = b.sb("rz", [128, 1])
    ngm = b.sb("ngm", [128, 1])
    remb = [b.sb(f"remb{i}", [128, TC], BF16) for i in range(NDT)]
    xq_t = remb
    mx = xb_t
    g_buf = b.sb("g_buf", [1, 2 * TC], BF16)
    t1 = b.sb("t1", [128, TC])
    row_m = b.sb("row_m", [1, TC])
    row_v = b.sb("row_v", [1, TC])
    row_r = b.sb("row_r", [1, TC])
    row_mr = row_v

    P = [b.ps(f"P{i}", [128, TC]) for i in range(6)]
    PT = b.ps("PT", [128, LBAND], BF16)
    PV = b.ps("PV", [128, 128])

    for i in range(NDT):
        rsl = slice(i * 128, (i + 1) * 128)
        b.dma("gpsimd", WqT[i][:], qkv_w[rsl, 0:D])
        b.dma("gpsimd", (KVW[i][:, 0:D], "k"), qkv_w[rsl, D:2 * D], f"q_gp_KVW{i}")
        b.dma("gpsimd", (KVW[i][:, D:2 * D], "v"), qkv_w[rsl, 2 * D:3 * D], f"q_gp_KVW{i}")
        b.dma("gpsimd", RqW[i][:], rq_w[rsl, :])
        b.dma("gpsimd", RkvW[i][:], rkv_w[rsl, :])
        b.dma("gpsimd", OW[i][:], out_w[rsl, :])
    for i in range(2):
        b.dma("gpsimd", RoW[i][:], rout_w[i * 128:(i + 1) * 128, :])
    b.dma("sync", qkvb_s[:], qkv_b[:])
    b.dma("gpsimd", vbr[:], vb_row[:])
    b.dma("sync", rqb_s[:], rq_b[:])
    b.dma("sync", rkb_s[:], rk_b[:])
    b.dma("gpsimd", rvbr[:], rvb_row[:])
    b.dma("sync", routb_s[:], rout_b[:])
    b.dma("sync", outb_s[:], out_b[:])
    b.dma("gpsimd", glr_s[:], glr_w[:])
    b.dma("gpsimd", maskl[:], maskl_d[:])
    b.dma("gpsimd", masklat[:], masklat_d[:])
    b.dma("gpsimd", ident[:], ident_d[:])
    b.auto("vector", lambda e: e.memset(ones_b[:], 1.0), wr=[ones_b])
    b.auto("vector", lambda e: e.memset(ones_f1[:], 1.0), wr=[ones_f1])
    b.auto("vector", lambda e: e.memset(ones_r128[:], 1.0), wr=[ones_r128])

    for bi in range(B_PER_CORE):
        for c in range(NCH):
            t0 = c * TC
            xsl = slice(t0, t0 + TC)
            for i in range(NDT):
                b.dma("sync", x_t[i][:], x_in[bi, i * 128:(i + 1) * 128, xsl])
            for i in range(NDT):
                b.auto("scalar", lambda e, i=i: e.copy(xb_t[i][:], x_t[i][:]),
                       rd=[x_t[i]], wr=[xb_t[i]])
                b.auto("scalar", lambda e, i=i: e.activation(xq_t[i][:, 0:TC], x_t[i][:], AF.Square),
                       rd=[x_t[i]], wr=[xq_t[i]])
            b.mm_group(P[4][0:1, :], [(ones_b[:], xb_t[i][:]) for i in range(NDT)])
            b.mm_group(P[5][0:1, :], [(ones_b[:], xq_t[i][:, 0:TC]) for i in range(NDT)])
            b.auto("vector", lambda e: e.tensor_single_scalar(row_m[:], P[4][0:1, :], 1.0 / D, ALU.mult),
                   rd=[P[4]], wr=[row_m])
            b.auto("vector", lambda e: e.tensor_single_scalar(row_v[:], P[5][0:1, :], 1.0 / D, ALU.mult),
                   rd=[P[5]], wr=[row_v])
            b.auto("vector", lambda e: e.tensor_mul(row_r[:], row_m[:], row_m[:]),
                   rd=[row_m], wr=[row_r])
            b.auto("vector", lambda e: e.scalar_tensor_tensor(
                row_v[:], row_v[:], 1e-5, row_r[:], ALU.add, ALU.subtract),
                rd=[row_v, row_r], wr=[row_v])
            b.auto("scalar", lambda e: e.activation(row_r[:], row_v[:], AF.Ln),
                   rd=[row_v], wr=[row_r])
            b.auto("scalar", lambda e: e.activation(row_r[:], row_r[:], AF.Exp, scale=-0.5),
                   rd=[row_r], wr=[row_r])
            b.auto("vector", lambda e: e.tensor_mul(row_mr[:], row_m[:], row_r[:]),
                   rd=[row_m, row_r], wr=[row_mr])
            b.mm_group(P[4][:], [(ones_f1[:], row_r[:])])
            b.mm_group(P[5][:], [(ones_f1[:], row_mr[:])])
            for i in range(NDT):
                b.auto("vector", lambda e, i=i: e.scalar_tensor_tensor(
                    t1[:], x_t[i][:], 1.0, P[4][:], ALU.mult, ALU.mult),
                    rd=[x_t[i], P[4]], wr=[t1])
                b.auto("vector", lambda e, i=i: e.tensor_sub(xb_t[i][:], t1[:], P[5][:]),
                       rd=[t1, P[5]], wr=[xb_t[i]])
            for i in range(NDT):
                b.auto("vector", lambda e, i=i, c=c: e.reduce_sum(
                    pooled_f[i][:, c * 16:(c + 1) * 16],
                    xb_t[i][:].rearrange("p (a w) -> p a w", w=32), axis=AX.X),
                    rd=[xb_t[i]], wr=[(pooled_f[i], c)])
            for m in range(2):
                b.mm_group(P[4][:], [(RqW[k][:, m * 128:(m + 1) * 128], xb_t[k][:])
                                     for k in range(NDT)])
                b.auto("vector", lambda e, o=rqT[m][:, xsl], m=m: e.tensor_single_scalar(
                    o, P[4][:], rqb_s[:, m:m + 1], ALU.add),
                    rd=[P[4], rqb_s], wr=[(rqT[m], c)])
            for i in range(NDT):
                if c == 0:
                    b.auto("vector", lambda e, i=i: e.memset(KT[i][:, 0:128], 0.0),
                           wr=[(KT[i], "prev")])
                else:
                    b.auto("vector", lambda e, i=i: e.tensor_copy(
                        KT[i][:, 0:128], KT[i][:, 512:640]),
                        rd=[(KT[i], "cur")], wr=[(KT[i], "prev")])
            if c == 0:
                b.auto("vector", lambda e: e.memset(VT[0][:], 0.0), wr=[VT[0]])
            else:
                b.auto("vector", lambda e: e.tensor_copy(VT[0][:], VT[4][:]),
                       rd=[VT[4]], wr=[VT[0]])
            for m in range(NDT):
                b.mm_group(P[m % 2][:], [(KVW[k][:, m * 128:(m + 1) * 128], xb_t[k][:])
                                         for k in range(NDT)])
                b.auto("vector", lambda e, m=m: e.tensor_single_scalar(
                    KT[m][:, 128:640], P[m % 2][:], qkvb_s[:, NDT + m:NDT + m + 1], ALU.add),
                    rd=[P[m % 2], qkvb_s], wr=[(KT[m], "cur")])
            for tt in range(4):
                tsl = slice(tt * 128, (tt + 1) * 128)
                for ch in range(2):
                    b.mm_group(P[2 + ch][:],
                               [(xb_t[k][:, tsl], KVW[k][:, D + ch * 512:D + (ch + 1) * 512])
                                for k in range(NDT)]
                               + [(ones_r128[0:1, 0:128], vbr[0:1, ch * 512:(ch + 1) * 512])])
                    b.auto("vector", lambda e, tt=tt, ch=ch: e.tensor_copy(
                        VT[1 + tt][:, ch * 512:(ch + 1) * 512], P[2 + ch][:]),
                        rd=[P[2 + ch]], wr=[(VT[1 + tt], ch)])
            for wloc in range(2):
                bb = 2 * c + wloc
                bq0 = wloc * LBLK
                for m in range(NDT):
                    b.mm_group(P[4][:, 0:LBLK],
                               [(WqT[k][:, m * 128:(m + 1) * 128], xb_t[k][:, bq0:bq0 + LBLK])
                                for k in range(NDT)])
                    b.auto("vector", lambda e, m=m: e.tensor_single_scalar(
                        qblk[m][:], P[4][:, 0:LBLK], qkvb_s[:, m:m + 1], ALU.add),
                        rd=[P[4], qkvb_s], wr=[qblk[m]])
                kroll0 = 256 * wloc
                for qt in range(2):
                    mv = qt if bb > 0 else (2 if qt == 0 else 1)
                    msl = slice(mv * LBAND, (mv + 1) * LBAND)
                    for h in range(HEADS):
                        m = h // 2
                        hr = slice((h % 2) * 64, (h % 2) * 64 + 64)
                        b.mm_group(P[5][:, 0:LBAND],
                                   [(qblk[m][hr, qt * 128:(qt + 1) * 128],
                                     KT[m][hr, kroll0:kroll0 + LBAND])])
                        b.auto("vector", lambda e: e.reduce_max(
                            ngm[:], P[5][:, 0:LBAND], axis=AX.X, negate=True),
                            rd=[P[5]], wr=[ngm])
                        b.auto("vector", lambda e: e.tensor_single_scalar(
                            ngm[:], ngm[:], 0.125, ALU.mult), rd=[ngm], wr=[ngm])
                        b.auto("scalar", lambda e: e.activation(
                            atb[:], P[5][:, 0:LBAND], AF.Exp, bias=ngm[:], scale=0.125),
                            rd=[P[5], ngm], wr=[atb])
                        b.auto("vector", lambda e, msl=msl: e.scalar_tensor_tensor(
                            atb[:], atb[:], 1.0, maskl[:, msl], ALU.mult, ALU.mult,
                            accum_out=zrow[:]),
                            rd=[atb, maskl], wr=[atb, zrow])
                        b.auto("vector", lambda e: e.tensor_single_scalar(
                            zrow[:], zrow[:], 1e-30, ALU.max), rd=[zrow], wr=[zrow])
                        b.auto("vector", lambda e: e.reciprocal(rz[:], zrow[:]),
                               rd=[zrow], wr=[rz])
                        b.auto("vector", lambda e: e.tensor_single_scalar(
                            atb[:], atb[:], rz[:], ALU.mult), rd=[atb, rz], wr=[atb])
                        for j in range(3):
                            b.auto("tensor", lambda e, j=j: e.transpose(
                                PT[:, j * 128:(j + 1) * 128], atb[:, j * 128:(j + 1) * 128],
                                ident[:]), rd=[atb, ident], wr=[(PT, j)])
                        b.auto("vector", lambda e: e.tensor_copy(attnT[:], PT[:]),
                               rd=[PT, (PT, 0), (PT, 1), (PT, 2)], wr=[attnT])
                        vr0 = 2 * wloc
                        b.mm_group(PV[hr, :],
                                   [(VT[vr0 + j][:, h * 64:(h + 1) * 64],
                                     attnT[:, j * 128:(j + 1) * 128]) for j in range(3)])
                        if h % 2 == 1:
                            b.auto("vector", lambda e, m=m, bb=bb, qt=qt: e.tensor_copy(
                                localT[m][:, bb * LBLK + qt * 128:bb * LBLK + (qt + 1) * 128],
                                PV[:]), rd=[PV], wr=[(localT[m], f"{bb}.{qt}")])
        for i in range(NDT):
            b.auto("vector", lambda e, i=i: e.tensor_single_scalar(
                pooled[i][:], pooled_f[i][:], 1.0 / 32, ALU.mult),
                rd=[pooled_f[i]] + [(pooled_f[i], c) for c in range(NCH)], wr=[pooled[i]])
        for m in range(2):
            b.mm_group(P[0][:, 0:NLAT], [(RkvW[k][:, m * 128:(m + 1) * 128], pooled[k][:])
                                         for k in range(NDT)])
            b.auto("vector", lambda e, m=m: e.tensor_single_scalar(
                rkT[m][:, :], P[0][:, 0:NLAT], rkb_s[:, m:m + 1], ALU.add),
                rd=[P[0], rkb_s], wr=[rkT[m]])
        b.mm_group(P[1][0:NLAT, 0:256],
                   [(pooled[k][:], RkvW[k][:, 256:512]) for k in range(NDT)]
                   + [(ones_r128[0:1, 0:NLAT], rvbr[:])])
        b.auto("vector", lambda e: e.tensor_copy(rv_sb[:], P[1][0:NLAT, 0:256]),
               rd=[P[1]], wr=[rv_sb])
        for tt in range(16):
            q0 = tt * 128
            for lh in range(NLH):
                mt = lh // 2
                hr = slice((lh % 2) * 64, (lh % 2) * 64 + 64)
                b.mm_group(P[4][:, 0:NLAT], [(rqT[mt][hr, q0:q0 + 128], rkT[mt][hr, :])])
                b.auto("vector", lambda e: e.reduce_max(
                    ngm[:], P[4][:, 0:NLAT], axis=AX.X, negate=True),
                    rd=[P[4]], wr=[ngm])
                b.auto("vector", lambda e: e.tensor_single_scalar(
                    ngm[:], ngm[:], 0.125, ALU.mult), rd=[ngm], wr=[ngm])
                b.auto("scalar", lambda e: e.activation(
                    atb[:, 0:NLAT], P[4][:, 0:NLAT], AF.Exp, bias=ngm[:], scale=0.125),
                    rd=[P[4], ngm], wr=[atb])
                b.auto("vector", lambda e, tt=tt: e.scalar_tensor_tensor(
                    atb[:, 0:NLAT], atb[:, 0:NLAT], 1.0, masklat[:, tt * NLAT:(tt + 1) * NLAT],
                    ALU.mult, ALU.mult, accum_out=zrow[:]),
                    rd=[atb, masklat], wr=[atb, zrow])
                b.auto("vector", lambda e: e.tensor_single_scalar(
                    zrow[:], zrow[:], 1e-30, ALU.max), rd=[zrow], wr=[zrow])
                b.auto("vector", lambda e: e.reciprocal(rz[:], zrow[:]), rd=[zrow], wr=[rz])
                b.auto("vector", lambda e: e.tensor_single_scalar(
                    atb[:, 0:NLAT], atb[:, 0:NLAT], rz[:], ALU.mult), rd=[atb, rz], wr=[atb])
                b.auto("tensor", lambda e: e.transpose(
                    PT[0:NLAT, 0:128], atb[:, 0:NLAT], ident[:]),
                    rd=[atb, ident], wr=[PT])
                b.auto("vector", lambda e: e.tensor_copy(attnT[0:NLAT, 0:128], PT[0:NLAT, 0:128]),
                       rd=[PT], wr=[attnT])
                b.mm_group(PV[hr, :], [(rv_sb[:, lh * 64:(lh + 1) * 64], attnT[0:NLAT, 0:128])])
                if lh % 2 == 1:
                    b.auto("vector", lambda e, mt=mt, q0=q0: e.tensor_copy(
                        latoT[mt][:, q0:q0 + 128], PV[:]), rd=[PV], wr=[(latoT[mt], tt)])
        for c in range(NCH):
            t0 = c * TC
            xsl = slice(t0, t0 + TC)
            for m in range(NDT):
                b.mm_group(P[m % 2][:], [(RoW[k][:, m * 128:(m + 1) * 128],
                                          latoT[k][:, xsl]) for k in range(2)])
                b.auto("vector", lambda e, m=m: e.tensor_single_scalar(
                    remb[m][:], P[m % 2][:], routb_s[:, m:m + 1], ALU.add),
                    rd=[P[m % 2], routb_s], wr=[remb[m]])
            for m in range(NDT):
                for hh in range(2):
                    hr = slice(hh * 64, hh * 64 + 64)
                    b.mm_group(P[2][0:1, :],
                               [(glr_s[hr, 0:1], localT[m][hr, xsl]),
                                (glr_s[hr, 1:2], remb[m][hr, :])])
                    b.auto("scalar", lambda e, hh=hh: e.activation(
                        g_buf[0:1, hh * TC:(hh + 1) * TC], P[2][0:1, :], AF.Sigmoid),
                        rd=[P[2]], wr=[(g_buf, hh)])
                for hh in range(2):
                    b.mm_group(P[3][hh * 64:hh * 64 + 64, :],
                               [(ones_r128[0:1, 0:64], g_buf[0:1, hh * TC:(hh + 1) * TC])])
                b.auto("vector", lambda e, src=localT[m][:, xsl], m=m: e.tensor_tensor(
                    t1[:], src, remb[m][:], ALU.subtract),
                    rd=[localT[m], remb[m]], wr=[t1])
                b.auto("vector", lambda e, m=m: e.tensor_tensor(
                    t1[:], t1[:], P[3][:], ALU.mult), rd=[t1, P[3]], wr=[t1])
                b.auto("vector", lambda e, m=m: e.tensor_tensor(
                    mx[m][:], t1[:], remb[m][:], ALU.add), rd=[t1, remb[m]], wr=[mx[m]])
            for mo in range(NDT):
                b.mm_group(P[4 + mo % 2][:], [(OW[k][:, mo * 128:(mo + 1) * 128], mx[k][:])
                                              for k in range(NDT)])
                b.dma("sync", x_t[mo][:], x_in[bi, mo * 128:(mo + 1) * 128, xsl],
                      sem=f"q_sy_x2_{mo}")
                b.auto("vector", lambda e, mo=mo: e.scalar_tensor_tensor(
                    x_t[mo][:], P[4 + mo % 2][:], outb_s[:, mo:mo + 1], x_t[mo][:],
                    ALU.add, ALU.add),
                    rd=[P[4 + mo % 2], outb_s, x_t[mo]], wr=[x_t[mo]])
                b.dma("sync", (y_out[bi, mo * 128:(mo + 1) * 128, xsl], f"{bi}.{c}.{mo}"),
                      x_t[mo][:])

    st = [(nm, b.cnt[nm]) for nm in list(b.cnt) if nm.startswith("q_sy_st_")]
    return b.finalize(st)


def _prep_phase_C(inputs):
    d = {}
    d["qkvw"] = np.ascontiguousarray(np.asarray(inputs["L1_attn_qkv_w"]))
    qb = np.asarray(inputs["L1_attn_qkv_b"])
    d["qkvb"] = np.ascontiguousarray(qb[:2 * D].reshape(2 * NDT, 128).T)
    d["qkvb"] = np.concatenate(
        [d["qkvb"], qb[2 * D:].reshape(NDT, 128).T], axis=1).astype(np.float32)
    d["qkvb"] = np.ascontiguousarray(d["qkvb"])
    d["vbrow"] = np.ascontiguousarray(qb[2 * D:][None, :])
    d["rqw"] = np.ascontiguousarray(np.asarray(inputs["L1_attn_rq_w"]))
    d["rqb"] = np.ascontiguousarray(np.asarray(inputs["L1_attn_rq_b"]).reshape(2, 128).T)
    d["rkvw"] = np.ascontiguousarray(np.asarray(inputs["L1_attn_rkv_w"]))
    rkvb = np.asarray(inputs["L1_attn_rkv_b"])
    d["rkb"] = np.ascontiguousarray(rkvb[:256].reshape(2, 128).T)
    d["rvbrow"] = np.ascontiguousarray(rkvb[256:][None, :])
    d["routw"] = np.ascontiguousarray(np.asarray(inputs["L1_attn_rout_w"]))
    d["routb"] = np.ascontiguousarray(np.asarray(inputs["L1_attn_rout_b"]).reshape(NDT, 128).T)
    d["outw"] = np.ascontiguousarray(np.asarray(inputs["L1_attn_out_w"]))
    d["outb"] = np.ascontiguousarray(np.asarray(inputs["L1_attn_out_b"]).reshape(NDT, 128).T)
    glr = np.concatenate(
        [np.asarray(inputs["L1_attn_gl_w"]), np.asarray(inputs["L1_attn_gr_w"])], axis=1)
    d["glrw"] = np.ascontiguousarray(np.tile(glr, (2, 1)))
    i = np.arange(128)[:, None]
    jj = np.arange(LBAND)[None, :]
    m0 = ((jj >= i) & (jj <= i + 127)).astype(np.float32)
    m1 = ((jj >= 128 + i) & (jj <= 255 + i)).astype(np.float32)
    m2 = m0 * (jj >= LW)
    d["maskl"] = np.ascontiguousarray(np.concatenate([m0, m1, m2], axis=1))
    mm = np.zeros((128, 16 * NLAT), dtype=np.float32)
    for tt in range(16):
        lim = 4 * tt + (np.arange(128) // 32)
        mm[:, tt * NLAT:(tt + 1) * NLAT] = (np.arange(NLAT)[None, :] < lim[:, None])
    d["masklat"] = np.ascontiguousarray(mm)
    d["ident"] = np.eye(128, dtype=np.float32)
    return d



NEXP = 8
DFF2 = 4096
NFF = DFF2 // 128


def build_phase_D():
    nc = bass.Bass("TRN2", target_bir_lowering=False)
    b = Bld(nc)

    x_in = nc.dram_tensor("x", [B_PER_CORE, D, T], F32, kind="ExternalInput")
    rw_d = nc.dram_tensor("rw", [D, NEXP], F32, kind="ExternalInput")
    w1_d = nc.dram_tensor("w1", [NEXP, D, DFF2], F32, kind="ExternalInput")
    b1_d = nc.dram_tensor("b1", [NEXP, 128, NFF], F32, kind="ExternalInput")
    w2_d = nc.dram_tensor("w2", [NEXP, DFF2, D], F32, kind="ExternalInput")
    b2_d = nc.dram_tensor("b2", [NEXP, 128, NDT], F32, kind="ExternalInput")
    gam_d = nc.dram_tensor("gam", [NEXP, 128, NDT], F32, kind="ExternalInput")
    bet_d = nc.dram_tensor("bet", [NEXP, 128, NDT], F32, kind="ExternalInput")
    identf_d = nc.dram_tensor("identf", [128, 128], F32, kind="ExternalInput")
    y_out = nc.dram_tensor("y", [B_PER_CORE, D, T], F32, kind="ExternalOutput")
    xn_scr = nc.dram_tensor("xnscr", [B_PER_CORE, D, T], BF16, kind="Internal")
    g_scr = nc.dram_tensor("gscr", [B_PER_CORE, NEXP, T], F32, kind="Internal")

    W1 = [b.sb(f"W1_{k}", [128, DFF2], BF16) for k in range(NDT)]
    W2 = [b.sb(f"W2_{j}", [128, D], BF16) for j in range(NFF)]
    rw_s = b.sb("rw_s", [128, NEXP * NDT])
    b1_s = b.sb("b1_s", [128, NFF])
    b2_s = b.sb("b2_s", [128, NDT])
    gam_s = b.sb("gam_s", [128, NDT])
    bet_s = b.sb("bet_s", [128, NDT])
    identf = b.sb("identf_s", [128, 128])
    ones_b = b.sb("ones_b", [128, 1], BF16)

    x_t = [b.sb(f"x_t{i}", [128, TC]) for i in range(NDT)]
    xnf = [b.sb(f"xnf{i}", [128, TC]) for i in range(NDT)]
    xb_t = [b.sb(f"xb_t{i}", [128, TC], BF16) for i in range(NDT)]
    hbuf = [b.sb(f"hb{j}", [128, TC], BF16) for j in range(NFF)]
    r_f = xnf
    sqb = b.sb("sqb", [128, TC], BF16)
    rb = sqb
    xnc = xb_t
    gt_t = b.sb("gt_t", [NEXP, 128], BF16)
    gfin = b.sb("gfin", [128, NEXP])
    g_sb = b.sb("g_sb", [1, TC], BF16)
    ones_bb = b.sb("ones_bb", [1, 128], BF16)
    gate_t = b.sb("gate_t", [128, NEXP])
    m1c = b.sb("m1c", [128, 1])
    m2c = b.sb("m2c", [128, 1])
    w1c = b.sb("w1c", [128, 1])
    w2c = b.sb("w2c", [128, 1])
    msk1 = b.sb("msk1", [128, NEXP])
    msk2 = b.sb("msk2", [128, NEXP])
    t1 = b.sb("t1", [128, TC])
    row_m = b.sb("row_m", [1, TC], BF16)
    row_v = b.sb("row_v", [1, TC], BF16)

    PH = [b.ps(f"PH{i}", [128, TC]) for i in range(2)]
    PW = [b.ps(f"PW{i}", [128, TC]) for i in range(2)]
    PB1 = b.ps("PB1", [128, TC])
    PB2 = b.ps("PB2", [128, TC])
    PGB = b.ps("PGB", [128, TC])
    PR8 = b.ps("PR8", [128, 128])

    for i in range(NDT):
        b.dma("sync", rw_s[:, i * NEXP:(i + 1) * NEXP], rw_d[i * 128:(i + 1) * 128, :],
              sem="q_sy_rw")
    b.dma("sync", identf[:], identf_d[:])
    b.auto("vector", lambda e: e.memset(ones_b[:], 1.0), wr=[ones_b])
    b.auto("vector", lambda e: e.memset(ones_bb[:], 1.0), wr=[ones_bb])

    for bi in range(B_PER_CORE):
        for c in range(NCH):
            t0 = c * TC
            xsl = slice(t0, t0 + TC)
            for i in range(NDT):
                b.dma("sync", x_t[i][:], x_in[bi, i * 128:(i + 1) * 128, xsl])
            for i in range(NDT):
                b.auto("scalar", lambda e, i=i: e.copy(xb_t[i][:], x_t[i][:]),
                       rd=[x_t[i]], wr=[xb_t[i]])
                b.auto("scalar", lambda e, i=i: e.activation(sqb[:], x_t[i][:], AF.Square),
                       rd=[x_t[i]], wr=[sqb])
                b.auto("tensor", lambda e, i=i: e.matmul(
                    PB2[0:1, :], ones_b[:], sqb[:], start=(i == 0), stop=(i == NDT - 1)),
                    rd=[ones_b, sqb], wr=([PB2] if i in (0, NDT - 1) else []))
            b.mm_group(PB1[0:1, :], [(ones_b[:], xb_t[i][:]) for i in range(NDT)])
            b.auto("vector", lambda e: e.tensor_single_scalar(row_m[:], PB1[0:1, :], 1.0 / D, ALU.mult),
                   rd=[PB1], wr=[row_m])
            b.auto("vector", lambda e: e.tensor_single_scalar(row_v[:], PB2[0:1, :], 1.0 / D, ALU.mult),
                   rd=[PB2], wr=[row_v])
            b.auto("vector", lambda e: e.tensor_mul(t1[0:1, :], row_m[:], row_m[:]),
                   rd=[row_m], wr=[(t1, "row")])
            b.auto("vector", lambda e: e.scalar_tensor_tensor(
                row_v[:], row_v[:], 1e-5, t1[0:1, :], ALU.add, ALU.subtract),
                rd=[row_v, (t1, "row")], wr=[row_v])
            b.auto("scalar", lambda e: e.activation(row_v[:], row_v[:], AF.Ln),
                   rd=[row_v], wr=[row_v])
            b.auto("scalar", lambda e: e.activation(row_v[:], row_v[:], AF.Exp, scale=-0.5),
                   rd=[row_v], wr=[row_v])
            b.auto("vector", lambda e: e.tensor_mul(row_m[:], row_m[:], row_v[:]),
                   rd=[row_m, row_v], wr=[row_m])
            b.mm_group(PB1[:], [(ones_bb[:], row_v[:])])
            b.mm_group(PB2[:], [(ones_bb[:], row_m[:])])
            for i in range(NDT):
                b.auto("vector", lambda e, i=i: e.scalar_tensor_tensor(
                    t1[:], x_t[i][:], 1.0, PB1[:], ALU.mult, ALU.mult),
                    rd=[x_t[i], PB1], wr=[t1])
                b.auto("vector", lambda e, i=i: e.tensor_sub(xnf[i][:], t1[:], PB2[:]),
                       rd=[t1, PB2], wr=[xnf[i]])
                b.auto("scalar", lambda e, i=i: e.copy(xb_t[i][:], xnf[i][:]),
                       rd=[xnf[i]], wr=[xb_t[i]])
                b.dma("sync", (xn_scr[bi, i * 128:(i + 1) * 128, xsl], f"{bi}.{c}.{i}"),
                      xb_t[i][:])
            for tt in range(4):
                tloc = slice(tt * 128, (tt + 1) * 128)
                b.mm_group(PR8[:, 0:NEXP],
                           [(xnf[k][:, tloc], rw_s[:, k * NEXP:(k + 1) * NEXP])
                            for k in range(NDT)])
                b.auto("vector", lambda e: e.reduce_max(m1c[:], PR8[:, 0:NEXP], axis=AX.X),
                       rd=[PR8], wr=[m1c])
                b.auto("vector", lambda e: e.tensor_single_scalar(
                    gate_t[:], PR8[:, 0:NEXP], m1c[:], ALU.is_ge),
                    rd=[PR8, m1c], wr=[gate_t])
                b.auto("vector", lambda e: e.tensor_copy(msk1[:], gate_t[:]),
                       rd=[gate_t], wr=[msk1])
                b.auto("vector", lambda e: e.scalar_tensor_tensor(
                    gate_t[:], msk1[:], -1e30, PR8[:, 0:NEXP], ALU.mult, ALU.add),
                    rd=[msk1, PR8], wr=[gate_t])
                b.auto("vector", lambda e: e.reduce_max(m2c[:], gate_t[:], axis=AX.X),
                       rd=[gate_t], wr=[m2c])
                b.auto("vector", lambda e: e.tensor_single_scalar(
                    msk2[:], gate_t[:], m2c[:], ALU.is_ge),
                    rd=[gate_t, m2c], wr=[msk2])
                b.auto("vector", lambda e: e.tensor_sub(w1c[:], m1c[:], m2c[:]),
                       rd=[m1c, m2c], wr=[w1c])
                b.auto("scalar", lambda e: e.activation(w1c[:], w1c[:], AF.Sigmoid),
                       rd=[w1c], wr=[w1c])
                b.auto("vector", lambda e: e.tensor_scalar(
                    w2c[:], w1c[:], -1.0, 1.0, ALU.mult, ALU.add),
                    rd=[w1c], wr=[w2c])
                b.auto("vector", lambda e: e.tensor_single_scalar(
                    msk1[:], msk1[:], w1c[:], ALU.mult), rd=[msk1, w1c], wr=[msk1])
                b.auto("vector", lambda e: e.scalar_tensor_tensor(
                    gfin[:], msk2[:], w2c[:], msk1[:], ALU.mult, ALU.add),
                    rd=[msk2, w2c, msk1], wr=[gfin])
                b.auto("tensor", lambda e: e.transpose(
                    PR8[0:NEXP, 0:128], gfin[:, 0:NEXP], identf[:]),
                    rd=[gfin, identf], wr=[PR8])
                b.auto("scalar", lambda e: e.copy(gt_t[:], PR8[0:NEXP, 0:128]),
                       rd=[PR8], wr=[gt_t])
                b.dma("gpsimd", (g_scr[bi, :, t0 + tt * 128:t0 + (tt + 1) * 128],
                                 f"{bi}.{c}.{tt}"), gt_t[:], sem="q_gp_st_gt")
    for e_i in range(NEXP):
        for k in range(NDT):
            b.dma("gpsimd", W1[k][:], w1_d[e_i, k * 128:(k + 1) * 128, :])
        for j in range(NFF):
            b.dma("gpsimd", W2[j][:], w2_d[e_i, j * 128:(j + 1) * 128, :])
        b.dma("sync", b1_s[:], b1_d[e_i], sem="q_sy_b1")
        b.dma("sync", b2_s[:], b2_d[e_i], sem="q_sy_b2")
        b.dma("sync", gam_s[:], gam_d[e_i], sem="q_sy_gam")
        b.dma("sync", bet_s[:], bet_d[e_i], sem="q_sy_bet")
        for bi in range(B_PER_CORE):
            for c in range(NCH):
                t0 = c * TC
                xsl = slice(t0, t0 + TC)
                for i in range(NDT):
                    rsl = slice(i * 128, (i + 1) * 128)
                    b.dma("sync", xnc[i][:], xn_scr[bi, rsl, xsl])
                    b.dma("sync", x_t[i][:], x_in[bi, rsl, xsl])
                b.dma("gpsimd", g_sb[:], g_scr[bi, e_i, xsl])
                b.mm_group(PGB[:], [(ones_bb[:], g_sb[:])])
                for j in range(NFF):
                    b.mm_group(PH[j % 2][:], [(W1[k][:, j * 128:(j + 1) * 128], xnc[k][:])
                                              for k in range(NDT)])
                    b.auto("scalar", lambda e, j=j: e.activation(
                        hbuf[j][:], PH[j % 2][:], AF.Gelu, bias=b1_s[:, j:j + 1]),
                        rd=[PH[j % 2], b1_s], wr=[hbuf[j]])
                for mo in range(NDT):
                    b.mm_group(PW[mo % 2][:], [(W2[j][:, mo * 128:(mo + 1) * 128], hbuf[j][:])
                                               for j in range(NFF)])
                    b.auto("vector", lambda e, mo=mo: e.scalar_tensor_tensor(
                        r_f[mo][:], PW[mo % 2][:], b2_s[:, mo:mo + 1], xnc[mo][:],
                        ALU.add, ALU.add),
                        rd=[PW[mo % 2], b2_s, xnc[mo]], wr=[r_f[mo]])
                    b.auto("scalar", lambda e, mo=mo: e.activation(sqb[:], r_f[mo][:], AF.Square),
                           rd=[r_f[mo]], wr=[sqb])
                    b.auto("tensor", lambda e, mo=mo: e.matmul(
                        PB2[0:1, :], ones_b[:], sqb[:], start=(mo == 0), stop=(mo == NDT - 1)),
                        rd=[ones_b, sqb], wr=([PB2] if mo in (0, NDT - 1) else []))
                    b.auto("scalar", lambda e, mo=mo: e.copy(rb[:], r_f[mo][:]),
                           rd=[r_f[mo]], wr=[rb])
                    b.auto("tensor", lambda e, mo=mo: e.matmul(
                        PB1[0:1, :], ones_b[:], rb[:], start=(mo == 0), stop=(mo == NDT - 1)),
                        rd=[ones_b, rb], wr=([PB1] if mo in (0, NDT - 1) else []))
                b.auto("vector", lambda e: e.tensor_single_scalar(row_m[:], PB1[0:1, :], 1.0 / D, ALU.mult),
                       rd=[PB1], wr=[row_m])
                b.auto("vector", lambda e: e.tensor_single_scalar(row_v[:], PB2[0:1, :], 1.0 / D, ALU.mult),
                       rd=[PB2], wr=[row_v])
                b.auto("vector", lambda e: e.tensor_mul(t1[0:1, :], row_m[:], row_m[:]),
                       rd=[row_m], wr=[(t1, "row")])
                b.auto("vector", lambda e: e.scalar_tensor_tensor(
                    row_v[:], row_v[:], 1e-6, t1[0:1, :], ALU.add, ALU.subtract),
                    rd=[row_v, (t1, "row")], wr=[row_v])
                b.auto("scalar", lambda e: e.activation(row_v[:], row_v[:], AF.Ln),
                       rd=[row_v], wr=[row_v])
                b.auto("scalar", lambda e: e.activation(row_v[:], row_v[:], AF.Exp, scale=-0.5),
                       rd=[row_v], wr=[row_v])
                b.auto("vector", lambda e: e.tensor_mul(row_m[:], row_m[:], row_v[:]),
                       rd=[row_m, row_v], wr=[row_m])
                b.mm_group(PB1[:], [(ones_bb[:], row_v[:])])
                b.mm_group(PB2[:], [(ones_bb[:], row_m[:])])
                for mo in range(NDT):
                    b.auto("vector", lambda e, mo=mo: e.scalar_tensor_tensor(
                        t1[:], r_f[mo][:], 1.0, PB1[:], ALU.mult, ALU.mult),
                        rd=[r_f[mo], PB1], wr=[t1])
                    b.auto("vector", lambda e: e.tensor_sub(t1[:], t1[:], PB2[:]),
                           rd=[t1, PB2], wr=[t1])
                    b.auto("vector", lambda e, mo=mo: e.tensor_scalar(
                        t1[:], t1[:], gam_s[:, mo:mo + 1], bet_s[:, mo:mo + 1],
                        ALU.mult, ALU.add),
                        rd=[t1, gam_s, bet_s], wr=[t1])
                    b.auto("vector", lambda e: e.tensor_tensor(
                        t1[:], t1[:], PGB[:], ALU.mult), rd=[t1, PGB], wr=[t1])
                    rsl = slice(mo * 128, (mo + 1) * 128)
                    if e_i == 0:
                        b.auto("vector", lambda e, mo=mo: e.tensor_tensor(
                            t1[:], t1[:], x_t[mo][:], ALU.add),
                            rd=[t1, x_t[mo]], wr=[t1])
                        b.dma("sync", (y_out[bi, rsl, xsl], f"{bi}.{c}.{mo}"), t1[:],
                              sem="q_sy_st_y")
                    else:
                        b.auto("gpsimd",
                               lambda e, o=y_out[bi, rsl, xsl]: e.dma_start(
                                   out=o, in_=t1[:], accum_op=ALU.add),
                               rd=[t1, (y_out, f"{bi}.{c}.{mo}")],
                               wr=[(y_out, f"{bi}.{c}.{mo}")],
                               sem="q_gp_st_y", incn=16)

    st = [(nm, b.cnt[nm]) for nm in list(b.cnt) if nm.startswith("q_sy_st_")]
    return b.finalize(st)


def _prep_phase_D(inputs):
    d = {}
    d["rw"] = np.ascontiguousarray(np.asarray(inputs["L1_ffn_router_w"]))
    d["w1"] = np.ascontiguousarray(np.asarray(inputs["L1_ffn_w1"]))
    d["b1"] = np.ascontiguousarray(
        np.asarray(inputs["L1_ffn_b1"]).reshape(NEXP, NFF, 128).transpose(0, 2, 1))
    d["w2"] = np.ascontiguousarray(np.asarray(inputs["L1_ffn_w2"]))
    d["b2"] = np.ascontiguousarray(
        np.asarray(inputs["L1_ffn_b2"]).reshape(NEXP, NDT, 128).transpose(0, 2, 1))
    d["gam"] = np.ascontiguousarray(
        np.asarray(inputs["L1_ffn_gamma"]).reshape(NEXP, NDT, 128).transpose(0, 2, 1))
    d["bet"] = np.ascontiguousarray(
        np.asarray(inputs["L1_ffn_beta"]).reshape(NEXP, NDT, 128).transpose(0, 2, 1))
    d["identf"] = np.eye(128, dtype=np.float32)
    return d



def _sel_matrices():
    R = np.zeros((128, 16, 128), dtype=np.float32)
    for k in range(16):
        for p in range(128):
            R[8 * k + p // 16, k, p] = 1.0
    S = np.zeros((128, 16, 128), dtype=np.float32)
    for k in range(16):
        for p in range(128):
            S[p, k, 8 * k + p // 16] = 1.0
    RB = np.zeros((16, 128), dtype=np.float32)
    for p in range(128):
        RB[p % 16, p] = 1.0
    return R.reshape(128, 16 * 128), S.reshape(128, 16 * 128), RB


def _rotary_tables():
    half = np.arange(0, 64, 2, dtype=np.float64)
    inv_freq = 1.0 / 10000.0 ** (half / 64.0)
    t = np.arange(T, dtype=np.float64)
    freqs = t[:, None] * inv_freq[None, :]
    base_scale = (half + 0.4 * 64) / (1.4 * 64)
    power = (t - T // 2) / (WIN // 2)
    scale = base_scale[None, :] ** power[:, None]
    j = np.arange(64)
    fr = freqs[:, j % 32]
    sc = scale[:, j % 32]
    sign = np.where(j < 32, -1.0, 1.0)
    cs_q = (np.cos(fr) * sc).T
    ss_q = (np.sin(fr) * sc).T * sign[:, None]
    cs_k = (np.cos(fr) / sc).T
    ss_k = (np.sin(fr) / sc).T * sign[:, None]
    tile2 = lambda a: np.concatenate([a, a], axis=0).astype(np.float32)
    return tile2(cs_q), tile2(ss_q), tile2(cs_k), tile2(ss_k)


def _masks3():
    i = np.arange(128)[:, None]
    jj = np.arange(BAND)[None, :]
    gen = ((jj >= i) & (jj <= i + 256)).astype(np.float32)
    m1 = gen * (jj >= 256)
    m2 = gen * (jj >= 128)
    return np.concatenate([gen, m1, m2], axis=1).astype(np.float32)


def _prep_phase_B_shared():
    d = {}
    d["csq"], d["ssq"], d["csk"], d["ssk"] = _rotary_tables()
    d["mask3"] = _masks3()
    bsel = np.zeros((128, 128), dtype=np.float32)
    for m in range(8):
        for p in range(128):
            bsel[p, m * 16 + 2 * m + p // 64] = 1.0
    d["bsel"] = bsel
    psel = np.zeros((128, 2), dtype=np.float32)
    psel[0:64, 0] = 1.0
    psel[64:128, 1] = 1.0
    d["psel"] = psel
    d["pselT"] = np.ascontiguousarray(psel.T)
    d["ident"] = np.eye(128, dtype=np.float32)
    return d


def _prep_phase_B_attn(inputs, l):
    d = {}
    Wqkv = np.asarray(inputs["L0_ffn_Wqkv"])[l]
    wq = Wqkv[:, 0:D]
    wk = Wqkv[:, D:2 * D]
    d["wv"] = np.ascontiguousarray(Wqkv[:, 2 * D:3 * D])
    c = np.arange(D)
    perm = (c // 64) * 64 + ((c % 64) + 32) % 64
    d["wq"] = np.ascontiguousarray(wq)
    d["wk"] = np.ascontiguousarray(wk)
    d["wqr"] = np.ascontiguousarray(wq[:, perm])
    d["wkr"] = np.ascontiguousarray(wk[:, perm])
    d["wo"] = np.ascontiguousarray(np.asarray(inputs["L0_ffn_Wo"])[l])
    qs = np.asarray(inputs["L0_ffn_q_scale"])[l]
    ks = np.asarray(inputs["L0_ffn_k_scale"])[l]
    p64 = ((np.arange(64) + 32) % 64)
    d["qsc"] = np.ascontiguousarray(np.stack(
        [np.tile(qs, 2), np.tile(qs[p64], 2)], axis=1))
    d["ksc"] = np.ascontiguousarray(np.stack(
        [np.tile(ks, 2), np.tile(ks[p64], 2)], axis=1))
    return d


def _prep_phase_B_ffn(inputs, l):
    d = {}
    d["w1"] = np.ascontiguousarray(np.asarray(inputs["L0_ffn_W1"])[l])
    d["w2"] = np.ascontiguousarray(np.asarray(inputs["L0_ffn_W2"])[l])
    d["lng"] = np.ascontiguousarray(np.asarray(inputs["L0_ffn_ln_g"])[l].reshape(NDT, 128).T)
    d["lnb"] = np.ascontiguousarray(np.asarray(inputs["L0_ffn_ln_b"])[l].reshape(NDT, 128).T)
    b1 = np.asarray(inputs["L0_ffn_b1"])[l]
    b1a = np.zeros((NJ * 128,), dtype=np.float32)
    b1a[:DFF] = b1[:DFF]
    b1g = np.zeros((NJ * 128,), dtype=np.float32)
    b1g[:DFF] = b1[DFF:]
    d["b1a"] = np.ascontiguousarray(b1a.reshape(NJ, 128).T)
    d["b1g"] = np.ascontiguousarray(b1g.reshape(NJ, 128).T)
    d["b2"] = np.ascontiguousarray(np.asarray(inputs["L0_ffn_b2"])[l].reshape(NDT, 128).T)
    return d


def _prep_phase_A(inputs):
    d = {}
    d["w_inT"] = np.ascontiguousarray(np.asarray(inputs["L0_attn_in_proj_w"]).T)
    d["conv_w"] = np.ascontiguousarray(np.asarray(inputs["L0_attn_conv_w"])[:, 0, :])
    d["conv_b"] = np.ascontiguousarray(np.asarray(inputs["L0_attn_conv_b"])[:, None])
    d["w_xpT"] = np.ascontiguousarray(np.asarray(inputs["L0_attn_x_proj_w"]).T)
    d["dtwb"] = np.ascontiguousarray(
        np.stack([np.asarray(inputs["L0_attn_dt_proj_w"])[:, 0],
                  np.asarray(inputs["L0_attn_dt_proj_b"])]))
    alog = np.asarray(inputs["L0_attn_A_log"])
    d["a_rep"] = np.ascontiguousarray(
        alog.reshape(8, 16, 8, 16).transpose(2, 3, 0, 1).reshape(128, 128))
    R, S, RB = _sel_matrices()
    d["r_sel"], d["s_sel"], d["rb16"] = R, S, RB
    d["w_outT"] = np.ascontiguousarray(np.asarray(inputs["L0_attn_out_proj_w"]).T)
    return d


def kernel(**inputs):
    x = np.asarray(inputs["L0_attn_x"])
    xT = np.ascontiguousarray(x.transpose(0, 2, 1))

    def launch(nc, shared, xT_cur, extra=None):
        in_maps = []
        for c in range(NCORES):
            m = dict(shared)
            m["x"] = np.ascontiguousarray(xT_cur[c * B_PER_CORE:(c + 1) * B_PER_CORE])
            if extra is not None:
                for k, v in extra.items():
                    m[k] = np.ascontiguousarray(v[c * B_PER_CORE:(c + 1) * B_PER_CORE])
            in_maps.append(m)
        res = run_bass_kernel_spmd(nc, in_maps, core_ids=list(range(NCORES)))
        return np.concatenate([r["y"] for r in res.results], axis=0)

    x1T = launch(build_phase_A(), _prep_phase_A(inputs), xT)
    yT = launch(build_phase_preln(), {}, x1T)
    shB = _prep_phase_B_shared()
    nc_ba = build_phase_B_attn()
    nc_bf = build_phase_B_ffn()
    for l in range(6):
        yT = launch(nc_ba, {**shB, **_prep_phase_B_attn(inputs, l)}, yT)
        yT = launch(nc_bf, _prep_phase_B_ffn(inputs, l), yT)
    yT = launch(build_phase_addres(), {}, yT, extra={"xr": x1T})
    yT = launch(build_phase_C(), _prep_phase_C(inputs), yT)
    yT = launch(build_phase_D(), _prep_phase_D(inputs), yT)
    return np.ascontiguousarray(yT.transpose(0, 2, 1))


# version 05
# speedup vs baseline: 1.0058x; 1.0058x over previous
import numpy as np
from contextlib import ExitStack

import concourse.bass as bass
import concourse.mybir as mybir
from concourse.bass_utils import run_bass_kernel_spmd

F32 = mybir.dt.float32
BF16 = mybir.dt.bfloat16
AF = mybir.ActivationFunctionType
ALU = mybir.AluOpType
AX = mybir.AxisListType

NCORES = 8
B_PER_CORE = 2
T = 2048
D = 1024
NDT = D // 128
TC = 512
NCH = T // TC

ENGS = ("sync", "tensor", "vector", "scalar", "gpsimd")
ESEM = {"sync": "e_sy", "tensor": "e_pe", "vector": "e_dv",
        "scalar": "e_sc", "gpsimd": "e_po"}


def _tname(x):
    if isinstance(x, tuple):
        return _tname(x[0]) + "#" + str(x[1])
    t = getattr(x, "tensor", None)
    if t is not None:
        return t.name
    return x.name


class Bld:

    def __init__(self, nc, coarse_dma_sems=False):
        self.nc = nc
        self.es = ExitStack()
        self.ops = {e: [] for e in ENGS}
        self.sems = {}
        self.cnt = {}
        self.lastw = {}
        self.readers = {}
        self.regions = {}
        self.clock = {e: {} for e in ENGS}
        self.coarse = coarse_dma_sems
        self.prefix = ""
        self.phase_es = None
        self.sem_alias = {}
        self.pool_used = 0

    def _alias(self, logical):
        nm = self.sem_alias.get(logical)
        if nm is None:
            nm = f"qp{self.pool_used}"
            self.pool_used += 1
            self.sem_alias[logical] = nm
        return nm

    def sem(self, name):
        if name not in self.sems:
            self.sems[name] = self.es.enter_context(self.nc.semaphore(name))
            self.cnt[name] = 0
        return self.sems[name]

    def sb(self, name, shape, dt=F32):
        es = self.phase_es if self.phase_es is not None else self.es
        return es.enter_context(self.nc.sbuf_tensor(self.prefix + name, shape, dt))

    def ps(self, name, shape, dt=F32):
        es = self.phase_es if self.phase_es is not None else self.es
        return es.enter_context(self.nc.psum_tensor(self.prefix + name, shape, dt))

    def phase(self, prefix):
        from contextlib import contextmanager

        @contextmanager
        def _cm():
            self.prefix = prefix + "_"
            self.phase_es = ExitStack()
            try:
                yield
            finally:
                self.barrier()
                self.phase_es.close()
                self.phase_es = None
                self.prefix = ""
        return _cm()

    def barrier(self):
        waits = [(n, v) for n, v in self.cnt.items() if v > 0]
        sems = self.sems
        for eng in ENGS:
            ck = self.clock[eng]
            todo = tuple((n, v) for n, v in waits if ck.get(n, 0) < v)
            for n, v in todo:
                ck[n] = v

            def fn(e, _t=todo):
                for n, v in _t:
                    e.wait_ge(sems[n], v)
            self.ops[eng].append(fn)
        self.sem_alias.clear()
        self.pool_used = 0

    def _conflicts(self, name):
        root = name.split("#")[0]
        if name == root:
            return [root] + sorted(self.regions.get(root, ()))
        return [name, root]

    def _deps(self, eng, rd, wr, extra):
        deps = {}

        def add(rec):
            if rec is None:
                return
            s, tok, peng = rec
            if eng == "tensor" and peng == "tensor":
                return
            if deps.get(s, 0) < tok:
                deps[s] = tok
        for x in rd:
            for n in self._conflicts(_tname(x)):
                add(self.lastw.get(n))
        for x in wr:
            for n in self._conflicts(_tname(x)):
                add(self.lastw.get(n))
                for rec in self.readers.get(n, ()):
                    add(rec)
        for s, tok in extra:
            if deps.get(s, 0) < tok:
                deps[s] = tok
        ck = self.clock[eng]
        out = [(s, v) for s, v in deps.items() if ck.get(s, 0) < v]
        for s, v in out:
            ck[s] = v
            self.sem(s)
        return out

    def _note_region(self, name):
        root = name.split("#")[0]
        if name != root:
            self.regions.setdefault(root, set()).add(name)

    def auto(self, eng, make, rd=(), wr=(), sem=None, incn=1, extra=()):
        sem = sem or ESEM[eng]
        if self.coarse and sem.startswith("q_"):
            sem = self._alias(sem)
        self.sem(sem)
        wait = self._deps(eng, rd, wr, extra)
        self.cnt[sem] += incn
        tok = self.cnt[sem]
        sems = self.sems

        def fn(e, _make=make, _wait=tuple(wait), _sem=sem, _incn=incn):
            for n, v in _wait:
                e.wait_ge(sems[n], v)
            _make(e).then_inc(sems[_sem], _incn)
        self.ops[eng].append(fn)
        rec = (sem, tok, eng)
        for x in rd:
            n = _tname(x)
            self._note_region(n)
            self.readers.setdefault(n, []).append(rec)
        for x in wr:
            n = _tname(x)
            self._note_region(n)
            self.lastw[n] = rec
            self.readers[n] = []
        return tok

    def dma(self, eng, out, in_, sem=None):
        ap_out = out[0] if isinstance(out, tuple) else out
        ap_in = in_[0] if isinstance(in_, tuple) else in_
        if sem is None:
            is_sb = getattr(ap_out, "tensor", ap_out).__class__.__name__.startswith("SB")
            root = (_tname(out).split("#")[0] if is_sb
                    else "st_" + _tname(in_).split("#")[0])
            sem = f"q_{eng[:2]}_{root}"
        return self.auto(eng, lambda e, o=ap_out, i=ap_in: e.dma_start(out=o, in_=i),
                         rd=[in_], wr=[out], sem=sem, incn=16)

    def mm_group(self, psum, tiles, extra=()):
        n = len(tiles)
        rd = []
        for lh, rh in tiles:
            rd.append(lh)
            rd.append(rh)
        wait = self._deps("tensor", rd, [psum], extra)
        sems = self.sems
        self.sem("e_pe")
        for idx, (lh, rh) in enumerate(tiles):
            last = idx == n - 1
            if last:
                self.cnt["e_pe"] += 1
            tok = self.cnt["e_pe"]

            def fn(e, _lh=lh, _rh=rh, _st=(idx == 0), _sp=last,
                   _wait=tuple(wait if idx == 0 else ()), _ps=psum, _inc=last, _tok=tok):
                for nm, v in _wait:
                    e.wait_ge(sems[nm], v)
                ins = e.matmul(_ps, _lh, _rh, start=_st, stop=_sp)
                if _inc:
                    ins.then_inc(sems["e_pe"], 1)
            self.ops["tensor"].append(fn)
        rec = ("e_pe", self.cnt["e_pe"], "tensor")
        for x in rd:
            n = _tname(x)
            self._note_region(n)
            self.readers.setdefault(n, []).append(rec)
        n_ = _tname(psum)
        self._note_region(n_)
        self.lastw[n_] = rec
        self.readers[n_] = []
        return self.cnt["e_pe"]

    def finalize(self, final_waits):
        nc = self.nc
        with self.es:
            for n, v in final_waits:
                self.sem(n)
            with nc.Block() as block:
                ops, sems = self.ops, self.sems

                @block.sync
                def _(e):
                    for fn in ops["sync"]:
                        fn(e)
                    for n, v in final_waits:
                        e.wait_ge(sems[n], v)

                @block.tensor
                def _(e):
                    for fn in ops["tensor"]:
                        fn(e)

                @block.vector
                def _(e):
                    for fn in ops["vector"]:
                        fn(e)

                @block.scalar
                def _(e):
                    for fn in ops["scalar"]:
                        fn(e)

                @block.gpsimd
                def _(e):
                    for fn in ops["gpsimd"]:
                        fn(e)
        return nc



A_EXT = {"w_inT": [D, 2 * D], "conv_w": [D, 4], "conv_b": [D, 1], "w_xpT": [D, 33],
         "dtwb": [2, D], "a_rep": [128, 128], "r_sel": [128, 16 * 128],
         "s_sel": [128, 16 * 128], "rb16": [16, 128], "w_outT": [D, D]}


def _mk_ext(nc, spec):
    ext = {"x": nc.dram_tensor("x", [B_PER_CORE, D, T], F32, kind="ExternalInput"),
           "y": nc.dram_tensor("y", [B_PER_CORE, D, T], F32, kind="ExternalOutput")}
    for name, shape in spec.items():
        ext[name] = nc.dram_tensor(name, shape, F32, kind="ExternalInput")
    return ext


def _st_waits(b):
    return [(nm, b.cnt[nm]) for nm in list(b.cnt) if nm.startswith("q_sy_st_")]


def emit_A(b, ext):
    nc = b.nc
    x_in = ext["x"]
    w_inT = ext["w_inT"]
    conv_w = ext["conv_w"]
    conv_b = ext["conv_b"]
    w_xpT = ext["w_xpT"]
    dtwb = ext["dtwb"]
    a_rep = ext["a_rep"]
    r_sel = ext["r_sel"]
    s_sel = ext["s_sel"]
    rb16 = ext["rb16"]
    w_outT = ext["w_outT"]
    y_out = ext["y"]

    ones_b = b.sb("ones_b", [128, 1], BF16)
    ones_f1 = b.sb("ones_f1", [1, 128], F32)
    WinT = [b.sb(f"WinT{i}", [128, 2 * D], BF16) for i in range(NDT)]
    XpT = [b.sb(f"XpT{i}", [128, 33], BF16) for i in range(NDT)]
    WoT = [b.sb(f"WoT{i}", [128, D], BF16) for i in range(NDT)]
    cw = b.sb("cw", [128, 4 * NDT], F32)
    cb = b.sb("cb", [128, NDT], F32)
    dtwb_s = b.sb("dtwb_s", [2, D], F32)
    a_rep_s = b.sb("a_rep_s", [128, 128], F32)
    Rsel = b.sb("Rsel", [128, 16 * 128], F32)
    Ssel = b.sb("Ssel", [128, 16 * 128], BF16)
    Rb16 = b.sb("Rb16", [16, 128], F32)

    x_t = [b.sb(f"x_t{i}", [128, TC]) for i in range(NDT)]
    xb_t = [b.sb(f"xb_t{i}", [128, TC], BF16) for i in range(NDT)]
    xn_t = [b.sb(f"xn_t{i}", [128, TC], BF16) for i in range(NDT)]
    xin_t = [b.sb(f"xin_t{i}", [128, TC + 3]) for i in range(NDT)]
    marg = b.sb("marg", [128, 3 * NDT])
    xc_t = [b.sb(f"xc_t{i}", [128, TC]) for i in range(NDT)]
    xcb_t = [b.sb(f"xcb_t{i}", [128, TC], BF16) for i in range(NDT)]
    xq_t = xcb_t
    sz_t = [b.sb(f"sz_t{i}", [128, TC]) for i in range(NDT)]
    dt_t = [b.sb(f"dt_t{i}", [128, TC]) for i in range(NDT)]
    ym_t = xn_t
    row_m = b.sb("row_m", [1, TC])
    row_v = b.sb("row_v", [1, TC])
    row_r = b.sb("row_r", [1, TC])
    row_mr = b.sb("row_mr", [1, TC])
    brow = b.sb("brow", [16, TC])
    crow = b.sb("crow", [16, TC])
    s1row = b.sb("s1row", [2, TC])
    sp_a = b.sb("sp_a", [128, TC])
    sp_b = b.sb("sp_b", [128, TC])
    brep = b.sb("brep", [128, TC])
    crep = b.sb("crep", [128, TC])
    dA = [b.sb(f"dA{j}", [128, TC]) for j in range(2)]
    dBx = [b.sb(f"dBx{j}", [128, TC]) for j in range(2)]
    hbuf = [b.sb(f"hbuf{j}", [128, TC]) for j in range(2)]
    hC = [b.sb(f"hC{j}", [128, TC], BF16) for j in range(2)]
    carry = b.sb("carry", [128, 128])

    P = [b.ps(f"P{i}", [128, TC]) for i in range(8)]

    for i in range(NDT):
        b.dma("gpsimd", WinT[i][:], w_inT[i * 128:(i + 1) * 128, :])
        b.dma("gpsimd", XpT[i][:], w_xpT[i * 128:(i + 1) * 128, :])
        b.dma("gpsimd", WoT[i][:], w_outT[i * 128:(i + 1) * 128, :])
        b.dma("sync", (cw[:, 4 * i:4 * i + 4], i), conv_w[i * 128:(i + 1) * 128, :], "q_cw")
        b.dma("sync", (cb[:, i:i + 1], i), conv_b[i * 128:(i + 1) * 128, :], "q_cb")
    b.dma("sync", dtwb_s[:], dtwb[:])
    b.dma("sync", a_rep_s[:], a_rep[:])
    b.dma("sync", Rsel[:], r_sel[:])
    b.dma("gpsimd", Ssel[:], s_sel[:])
    b.dma("sync", Rb16[:], rb16[:])

    b.auto("vector", lambda e: e.memset(ones_b[:], 1.0), wr=[ones_b])
    b.auto("vector", lambda e: e.memset(ones_f1[:], 1.0), wr=[ones_f1])
    b.auto("vector", lambda e: e.memset(s1row[:], 1.0), wr=[s1row])
    b.auto("scalar", lambda e: e.activation(a_rep_s[:], a_rep_s[:], AF.Exp),
           rd=[a_rep_s], wr=[a_rep_s])
    b.auto("vector", lambda e: e.tensor_single_scalar(a_rep_s[:], a_rep_s[:], -1.0, ALU.mult),
           rd=[a_rep_s], wr=[a_rep_s])

    for bi in range(B_PER_CORE):
        b.auto("vector", lambda e: e.memset(marg[:], 0.0), wr=[marg])
        b.auto("vector", lambda e: e.memset(carry[:], 0.0), wr=[carry])
        for c in range(NCH):
            t0 = c * TC
            xsl = slice(t0, t0 + TC)
            for i in range(NDT):
                b.dma("sync", x_t[i][:], x_in[bi, i * 128:(i + 1) * 128, xsl])

            for i in range(NDT):
                b.auto("scalar", lambda e, i=i: e.copy(xb_t[i][:], x_t[i][:]),
                       rd=[x_t[i]], wr=[xb_t[i]])
                b.auto("scalar", lambda e, i=i: e.activation(xq_t[i][:], x_t[i][:], AF.Square),
                       rd=[x_t[i]], wr=[xq_t[i]])

            b.mm_group(P[6][0:1, :], [(ones_b[:], xb_t[i][:]) for i in range(NDT)])
            b.mm_group(P[7][0:1, :], [(ones_b[:], xq_t[i][:]) for i in range(NDT)])

            b.auto("vector", lambda e: e.tensor_single_scalar(row_m[:], P[6][0:1, :], 1.0 / D, ALU.mult),
                   rd=[P[6]], wr=[row_m])
            b.auto("vector", lambda e: e.tensor_single_scalar(row_v[:], P[7][0:1, :], 1.0 / D, ALU.mult),
                   rd=[P[7]], wr=[row_v])
            b.auto("vector", lambda e: e.tensor_mul(row_r[:], row_m[:], row_m[:]),
                   rd=[row_m], wr=[row_r])
            b.auto("vector", lambda e: e.scalar_tensor_tensor(
                row_v[:], row_v[:], 1e-5, row_r[:], ALU.add, ALU.subtract),
                rd=[row_v, row_r], wr=[row_v])
            b.auto("scalar", lambda e: e.activation(row_r[:], row_v[:], AF.Ln),
                   rd=[row_v], wr=[row_r])
            b.auto("scalar", lambda e: e.activation(row_r[:], row_r[:], AF.Exp, scale=-0.5),
                   rd=[row_r], wr=[row_r])
            b.auto("vector", lambda e: e.tensor_mul(row_mr[:], row_m[:], row_r[:]),
                   rd=[row_m, row_r], wr=[row_mr])

            b.mm_group(P[6][:], [(ones_f1[:], row_r[:])])
            b.mm_group(P[7][:], [(ones_f1[:], row_mr[:])])

            for i in range(NDT):
                b.auto("vector", lambda e, i=i: e.scalar_tensor_tensor(
                    sp_a[:], x_t[i][:], 1.0, P[6][:], ALU.mult, ALU.mult),
                    rd=[x_t[i], P[6]], wr=[sp_a])
                b.auto("vector", lambda e, i=i: e.tensor_sub(xn_t[i][:], sp_a[:], P[7][:]),
                       rd=[sp_a, P[7]], wr=[xn_t[i]])

            for m in range(16):
                slot = m % 2
                b.mm_group(P[slot][:],
                           [(WinT[k][:, m * 128:(m + 1) * 128], xn_t[k][:]) for k in range(NDT)])
                if m < 8:
                    b.auto("scalar", lambda e, m=m, slot=slot: e.copy(
                        xin_t[m][:, 3:TC + 3], P[slot][:]), rd=[P[slot]], wr=[xin_t[m]])
                else:
                    b.auto("scalar", lambda e, m=m, slot=slot: e.activation(
                        sz_t[m - 8][:], P[slot][:], AF.Silu), rd=[P[slot]], wr=[sz_t[m - 8]])

            for i in range(NDT):
                b.auto("vector", lambda e, i=i: e.tensor_copy(
                    xin_t[i][:, 0:3], marg[:, 3 * i:3 * i + 3]), rd=[marg], wr=[xin_t[i]])
            for i in range(NDT):
                b.auto("vector", lambda e, i=i: e.tensor_single_scalar(
                    xc_t[i][:], xin_t[i][:, 0:TC], cw[:, 4 * i:4 * i + 1], ALU.mult),
                    rd=[xin_t[i], cw], wr=[xc_t[i]])
                for k in range(1, 4):
                    b.auto("vector", lambda e, i=i, k=k: e.scalar_tensor_tensor(
                        xc_t[i][:], xin_t[i][:, k:TC + k], cw[:, 4 * i + k:4 * i + k + 1],
                        xc_t[i][:], ALU.mult, ALU.add),
                        rd=[xin_t[i], cw, xc_t[i]], wr=[xc_t[i]])
            for i in range(NDT):
                b.auto("vector", lambda e, i=i: e.tensor_copy(
                    marg[:, 3 * i:3 * i + 3], xin_t[i][:, TC:TC + 3]),
                    rd=[xin_t[i]], wr=[marg])

            for i in range(NDT):
                b.auto("scalar", lambda e, i=i: e.activation(
                    xc_t[i][:], xc_t[i][:], AF.Silu, bias=cb[:, i:i + 1]),
                    rd=[xc_t[i], cb], wr=[xc_t[i]])
                b.auto("scalar", lambda e, i=i: e.copy(xcb_t[i][:], xc_t[i][:]),
                       rd=[xc_t[i]], wr=[xcb_t[i]])

            for (slot, c0, c1, dst) in ((0, 0, 16, brow), (1, 16, 32, crow), (2, 32, 33, None)):
                mm_rows = c1 - c0
                b.mm_group(P[slot][0:mm_rows, :],
                           [(XpT[k][:, c0:c1], xcb_t[k][:]) for k in range(NDT)])
                if dst is not None:
                    b.auto("scalar", lambda e, dst=dst, mm_rows=mm_rows, slot=slot:
                           e.copy(dst[:], P[slot][0:mm_rows, :]), rd=[P[slot]], wr=[dst])
                else:
                    b.auto("vector", lambda e: e.tensor_copy(s1row[0:1, :], P[2][0:1, :]),
                           rd=[P[2]], wr=[s1row])

            for i in range(NDT):
                slot = 3 + (i % 2)
                b.mm_group(P[slot][:], [(dtwb_s[:, i * 128:(i + 1) * 128], s1row[:])])
                b.auto("scalar", lambda e, slot=slot: e.activation(sp_a[:], P[slot][:], AF.Abs),
                       rd=[P[slot]], wr=[sp_a])
                b.auto("scalar", lambda e: e.activation(sp_b[:], sp_a[:], AF.Exp, scale=-1.0),
                       rd=[sp_a], wr=[sp_b])
                b.auto("scalar", lambda e: e.activation(sp_b[:], sp_b[:], AF.Ln, bias=1.0),
                       rd=[sp_b], wr=[sp_b])
                b.auto("scalar", lambda e, slot=slot, i=i: e.activation(
                    dt_t[i][:], P[slot][:], AF.Relu), rd=[P[slot]], wr=[dt_t[i]])
                b.auto("vector", lambda e, i=i: e.tensor_add(dt_t[i][:], dt_t[i][:], sp_b[:]),
                       rd=[dt_t[i], sp_b], wr=[dt_t[i]])
                b.auto("vector", lambda e, i=i: e.tensor_mul(xc_t[i][:], dt_t[i][:], xc_t[i][:]),
                       rd=[dt_t[i], xc_t[i]], wr=[xc_t[i]])

            b.mm_group(P[5][:], [(Rb16[:], brow[:])])
            b.auto("scalar", lambda e: e.copy(brep[:], P[5][:]), rd=[P[5]], wr=[brep])
            b.mm_group(P[6][:], [(Rb16[:], crow[:])])
            b.auto("scalar", lambda e: e.copy(crep[:], P[6][:]), rd=[P[6]], wr=[crep])

            for db in range(NDT):
                for k in range(16):
                    tau = db * 16 + k
                    j = k % 2
                    sA, sU = j, 2 + j
                    ksl = slice(k * 128, (k + 1) * 128)
                    b.mm_group(P[sA][:], [(Rsel[:, ksl], dt_t[db][:])])
                    b.mm_group(P[sU][:], [(Rsel[:, ksl], xc_t[db][:])])
                    b.auto("scalar", lambda e, j=j, sA=sA, tau=tau: e.activation(
                        dA[j][:], P[sA][:], AF.Exp, scale=a_rep_s[:, tau:tau + 1]),
                        rd=[P[sA], a_rep_s], wr=[dA[j]])
                    b.auto("vector", lambda e, j=j, sU=sU: e.tensor_tensor(
                        dBx[j][:], P[sU][:], brep[:], ALU.mult),
                        rd=[P[sU], brep], wr=[dBx[j]])
                    b.auto("vector", lambda e, j=j, tau=tau: e.tensor_tensor_scan(
                        hbuf[j][:], dA[j][:], dBx[j][:], carry[:, tau:tau + 1],
                        ALU.mult, ALU.add),
                        rd=[dA[j], dBx[j], carry], wr=[hbuf[j]])
                    b.auto("vector", lambda e, j=j, tau=tau: e.tensor_copy(
                        carry[:, tau:tau + 1], hbuf[j][:, TC - 1:TC]),
                        rd=[hbuf[j]], wr=[carry])
                    b.auto("vector", lambda e, j=j: e.tensor_tensor(
                        hC[j][:], hbuf[j][:], crep[:], ALU.mult),
                        rd=[hbuf[j], crep], wr=[hC[j]])
                    nfirst = (k == 0)
                    nlast = (k == 15)
                    b.auto(
                        "tensor",
                        (lambda e, _ksl=ksl, _j=j, _st=nfirst, _sp=nlast:
                         e.matmul(P[4][:], Ssel[:, _ksl], hC[_j][:], start=_st, stop=_sp)),
                        rd=[Ssel, hC[j]], wr=([P[4]] if (nfirst or nlast) else []))
                b.auto("vector", lambda e, db=db: e.tensor_tensor(
                    ym_t[db][:], P[4][:], sz_t[db][:], ALU.mult),
                    rd=[P[4], sz_t[db]], wr=[ym_t[db]])

            for m in range(NDT):
                slot = 5 + (m % 2)
                b.mm_group(P[slot][:],
                           [(WoT[k][:, m * 128:(m + 1) * 128], ym_t[k][:]) for k in range(NDT)])
                b.auto("vector", lambda e, m=m, slot=slot: e.tensor_tensor(
                    x_t[m][:], P[slot][:], x_t[m][:], ALU.add),
                    rd=[P[slot], x_t[m]], wr=[x_t[m]])
                b.dma("sync", (y_out[bi, m * 128:(m + 1) * 128, xsl], f"{bi}.{c}.{m}"),
                      x_t[m][:])


def build_phase_A():
    nc = bass.Bass("TRN2", target_bir_lowering=False)
    b = Bld(nc)
    emit_A(b, _mk_ext(nc, A_EXT))
    return b.finalize(_st_waits(b))



def emit_preln(b, ext):
    x_in = ext["x"]
    y_out = ext["y"]
    ones_b = b.sb("ones_b", [128, 1], BF16)
    ones_f1 = b.sb("ones_f1", [1, 128], F32)
    x_t = [b.sb(f"x_t{i}", [128, TC]) for i in range(NDT)]
    xb_t = [b.sb(f"xb_t{i}", [128, TC], BF16) for i in range(NDT)]
    sq_t = [b.sb(f"sq_t{i}", [128, TC], BF16) for i in range(NDT)]
    t1 = b.sb("t1", [128, TC])
    row_m = b.sb("row_m", [1, TC])
    row_v = b.sb("row_v", [1, TC])
    row_r = b.sb("row_r", [1, TC])
    row_mr = b.sb("row_mr", [1, TC])
    P = [b.ps(f"P{i}", [128, TC]) for i in range(2)]
    b.auto("vector", lambda e: e.memset(ones_b[:], 1.0), wr=[ones_b])
    b.auto("vector", lambda e: e.memset(ones_f1[:], 1.0), wr=[ones_f1])
    for bi in range(B_PER_CORE):
        for c in range(NCH):
            xsl = slice(c * TC, (c + 1) * TC)
            for i in range(NDT):
                b.dma("sync", x_t[i][:], x_in[bi, i * 128:(i + 1) * 128, xsl])
            for i in range(NDT):
                b.auto("scalar", lambda e, i=i: e.copy(xb_t[i][:], x_t[i][:]),
                       rd=[x_t[i]], wr=[xb_t[i]])
                b.auto("scalar", lambda e, i=i: e.activation(sq_t[i][:], x_t[i][:], AF.Square),
                       rd=[x_t[i]], wr=[sq_t[i]])
            b.mm_group(P[0][0:1, :], [(ones_b[:], xb_t[i][:]) for i in range(NDT)])
            b.mm_group(P[1][0:1, :], [(ones_b[:], sq_t[i][:]) for i in range(NDT)])
            b.auto("vector", lambda e: e.tensor_single_scalar(row_m[:], P[0][0:1, :], 1.0 / D, ALU.mult),
                   rd=[P[0]], wr=[row_m])
            b.auto("vector", lambda e: e.tensor_single_scalar(row_v[:], P[1][0:1, :], 1.0 / D, ALU.mult),
                   rd=[P[1]], wr=[row_v])
            b.auto("vector", lambda e: e.tensor_mul(row_r[:], row_m[:], row_m[:]),
                   rd=[row_m], wr=[row_r])
            b.auto("vector", lambda e: e.scalar_tensor_tensor(
                row_v[:], row_v[:], 1e-5, row_r[:], ALU.add, ALU.subtract),
                rd=[row_v, row_r], wr=[row_v])
            b.auto("scalar", lambda e: e.activation(row_r[:], row_v[:], AF.Ln),
                   rd=[row_v], wr=[row_r])
            b.auto("scalar", lambda e: e.activation(row_r[:], row_r[:], AF.Exp, scale=-0.5),
                   rd=[row_r], wr=[row_r])
            b.auto("vector", lambda e: e.tensor_mul(row_mr[:], row_m[:], row_r[:]),
                   rd=[row_m, row_r], wr=[row_mr])
            b.mm_group(P[0][:], [(ones_f1[:], row_r[:])])
            b.mm_group(P[1][:], [(ones_f1[:], row_mr[:])])
            for i in range(NDT):
                b.auto("vector", lambda e, i=i: e.scalar_tensor_tensor(
                    t1[:], x_t[i][:], 1.0, P[0][:], ALU.mult, ALU.mult),
                    rd=[x_t[i], P[0]], wr=[t1])
                b.auto("vector", lambda e, i=i: e.tensor_sub(x_t[i][:], t1[:], P[1][:]),
                       rd=[t1, P[1]], wr=[x_t[i]])
                b.dma("sync", (y_out[bi, i * 128:(i + 1) * 128, xsl], f"{bi}.{c}.{i}"),
                      x_t[i][:])


def build_phase_preln():
    nc = bass.Bass("TRN2", target_bir_lowering=False)
    b = Bld(nc)
    emit_preln(b, _mk_ext(nc, {}))
    return b.finalize(_st_waits(b))



def emit_addres(b, ext):
    x_in = ext["x"]
    xr_in = ext["xr"]
    y_out = ext["y"]
    a_t = [b.sb(f"a_t{i}", [128, T]) for i in range(NDT)]
    r_t = [b.sb(f"r_t{i}", [128, T]) for i in range(NDT)]
    for bi in range(B_PER_CORE):
        for i in range(NDT):
            rsl = slice(i * 128, (i + 1) * 128)
            b.dma("sync", (a_t[i][:], bi), x_in[bi, rsl, :])
            b.dma("gpsimd", (r_t[i][:], bi), xr_in[bi, rsl, :])
            b.auto("vector", lambda e, i=i: e.tensor_add(a_t[i][:], a_t[i][:], r_t[i][:]),
                   rd=[(a_t[i], bi), (r_t[i], bi), a_t[i], r_t[i]], wr=[a_t[i]])
            b.dma("sync", (y_out[bi, rsl, :], f"{bi}.{i}"), a_t[i][:])


def build_phase_addres():
    nc = bass.Bass("TRN2", target_bir_lowering=False)
    b = Bld(nc)
    ext = _mk_ext(nc, {})
    ext["xr"] = nc.dram_tensor("xr", [B_PER_CORE, D, T], F32, kind="ExternalInput")
    emit_addres(b, ext)
    return b.finalize(_st_waits(b))



HEADS = 16
DH = 64
WIN = 256
PAIR = 512
NPAIR = T // PAIR
BAND = 384


BA_EXT = {"wq": [D, D], "wqr": [D, D], "wk": [D, D], "wkr": [D, D], "wv": [D, D],
          "wo": [D, D], "qsc": [128, 2], "ksc": [128, 2]}
BSH_EXT = {"csq": [128, T], "ssq": [128, T], "csk": [128, T], "ssk": [128, T],
           "mask3": [128, 3 * BAND], "bsel": [128, 128], "psel": [128, 2],
           "pselT": [2, 128], "ident": [128, 128]}


def emit_B_attn(b, ext):
    x_in = ext["x"]
    wq = ext["wq"]
    wqr = ext["wqr"]
    wk = ext["wk"]
    wkr = ext["wkr"]
    wv = ext["wv"]
    wo = ext["wo"]
    qsc = ext["qsc"]
    ksc = ext["ksc"]
    csq_d = ext["csq"]
    ssq_d = ext["ssq"]
    csk_d = ext["csk"]
    ssk_d = ext["ssk"]
    mask_d = ext["mask3"]
    bsel_d = ext["bsel"]
    psel_d = ext["psel"]
    pselT_d = ext["pselT"]
    ident_d = ext["ident"]
    y_out = ext["y"]

    WqT = [b.sb(f"WqT{i}", [128, D], BF16) for i in range(NDT)]
    WqrT = [b.sb(f"WqrT{i}", [128, D], BF16) for i in range(NDT)]
    WkT = [b.sb(f"WkT{i}", [128, D], BF16) for i in range(NDT)]
    WkrT = [b.sb(f"WkrT{i}", [128, D], BF16) for i in range(NDT)]
    WvT = [b.sb(f"WvT{i}", [128, D], BF16) for i in range(NDT)]
    WoT = [b.sb(f"WoT{i}", [128, D], BF16) for i in range(NDT)]
    qs_s = b.sb("qs_s", [128, 2])
    ks_s = b.sb("ks_s", [128, 2])
    csq = b.sb("csq_s", [128, T], BF16)
    ssq = b.sb("ssq_s", [128, T], BF16)
    csk = b.sb("csk_s", [128, T], BF16)
    ssk = b.sb("ssk_s", [128, T], BF16)
    mask3 = b.sb("mask3_s", [128, 3 * BAND], BF16)
    bsel = b.sb("bsel_s", [128, 128], BF16)
    psel = b.sb("psel_s", [128, 2], BF16)
    pselT = b.sb("pselT_s", [2, 128])
    ident = b.sb("ident_s", [128, 128], BF16)

    xw = [b.sb(f"xw{i}", [128, PAIR + 1]) for i in range(NDT)]
    xs = [b.sb(f"xs{i}", [128, PAIR], BF16) for i in range(NDT)]
    Kc = [b.sb(f"Kc{i}", [128, 768], BF16) for i in range(NDT)]
    Vc = [b.sb(f"Vc{i}", [128, D], BF16) for i in range(6)]
    Qp = [b.sb(f"Qp{i}", [128, PAIR], BF16) for i in range(NDT)]
    sqq = [b.sb(f"sqq{i}", [128, PAIR], BF16) for i in range(NDT)]
    sqk = b.sb("sqk", [128, PAIR], BF16)
    ao = xs
    rt1 = b.sb("rt1", [128, PAIR])
    rt2 = b.sb("rt2", [128, PAIR])
    rnk = b.sb("rnk", [2, PAIR])
    rsq8 = [b.sb(f"rsq8{i}", [128, HEADS]) for i in range(4)]
    nrm_a = b.sb("nrm_a", [2, PAIR])
    esb = b.sb("esb", [128, BAND])
    atb = b.sb("atb", [128, BAND], BF16)
    attnT = b.sb("attnT", [128, BAND], BF16)
    zrow = b.sb("zrow", [128, 1])
    rz = b.sb("rz", [128, 1])
    ngm = b.sb("ngm", [128, 1])
    bia = b.sb("bia", [128, 1])

    P = [b.ps(f"P{i}", [128, PAIR]) for i in range(6)]
    PT = b.ps("PT", [128, BAND], BF16)
    PV = b.ps("PV", [128, 128])

    for i in range(NDT):
        rsl = slice(i * 128, (i + 1) * 128)
        b.dma("gpsimd", WqT[i][:], wq[rsl, :])
        b.dma("gpsimd", WqrT[i][:], wqr[rsl, :])
        b.dma("gpsimd", WkT[i][:], wk[rsl, :])
        b.dma("gpsimd", WkrT[i][:], wkr[rsl, :])
        b.dma("gpsimd", WvT[i][:], wv[rsl, :])
        b.dma("gpsimd", WoT[i][:], wo[rsl, :])
    b.dma("sync", qs_s[:], qsc[:])
    b.dma("sync", ks_s[:], ksc[:])
    b.dma("gpsimd", csq[:], csq_d[:])
    b.dma("gpsimd", ssq[:], ssq_d[:])
    b.dma("gpsimd", csk[:], csk_d[:])
    b.dma("gpsimd", ssk[:], ssk_d[:])
    b.dma("gpsimd", mask3[:], mask_d[:])
    b.dma("gpsimd", bsel[:], bsel_d[:])
    b.dma("gpsimd", psel[:], psel_d[:])
    b.dma("sync", pselT[:], pselT_d[:])
    b.dma("gpsimd", ident[:], ident_d[:])

    for bi in range(B_PER_CORE):
        for p in range(NPAIR):
            t0 = p * PAIR
            for i in range(NDT):
                rsl = slice(i * 128, (i + 1) * 128)
                if p == 0:
                    b.auto("vector", lambda e, i=i: e.memset(xw[i][:, 0:1], 0.0),
                           wr=[(xw[i], "m")])
                    b.dma("sync", (xw[i][:, 1:PAIR + 1], "b"), x_in[bi, rsl, 0:PAIR])
                else:
                    b.dma("sync", xw[i][:], x_in[bi, rsl, t0 - 1:t0 + PAIR])
            for i in range(NDT):
                src = (xw[i][:, 1:PAIR + 1]) if i < 4 else (xw[i][:, 0:PAIR])
                b.auto("scalar", lambda e, i=i, src=src: e.copy(xs[i][:], src),
                       rd=[xw[i], (xw[i], "m"), (xw[i], "b")], wr=[xs[i]])
            if p == 0:
                for i in range(NDT):
                    b.auto("vector", lambda e, i=i: e.memset(Kc[i][:, 0:256], 0.0),
                           rd=[], wr=[(Kc[i], "prev")])
                for i in range(2):
                    b.auto("vector", lambda e, i=i: e.memset(Vc[i][:], 0.0), wr=[Vc[i]])
            else:
                for i in range(NDT):
                    b.auto("vector", lambda e, i=i: e.tensor_copy(
                        Kc[i][:, 0:256], Kc[i][:, 512:768]),
                        rd=[(Kc[i], "cur")], wr=[(Kc[i], "prev")])
                for i in range(2):
                    b.auto("vector", lambda e, i=i: e.tensor_copy(Vc[i][:], Vc[i + 4][:]),
                           rd=[Vc[i + 4]], wr=[Vc[i]])
            for tt in range(4):
                tsl = slice(tt * 128, (tt + 1) * 128)
                for ch in range(2):
                    b.mm_group(P[ch][:, 0:512],
                               [(xs[k][:, tsl], WvT[k][:, ch * 512:(ch + 1) * 512])
                                for k in range(NDT)])
                    b.auto("vector", lambda e, tt=tt, ch=ch: e.tensor_copy(
                        Vc[2 + tt][:, ch * 512:(ch + 1) * 512], P[ch][:, 0:512]),
                        rd=[P[ch]], wr=[(Vc[2 + tt], ch)])
            for m in range(NDT):
                msl = slice(m * 128, (m + 1) * 128)
                b.mm_group(P[0][:], [(WqT[k][:, msl], xs[k][:]) for k in range(NDT)])
                b.mm_group(P[1][:], [(WqrT[k][:, msl], xs[k][:]) for k in range(NDT)])
                b.mm_group(P[2][:], [(WkT[k][:, msl], xs[k][:]) for k in range(NDT)])
                b.mm_group(P[3][:], [(WkrT[k][:, msl], xs[k][:]) for k in range(NDT)])
                b.auto("scalar", lambda e, m=m: e.activation(sqq[m][:], P[0][:], AF.Square),
                       rd=[P[0]], wr=[sqq[m]])
                b.auto("scalar", lambda e: e.activation(sqk[:], P[2][:], AF.Square),
                       rd=[P[2]], wr=[sqk])
                b.mm_group(P[4][0:2, :], [(psel[:], sqk[:])])
                b.auto("scalar", lambda e: e.activation(nrm_a[:], P[4][0:2, :], AF.Ln),
                       rd=[P[4]], wr=[nrm_a])
                b.auto("scalar", lambda e: e.activation(nrm_a[:], nrm_a[:], AF.Exp, scale=0.5),
                       rd=[nrm_a], wr=[nrm_a])
                b.auto("vector", lambda e: e.tensor_single_scalar(nrm_a[:], nrm_a[:], 1e-12, ALU.max),
                       rd=[nrm_a], wr=[nrm_a])
                b.auto("vector", lambda e: e.reciprocal(rnk[:], nrm_a[:]),
                       rd=[nrm_a], wr=[rnk])
                b.mm_group(P[4][:], [(pselT[:], rnk[:])])
                tq = slice(t0, t0 + PAIR)
                b.auto("vector", lambda e, m=m, tq=tq: e.scalar_tensor_tensor(
                    rt1[:], P[0][:], qs_s[:, 0:1], csq[:, tq], ALU.mult, ALU.mult),
                    rd=[P[0], qs_s, csq], wr=[rt1])
                b.auto("vector", lambda e, m=m, tq=tq: e.scalar_tensor_tensor(
                    rt2[:], P[1][:], qs_s[:, 1:2], ssq[:, tq], ALU.mult, ALU.mult),
                    rd=[P[1], qs_s, ssq], wr=[rt2])
                b.auto("vector", lambda e, m=m: e.tensor_add(Qp[m][:], rt1[:], rt2[:]),
                       rd=[rt1, rt2], wr=[Qp[m]])
                b.auto("vector", lambda e, m=m, tq=tq: e.scalar_tensor_tensor(
                    rt1[:], P[2][:], ks_s[:, 0:1], csk[:, tq], ALU.mult, ALU.mult),
                    rd=[P[2], ks_s, csk], wr=[rt1])
                b.auto("vector", lambda e, m=m, tq=tq: e.scalar_tensor_tensor(
                    rt2[:], P[3][:], ks_s[:, 1:2], ssk[:, tq], ALU.mult, ALU.mult),
                    rd=[P[3], ks_s, ssk], wr=[rt2])
                b.auto("vector", lambda e: e.tensor_add(rt1[:], rt1[:], rt2[:]),
                       rd=[rt1, rt2], wr=[rt1])
                b.auto("vector", lambda e, m=m: e.tensor_tensor(
                    Kc[m][:, 256:768], rt1[:], P[4][:], ALU.mult),
                    rd=[rt1, P[4]], wr=[(Kc[m], "cur")])
            for tt in range(4):
                tsl = slice(tt * 128, (tt + 1) * 128)
                b.mm_group(P[5][:, 0:16],
                           [(sqq[m][:, tsl], bsel[:, m * 16:(m + 1) * 16]) for m in range(NDT)])
                b.auto("scalar", lambda e, tt=tt: e.activation(
                    rsq8[tt][:], P[5][:, 0:16], AF.Ln), rd=[P[5]], wr=[rsq8[tt]])
                b.auto("scalar", lambda e, tt=tt: e.activation(
                    rsq8[tt][:], rsq8[tt][:], AF.Exp, scale=0.5), rd=[rsq8[tt]], wr=[rsq8[tt]])
                b.auto("vector", lambda e, tt=tt: e.tensor_single_scalar(
                    rsq8[tt][:], rsq8[tt][:], 1e-12, ALU.max), rd=[rsq8[tt]], wr=[rsq8[tt]])
                b.auto("vector", lambda e, tt=tt: e.reciprocal(rsq8[tt][:], rsq8[tt][:]),
                       rd=[rsq8[tt]], wr=[rsq8[tt]])
                b.auto("vector", lambda e, tt=tt: e.tensor_single_scalar(
                    rsq8[tt][:], rsq8[tt][:], 8.0, ALU.mult), rd=[rsq8[tt]], wr=[rsq8[tt]])
            for wloc in range(2):
                w_g = 2 * p + wloc
                for qt in range(2):
                    tt = 2 * wloc + qt
                    for h in range(HEADS):
                        m = h // 2
                        hr = slice((h % 2) * 64, (h % 2) * 64 + 64)
                        qs_l = slice(tt * 128, (tt + 1) * 128)
                        koff = 256 * wloc + 128 * qt
                        mv = 0 if w_g > 0 else (1 + qt)
                        msl = slice(mv * BAND, (mv + 1) * BAND)
                        b.mm_group(P[5][:, 0:BAND],
                                   [(Qp[m][hr, qs_l], Kc[m][hr, koff:koff + BAND])])
                        b.auto("vector", lambda e: e.reduce_max(
                            ngm[:], P[5][:, 0:BAND], axis=AX.X, negate=True),
                            rd=[P[5]], wr=[ngm])
                        b.auto("vector", lambda e, tt=tt, h=h: e.tensor_mul(
                            bia[:], ngm[:], rsq8[tt][:, h:h + 1]),
                            rd=[ngm, rsq8[tt]], wr=[bia])
                        b.auto("scalar", lambda e, tt=tt, h=h: e.activation(
                            esb[:], P[5][:, 0:BAND], AF.Exp,
                            bias=bia[:], scale=rsq8[tt][:, h:h + 1]),
                            rd=[P[5], bia, rsq8[tt]], wr=[esb])
                        b.auto("vector", lambda e, msl=msl: e.scalar_tensor_tensor(
                            atb[:], esb[:], 1.0, mask3[:, msl], ALU.mult, ALU.mult,
                            accum_out=zrow[:]),
                            rd=[esb, mask3], wr=[atb, zrow])
                        b.auto("vector", lambda e: e.reciprocal(rz[:], zrow[:]),
                               rd=[zrow], wr=[rz])
                        b.auto("vector", lambda e: e.tensor_single_scalar(
                            atb[:], atb[:], rz[:], ALU.mult), rd=[atb, rz], wr=[atb])
                        for j in range(3):
                            b.auto("tensor", lambda e, j=j: e.transpose(
                                PT[:, j * 128:(j + 1) * 128], atb[:, j * 128:(j + 1) * 128],
                                ident[:]), rd=[atb, ident], wr=[(PT, j)])
                        b.auto("vector", lambda e: e.tensor_copy(attnT[:], PT[:]),
                               rd=[PT, (PT, 0), (PT, 1), (PT, 2)], wr=[attnT])
                        vbase = 2 * wloc + qt
                        b.mm_group(PV[hr, :],
                                   [(Vc[vbase + j][:, h * 64:(h + 1) * 64],
                                     attnT[:, j * 128:(j + 1) * 128]) for j in range(3)])
                        if h % 2 == 1:
                            b.auto("vector", lambda e, m=m, tt=tt: e.tensor_copy(
                                ao[m][:, tt * 128:(tt + 1) * 128], PV[:]),
                                rd=[PV], wr=[(ao[m], tt)])
            for mo in range(NDT):
                msl = slice(mo * 128, (mo + 1) * 128)
                b.mm_group(P[mo % 2][:],
                           [(WoT[k][:, msl], ao[k][:]) for k in range(NDT)])
                b.auto("vector", lambda e, mo=mo: e.tensor_tensor(
                    xw[mo][:, 1:PAIR + 1], P[mo % 2][:], xw[mo][:, 1:PAIR + 1], ALU.add),
                    rd=[P[mo % 2], xw[mo], (xw[mo], "b")], wr=[(xw[mo], "b"), xw[mo]])
                b.dma("sync", (y_out[bi, msl, t0:t0 + PAIR], f"{bi}.{p}.{mo}"),
                      xw[mo][:, 1:PAIR + 1])


def build_phase_B_attn():
    nc = bass.Bass("TRN2", target_bir_lowering=False)
    b = Bld(nc)
    emit_B_attn(b, _mk_ext(nc, {**BA_EXT, **BSH_EXT}))
    return b.finalize(_st_waits(b))



TC2 = 256
DFF = 2730
NJ = 22


BF_EXT = {"w1": [D, 2 * DFF], "w2": [DFF, D], "lng": [128, NDT], "lnb": [128, NDT],
          "b1a": [128, NJ], "b1g": [128, NJ], "b2": [128, NDT]}


def emit_B_ffn(b, ext):
    x_in = ext["x"]
    w1_d = ext["w1"]
    w2_d = ext["w2"]
    g_d = ext["lng"]
    bb_d = ext["lnb"]
    b1a_d = ext["b1a"]
    b1g_d = ext["b1g"]
    b2_d = ext["b2"]
    y_out = ext["y"]

    W1 = [b.sb(f"W1_{i}", [128, 2 * DFF], BF16) for i in range(NDT)]
    W2 = [b.sb(f"W2_{j}", [128 if j < NJ - 1 else 42, D], BF16) for j in range(NJ)]
    g_s = b.sb("g_s", [128, NDT])
    bb_s = b.sb("bb_s", [128, NDT])
    b1a_s = b.sb("b1a_s", [128, NJ])
    b1g_s = b.sb("b1g_s", [128, NJ])
    b2_s = b.sb("b2_s", [128, NDT])
    ones_b = b.sb("ones_b", [128, 1], BF16)
    ones_f1 = b.sb("ones_f1", [1, 128])

    xw = [b.sb(f"xw{i}", [128, TC2 + 1]) for i in range(NDT)]
    xs_b = [b.sb(f"xsb{i}", [128, TC2], BF16) for i in range(NDT)]
    sq_b = [b.sb(f"sqb{i}", [128, TC2], BF16) for i in range(NDT)]
    xn = [b.sb(f"xn{i}", [128, TC2], BF16) for i in range(NDT)]
    hjs = [b.sb(f"hj{j}", [128, TC2], BF16) for j in range(NJ)]
    gel = b.sb("gel", [128, TC2])
    row_m = b.sb("row_m", [1, TC2])
    row_v = b.sb("row_v", [1, TC2])
    row_r = b.sb("row_r", [1, TC2])
    row_mr = b.sb("row_mr", [1, TC2])
    t1 = b.sb("t1", [128, TC2])

    PY4 = [b.ps(f"PY{i}", [128, 2 * TC2]) for i in range(4)]
    PY = [PY4[i // 2][:, (i % 2) * TC2:(i % 2 + 1) * TC2] for i in range(NDT)]
    PA = b.ps("PA", [128, TC2])
    PG = b.ps("PG", [128, TC2])
    PB1 = b.ps("PB1", [128, TC2])
    PB2 = b.ps("PB2", [128, TC2])
    PS = PB1[0:1, :]
    PQ = PB2[0:1, :]
    PR = PB1[:]
    PMR = PB2[:]

    for i in range(NDT):
        b.dma("gpsimd", W1[i][:], w1_d[i * 128:(i + 1) * 128, :])
    for j in range(NJ):
        r0 = j * 128
        r1 = min(DFF, r0 + 128)
        b.dma("gpsimd", W2[j][:], w2_d[r0:r1, :])
    b.dma("sync", g_s[:], g_d[:])
    b.dma("sync", bb_s[:], bb_d[:])
    b.dma("sync", b1a_s[:], b1a_d[:])
    b.dma("sync", b1g_s[:], b1g_d[:])
    b.dma("sync", b2_s[:], b2_d[:])
    b.auto("vector", lambda e: e.memset(ones_b[:], 1.0), wr=[ones_b])
    b.auto("vector", lambda e: e.memset(ones_f1[:], 1.0), wr=[ones_f1])

    for bi in range(B_PER_CORE):
        for c in range(T // TC2):
            t0 = c * TC2
            for i in range(NDT):
                rsl = slice(i * 128, (i + 1) * 128)
                if c == 0:
                    b.auto("vector", lambda e, i=i: e.memset(xw[i][:, 0:1], 0.0),
                           wr=[(xw[i], "m")])
                    b.dma("sync", (xw[i][:, 1:TC2 + 1], "b"), x_in[bi, rsl, 0:TC2])
                else:
                    b.dma("sync", xw[i][:], x_in[bi, rsl, t0 - 1:t0 + TC2])
            for i in range(NDT):
                src = (xw[i][:, 1:TC2 + 1]) if i < 4 else (xw[i][:, 0:TC2])
                b.auto("scalar", lambda e, i=i, src=src: e.copy(xs_b[i][:], src),
                       rd=[xw[i], (xw[i], "m"), (xw[i], "b")], wr=[xs_b[i]])
                b.auto("scalar", lambda e, i=i, src=src: e.activation(sq_b[i][:], src, AF.Square),
                       rd=[xw[i], (xw[i], "m"), (xw[i], "b")], wr=[sq_b[i]])
            b.mm_group(PS, [(ones_b[:], xs_b[i][:]) for i in range(NDT)])
            b.mm_group(PQ, [(ones_b[:], sq_b[i][:]) for i in range(NDT)])
            b.auto("vector", lambda e: e.tensor_single_scalar(row_m[:], PS, 1.0 / D, ALU.mult),
                   rd=[PS], wr=[row_m])
            b.auto("vector", lambda e: e.tensor_single_scalar(row_v[:], PQ, 1.0 / D, ALU.mult),
                   rd=[PQ], wr=[row_v])
            b.auto("vector", lambda e: e.tensor_mul(row_r[:], row_m[:], row_m[:]),
                   rd=[row_m], wr=[row_r])
            b.auto("vector", lambda e: e.scalar_tensor_tensor(
                row_v[:], row_v[:], 1e-5, row_r[:], ALU.add, ALU.subtract),
                rd=[row_v, row_r], wr=[row_v])
            b.auto("scalar", lambda e: e.activation(row_r[:], row_v[:], AF.Ln),
                   rd=[row_v], wr=[row_r])
            b.auto("scalar", lambda e: e.activation(row_r[:], row_r[:], AF.Exp, scale=-0.5),
                   rd=[row_r], wr=[row_r])
            b.auto("vector", lambda e: e.tensor_mul(row_mr[:], row_m[:], row_r[:]),
                   rd=[row_m, row_r], wr=[row_mr])
            b.mm_group(PR, [(ones_f1[:], row_r[:])])
            b.mm_group(PMR, [(ones_f1[:], row_mr[:])])
            for i in range(NDT):
                src = (xw[i][:, 1:TC2 + 1]) if i < 4 else (xw[i][:, 0:TC2])
                b.auto("vector", lambda e, i=i, src=src: e.scalar_tensor_tensor(
                    t1[:], src, 1.0, PR, ALU.mult, ALU.mult),
                    rd=[xw[i], PR], wr=[t1])
                b.auto("vector", lambda e, i=i: e.tensor_sub(t1[:], t1[:], PMR),
                       rd=[t1, PMR], wr=[t1])
                b.auto("vector", lambda e, i=i: e.tensor_scalar(
                    xn[i][:], t1[:], g_s[:, i:i + 1], bb_s[:, i:i + 1], ALU.mult, ALU.add),
                    rd=[t1, g_s, bb_s], wr=[xn[i]])
            for j in range(NJ):
                rows = 128 if j < NJ - 1 else 42
                a0 = j * 128
                g0 = DFF + j * 128
                b.mm_group(PA[0:rows, :], [(W1[k][:, a0:a0 + rows], xn[k][:]) for k in range(NDT)])
                b.mm_group(PG[0:rows, :], [(W1[k][:, g0:g0 + rows], xn[k][:]) for k in range(NDT)])
                b.auto("scalar", lambda e, rows=rows, j=j: e.activation(
                    gel[0:rows, :], PG[0:rows, :], AF.Gelu, bias=b1g_s[0:rows, j:j + 1]),
                    rd=[PG, b1g_s], wr=[gel])
                b.auto("vector", lambda e, rows=rows, j=j: e.scalar_tensor_tensor(
                    hjs[j][0:rows, :], PA[0:rows, :], b1a_s[0:rows, j:j + 1], gel[0:rows, :],
                    ALU.add, ALU.mult),
                    rd=[PA, b1a_s, gel], wr=[hjs[j]])
            for mo in range(NDT):
                b.mm_group(PY[mo],
                           [(W2[j][0:(128 if j < NJ - 1 else 42), mo * 128:(mo + 1) * 128],
                             hjs[j][0:(128 if j < NJ - 1 else 42), :]) for j in range(NJ)],)
            for mo in range(NDT):
                b.auto("vector", lambda e, mo=mo: e.scalar_tensor_tensor(
                    xw[mo][:, 1:TC2 + 1], PY[mo], b2_s[:, mo:mo + 1],
                    xw[mo][:, 1:TC2 + 1], ALU.add, ALU.add),
                    rd=[PY4[mo // 2], b2_s, xw[mo], (xw[mo], "b")],
                    wr=[(xw[mo], "b"), xw[mo]])
                b.dma("sync", (y_out[bi, mo * 128:(mo + 1) * 128, t0:t0 + TC2],
                               f"{bi}.{c}.{mo}"), xw[mo][:, 1:TC2 + 1])


def build_phase_B_ffn():
    nc = bass.Bass("TRN2", target_bir_lowering=False)
    b = Bld(nc)
    emit_B_ffn(b, _mk_ext(nc, BF_EXT))
    return b.finalize(_st_waits(b))



LBLK = 256
LW = 128
LBAND = 384
NLAT = 64
LHD = 64
NLH = 4


C_EXT = {"qkvw": [D, 3 * D], "qkvb": [128, 3 * NDT], "vbrow": [1, D], "rqw": [D, 256],
         "rqb": [128, 2], "rkvw": [D, 512], "rkb": [128, 2], "rvbrow": [1, 256],
         "routw": [256, D], "routb": [128, NDT], "outw": [D, D], "outb": [128, NDT],
         "glrw": [128, 2], "maskl": [128, 3 * LBAND], "masklat": [128, 16 * NLAT],
         "identc": [128, 128]}


def emit_C(b, ext):
    x_in = ext["x"]
    qkv_w = ext["qkvw"]
    qkv_b = ext["qkvb"]
    vb_row = ext["vbrow"]
    rq_w = ext["rqw"]
    rq_b = ext["rqb"]
    rkv_w = ext["rkvw"]
    rk_b = ext["rkb"]
    rvb_row = ext["rvbrow"]
    rout_w = ext["routw"]
    rout_b = ext["routb"]
    out_w = ext["outw"]
    out_b = ext["outb"]
    glr_w = ext["glrw"]
    maskl_d = ext["maskl"]
    masklat_d = ext["masklat"]
    ident_d = ext["identc"]
    y_out = ext["y"]

    KVW = [b.sb(f"KVW{i}", [128, 2 * D], BF16) for i in range(NDT)]
    WqT = [b.sb(f"WqT{i}", [128, D], BF16) for i in range(NDT)]
    RqW = [b.sb(f"RqW{i}", [128, 256], BF16) for i in range(NDT)]
    RkvW = [b.sb(f"RkvW{i}", [128, 512], BF16) for i in range(NDT)]
    RoW = [b.sb(f"RoW{i}", [128, D], BF16) for i in range(2)]
    OW = [b.sb(f"OW{i}", [128, D], BF16) for i in range(NDT)]
    qkvb_s = b.sb("qkvb_s", [128, 3 * NDT])
    vbr = b.sb("vbr", [1, D], BF16)
    rqb_s = b.sb("rqb_s", [128, 2])
    rkb_s = b.sb("rkb_s", [128, 2])
    rvbr = b.sb("rvbr", [1, 256], BF16)
    routb_s = b.sb("routb_s", [128, NDT])
    outb_s = b.sb("outb_s", [128, NDT])
    glr_s = b.sb("glr_s", [128, 2], BF16)
    maskl = b.sb("maskl_s", [128, 3 * LBAND], BF16)
    masklat = b.sb("masklat_s", [128, 16 * NLAT], BF16)
    ident = b.sb("ident_s", [128, 128], BF16)
    ones_b = b.sb("ones_b", [128, 1], BF16)
    ones_f1 = b.sb("ones_f1", [1, 128])
    ones_r128 = b.sb("ones_r128", [1, 128], BF16)

    KT = [b.sb(f"KT{i}", [128, 640], BF16) for i in range(NDT)]
    VT = [b.sb(f"VT{i}", [128, D], BF16) for i in range(5)]
    localT = [b.sb(f"localT{i}", [128, T], BF16) for i in range(NDT)]
    rqT = [b.sb(f"rqT{i}", [128, T], BF16) for i in range(2)]
    latoT = [b.sb(f"latoT{i}", [128, T], BF16) for i in range(2)]
    pooled = [b.sb(f"pooled{i}", [128, NLAT], BF16) for i in range(NDT)]
    pooled_f = [b.sb(f"pooledf{i}", [128, NLAT]) for i in range(NDT)]
    rkT = [b.sb(f"rkT{i}", [128, NLAT], BF16) for i in range(2)]
    rv_sb = b.sb("rv_sb", [NLAT, 256], BF16)
    x_t = [b.sb(f"x_t{i}", [128, TC]) for i in range(NDT)]
    xb_t = [b.sb(f"xb_t{i}", [128, TC], BF16) for i in range(NDT)]
    qblk = [b.sb(f"qblk{i}", [128, LBLK], BF16) for i in range(NDT)]
    atb = b.sb("atb", [128, LBAND], BF16)
    esb = atb
    attnT = b.sb("attnT", [128, LBAND], BF16)
    zrow = b.sb("zrow", [128, 1])
    rz = b.sb("rz", [128, 1])
    ngm = b.sb("ngm", [128, 1])
    remb = [b.sb(f"remb{i}", [128, TC], BF16) for i in range(NDT)]
    xq_t = remb
    mx = xb_t
    g_buf = b.sb("g_buf", [1, 2 * TC], BF16)
    t1 = b.sb("t1", [128, TC])
    row_m = b.sb("row_m", [1, TC])
    row_v = b.sb("row_v", [1, TC])
    row_r = b.sb("row_r", [1, TC])
    row_mr = row_v

    P = [b.ps(f"P{i}", [128, TC]) for i in range(6)]
    PT = b.ps("PT", [128, LBAND], BF16)
    PV = b.ps("PV", [128, 128])

    for i in range(NDT):
        rsl = slice(i * 128, (i + 1) * 128)
        b.dma("gpsimd", WqT[i][:], qkv_w[rsl, 0:D])
        b.dma("gpsimd", (KVW[i][:, 0:D], "k"), qkv_w[rsl, D:2 * D], f"q_gp_KVW{i}")
        b.dma("gpsimd", (KVW[i][:, D:2 * D], "v"), qkv_w[rsl, 2 * D:3 * D], f"q_gp_KVW{i}")
        b.dma("gpsimd", RqW[i][:], rq_w[rsl, :])
        b.dma("gpsimd", RkvW[i][:], rkv_w[rsl, :])
        b.dma("gpsimd", OW[i][:], out_w[rsl, :])
    for i in range(2):
        b.dma("gpsimd", RoW[i][:], rout_w[i * 128:(i + 1) * 128, :])
    b.dma("sync", qkvb_s[:], qkv_b[:])
    b.dma("gpsimd", vbr[:], vb_row[:])
    b.dma("sync", rqb_s[:], rq_b[:])
    b.dma("sync", rkb_s[:], rk_b[:])
    b.dma("gpsimd", rvbr[:], rvb_row[:])
    b.dma("sync", routb_s[:], rout_b[:])
    b.dma("sync", outb_s[:], out_b[:])
    b.dma("gpsimd", glr_s[:], glr_w[:])
    b.dma("gpsimd", maskl[:], maskl_d[:])
    b.dma("gpsimd", masklat[:], masklat_d[:])
    b.dma("gpsimd", ident[:], ident_d[:])
    b.auto("vector", lambda e: e.memset(ones_b[:], 1.0), wr=[ones_b])
    b.auto("vector", lambda e: e.memset(ones_f1[:], 1.0), wr=[ones_f1])
    b.auto("vector", lambda e: e.memset(ones_r128[:], 1.0), wr=[ones_r128])

    for bi in range(B_PER_CORE):
        for c in range(NCH):
            t0 = c * TC
            xsl = slice(t0, t0 + TC)
            for i in range(NDT):
                b.dma("sync", x_t[i][:], x_in[bi, i * 128:(i + 1) * 128, xsl])
            for i in range(NDT):
                b.auto("scalar", lambda e, i=i: e.copy(xb_t[i][:], x_t[i][:]),
                       rd=[x_t[i]], wr=[xb_t[i]])
                b.auto("scalar", lambda e, i=i: e.activation(xq_t[i][:, 0:TC], x_t[i][:], AF.Square),
                       rd=[x_t[i]], wr=[xq_t[i]])
            b.mm_group(P[4][0:1, :], [(ones_b[:], xb_t[i][:]) for i in range(NDT)])
            b.mm_group(P[5][0:1, :], [(ones_b[:], xq_t[i][:, 0:TC]) for i in range(NDT)])
            b.auto("vector", lambda e: e.tensor_single_scalar(row_m[:], P[4][0:1, :], 1.0 / D, ALU.mult),
                   rd=[P[4]], wr=[row_m])
            b.auto("vector", lambda e: e.tensor_single_scalar(row_v[:], P[5][0:1, :], 1.0 / D, ALU.mult),
                   rd=[P[5]], wr=[row_v])
            b.auto("vector", lambda e: e.tensor_mul(row_r[:], row_m[:], row_m[:]),
                   rd=[row_m], wr=[row_r])
            b.auto("vector", lambda e: e.scalar_tensor_tensor(
                row_v[:], row_v[:], 1e-5, row_r[:], ALU.add, ALU.subtract),
                rd=[row_v, row_r], wr=[row_v])
            b.auto("scalar", lambda e: e.activation(row_r[:], row_v[:], AF.Ln),
                   rd=[row_v], wr=[row_r])
            b.auto("scalar", lambda e: e.activation(row_r[:], row_r[:], AF.Exp, scale=-0.5),
                   rd=[row_r], wr=[row_r])
            b.auto("vector", lambda e: e.tensor_mul(row_mr[:], row_m[:], row_r[:]),
                   rd=[row_m, row_r], wr=[row_mr])
            b.mm_group(P[4][:], [(ones_f1[:], row_r[:])])
            b.mm_group(P[5][:], [(ones_f1[:], row_mr[:])])
            for i in range(NDT):
                b.auto("vector", lambda e, i=i: e.scalar_tensor_tensor(
                    t1[:], x_t[i][:], 1.0, P[4][:], ALU.mult, ALU.mult),
                    rd=[x_t[i], P[4]], wr=[t1])
                b.auto("vector", lambda e, i=i: e.tensor_sub(xb_t[i][:], t1[:], P[5][:]),
                       rd=[t1, P[5]], wr=[xb_t[i]])
            for i in range(NDT):
                b.auto("vector", lambda e, i=i, c=c: e.reduce_sum(
                    pooled_f[i][:, c * 16:(c + 1) * 16],
                    xb_t[i][:].rearrange("p (a w) -> p a w", w=32), axis=AX.X),
                    rd=[xb_t[i]], wr=[(pooled_f[i], c)])
            for m in range(2):
                b.mm_group(P[4][:], [(RqW[k][:, m * 128:(m + 1) * 128], xb_t[k][:])
                                     for k in range(NDT)])
                b.auto("vector", lambda e, o=rqT[m][:, xsl], m=m: e.tensor_single_scalar(
                    o, P[4][:], rqb_s[:, m:m + 1], ALU.add),
                    rd=[P[4], rqb_s], wr=[(rqT[m], c)])
            for i in range(NDT):
                if c == 0:
                    b.auto("vector", lambda e, i=i: e.memset(KT[i][:, 0:128], 0.0),
                           wr=[(KT[i], "prev")])
                else:
                    b.auto("vector", lambda e, i=i: e.tensor_copy(
                        KT[i][:, 0:128], KT[i][:, 512:640]),
                        rd=[(KT[i], "cur")], wr=[(KT[i], "prev")])
            if c == 0:
                b.auto("vector", lambda e: e.memset(VT[0][:], 0.0), wr=[VT[0]])
            else:
                b.auto("vector", lambda e: e.tensor_copy(VT[0][:], VT[4][:]),
                       rd=[VT[4]], wr=[VT[0]])
            for m in range(NDT):
                b.mm_group(P[m % 2][:], [(KVW[k][:, m * 128:(m + 1) * 128], xb_t[k][:])
                                         for k in range(NDT)])
                b.auto("vector", lambda e, m=m: e.tensor_single_scalar(
                    KT[m][:, 128:640], P[m % 2][:], qkvb_s[:, NDT + m:NDT + m + 1], ALU.add),
                    rd=[P[m % 2], qkvb_s], wr=[(KT[m], "cur")])
            for tt in range(4):
                tsl = slice(tt * 128, (tt + 1) * 128)
                for ch in range(2):
                    b.mm_group(P[2 + ch][:],
                               [(xb_t[k][:, tsl], KVW[k][:, D + ch * 512:D + (ch + 1) * 512])
                                for k in range(NDT)]
                               + [(ones_r128[0:1, 0:128], vbr[0:1, ch * 512:(ch + 1) * 512])])
                    b.auto("vector", lambda e, tt=tt, ch=ch: e.tensor_copy(
                        VT[1 + tt][:, ch * 512:(ch + 1) * 512], P[2 + ch][:]),
                        rd=[P[2 + ch]], wr=[(VT[1 + tt], ch)])
            for wloc in range(2):
                bb = 2 * c + wloc
                bq0 = wloc * LBLK
                for m in range(NDT):
                    b.mm_group(P[4][:, 0:LBLK],
                               [(WqT[k][:, m * 128:(m + 1) * 128], xb_t[k][:, bq0:bq0 + LBLK])
                                for k in range(NDT)])
                    b.auto("vector", lambda e, m=m: e.tensor_single_scalar(
                        qblk[m][:], P[4][:, 0:LBLK], qkvb_s[:, m:m + 1], ALU.add),
                        rd=[P[4], qkvb_s], wr=[qblk[m]])
                kroll0 = 256 * wloc
                for qt in range(2):
                    mv = qt if bb > 0 else (2 if qt == 0 else 1)
                    msl = slice(mv * LBAND, (mv + 1) * LBAND)
                    for h in range(HEADS):
                        m = h // 2
                        hr = slice((h % 2) * 64, (h % 2) * 64 + 64)
                        b.mm_group(P[5][:, 0:LBAND],
                                   [(qblk[m][hr, qt * 128:(qt + 1) * 128],
                                     KT[m][hr, kroll0:kroll0 + LBAND])])
                        b.auto("vector", lambda e: e.reduce_max(
                            ngm[:], P[5][:, 0:LBAND], axis=AX.X, negate=True),
                            rd=[P[5]], wr=[ngm])
                        b.auto("vector", lambda e: e.tensor_single_scalar(
                            ngm[:], ngm[:], 0.125, ALU.mult), rd=[ngm], wr=[ngm])
                        b.auto("scalar", lambda e: e.activation(
                            atb[:], P[5][:, 0:LBAND], AF.Exp, bias=ngm[:], scale=0.125),
                            rd=[P[5], ngm], wr=[atb])
                        b.auto("vector", lambda e, msl=msl: e.scalar_tensor_tensor(
                            atb[:], atb[:], 1.0, maskl[:, msl], ALU.mult, ALU.mult,
                            accum_out=zrow[:]),
                            rd=[atb, maskl], wr=[atb, zrow])
                        b.auto("vector", lambda e: e.tensor_single_scalar(
                            zrow[:], zrow[:], 1e-30, ALU.max), rd=[zrow], wr=[zrow])
                        b.auto("vector", lambda e: e.reciprocal(rz[:], zrow[:]),
                               rd=[zrow], wr=[rz])
                        b.auto("vector", lambda e: e.tensor_single_scalar(
                            atb[:], atb[:], rz[:], ALU.mult), rd=[atb, rz], wr=[atb])
                        for j in range(3):
                            b.auto("tensor", lambda e, j=j: e.transpose(
                                PT[:, j * 128:(j + 1) * 128], atb[:, j * 128:(j + 1) * 128],
                                ident[:]), rd=[atb, ident], wr=[(PT, j)])
                        b.auto("vector", lambda e: e.tensor_copy(attnT[:], PT[:]),
                               rd=[PT, (PT, 0), (PT, 1), (PT, 2)], wr=[attnT])
                        vr0 = 2 * wloc
                        b.mm_group(PV[hr, :],
                                   [(VT[vr0 + j][:, h * 64:(h + 1) * 64],
                                     attnT[:, j * 128:(j + 1) * 128]) for j in range(3)])
                        if h % 2 == 1:
                            b.auto("vector", lambda e, m=m, bb=bb, qt=qt: e.tensor_copy(
                                localT[m][:, bb * LBLK + qt * 128:bb * LBLK + (qt + 1) * 128],
                                PV[:]), rd=[PV], wr=[(localT[m], f"{bb}.{qt}")])
        for i in range(NDT):
            b.auto("vector", lambda e, i=i: e.tensor_single_scalar(
                pooled[i][:], pooled_f[i][:], 1.0 / 32, ALU.mult),
                rd=[pooled_f[i]] + [(pooled_f[i], c) for c in range(NCH)], wr=[pooled[i]])
        for m in range(2):
            b.mm_group(P[0][:, 0:NLAT], [(RkvW[k][:, m * 128:(m + 1) * 128], pooled[k][:])
                                         for k in range(NDT)])
            b.auto("vector", lambda e, m=m: e.tensor_single_scalar(
                rkT[m][:, :], P[0][:, 0:NLAT], rkb_s[:, m:m + 1], ALU.add),
                rd=[P[0], rkb_s], wr=[rkT[m]])
        b.mm_group(P[1][0:NLAT, 0:256],
                   [(pooled[k][:], RkvW[k][:, 256:512]) for k in range(NDT)]
                   + [(ones_r128[0:1, 0:NLAT], rvbr[:])])
        b.auto("vector", lambda e: e.tensor_copy(rv_sb[:], P[1][0:NLAT, 0:256]),
               rd=[P[1]], wr=[rv_sb])
        for tt in range(16):
            q0 = tt * 128
            for lh in range(NLH):
                mt = lh // 2
                hr = slice((lh % 2) * 64, (lh % 2) * 64 + 64)
                b.mm_group(P[4][:, 0:NLAT], [(rqT[mt][hr, q0:q0 + 128], rkT[mt][hr, :])])
                b.auto("vector", lambda e: e.reduce_max(
                    ngm[:], P[4][:, 0:NLAT], axis=AX.X, negate=True),
                    rd=[P[4]], wr=[ngm])
                b.auto("vector", lambda e: e.tensor_single_scalar(
                    ngm[:], ngm[:], 0.125, ALU.mult), rd=[ngm], wr=[ngm])
                b.auto("scalar", lambda e: e.activation(
                    atb[:, 0:NLAT], P[4][:, 0:NLAT], AF.Exp, bias=ngm[:], scale=0.125),
                    rd=[P[4], ngm], wr=[atb])
                b.auto("vector", lambda e, tt=tt: e.scalar_tensor_tensor(
                    atb[:, 0:NLAT], atb[:, 0:NLAT], 1.0, masklat[:, tt * NLAT:(tt + 1) * NLAT],
                    ALU.mult, ALU.mult, accum_out=zrow[:]),
                    rd=[atb, masklat], wr=[atb, zrow])
                b.auto("vector", lambda e: e.tensor_single_scalar(
                    zrow[:], zrow[:], 1e-30, ALU.max), rd=[zrow], wr=[zrow])
                b.auto("vector", lambda e: e.reciprocal(rz[:], zrow[:]), rd=[zrow], wr=[rz])
                b.auto("vector", lambda e: e.tensor_single_scalar(
                    atb[:, 0:NLAT], atb[:, 0:NLAT], rz[:], ALU.mult), rd=[atb, rz], wr=[atb])
                b.auto("tensor", lambda e: e.transpose(
                    PT[0:NLAT, 0:128], atb[:, 0:NLAT], ident[:]),
                    rd=[atb, ident], wr=[PT])
                b.auto("vector", lambda e: e.tensor_copy(attnT[0:NLAT, 0:128], PT[0:NLAT, 0:128]),
                       rd=[PT], wr=[attnT])
                b.mm_group(PV[hr, :], [(rv_sb[:, lh * 64:(lh + 1) * 64], attnT[0:NLAT, 0:128])])
                if lh % 2 == 1:
                    b.auto("vector", lambda e, mt=mt, q0=q0: e.tensor_copy(
                        latoT[mt][:, q0:q0 + 128], PV[:]), rd=[PV], wr=[(latoT[mt], tt)])
        for c in range(NCH):
            t0 = c * TC
            xsl = slice(t0, t0 + TC)
            for m in range(NDT):
                b.mm_group(P[m % 2][:], [(RoW[k][:, m * 128:(m + 1) * 128],
                                          latoT[k][:, xsl]) for k in range(2)])
                b.auto("vector", lambda e, m=m: e.tensor_single_scalar(
                    remb[m][:], P[m % 2][:], routb_s[:, m:m + 1], ALU.add),
                    rd=[P[m % 2], routb_s], wr=[remb[m]])
            for m in range(NDT):
                for hh in range(2):
                    hr = slice(hh * 64, hh * 64 + 64)
                    b.mm_group(P[2][0:1, :],
                               [(glr_s[hr, 0:1], localT[m][hr, xsl]),
                                (glr_s[hr, 1:2], remb[m][hr, :])])
                    b.auto("scalar", lambda e, hh=hh: e.activation(
                        g_buf[0:1, hh * TC:(hh + 1) * TC], P[2][0:1, :], AF.Sigmoid),
                        rd=[P[2]], wr=[(g_buf, hh)])
                for hh in range(2):
                    b.mm_group(P[3][hh * 64:hh * 64 + 64, :],
                               [(ones_r128[0:1, 0:64], g_buf[0:1, hh * TC:(hh + 1) * TC])])
                b.auto("vector", lambda e, src=localT[m][:, xsl], m=m: e.tensor_tensor(
                    t1[:], src, remb[m][:], ALU.subtract),
                    rd=[localT[m], remb[m]], wr=[t1])
                b.auto("vector", lambda e, m=m: e.tensor_tensor(
                    t1[:], t1[:], P[3][:], ALU.mult), rd=[t1, P[3]], wr=[t1])
                b.auto("vector", lambda e, m=m: e.tensor_tensor(
                    mx[m][:], t1[:], remb[m][:], ALU.add), rd=[t1, remb[m]], wr=[mx[m]])
            for mo in range(NDT):
                b.mm_group(P[4 + mo % 2][:], [(OW[k][:, mo * 128:(mo + 1) * 128], mx[k][:])
                                              for k in range(NDT)])
                b.dma("sync", x_t[mo][:], x_in[bi, mo * 128:(mo + 1) * 128, xsl],
                      sem=f"q_sy_x2_{mo}")
                b.auto("vector", lambda e, mo=mo: e.scalar_tensor_tensor(
                    x_t[mo][:], P[4 + mo % 2][:], outb_s[:, mo:mo + 1], x_t[mo][:],
                    ALU.add, ALU.add),
                    rd=[P[4 + mo % 2], outb_s, x_t[mo]], wr=[x_t[mo]])
                b.dma("sync", (y_out[bi, mo * 128:(mo + 1) * 128, xsl], f"{bi}.{c}.{mo}"),
                      x_t[mo][:])


def build_phase_C():
    nc = bass.Bass("TRN2", target_bir_lowering=False)
    b = Bld(nc)
    emit_C(b, _mk_ext(nc, C_EXT))
    return b.finalize(_st_waits(b))


def _prep_phase_C(inputs):
    d = {}
    d["qkvw"] = np.ascontiguousarray(np.asarray(inputs["L1_attn_qkv_w"]))
    qb = np.asarray(inputs["L1_attn_qkv_b"])
    d["qkvb"] = np.ascontiguousarray(qb[:2 * D].reshape(2 * NDT, 128).T)
    d["qkvb"] = np.concatenate(
        [d["qkvb"], qb[2 * D:].reshape(NDT, 128).T], axis=1).astype(np.float32)
    d["qkvb"] = np.ascontiguousarray(d["qkvb"])
    d["vbrow"] = np.ascontiguousarray(qb[2 * D:][None, :])
    d["rqw"] = np.ascontiguousarray(np.asarray(inputs["L1_attn_rq_w"]))
    d["rqb"] = np.ascontiguousarray(np.asarray(inputs["L1_attn_rq_b"]).reshape(2, 128).T)
    d["rkvw"] = np.ascontiguousarray(np.asarray(inputs["L1_attn_rkv_w"]))
    rkvb = np.asarray(inputs["L1_attn_rkv_b"])
    d["rkb"] = np.ascontiguousarray(rkvb[:256].reshape(2, 128).T)
    d["rvbrow"] = np.ascontiguousarray(rkvb[256:][None, :])
    d["routw"] = np.ascontiguousarray(np.asarray(inputs["L1_attn_rout_w"]))
    d["routb"] = np.ascontiguousarray(np.asarray(inputs["L1_attn_rout_b"]).reshape(NDT, 128).T)
    d["outw"] = np.ascontiguousarray(np.asarray(inputs["L1_attn_out_w"]))
    d["outb"] = np.ascontiguousarray(np.asarray(inputs["L1_attn_out_b"]).reshape(NDT, 128).T)
    glr = np.concatenate(
        [np.asarray(inputs["L1_attn_gl_w"]), np.asarray(inputs["L1_attn_gr_w"])], axis=1)
    d["glrw"] = np.ascontiguousarray(np.tile(glr, (2, 1)))
    i = np.arange(128)[:, None]
    jj = np.arange(LBAND)[None, :]
    m0 = ((jj >= i) & (jj <= i + 127)).astype(np.float32)
    m1 = ((jj >= 128 + i) & (jj <= 255 + i)).astype(np.float32)
    m2 = m0 * (jj >= LW)
    d["maskl"] = np.ascontiguousarray(np.concatenate([m0, m1, m2], axis=1))
    mm = np.zeros((128, 16 * NLAT), dtype=np.float32)
    for tt in range(16):
        lim = 4 * tt + (np.arange(128) // 32)
        mm[:, tt * NLAT:(tt + 1) * NLAT] = (np.arange(NLAT)[None, :] < lim[:, None])
    d["masklat"] = np.ascontiguousarray(mm)
    d["identc"] = np.eye(128, dtype=np.float32)
    return d



NEXP = 8
DFF2 = 4096
NFF = DFF2 // 128


D_EXT = {"rw": [D, NEXP], "w1": [NEXP, D, DFF2], "b1": [NEXP, 128, NFF],
         "w2": [NEXP, DFF2, D], "b2": [NEXP, 128, NDT], "gam": [NEXP, 128, NDT],
         "bet": [NEXP, 128, NDT], "identf": [128, 128]}


def emit_D(b, ext):
    nc = b.nc
    x_in = ext["x"]
    rw_d = ext["rw"]
    w1_d = ext["w1"]
    b1_d = ext["b1"]
    w2_d = ext["w2"]
    b2_d = ext["b2"]
    gam_d = ext["gam"]
    bet_d = ext["bet"]
    identf_d = ext["identf"]
    y_out = ext["y"]
    xn_scr = ext["xnscr"]
    g_scr = ext["gscr"]

    W1 = [b.sb(f"W1_{k}", [128, DFF2], BF16) for k in range(NDT)]
    W2 = [b.sb(f"W2_{j}", [128, D], BF16) for j in range(NFF)]
    rw_s = b.sb("rw_s", [128, NEXP * NDT])
    b1_s = b.sb("b1_s", [128, NFF])
    b2_s = b.sb("b2_s", [128, NDT])
    gam_s = b.sb("gam_s", [128, NDT])
    bet_s = b.sb("bet_s", [128, NDT])
    identf = b.sb("identf_s", [128, 128])
    ones_b = b.sb("ones_b", [128, 1], BF16)

    x_t = [b.sb(f"x_t{i}", [128, TC]) for i in range(NDT)]
    xnf = [b.sb(f"xnf{i}", [128, TC]) for i in range(NDT)]
    xb_t = [b.sb(f"xb_t{i}", [128, TC], BF16) for i in range(NDT)]
    hbuf = [b.sb(f"hb{j}", [128, TC], BF16) for j in range(NFF)]
    r_f = xnf
    sqb = b.sb("sqb", [128, TC], BF16)
    rb = sqb
    xnc = xb_t
    gt_t = b.sb("gt_t", [NEXP, 128], BF16)
    gfin = b.sb("gfin", [128, NEXP])
    g_sb = b.sb("g_sb", [1, TC], BF16)
    ones_bb = b.sb("ones_bb", [1, 128], BF16)
    gate_t = b.sb("gate_t", [128, NEXP])
    m1c = b.sb("m1c", [128, 1])
    m2c = b.sb("m2c", [128, 1])
    w1c = b.sb("w1c", [128, 1])
    w2c = b.sb("w2c", [128, 1])
    msk1 = b.sb("msk1", [128, NEXP])
    msk2 = b.sb("msk2", [128, NEXP])
    t1 = b.sb("t1", [128, TC])
    row_m = b.sb("row_m", [1, TC], BF16)
    row_v = b.sb("row_v", [1, TC], BF16)

    PH = [b.ps(f"PH{i}", [128, TC]) for i in range(2)]
    PW = [b.ps(f"PW{i}", [128, TC]) for i in range(2)]
    PB1 = b.ps("PB1", [128, TC])
    PB2 = b.ps("PB2", [128, TC])
    PGB = b.ps("PGB", [128, TC])
    PR8 = b.ps("PR8", [128, 128])

    for i in range(NDT):
        b.dma("sync", rw_s[:, i * NEXP:(i + 1) * NEXP], rw_d[i * 128:(i + 1) * 128, :],
              sem="q_sy_rw")
    b.dma("sync", identf[:], identf_d[:])
    b.auto("vector", lambda e: e.memset(ones_b[:], 1.0), wr=[ones_b])
    b.auto("vector", lambda e: e.memset(ones_bb[:], 1.0), wr=[ones_bb])

    for bi in range(B_PER_CORE):
        for c in range(NCH):
            t0 = c * TC
            xsl = slice(t0, t0 + TC)
            for i in range(NDT):
                b.dma("sync", x_t[i][:], x_in[bi, i * 128:(i + 1) * 128, xsl])
            for i in range(NDT):
                b.auto("scalar", lambda e, i=i: e.copy(xb_t[i][:], x_t[i][:]),
                       rd=[x_t[i]], wr=[xb_t[i]])
                b.auto("scalar", lambda e, i=i: e.activation(sqb[:], x_t[i][:], AF.Square),
                       rd=[x_t[i]], wr=[sqb])
                b.auto("tensor", lambda e, i=i: e.matmul(
                    PB2[0:1, :], ones_b[:], sqb[:], start=(i == 0), stop=(i == NDT - 1)),
                    rd=[ones_b, sqb], wr=([PB2] if i in (0, NDT - 1) else []))
            b.mm_group(PB1[0:1, :], [(ones_b[:], xb_t[i][:]) for i in range(NDT)])
            b.auto("vector", lambda e: e.tensor_single_scalar(row_m[:], PB1[0:1, :], 1.0 / D, ALU.mult),
                   rd=[PB1], wr=[row_m])
            b.auto("vector", lambda e: e.tensor_single_scalar(row_v[:], PB2[0:1, :], 1.0 / D, ALU.mult),
                   rd=[PB2], wr=[row_v])
            b.auto("vector", lambda e: e.tensor_mul(t1[0:1, :], row_m[:], row_m[:]),
                   rd=[row_m], wr=[(t1, "row")])
            b.auto("vector", lambda e: e.scalar_tensor_tensor(
                row_v[:], row_v[:], 1e-5, t1[0:1, :], ALU.add, ALU.subtract),
                rd=[row_v, (t1, "row")], wr=[row_v])
            b.auto("scalar", lambda e: e.activation(row_v[:], row_v[:], AF.Ln),
                   rd=[row_v], wr=[row_v])
            b.auto("scalar", lambda e: e.activation(row_v[:], row_v[:], AF.Exp, scale=-0.5),
                   rd=[row_v], wr=[row_v])
            b.auto("vector", lambda e: e.tensor_mul(row_m[:], row_m[:], row_v[:]),
                   rd=[row_m, row_v], wr=[row_m])
            b.mm_group(PB1[:], [(ones_bb[:], row_v[:])])
            b.mm_group(PB2[:], [(ones_bb[:], row_m[:])])
            for i in range(NDT):
                b.auto("vector", lambda e, i=i: e.scalar_tensor_tensor(
                    t1[:], x_t[i][:], 1.0, PB1[:], ALU.mult, ALU.mult),
                    rd=[x_t[i], PB1], wr=[t1])
                b.auto("vector", lambda e, i=i: e.tensor_sub(xnf[i][:], t1[:], PB2[:]),
                       rd=[t1, PB2], wr=[xnf[i]])
                b.auto("scalar", lambda e, i=i: e.copy(xb_t[i][:], xnf[i][:]),
                       rd=[xnf[i]], wr=[xb_t[i]])
                b.dma("sync", (xn_scr[bi, i * 128:(i + 1) * 128, xsl], f"{bi}.{c}.{i}"),
                      xb_t[i][:])
            for tt in range(4):
                tloc = slice(tt * 128, (tt + 1) * 128)
                b.mm_group(PR8[:, 0:NEXP],
                           [(xnf[k][:, tloc], rw_s[:, k * NEXP:(k + 1) * NEXP])
                            for k in range(NDT)])
                b.auto("vector", lambda e: e.reduce_max(m1c[:], PR8[:, 0:NEXP], axis=AX.X),
                       rd=[PR8], wr=[m1c])
                b.auto("vector", lambda e: e.tensor_single_scalar(
                    gate_t[:], PR8[:, 0:NEXP], m1c[:], ALU.is_ge),
                    rd=[PR8, m1c], wr=[gate_t])
                b.auto("vector", lambda e: e.tensor_copy(msk1[:], gate_t[:]),
                       rd=[gate_t], wr=[msk1])
                b.auto("vector", lambda e: e.scalar_tensor_tensor(
                    gate_t[:], msk1[:], -1e30, PR8[:, 0:NEXP], ALU.mult, ALU.add),
                    rd=[msk1, PR8], wr=[gate_t])
                b.auto("vector", lambda e: e.reduce_max(m2c[:], gate_t[:], axis=AX.X),
                       rd=[gate_t], wr=[m2c])
                b.auto("vector", lambda e: e.tensor_single_scalar(
                    msk2[:], gate_t[:], m2c[:], ALU.is_ge),
                    rd=[gate_t, m2c], wr=[msk2])
                b.auto("vector", lambda e: e.tensor_sub(w1c[:], m1c[:], m2c[:]),
                       rd=[m1c, m2c], wr=[w1c])
                b.auto("scalar", lambda e: e.activation(w1c[:], w1c[:], AF.Sigmoid),
                       rd=[w1c], wr=[w1c])
                b.auto("vector", lambda e: e.tensor_scalar(
                    w2c[:], w1c[:], -1.0, 1.0, ALU.mult, ALU.add),
                    rd=[w1c], wr=[w2c])
                b.auto("vector", lambda e: e.tensor_single_scalar(
                    msk1[:], msk1[:], w1c[:], ALU.mult), rd=[msk1, w1c], wr=[msk1])
                b.auto("vector", lambda e: e.scalar_tensor_tensor(
                    gfin[:], msk2[:], w2c[:], msk1[:], ALU.mult, ALU.add),
                    rd=[msk2, w2c, msk1], wr=[gfin])
                b.auto("tensor", lambda e: e.transpose(
                    PR8[0:NEXP, 0:128], gfin[:, 0:NEXP], identf[:]),
                    rd=[gfin, identf], wr=[PR8])
                b.auto("scalar", lambda e: e.copy(gt_t[:], PR8[0:NEXP, 0:128]),
                       rd=[PR8], wr=[gt_t])
                b.dma("gpsimd", (g_scr[bi, :, t0 + tt * 128:t0 + (tt + 1) * 128],
                                 f"{bi}.{c}.{tt}"), gt_t[:], sem="q_gp_st_gt")
    for e_i in range(NEXP):
        for k in range(NDT):
            b.dma("gpsimd", W1[k][:], w1_d[e_i, k * 128:(k + 1) * 128, :])
        for j in range(NFF):
            b.dma("gpsimd", W2[j][:], w2_d[e_i, j * 128:(j + 1) * 128, :])
        b.dma("sync", b1_s[:], b1_d[e_i], sem="q_sy_b1")
        b.dma("sync", b2_s[:], b2_d[e_i], sem="q_sy_b2")
        b.dma("sync", gam_s[:], gam_d[e_i], sem="q_sy_gam")
        b.dma("sync", bet_s[:], bet_d[e_i], sem="q_sy_bet")
        for bi in range(B_PER_CORE):
            for c in range(NCH):
                t0 = c * TC
                xsl = slice(t0, t0 + TC)
                for i in range(NDT):
                    rsl = slice(i * 128, (i + 1) * 128)
                    b.dma("sync", xnc[i][:], xn_scr[bi, rsl, xsl])
                    b.dma("sync", x_t[i][:], x_in[bi, rsl, xsl])
                b.dma("gpsimd", g_sb[:], g_scr[bi, e_i, xsl])
                b.mm_group(PGB[:], [(ones_bb[:], g_sb[:])])
                for j in range(NFF):
                    b.mm_group(PH[j % 2][:], [(W1[k][:, j * 128:(j + 1) * 128], xnc[k][:])
                                              for k in range(NDT)])
                    b.auto("scalar", lambda e, j=j: e.activation(
                        hbuf[j][:], PH[j % 2][:], AF.Gelu, bias=b1_s[:, j:j + 1]),
                        rd=[PH[j % 2], b1_s], wr=[hbuf[j]])
                for mo in range(NDT):
                    b.mm_group(PW[mo % 2][:], [(W2[j][:, mo * 128:(mo + 1) * 128], hbuf[j][:])
                                               for j in range(NFF)])
                    b.auto("vector", lambda e, mo=mo: e.scalar_tensor_tensor(
                        r_f[mo][:], PW[mo % 2][:], b2_s[:, mo:mo + 1], xnc[mo][:],
                        ALU.add, ALU.add),
                        rd=[PW[mo % 2], b2_s, xnc[mo]], wr=[r_f[mo]])
                    b.auto("scalar", lambda e, mo=mo: e.activation(sqb[:], r_f[mo][:], AF.Square),
                           rd=[r_f[mo]], wr=[sqb])
                    b.auto("tensor", lambda e, mo=mo: e.matmul(
                        PB2[0:1, :], ones_b[:], sqb[:], start=(mo == 0), stop=(mo == NDT - 1)),
                        rd=[ones_b, sqb], wr=([PB2] if mo in (0, NDT - 1) else []))
                    b.auto("scalar", lambda e, mo=mo: e.copy(rb[:], r_f[mo][:]),
                           rd=[r_f[mo]], wr=[rb])
                    b.auto("tensor", lambda e, mo=mo: e.matmul(
                        PB1[0:1, :], ones_b[:], rb[:], start=(mo == 0), stop=(mo == NDT - 1)),
                        rd=[ones_b, rb], wr=([PB1] if mo in (0, NDT - 1) else []))
                b.auto("vector", lambda e: e.tensor_single_scalar(row_m[:], PB1[0:1, :], 1.0 / D, ALU.mult),
                       rd=[PB1], wr=[row_m])
                b.auto("vector", lambda e: e.tensor_single_scalar(row_v[:], PB2[0:1, :], 1.0 / D, ALU.mult),
                       rd=[PB2], wr=[row_v])
                b.auto("vector", lambda e: e.tensor_mul(t1[0:1, :], row_m[:], row_m[:]),
                       rd=[row_m], wr=[(t1, "row")])
                b.auto("vector", lambda e: e.scalar_tensor_tensor(
                    row_v[:], row_v[:], 1e-6, t1[0:1, :], ALU.add, ALU.subtract),
                    rd=[row_v, (t1, "row")], wr=[row_v])
                b.auto("scalar", lambda e: e.activation(row_v[:], row_v[:], AF.Ln),
                       rd=[row_v], wr=[row_v])
                b.auto("scalar", lambda e: e.activation(row_v[:], row_v[:], AF.Exp, scale=-0.5),
                       rd=[row_v], wr=[row_v])
                b.auto("vector", lambda e: e.tensor_mul(row_m[:], row_m[:], row_v[:]),
                       rd=[row_m, row_v], wr=[row_m])
                b.mm_group(PB1[:], [(ones_bb[:], row_v[:])])
                b.mm_group(PB2[:], [(ones_bb[:], row_m[:])])
                for mo in range(NDT):
                    b.auto("vector", lambda e, mo=mo: e.scalar_tensor_tensor(
                        t1[:], r_f[mo][:], 1.0, PB1[:], ALU.mult, ALU.mult),
                        rd=[r_f[mo], PB1], wr=[t1])
                    b.auto("vector", lambda e: e.tensor_sub(t1[:], t1[:], PB2[:]),
                           rd=[t1, PB2], wr=[t1])
                    b.auto("vector", lambda e, mo=mo: e.tensor_scalar(
                        t1[:], t1[:], gam_s[:, mo:mo + 1], bet_s[:, mo:mo + 1],
                        ALU.mult, ALU.add),
                        rd=[t1, gam_s, bet_s], wr=[t1])
                    b.auto("vector", lambda e: e.tensor_tensor(
                        t1[:], t1[:], PGB[:], ALU.mult), rd=[t1, PGB], wr=[t1])
                    rsl = slice(mo * 128, (mo + 1) * 128)
                    if e_i == 0:
                        b.auto("vector", lambda e, mo=mo: e.tensor_tensor(
                            t1[:], t1[:], x_t[mo][:], ALU.add),
                            rd=[t1, x_t[mo]], wr=[t1])
                        b.dma("sync", (y_out[bi, rsl, xsl], f"{bi}.{c}.{mo}"), t1[:],
                              sem="q_sy_st_y")
                    else:
                        b.auto("gpsimd",
                               lambda e, o=y_out[bi, rsl, xsl]: e.dma_start(
                                   out=o, in_=t1[:], accum_op=ALU.add),
                               rd=[t1, (y_out, f"{bi}.{c}.{mo}")],
                               wr=[(y_out, f"{bi}.{c}.{mo}")],
                               sem="q_gp_st_y", incn=16)


def build_phase_D():
    nc = bass.Bass("TRN2", target_bir_lowering=False)
    b = Bld(nc)
    ext = _mk_ext(nc, D_EXT)
    ext["xnscr"] = nc.dram_tensor("xnscr", [B_PER_CORE, D, T], BF16, kind="Internal")
    ext["gscr"] = nc.dram_tensor("gscr", [B_PER_CORE, NEXP, T], F32, kind="Internal")
    emit_D(b, ext)
    return b.finalize(_st_waits(b))


def _prep_phase_D(inputs):
    d = {}
    d["rw"] = np.ascontiguousarray(np.asarray(inputs["L1_ffn_router_w"]))
    d["w1"] = np.ascontiguousarray(np.asarray(inputs["L1_ffn_w1"]))
    d["b1"] = np.ascontiguousarray(
        np.asarray(inputs["L1_ffn_b1"]).reshape(NEXP, NFF, 128).transpose(0, 2, 1))
    d["w2"] = np.ascontiguousarray(np.asarray(inputs["L1_ffn_w2"]))
    d["b2"] = np.ascontiguousarray(
        np.asarray(inputs["L1_ffn_b2"]).reshape(NEXP, NDT, 128).transpose(0, 2, 1))
    d["gam"] = np.ascontiguousarray(
        np.asarray(inputs["L1_ffn_gamma"]).reshape(NEXP, NDT, 128).transpose(0, 2, 1))
    d["bet"] = np.ascontiguousarray(
        np.asarray(inputs["L1_ffn_beta"]).reshape(NEXP, NDT, 128).transpose(0, 2, 1))
    d["identf"] = np.eye(128, dtype=np.float32)
    return d



def _sel_matrices():
    R = np.zeros((128, 16, 128), dtype=np.float32)
    for k in range(16):
        for p in range(128):
            R[8 * k + p // 16, k, p] = 1.0
    S = np.zeros((128, 16, 128), dtype=np.float32)
    for k in range(16):
        for p in range(128):
            S[p, k, 8 * k + p // 16] = 1.0
    RB = np.zeros((16, 128), dtype=np.float32)
    for p in range(128):
        RB[p % 16, p] = 1.0
    return R.reshape(128, 16 * 128), S.reshape(128, 16 * 128), RB


def _rotary_tables():
    half = np.arange(0, 64, 2, dtype=np.float64)
    inv_freq = 1.0 / 10000.0 ** (half / 64.0)
    t = np.arange(T, dtype=np.float64)
    freqs = t[:, None] * inv_freq[None, :]
    base_scale = (half + 0.4 * 64) / (1.4 * 64)
    power = (t - T // 2) / (WIN // 2)
    scale = base_scale[None, :] ** power[:, None]
    j = np.arange(64)
    fr = freqs[:, j % 32]
    sc = scale[:, j % 32]
    sign = np.where(j < 32, -1.0, 1.0)
    cs_q = (np.cos(fr) * sc).T
    ss_q = (np.sin(fr) * sc).T * sign[:, None]
    cs_k = (np.cos(fr) / sc).T
    ss_k = (np.sin(fr) / sc).T * sign[:, None]
    tile2 = lambda a: np.concatenate([a, a], axis=0).astype(np.float32)
    return tile2(cs_q), tile2(ss_q), tile2(cs_k), tile2(ss_k)


def _masks3():
    i = np.arange(128)[:, None]
    jj = np.arange(BAND)[None, :]
    gen = ((jj >= i) & (jj <= i + 256)).astype(np.float32)
    m1 = gen * (jj >= 256)
    m2 = gen * (jj >= 128)
    return np.concatenate([gen, m1, m2], axis=1).astype(np.float32)


def _prep_phase_B_shared():
    d = {}
    d["csq"], d["ssq"], d["csk"], d["ssk"] = _rotary_tables()
    d["mask3"] = _masks3()
    bsel = np.zeros((128, 128), dtype=np.float32)
    for m in range(8):
        for p in range(128):
            bsel[p, m * 16 + 2 * m + p // 64] = 1.0
    d["bsel"] = bsel
    psel = np.zeros((128, 2), dtype=np.float32)
    psel[0:64, 0] = 1.0
    psel[64:128, 1] = 1.0
    d["psel"] = psel
    d["pselT"] = np.ascontiguousarray(psel.T)
    d["ident"] = np.eye(128, dtype=np.float32)
    return d


def _prep_phase_B_attn(inputs, l):
    d = {}
    Wqkv = np.asarray(inputs["L0_ffn_Wqkv"])[l]
    wq = Wqkv[:, 0:D]
    wk = Wqkv[:, D:2 * D]
    d["wv"] = np.ascontiguousarray(Wqkv[:, 2 * D:3 * D])
    c = np.arange(D)
    perm = (c // 64) * 64 + ((c % 64) + 32) % 64
    d["wq"] = np.ascontiguousarray(wq)
    d["wk"] = np.ascontiguousarray(wk)
    d["wqr"] = np.ascontiguousarray(wq[:, perm])
    d["wkr"] = np.ascontiguousarray(wk[:, perm])
    d["wo"] = np.ascontiguousarray(np.asarray(inputs["L0_ffn_Wo"])[l])
    qs = np.asarray(inputs["L0_ffn_q_scale"])[l]
    ks = np.asarray(inputs["L0_ffn_k_scale"])[l]
    p64 = ((np.arange(64) + 32) % 64)
    d["qsc"] = np.ascontiguousarray(np.stack(
        [np.tile(qs, 2), np.tile(qs[p64], 2)], axis=1))
    d["ksc"] = np.ascontiguousarray(np.stack(
        [np.tile(ks, 2), np.tile(ks[p64], 2)], axis=1))
    return d


def _prep_phase_B_ffn(inputs, l):
    d = {}
    d["w1"] = np.ascontiguousarray(np.asarray(inputs["L0_ffn_W1"])[l])
    d["w2"] = np.ascontiguousarray(np.asarray(inputs["L0_ffn_W2"])[l])
    d["lng"] = np.ascontiguousarray(np.asarray(inputs["L0_ffn_ln_g"])[l].reshape(NDT, 128).T)
    d["lnb"] = np.ascontiguousarray(np.asarray(inputs["L0_ffn_ln_b"])[l].reshape(NDT, 128).T)
    b1 = np.asarray(inputs["L0_ffn_b1"])[l]
    b1a = np.zeros((NJ * 128,), dtype=np.float32)
    b1a[:DFF] = b1[:DFF]
    b1g = np.zeros((NJ * 128,), dtype=np.float32)
    b1g[:DFF] = b1[DFF:]
    d["b1a"] = np.ascontiguousarray(b1a.reshape(NJ, 128).T)
    d["b1g"] = np.ascontiguousarray(b1g.reshape(NJ, 128).T)
    d["b2"] = np.ascontiguousarray(np.asarray(inputs["L0_ffn_b2"])[l].reshape(NDT, 128).T)
    return d


def _prep_phase_A(inputs):
    d = {}
    d["w_inT"] = np.ascontiguousarray(np.asarray(inputs["L0_attn_in_proj_w"]).T)
    d["conv_w"] = np.ascontiguousarray(np.asarray(inputs["L0_attn_conv_w"])[:, 0, :])
    d["conv_b"] = np.ascontiguousarray(np.asarray(inputs["L0_attn_conv_b"])[:, None])
    d["w_xpT"] = np.ascontiguousarray(np.asarray(inputs["L0_attn_x_proj_w"]).T)
    d["dtwb"] = np.ascontiguousarray(
        np.stack([np.asarray(inputs["L0_attn_dt_proj_w"])[:, 0],
                  np.asarray(inputs["L0_attn_dt_proj_b"])]))
    alog = np.asarray(inputs["L0_attn_A_log"])
    d["a_rep"] = np.ascontiguousarray(
        alog.reshape(8, 16, 8, 16).transpose(2, 3, 0, 1).reshape(128, 128))
    R, S, RB = _sel_matrices()
    d["r_sel"], d["s_sel"], d["rb16"] = R, S, RB
    d["w_outT"] = np.ascontiguousarray(np.asarray(inputs["L0_attn_out_proj_w"]).T)
    return d



def build_fused():
    nc = bass.Bass("TRN2", target_bir_lowering=False)
    b = Bld(nc, coarse_dma_sems=True)
    h = {}

    def dt(name, shape, dtype=F32, kind="ExternalInput"):
        h[name] = nc.dram_tensor(name, shape, dtype, kind=kind)
        return h[name]

    x_in = dt("x", [B_PER_CORE, D, T])
    y_out = dt("y", [B_PER_CORE, D, T], kind="ExternalOutput")
    s0 = dt("s0", [B_PER_CORE, D, T], kind="Internal")
    s1 = dt("s1", [B_PER_CORE, D, T], kind="Internal")
    s2 = dt("s2", [B_PER_CORE, D, T], kind="Internal")
    xnscr = dt("xnscr", [B_PER_CORE, D, T], BF16, kind="Internal")
    gscr = dt("gscr", [B_PER_CORE, NEXP, T], kind="Internal")
    for nm, shape in A_EXT.items():
        dt("A_" + nm, shape)
    for nm, shape in BSH_EXT.items():
        dt("Bs_" + nm, shape)
    for l in range(6):
        for nm, shape in BA_EXT.items():
            dt(f"Ba{l}_" + nm, shape)
        for nm, shape in BF_EXT.items():
            dt(f"Bf{l}_" + nm, shape)
    for nm, shape in C_EXT.items():
        dt("C_" + nm, shape)
    for nm, shape in D_EXT.items():
        dt("D_" + nm, shape)

    def ext_for(prefix, spec, x, y, extra=None):
        e = {nm: h[prefix + nm] for nm in spec}
        e["x"] = x
        e["y"] = y
        if extra:
            e.update(extra)
        return e

    bsh = {nm: h["Bs_" + nm] for nm in BSH_EXT}
    with b.phase("A"):
        emit_A(b, ext_for("A_", A_EXT, x_in, s0))
    with b.phase("P"):
        emit_preln(b, {"x": s0, "y": s1})
    cur, alt = s1, s2
    for l in range(6):
        with b.phase(f"ba{l}"):
            emit_B_attn(b, ext_for(f"Ba{l}_", BA_EXT, cur, alt, extra=bsh))
        cur, alt = alt, cur
        with b.phase(f"bf{l}"):
            emit_B_ffn(b, ext_for(f"Bf{l}_", BF_EXT, cur, alt))
        cur, alt = alt, cur
    with b.phase("R"):
        emit_addres(b, {"x": cur, "xr": s0, "y": alt})
    cur, alt = alt, cur
    with b.phase("C"):
        emit_C(b, ext_for("C_", C_EXT, cur, alt))
    cur, alt = alt, cur
    with b.phase("Dx"):
        emit_D(b, ext_for("D_", D_EXT, cur, y_out,
                          extra={"xnscr": xnscr, "gscr": gscr}))
    return b.finalize([(nm, v) for nm, v in b.cnt.items() if v > 0])


def _prep_fused(inputs):
    shared = {}
    for nm, v in _prep_phase_A(inputs).items():
        shared["A_" + nm] = v
    for nm, v in _prep_phase_B_shared().items():
        shared["Bs_" + nm] = v
    for l in range(6):
        for nm, v in _prep_phase_B_attn(inputs, l).items():
            shared[f"Ba{l}_" + nm] = v
        for nm, v in _prep_phase_B_ffn(inputs, l).items():
            shared[f"Bf{l}_" + nm] = v
    for nm, v in _prep_phase_C(inputs).items():
        shared["C_" + nm] = v
    for nm, v in _prep_phase_D(inputs).items():
        shared["D_" + nm] = v
    return shared


def kernel(**inputs):
    x = np.asarray(inputs["L0_attn_x"])
    xT = np.ascontiguousarray(x.transpose(0, 2, 1))
    shared = _prep_fused(inputs)
    nc = build_fused()
    in_maps = []
    for c in range(NCORES):
        m = dict(shared)
        m["x"] = np.ascontiguousarray(xT[c * B_PER_CORE:(c + 1) * B_PER_CORE])
        in_maps.append(m)
    res = run_bass_kernel_spmd(nc, in_maps, core_ids=list(range(NCORES)))
    yT = np.concatenate([r["y"] for r in res.results], axis=0)
    return np.ascontiguousarray(yT.transpose(0, 2, 1))


# version 06
# speedup vs baseline: 1.0919x; 1.0856x over previous
import numpy as np
from contextlib import ExitStack

import concourse.bass as bass
import concourse.mybir as mybir
from concourse.bass_utils import run_bass_kernel_spmd

F32 = mybir.dt.float32
BF16 = mybir.dt.bfloat16
AF = mybir.ActivationFunctionType
ALU = mybir.AluOpType
AX = mybir.AxisListType

NCORES = 8
B_PER_CORE = 2
T = 2048
D = 1024
NDT = D // 128
TC = 512
NCH = T // TC

ENGS = ("sync", "tensor", "vector", "scalar", "gpsimd")
ESEM = {"sync": "e_sy", "tensor": "e_pe", "vector": "e_dv",
        "scalar": "e_sc", "gpsimd": "e_po"}


def _tname(x):
    if isinstance(x, tuple):
        return _tname(x[0]) + "#" + str(x[1])
    t = getattr(x, "tensor", None)
    if t is not None:
        return t.name
    return x.name


class Bld:

    def __init__(self, nc, coarse_dma_sems=False):
        self.nc = nc
        self.es = ExitStack()
        self.ops = {e: [] for e in ENGS}
        self.sems = {}
        self.cnt = {}
        self.lastw = {}
        self.readers = {}
        self.regions = {}
        self.clock = {e: {} for e in ENGS}
        self.coarse = coarse_dma_sems
        self.prefix = ""
        self.phase_es = None
        self.sem_alias = {}
        self.pool_used = 0

    def _alias(self, logical):
        nm = self.sem_alias.get(logical)
        if nm is None:
            nm = f"qp{self.pool_used}"
            self.pool_used += 1
            self.sem_alias[logical] = nm
        return nm

    def sem(self, name):
        if name not in self.sems:
            self.sems[name] = self.es.enter_context(self.nc.semaphore(name))
            self.cnt[name] = 0
        return self.sems[name]

    def sb(self, name, shape, dt=F32):
        es = self.phase_es if self.phase_es is not None else self.es
        return es.enter_context(self.nc.sbuf_tensor(self.prefix + name, shape, dt))

    def ps(self, name, shape, dt=F32):
        es = self.phase_es if self.phase_es is not None else self.es
        return es.enter_context(self.nc.psum_tensor(self.prefix + name, shape, dt))

    def phase(self, prefix):
        from contextlib import contextmanager

        @contextmanager
        def _cm():
            self.prefix = prefix + "_"
            self.phase_es = ExitStack()
            try:
                yield
            finally:
                self.barrier()
                self.phase_es.close()
                self.phase_es = None
                self.prefix = ""
        return _cm()

    def barrier(self):
        waits = [(n, v) for n, v in self.cnt.items() if v > 0]
        sems = self.sems
        for eng in ENGS:
            ck = self.clock[eng]
            todo = tuple((n, v) for n, v in waits if ck.get(n, 0) < v)
            for n, v in todo:
                ck[n] = v

            def fn(e, _t=todo):
                for n, v in _t:
                    e.wait_ge(sems[n], v)
            self.ops[eng].append(fn)
        self.sem_alias.clear()
        self.pool_used = 0

    def _conflicts(self, name):
        root = name.split("#")[0]
        if name == root:
            return [root] + sorted(self.regions.get(root, ()))
        return [name, root]

    def _deps(self, eng, rd, wr, extra):
        deps = {}

        def add(rec):
            if rec is None:
                return
            s, tok, peng = rec
            if eng == "tensor" and peng == "tensor":
                return
            if deps.get(s, 0) < tok:
                deps[s] = tok
        for x in rd:
            for n in self._conflicts(_tname(x)):
                add(self.lastw.get(n))
        for x in wr:
            for n in self._conflicts(_tname(x)):
                add(self.lastw.get(n))
                for rec in self.readers.get(n, ()):
                    add(rec)
        for s, tok in extra:
            if deps.get(s, 0) < tok:
                deps[s] = tok
        ck = self.clock[eng]
        out = [(s, v) for s, v in deps.items() if ck.get(s, 0) < v]
        for s, v in out:
            ck[s] = v
            self.sem(s)
        return out

    def _note_region(self, name):
        root = name.split("#")[0]
        if name != root:
            self.regions.setdefault(root, set()).add(name)

    def auto(self, eng, make, rd=(), wr=(), sem=None, incn=1, extra=()):
        sem = sem or ESEM[eng]
        if self.coarse and sem.startswith("q_"):
            sem = self._alias(sem)
        self.sem(sem)
        wait = self._deps(eng, rd, wr, extra)
        self.cnt[sem] += incn
        tok = self.cnt[sem]
        sems = self.sems

        def fn(e, _make=make, _wait=tuple(wait), _sem=sem, _incn=incn):
            for n, v in _wait:
                e.wait_ge(sems[n], v)
            _make(e).then_inc(sems[_sem], _incn)
        self.ops[eng].append(fn)
        rec = (sem, tok, eng)
        for x in rd:
            n = _tname(x)
            self._note_region(n)
            self.readers.setdefault(n, []).append(rec)
        for x in wr:
            n = _tname(x)
            self._note_region(n)
            self.lastw[n] = rec
            self.readers[n] = []
        return tok

    def dma(self, eng, out, in_, sem=None):
        ap_out = out[0] if isinstance(out, tuple) else out
        ap_in = in_[0] if isinstance(in_, tuple) else in_
        if sem is None:
            is_sb = getattr(ap_out, "tensor", ap_out).__class__.__name__.startswith("SB")
            root = (_tname(out).split("#")[0] if is_sb
                    else "st_" + _tname(in_).split("#")[0])
            sem = f"q_{eng[:2]}_{root}"
        return self.auto(eng, lambda e, o=ap_out, i=ap_in: e.dma_start(out=o, in_=i),
                         rd=[in_], wr=[out], sem=sem, incn=16)

    def mm_group(self, psum, tiles, extra=()):
        ps_ap = psum[0] if isinstance(psum, tuple) else psum
        n = len(tiles)
        rd = []
        for lh, rh in tiles:
            rd.append(lh)
            rd.append(rh)
        wait = self._deps("tensor", rd, [psum], extra)
        sems = self.sems
        self.sem("e_pe")
        for idx, (lh, rh) in enumerate(tiles):
            last = idx == n - 1
            if last:
                self.cnt["e_pe"] += 1
            tok = self.cnt["e_pe"]

            def fn(e, _lh=lh, _rh=rh, _st=(idx == 0), _sp=last,
                   _wait=tuple(wait if idx == 0 else ()), _ps=ps_ap, _inc=last, _tok=tok):
                for nm, v in _wait:
                    e.wait_ge(sems[nm], v)
                ins = e.matmul(_ps, _lh, _rh, start=_st, stop=_sp)
                if _inc:
                    ins.then_inc(sems["e_pe"], 1)
            self.ops["tensor"].append(fn)
        rec = ("e_pe", self.cnt["e_pe"], "tensor")
        for x in rd:
            n = _tname(x)
            self._note_region(n)
            self.readers.setdefault(n, []).append(rec)
        n_ = _tname(psum)
        self._note_region(n_)
        self.lastw[n_] = rec
        self.readers[n_] = []
        return self.cnt["e_pe"]

    def finalize(self, final_waits):
        nc = self.nc
        with self.es:
            for n, v in final_waits:
                self.sem(n)
            with nc.Block() as block:
                ops, sems = self.ops, self.sems

                @block.sync
                def _(e):
                    for fn in ops["sync"]:
                        fn(e)
                    for n, v in final_waits:
                        e.wait_ge(sems[n], v)

                @block.tensor
                def _(e):
                    for fn in ops["tensor"]:
                        fn(e)

                @block.vector
                def _(e):
                    for fn in ops["vector"]:
                        fn(e)

                @block.scalar
                def _(e):
                    for fn in ops["scalar"]:
                        fn(e)

                @block.gpsimd
                def _(e):
                    for fn in ops["gpsimd"]:
                        fn(e)
        return nc



A_EXT = {"w_inT": [D, 2 * D], "conv_w": [D, 4], "conv_b": [D, 1], "w_xpT": [D, 33],
         "dtwb": [2, D], "a_rep": [128, 128], "r_sel": [128, 16 * 128],
         "s_sel": [128, 16 * 128], "rb16": [16, 128], "w_outT": [D, D]}


def _mk_ext(nc, spec):
    ext = {"x": nc.dram_tensor("x", [B_PER_CORE, D, T], F32, kind="ExternalInput"),
           "y": nc.dram_tensor("y", [B_PER_CORE, D, T], F32, kind="ExternalOutput")}
    for name, shape in spec.items():
        ext[name] = nc.dram_tensor(name, shape, F32, kind="ExternalInput")
    return ext


def _st_waits(b):
    return [(nm, b.cnt[nm]) for nm in list(b.cnt) if nm.startswith("q_sy_st_")]


def emit_A(b, ext):
    nc = b.nc
    x_in = ext["x"]
    w_inT = ext["w_inT"]
    conv_w = ext["conv_w"]
    conv_b = ext["conv_b"]
    w_xpT = ext["w_xpT"]
    dtwb = ext["dtwb"]
    a_rep = ext["a_rep"]
    r_sel = ext["r_sel"]
    s_sel = ext["s_sel"]
    rb16 = ext["rb16"]
    w_outT = ext["w_outT"]
    y_out = ext["y"]

    ones_b = b.sb("ones_b", [128, 1], BF16)
    ones_f1 = b.sb("ones_f1", [1, 128], F32)
    WinT = [b.sb(f"WinT{i}", [128, 2 * D], BF16) for i in range(NDT)]
    XpT = [b.sb(f"XpT{i}", [128, 33], BF16) for i in range(NDT)]
    WoT = [b.sb(f"WoT{i}", [128, D], BF16) for i in range(NDT)]
    cw = b.sb("cw", [128, 4 * NDT], F32)
    cb = b.sb("cb", [128, NDT], F32)
    dtwb_s = b.sb("dtwb_s", [2, D], F32)
    a_rep_s = b.sb("a_rep_s", [128, 128], F32)
    Rsel = b.sb("Rsel", [128, 16 * 128], F32)
    Ssel = b.sb("Ssel", [128, 16 * 128], BF16)
    Rb16 = b.sb("Rb16", [16, 128], F32)

    x_t = [b.sb(f"x_t{i}", [128, TC]) for i in range(NDT)]
    xb_t = [b.sb(f"xb_t{i}", [128, TC], BF16) for i in range(NDT)]
    xn_t = [b.sb(f"xn_t{i}", [128, TC], BF16) for i in range(NDT)]
    xin_t = [b.sb(f"xin_t{i}", [128, TC + 3]) for i in range(NDT)]
    marg = b.sb("marg", [128, 3 * NDT])
    xc_t = [b.sb(f"xc_t{i}", [128, TC]) for i in range(NDT)]
    xcb_t = [b.sb(f"xcb_t{i}", [128, TC], BF16) for i in range(NDT)]
    xq_t = xcb_t
    sz_t = [b.sb(f"sz_t{i}", [128, TC]) for i in range(NDT)]
    dt_t = [b.sb(f"dt_t{i}", [128, TC]) for i in range(NDT)]
    ym_t = xn_t
    row_m = b.sb("row_m", [1, TC])
    row_v = b.sb("row_v", [1, TC])
    row_r = b.sb("row_r", [1, TC])
    row_mr = b.sb("row_mr", [1, TC])
    brow = b.sb("brow", [16, TC])
    crow = b.sb("crow", [16, TC])
    s1row = b.sb("s1row", [2, TC])
    sp_a = b.sb("sp_a", [128, TC])
    sp_b = b.sb("sp_b", [128, TC])
    brep = b.sb("brep", [128, TC])
    crep = b.sb("crep", [128, TC])
    dA = [b.sb(f"dA{j}", [128, TC]) for j in range(2)]
    dBx = [b.sb(f"dBx{j}", [128, TC]) for j in range(2)]
    hbuf = [b.sb(f"hbuf{j}", [128, TC]) for j in range(2)]
    hC = [b.sb(f"hC{j}", [128, TC], BF16) for j in range(2)]
    carry = b.sb("carry", [128, 128])

    P = [b.ps(f"P{i}", [128, TC]) for i in range(8)]

    for i in range(NDT):
        b.dma("gpsimd", WinT[i][:], w_inT[i * 128:(i + 1) * 128, :])
        b.dma("gpsimd", XpT[i][:], w_xpT[i * 128:(i + 1) * 128, :])
        b.dma("gpsimd", WoT[i][:], w_outT[i * 128:(i + 1) * 128, :])
        b.dma("sync", (cw[:, 4 * i:4 * i + 4], i), conv_w[i * 128:(i + 1) * 128, :], "q_cw")
        b.dma("sync", (cb[:, i:i + 1], i), conv_b[i * 128:(i + 1) * 128, :], "q_cb")
    b.dma("sync", dtwb_s[:], dtwb[:])
    b.dma("sync", a_rep_s[:], a_rep[:])
    b.dma("sync", Rsel[:], r_sel[:])
    b.dma("gpsimd", Ssel[:], s_sel[:])
    b.dma("sync", Rb16[:], rb16[:])

    b.auto("vector", lambda e: e.memset(ones_b[:], 1.0), wr=[ones_b])
    b.auto("vector", lambda e: e.memset(ones_f1[:], 1.0), wr=[ones_f1])
    b.auto("vector", lambda e: e.memset(s1row[:], 1.0), wr=[s1row])
    b.auto("scalar", lambda e: e.activation(a_rep_s[:], a_rep_s[:], AF.Exp),
           rd=[a_rep_s], wr=[a_rep_s])
    b.auto("vector", lambda e: e.tensor_single_scalar(a_rep_s[:], a_rep_s[:], -1.0, ALU.mult),
           rd=[a_rep_s], wr=[a_rep_s])

    for bi in range(B_PER_CORE):
        b.auto("vector", lambda e: e.memset(marg[:], 0.0), wr=[marg])
        b.auto("vector", lambda e: e.memset(carry[:], 0.0), wr=[carry])
        for c in range(NCH):
            t0 = c * TC
            xsl = slice(t0, t0 + TC)
            for i in range(NDT):
                b.dma("sync", x_t[i][:], x_in[bi, i * 128:(i + 1) * 128, xsl])

            for i in range(NDT):
                b.auto("scalar", lambda e, i=i: e.copy(xb_t[i][:], x_t[i][:]),
                       rd=[x_t[i]], wr=[xb_t[i]])
                b.auto("scalar", lambda e, i=i: e.activation(xq_t[i][:], x_t[i][:], AF.Square),
                       rd=[x_t[i]], wr=[xq_t[i]])

            b.mm_group(P[6][0:1, :], [(ones_b[:], xb_t[i][:]) for i in range(NDT)])
            b.mm_group(P[7][0:1, :], [(ones_b[:], xq_t[i][:]) for i in range(NDT)])

            b.auto("vector", lambda e: e.tensor_single_scalar(row_m[:], P[6][0:1, :], 1.0 / D, ALU.mult),
                   rd=[P[6]], wr=[row_m])
            b.auto("vector", lambda e: e.tensor_single_scalar(row_v[:], P[7][0:1, :], 1.0 / D, ALU.mult),
                   rd=[P[7]], wr=[row_v])
            b.auto("vector", lambda e: e.tensor_mul(row_r[:], row_m[:], row_m[:]),
                   rd=[row_m], wr=[row_r])
            b.auto("vector", lambda e: e.scalar_tensor_tensor(
                row_v[:], row_v[:], 1e-5, row_r[:], ALU.add, ALU.subtract),
                rd=[row_v, row_r], wr=[row_v])
            b.auto("scalar", lambda e: e.activation(row_r[:], row_v[:], AF.Ln),
                   rd=[row_v], wr=[row_r])
            b.auto("scalar", lambda e: e.activation(row_r[:], row_r[:], AF.Exp, scale=-0.5),
                   rd=[row_r], wr=[row_r])
            b.auto("vector", lambda e: e.tensor_mul(row_mr[:], row_m[:], row_r[:]),
                   rd=[row_m, row_r], wr=[row_mr])

            b.mm_group(P[6][:], [(ones_f1[:], row_r[:])])
            b.mm_group(P[7][:], [(ones_f1[:], row_mr[:])])

            for i in range(NDT):
                b.auto("vector", lambda e, i=i: e.scalar_tensor_tensor(
                    sp_a[:], x_t[i][:], 1.0, P[6][:], ALU.mult, ALU.mult),
                    rd=[x_t[i], P[6]], wr=[sp_a])
                b.auto("vector", lambda e, i=i: e.tensor_sub(xn_t[i][:], sp_a[:], P[7][:]),
                       rd=[sp_a, P[7]], wr=[xn_t[i]])

            for m in range(16):
                slot = m % 2
                b.mm_group(P[slot][:],
                           [(WinT[k][:, m * 128:(m + 1) * 128], xn_t[k][:]) for k in range(NDT)])
                if m < 8:
                    b.auto("scalar", lambda e, m=m, slot=slot: e.copy(
                        xin_t[m][:, 3:TC + 3], P[slot][:]), rd=[P[slot]], wr=[xin_t[m]])
                else:
                    b.auto("scalar", lambda e, m=m, slot=slot: e.activation(
                        sz_t[m - 8][:], P[slot][:], AF.Silu), rd=[P[slot]], wr=[sz_t[m - 8]])

            for i in range(NDT):
                b.auto("vector", lambda e, i=i: e.tensor_copy(
                    xin_t[i][:, 0:3], marg[:, 3 * i:3 * i + 3]), rd=[marg], wr=[xin_t[i]])
            for i in range(NDT):
                b.auto("vector", lambda e, i=i: e.tensor_single_scalar(
                    xc_t[i][:], xin_t[i][:, 0:TC], cw[:, 4 * i:4 * i + 1], ALU.mult),
                    rd=[xin_t[i], cw], wr=[xc_t[i]])
                for k in range(1, 4):
                    b.auto("vector", lambda e, i=i, k=k: e.scalar_tensor_tensor(
                        xc_t[i][:], xin_t[i][:, k:TC + k], cw[:, 4 * i + k:4 * i + k + 1],
                        xc_t[i][:], ALU.mult, ALU.add),
                        rd=[xin_t[i], cw, xc_t[i]], wr=[xc_t[i]])
            for i in range(NDT):
                b.auto("vector", lambda e, i=i: e.tensor_copy(
                    marg[:, 3 * i:3 * i + 3], xin_t[i][:, TC:TC + 3]),
                    rd=[xin_t[i]], wr=[marg])

            for i in range(NDT):
                b.auto("scalar", lambda e, i=i: e.activation(
                    xc_t[i][:], xc_t[i][:], AF.Silu, bias=cb[:, i:i + 1]),
                    rd=[xc_t[i], cb], wr=[xc_t[i]])
                b.auto("scalar", lambda e, i=i: e.copy(xcb_t[i][:], xc_t[i][:]),
                       rd=[xc_t[i]], wr=[xcb_t[i]])

            for (slot, c0, c1, dst) in ((0, 0, 16, brow), (1, 16, 32, crow), (2, 32, 33, None)):
                mm_rows = c1 - c0
                b.mm_group(P[slot][0:mm_rows, :],
                           [(XpT[k][:, c0:c1], xcb_t[k][:]) for k in range(NDT)])
                if dst is not None:
                    b.auto("scalar", lambda e, dst=dst, mm_rows=mm_rows, slot=slot:
                           e.copy(dst[:], P[slot][0:mm_rows, :]), rd=[P[slot]], wr=[dst])
                else:
                    b.auto("vector", lambda e: e.tensor_copy(s1row[0:1, :], P[2][0:1, :]),
                           rd=[P[2]], wr=[s1row])

            for i in range(NDT):
                slot = 3 + (i % 2)
                b.mm_group(P[slot][:], [(dtwb_s[:, i * 128:(i + 1) * 128], s1row[:])])
                b.auto("scalar", lambda e, slot=slot: e.activation(sp_a[:], P[slot][:], AF.Abs),
                       rd=[P[slot]], wr=[sp_a])
                b.auto("scalar", lambda e: e.activation(sp_b[:], sp_a[:], AF.Exp, scale=-1.0),
                       rd=[sp_a], wr=[sp_b])
                b.auto("scalar", lambda e: e.activation(sp_b[:], sp_b[:], AF.Ln, bias=1.0),
                       rd=[sp_b], wr=[sp_b])
                b.auto("scalar", lambda e, slot=slot, i=i: e.activation(
                    dt_t[i][:], P[slot][:], AF.Relu), rd=[P[slot]], wr=[dt_t[i]])
                b.auto("vector", lambda e, i=i: e.tensor_add(dt_t[i][:], dt_t[i][:], sp_b[:]),
                       rd=[dt_t[i], sp_b], wr=[dt_t[i]])
                b.auto("vector", lambda e, i=i: e.tensor_mul(xc_t[i][:], dt_t[i][:], xc_t[i][:]),
                       rd=[dt_t[i], xc_t[i]], wr=[xc_t[i]])

            b.mm_group(P[5][:], [(Rb16[:], brow[:])])
            b.auto("scalar", lambda e: e.copy(brep[:], P[5][:]), rd=[P[5]], wr=[brep])
            b.mm_group(P[6][:], [(Rb16[:], crow[:])])
            b.auto("scalar", lambda e: e.copy(crep[:], P[6][:]), rd=[P[6]], wr=[crep])

            for db in range(NDT):
                for k in range(16):
                    tau = db * 16 + k
                    j = k % 2
                    sA, sU = j, 2 + j
                    ksl = slice(k * 128, (k + 1) * 128)
                    b.mm_group(P[sA][:], [(Rsel[:, ksl], dt_t[db][:])])
                    b.mm_group(P[sU][:], [(Rsel[:, ksl], xc_t[db][:])])
                    b.auto("scalar", lambda e, j=j, sA=sA, tau=tau: e.activation(
                        dA[j][:], P[sA][:], AF.Exp, scale=a_rep_s[:, tau:tau + 1]),
                        rd=[P[sA], a_rep_s], wr=[dA[j]])
                    b.auto("vector", lambda e, j=j, sU=sU: e.tensor_tensor(
                        dBx[j][:], P[sU][:], brep[:], ALU.mult),
                        rd=[P[sU], brep], wr=[dBx[j]])
                    b.auto("vector", lambda e, j=j, tau=tau: e.tensor_tensor_scan(
                        hbuf[j][:], dA[j][:], dBx[j][:], carry[:, tau:tau + 1],
                        ALU.mult, ALU.add),
                        rd=[dA[j], dBx[j], carry], wr=[hbuf[j]])
                    b.auto("vector", lambda e, j=j, tau=tau: e.tensor_copy(
                        carry[:, tau:tau + 1], hbuf[j][:, TC - 1:TC]),
                        rd=[hbuf[j]], wr=[carry])
                    b.auto("vector", lambda e, j=j: e.tensor_tensor(
                        hC[j][:], hbuf[j][:], crep[:], ALU.mult),
                        rd=[hbuf[j], crep], wr=[hC[j]])
                    nfirst = (k == 0)
                    nlast = (k == 15)
                    b.auto(
                        "tensor",
                        (lambda e, _ksl=ksl, _j=j, _st=nfirst, _sp=nlast:
                         e.matmul(P[4][:], Ssel[:, _ksl], hC[_j][:], start=_st, stop=_sp)),
                        rd=[Ssel, hC[j]], wr=([P[4]] if (nfirst or nlast) else []))
                b.auto("vector", lambda e, db=db: e.tensor_tensor(
                    ym_t[db][:], P[4][:], sz_t[db][:], ALU.mult),
                    rd=[P[4], sz_t[db]], wr=[ym_t[db]])

            for m in range(NDT):
                slot = 5 + (m % 2)
                b.mm_group(P[slot][:],
                           [(WoT[k][:, m * 128:(m + 1) * 128], ym_t[k][:]) for k in range(NDT)])
                b.auto("vector", lambda e, m=m, slot=slot: e.tensor_tensor(
                    x_t[m][:], P[slot][:], x_t[m][:], ALU.add),
                    rd=[P[slot], x_t[m]], wr=[x_t[m]])
                b.dma("sync", (y_out[bi, m * 128:(m + 1) * 128, xsl], f"{bi}.{c}.{m}"),
                      x_t[m][:])


def build_phase_A():
    nc = bass.Bass("TRN2", target_bir_lowering=False)
    b = Bld(nc)
    emit_A(b, _mk_ext(nc, A_EXT))
    return b.finalize(_st_waits(b))



def emit_preln(b, ext):
    x_in = ext["x"]
    y_out = ext["y"]
    ones_b = b.sb("ones_b", [128, 1], BF16)
    ones_f1 = b.sb("ones_f1", [1, 128], F32)
    x_t = [b.sb(f"x_t{i}", [128, TC]) for i in range(NDT)]
    xb_t = [b.sb(f"xb_t{i}", [128, TC], BF16) for i in range(NDT)]
    sq_t = [b.sb(f"sq_t{i}", [128, TC], BF16) for i in range(NDT)]
    t1 = b.sb("t1", [128, TC])
    row_m = b.sb("row_m", [1, TC])
    row_v = b.sb("row_v", [1, TC])
    row_r = b.sb("row_r", [1, TC])
    row_mr = b.sb("row_mr", [1, TC])
    P = [b.ps(f"P{i}", [128, TC]) for i in range(2)]
    b.auto("vector", lambda e: e.memset(ones_b[:], 1.0), wr=[ones_b])
    b.auto("vector", lambda e: e.memset(ones_f1[:], 1.0), wr=[ones_f1])
    for bi in range(B_PER_CORE):
        for c in range(NCH):
            xsl = slice(c * TC, (c + 1) * TC)
            for i in range(NDT):
                b.dma("sync", x_t[i][:], x_in[bi, i * 128:(i + 1) * 128, xsl])
            for i in range(NDT):
                b.auto("scalar", lambda e, i=i: e.copy(xb_t[i][:], x_t[i][:]),
                       rd=[x_t[i]], wr=[xb_t[i]])
                b.auto("scalar", lambda e, i=i: e.activation(sq_t[i][:], x_t[i][:], AF.Square),
                       rd=[x_t[i]], wr=[sq_t[i]])
            b.mm_group(P[0][0:1, :], [(ones_b[:], xb_t[i][:]) for i in range(NDT)])
            b.mm_group(P[1][0:1, :], [(ones_b[:], sq_t[i][:]) for i in range(NDT)])
            b.auto("vector", lambda e: e.tensor_single_scalar(row_m[:], P[0][0:1, :], 1.0 / D, ALU.mult),
                   rd=[P[0]], wr=[row_m])
            b.auto("vector", lambda e: e.tensor_single_scalar(row_v[:], P[1][0:1, :], 1.0 / D, ALU.mult),
                   rd=[P[1]], wr=[row_v])
            b.auto("vector", lambda e: e.tensor_mul(row_r[:], row_m[:], row_m[:]),
                   rd=[row_m], wr=[row_r])
            b.auto("vector", lambda e: e.scalar_tensor_tensor(
                row_v[:], row_v[:], 1e-5, row_r[:], ALU.add, ALU.subtract),
                rd=[row_v, row_r], wr=[row_v])
            b.auto("scalar", lambda e: e.activation(row_r[:], row_v[:], AF.Ln),
                   rd=[row_v], wr=[row_r])
            b.auto("scalar", lambda e: e.activation(row_r[:], row_r[:], AF.Exp, scale=-0.5),
                   rd=[row_r], wr=[row_r])
            b.auto("vector", lambda e: e.tensor_mul(row_mr[:], row_m[:], row_r[:]),
                   rd=[row_m, row_r], wr=[row_mr])
            b.mm_group(P[0][:], [(ones_f1[:], row_r[:])])
            b.mm_group(P[1][:], [(ones_f1[:], row_mr[:])])
            for i in range(NDT):
                b.auto("vector", lambda e, i=i: e.scalar_tensor_tensor(
                    t1[:], x_t[i][:], 1.0, P[0][:], ALU.mult, ALU.mult),
                    rd=[x_t[i], P[0]], wr=[t1])
                b.auto("vector", lambda e, i=i: e.tensor_sub(x_t[i][:], t1[:], P[1][:]),
                       rd=[t1, P[1]], wr=[x_t[i]])
                b.dma("sync", (y_out[bi, i * 128:(i + 1) * 128, xsl], f"{bi}.{c}.{i}"),
                      x_t[i][:])


def build_phase_preln():
    nc = bass.Bass("TRN2", target_bir_lowering=False)
    b = Bld(nc)
    emit_preln(b, _mk_ext(nc, {}))
    return b.finalize(_st_waits(b))



def emit_addres(b, ext):
    x_in = ext["x"]
    xr_in = ext["xr"]
    y_out = ext["y"]
    a_t = [b.sb(f"a_t{i}", [128, T]) for i in range(NDT)]
    r_t = [b.sb(f"r_t{i}", [128, T]) for i in range(NDT)]
    for bi in range(B_PER_CORE):
        for i in range(NDT):
            rsl = slice(i * 128, (i + 1) * 128)
            b.dma("sync", (a_t[i][:], bi), x_in[bi, rsl, :])
            b.dma("gpsimd", (r_t[i][:], bi), xr_in[bi, rsl, :])
            b.auto("vector", lambda e, i=i: e.tensor_add(a_t[i][:], a_t[i][:], r_t[i][:]),
                   rd=[(a_t[i], bi), (r_t[i], bi), a_t[i], r_t[i]], wr=[a_t[i]])
            b.dma("sync", (y_out[bi, rsl, :], f"{bi}.{i}"), a_t[i][:])


def build_phase_addres():
    nc = bass.Bass("TRN2", target_bir_lowering=False)
    b = Bld(nc)
    ext = _mk_ext(nc, {})
    ext["xr"] = nc.dram_tensor("xr", [B_PER_CORE, D, T], F32, kind="ExternalInput")
    emit_addres(b, ext)
    return b.finalize(_st_waits(b))



HEADS = 16
DH = 64
WIN = 256
PAIR = 512
NPAIR = T // PAIR
BAND = 384


BA_EXT = {"wq": [D, D], "wqr": [D, D], "wk": [D, D], "wkr": [D, D], "wv": [D, D],
          "wo": [D, D], "qsc": [128, 2], "ksc": [128, 2]}
BSH_EXT = {"csq": [128, T], "ssq": [128, T], "csk": [128, T], "ssk": [128, T],
           "mask3": [128, 3 * BAND], "bsel": [128, 128], "psel": [128, 2],
           "pselT": [2, 128], "ident": [128, 128]}


def emit_B_attn(b, ext):
    x_in = ext["x"]
    wq = ext["wq"]
    wqr = ext["wqr"]
    wk = ext["wk"]
    wkr = ext["wkr"]
    wv = ext["wv"]
    wo = ext["wo"]
    qsc = ext["qsc"]
    ksc = ext["ksc"]
    csq_d = ext["csq"]
    ssq_d = ext["ssq"]
    csk_d = ext["csk"]
    ssk_d = ext["ssk"]
    mask_d = ext["mask3"]
    bsel_d = ext["bsel"]
    psel_d = ext["psel"]
    pselT_d = ext["pselT"]
    ident_d = ext["ident"]
    y_out = ext["y"]

    WqT = [b.sb(f"WqT{i}", [128, D], BF16) for i in range(NDT)]
    WqrT = [b.sb(f"WqrT{i}", [128, D], BF16) for i in range(NDT)]
    WkT = [b.sb(f"WkT{i}", [128, D], BF16) for i in range(NDT)]
    WkrT = [b.sb(f"WkrT{i}", [128, D], BF16) for i in range(NDT)]
    WvT = [b.sb(f"WvT{i}", [128, D], BF16) for i in range(NDT)]
    WoT = [b.sb(f"WoT{i}", [128, D], BF16) for i in range(NDT)]
    qs_s = b.sb("qs_s", [128, 2])
    ks_s = b.sb("ks_s", [128, 2])
    csq = b.sb("csq_s", [128, T], BF16)
    ssq = b.sb("ssq_s", [128, T], BF16)
    csk = b.sb("csk_s", [128, T], BF16)
    ssk = b.sb("ssk_s", [128, T], BF16)
    mask3 = b.sb("mask3_s", [128, 3 * BAND], BF16)
    bsel = b.sb("bsel_s", [128, 128], BF16)
    psel = b.sb("psel_s", [128, 2], BF16)
    pselT = b.sb("pselT_s", [2, 128])
    ident = b.sb("ident_s", [128, 128], BF16)

    xw = [b.sb(f"xw{i}", [128, PAIR + 1]) for i in range(NDT)]
    xs = [b.sb(f"xs{i}", [128, PAIR], BF16) for i in range(NDT)]
    Kc = [b.sb(f"Kc{i}", [128, 768], BF16) for i in range(NDT)]
    Vc = [b.sb(f"Vc{i}", [128, D], BF16) for i in range(6)]
    Qp = [b.sb(f"Qp{i}", [128, PAIR], BF16) for i in range(NDT)]
    sqq = [b.sb(f"sqq{i}", [128, PAIR], BF16) for i in range(NDT)]
    sqk = b.sb("sqk", [128, PAIR], BF16)
    ao = xs
    rt1 = b.sb("rt1", [128, PAIR])
    rt2 = b.sb("rt2", [128, PAIR])
    rnk = b.sb("rnk", [2, PAIR])
    rsq8 = [b.sb(f"rsq8{i}", [128, HEADS]) for i in range(4)]
    nrm_a = b.sb("nrm_a", [2, PAIR])
    atb2 = [b.sb(f"atb{j}", [128, BAND], BF16) for j in range(2)]
    attnT2 = [b.sb(f"attnT{j}", [128, BAND], BF16) for j in range(2)]
    zrow2 = [b.sb(f"zrow{j}", [128, 1]) for j in range(2)]
    rz2 = [b.sb(f"rz{j}", [128, 1]) for j in range(2)]
    ngm2 = [b.sb(f"ngm{j}", [128, 1]) for j in range(2)]
    bia2 = [b.sb(f"bia{j}", [128, 1]) for j in range(2)]

    P = [b.ps(f"P{i}", [128, PAIR]) for i in range(6)]
    PTX = b.ps("PTX", [128, 2 * BAND], BF16)
    PVX = b.ps("PVX", [128, 256])
    Psc = [P[5], P[4]]

    for i in range(NDT):
        rsl = slice(i * 128, (i + 1) * 128)
        b.dma("gpsimd", WqT[i][:], wq[rsl, :])
        b.dma("gpsimd", WqrT[i][:], wqr[rsl, :])
        b.dma("gpsimd", WkT[i][:], wk[rsl, :])
        b.dma("gpsimd", WkrT[i][:], wkr[rsl, :])
        b.dma("gpsimd", WvT[i][:], wv[rsl, :])
        b.dma("gpsimd", WoT[i][:], wo[rsl, :])
    b.dma("sync", qs_s[:], qsc[:])
    b.dma("sync", ks_s[:], ksc[:])
    b.dma("gpsimd", csq[:], csq_d[:])
    b.dma("gpsimd", ssq[:], ssq_d[:])
    b.dma("gpsimd", csk[:], csk_d[:])
    b.dma("gpsimd", ssk[:], ssk_d[:])
    b.dma("gpsimd", mask3[:], mask_d[:])
    b.dma("gpsimd", bsel[:], bsel_d[:])
    b.dma("gpsimd", psel[:], psel_d[:])
    b.dma("sync", pselT[:], pselT_d[:])
    b.dma("gpsimd", ident[:], ident_d[:])

    for bi in range(B_PER_CORE):
        for p in range(NPAIR):
            t0 = p * PAIR
            for i in range(NDT):
                rsl = slice(i * 128, (i + 1) * 128)
                if p == 0:
                    b.auto("vector", lambda e, i=i: e.memset(xw[i][:, 0:1], 0.0),
                           wr=[(xw[i], "m")])
                    b.dma("sync", (xw[i][:, 1:PAIR + 1], "b"), x_in[bi, rsl, 0:PAIR])
                else:
                    b.dma("sync", xw[i][:], x_in[bi, rsl, t0 - 1:t0 + PAIR])
            for i in range(NDT):
                src = (xw[i][:, 1:PAIR + 1]) if i < 4 else (xw[i][:, 0:PAIR])
                b.auto("scalar", lambda e, i=i, src=src: e.copy(xs[i][:], src),
                       rd=[xw[i], (xw[i], "m"), (xw[i], "b")], wr=[xs[i]])
            if p == 0:
                for i in range(NDT):
                    b.auto("vector", lambda e, i=i: e.memset(Kc[i][:, 0:256], 0.0),
                           rd=[], wr=[(Kc[i], "prev")])
                for i in range(2):
                    b.auto("vector", lambda e, i=i: e.memset(Vc[i][:], 0.0), wr=[Vc[i]])
            else:
                for i in range(NDT):
                    b.auto("vector", lambda e, i=i: e.tensor_copy(
                        Kc[i][:, 0:256], Kc[i][:, 512:768]),
                        rd=[(Kc[i], "cur")], wr=[(Kc[i], "prev")])
                for i in range(2):
                    b.auto("vector", lambda e, i=i: e.tensor_copy(Vc[i][:], Vc[i + 4][:]),
                           rd=[Vc[i + 4]], wr=[Vc[i]])
            for tt in range(4):
                tsl = slice(tt * 128, (tt + 1) * 128)
                for ch in range(2):
                    b.mm_group(P[ch][:, 0:512],
                               [(xs[k][:, tsl], WvT[k][:, ch * 512:(ch + 1) * 512])
                                for k in range(NDT)])
                    b.auto("vector", lambda e, tt=tt, ch=ch: e.tensor_copy(
                        Vc[2 + tt][:, ch * 512:(ch + 1) * 512], P[ch][:, 0:512]),
                        rd=[P[ch]], wr=[(Vc[2 + tt], ch)])
            for m in range(NDT):
                msl = slice(m * 128, (m + 1) * 128)
                b.mm_group(P[0][:], [(WqT[k][:, msl], xs[k][:]) for k in range(NDT)])
                b.mm_group(P[1][:], [(WqrT[k][:, msl], xs[k][:]) for k in range(NDT)])
                b.mm_group(P[2][:], [(WkT[k][:, msl], xs[k][:]) for k in range(NDT)])
                b.mm_group(P[3][:], [(WkrT[k][:, msl], xs[k][:]) for k in range(NDT)])
                b.auto("scalar", lambda e, m=m: e.activation(sqq[m][:], P[0][:], AF.Square),
                       rd=[P[0]], wr=[sqq[m]])
                b.auto("scalar", lambda e: e.activation(sqk[:], P[2][:], AF.Square),
                       rd=[P[2]], wr=[sqk])
                b.mm_group(P[4][0:2, :], [(psel[:], sqk[:])])
                b.auto("scalar", lambda e: e.activation(nrm_a[:], P[4][0:2, :], AF.Ln),
                       rd=[P[4]], wr=[nrm_a])
                b.auto("scalar", lambda e: e.activation(nrm_a[:], nrm_a[:], AF.Exp, scale=0.5),
                       rd=[nrm_a], wr=[nrm_a])
                b.auto("vector", lambda e: e.tensor_single_scalar(nrm_a[:], nrm_a[:], 1e-12, ALU.max),
                       rd=[nrm_a], wr=[nrm_a])
                b.auto("vector", lambda e: e.reciprocal(rnk[:], nrm_a[:]),
                       rd=[nrm_a], wr=[rnk])
                b.mm_group(P[4][:], [(pselT[:], rnk[:])])
                tq = slice(t0, t0 + PAIR)
                b.auto("vector", lambda e, m=m, tq=tq: e.scalar_tensor_tensor(
                    rt1[:], P[0][:], qs_s[:, 0:1], csq[:, tq], ALU.mult, ALU.mult),
                    rd=[P[0], qs_s, csq], wr=[rt1])
                b.auto("vector", lambda e, m=m, tq=tq: e.scalar_tensor_tensor(
                    rt2[:], P[1][:], qs_s[:, 1:2], ssq[:, tq], ALU.mult, ALU.mult),
                    rd=[P[1], qs_s, ssq], wr=[rt2])
                b.auto("vector", lambda e, m=m: e.tensor_add(Qp[m][:], rt1[:], rt2[:]),
                       rd=[rt1, rt2], wr=[Qp[m]])
                b.auto("vector", lambda e, m=m, tq=tq: e.scalar_tensor_tensor(
                    rt1[:], P[2][:], ks_s[:, 0:1], csk[:, tq], ALU.mult, ALU.mult),
                    rd=[P[2], ks_s, csk], wr=[rt1])
                b.auto("vector", lambda e, m=m, tq=tq: e.scalar_tensor_tensor(
                    rt2[:], P[3][:], ks_s[:, 1:2], ssk[:, tq], ALU.mult, ALU.mult),
                    rd=[P[3], ks_s, ssk], wr=[rt2])
                b.auto("vector", lambda e: e.tensor_add(rt1[:], rt1[:], rt2[:]),
                       rd=[rt1, rt2], wr=[rt1])
                b.auto("vector", lambda e, m=m: e.tensor_tensor(
                    Kc[m][:, 256:768], rt1[:], P[4][:], ALU.mult),
                    rd=[rt1, P[4]], wr=[(Kc[m], "cur")])
            for tt in range(4):
                tsl = slice(tt * 128, (tt + 1) * 128)
                b.mm_group(P[5][:, 0:16],
                           [(sqq[m][:, tsl], bsel[:, m * 16:(m + 1) * 16]) for m in range(NDT)])
                b.auto("scalar", lambda e, tt=tt: e.activation(
                    rsq8[tt][:], P[5][:, 0:16], AF.Ln), rd=[P[5]], wr=[rsq8[tt]])
                b.auto("scalar", lambda e, tt=tt: e.activation(
                    rsq8[tt][:], rsq8[tt][:], AF.Exp, scale=0.5), rd=[rsq8[tt]], wr=[rsq8[tt]])
                b.auto("vector", lambda e, tt=tt: e.tensor_single_scalar(
                    rsq8[tt][:], rsq8[tt][:], 1e-12, ALU.max), rd=[rsq8[tt]], wr=[rsq8[tt]])
                b.auto("vector", lambda e, tt=tt: e.reciprocal(rsq8[tt][:], rsq8[tt][:]),
                       rd=[rsq8[tt]], wr=[rsq8[tt]])
                b.auto("vector", lambda e, tt=tt: e.tensor_single_scalar(
                    rsq8[tt][:], rsq8[tt][:], 8.0, ALU.mult), rd=[rsq8[tt]], wr=[rsq8[tt]])
            iters = [(wloc, qt, h)
                     for wloc in range(2) for qt in range(2) for h in range(HEADS)]

            def score_mm(idx):
                wloc, qt, h = iters[idx]
                m = h // 2
                hr = slice((h % 2) * 64, (h % 2) * 64 + 64)
                tt = 2 * wloc + qt
                qs_l = slice(tt * 128, (tt + 1) * 128)
                koff = 256 * wloc + 128 * qt
                w_g = 2 * p + wloc
                mv = 0 if w_g > 0 else (1 + qt)
                msl = slice(mv * BAND, (mv + 1) * BAND)
                b.mm_group(Psc[idx % 2][:, 0:BAND],
                           [(Qp[m][hr, qs_l], Kc[m][hr, koff:koff + BAND]),
                            (ident[:], mask3[:, msl])])

            score_mm(0)
            for idx in range(len(iters)):
                wloc, qt, h = iters[idx]
                m = h // 2
                hr = slice((h % 2) * 64, (h % 2) * 64 + 64)
                tt = 2 * wloc + qt
                sl = idx % 2
                psl = (idx // 2) % 2
                if idx + 1 < len(iters):
                    score_mm(idx + 1)
                b.auto("vector", lambda e, sl=sl: e.reduce_max(
                    ngm2[sl][:], Psc[sl][:, 0:BAND], axis=AX.X, negate=True),
                    rd=[Psc[sl]], wr=[ngm2[sl]])
                b.auto("vector", lambda e, sl=sl, tt=tt, h=h: e.tensor_mul(
                    bia2[sl][:], ngm2[sl][:], rsq8[tt][:, h:h + 1]),
                    rd=[ngm2[sl], rsq8[tt]], wr=[bia2[sl]])
                b.auto("scalar", lambda e, sl=sl, tt=tt, h=h: e.activation(
                    atb2[sl][:], Psc[sl][:, 0:BAND], AF.Exp,
                    bias=bia2[sl][:], scale=rsq8[tt][:, h:h + 1],
                    accum_out=zrow2[sl][:]),
                    rd=[Psc[sl], bia2[sl], rsq8[tt]], wr=[atb2[sl], zrow2[sl]])
                b.auto("vector", lambda e, sl=sl: e.reciprocal(rz2[sl][:], zrow2[sl][:]),
                       rd=[zrow2[sl]], wr=[rz2[sl]])
                b.auto("vector", lambda e, sl=sl: e.tensor_single_scalar(
                    atb2[sl][:], atb2[sl][:], rz2[sl][:], ALU.mult),
                    rd=[atb2[sl], rz2[sl]], wr=[atb2[sl]])
                for j in range(3):
                    b.auto("tensor", lambda e, sl=sl, j=j: e.transpose(
                        PTX[:, sl * BAND + j * 128:sl * BAND + (j + 1) * 128],
                        atb2[sl][:, j * 128:(j + 1) * 128], ident[:]),
                        rd=[atb2[sl], ident], wr=[(PTX, f"{sl}.{j}")])
                b.auto("scalar", lambda e, sl=sl: e.copy(
                    attnT2[sl][:], PTX[:, sl * BAND:(sl + 1) * BAND]),
                    rd=[(PTX, f"{sl}.0"), (PTX, f"{sl}.1"), (PTX, f"{sl}.2")],
                    wr=[attnT2[sl]])
                vbase = 2 * wloc + qt
                b.mm_group((PVX[hr, psl * 128:(psl + 1) * 128], psl),
                           [(Vc[vbase + j][:, h * 64:(h + 1) * 64],
                             attnT2[sl][:, j * 128:(j + 1) * 128]) for j in range(3)])
                if h % 2 == 1:
                    b.auto("scalar", lambda e, m=m, tt=tt, psl=psl: e.copy(
                        ao[m][:, tt * 128:(tt + 1) * 128],
                        PVX[:, psl * 128:(psl + 1) * 128]),
                        rd=[(PVX, psl)], wr=[(ao[m], tt)])
            for mo in range(NDT):
                msl = slice(mo * 128, (mo + 1) * 128)
                b.mm_group(P[mo % 2][:],
                           [(WoT[k][:, msl], ao[k][:]) for k in range(NDT)])
                b.auto("vector", lambda e, mo=mo: e.tensor_tensor(
                    xw[mo][:, 1:PAIR + 1], P[mo % 2][:], xw[mo][:, 1:PAIR + 1], ALU.add),
                    rd=[P[mo % 2], xw[mo], (xw[mo], "b")], wr=[(xw[mo], "b"), xw[mo]])
                b.dma("sync", (y_out[bi, msl, t0:t0 + PAIR], f"{bi}.{p}.{mo}"),
                      xw[mo][:, 1:PAIR + 1])


def build_phase_B_attn():
    nc = bass.Bass("TRN2", target_bir_lowering=False)
    b = Bld(nc)
    emit_B_attn(b, _mk_ext(nc, {**BA_EXT, **BSH_EXT}))
    return b.finalize(_st_waits(b))



TC2 = 256
DFF = 2730
NJ = 22


BF_EXT = {"w1": [D, 2 * DFF], "w2": [DFF, D], "lng": [128, NDT], "lnb": [128, NDT],
          "b1a": [128, NJ], "b1g": [128, NJ], "b2": [128, NDT]}


def emit_B_ffn(b, ext):
    x_in = ext["x"]
    w1_d = ext["w1"]
    w2_d = ext["w2"]
    g_d = ext["lng"]
    bb_d = ext["lnb"]
    b1a_d = ext["b1a"]
    b1g_d = ext["b1g"]
    b2_d = ext["b2"]
    y_out = ext["y"]

    W1 = [b.sb(f"W1_{i}", [128, 2 * DFF], BF16) for i in range(NDT)]
    W2 = [b.sb(f"W2_{j}", [128 if j < NJ - 1 else 42, D], BF16) for j in range(NJ)]
    g_s = b.sb("g_s", [128, NDT])
    bb_s = b.sb("bb_s", [128, NDT])
    b1a_s = b.sb("b1a_s", [128, NJ])
    b1g_s = b.sb("b1g_s", [128, NJ])
    b2_s = b.sb("b2_s", [128, NDT])
    ones_b = b.sb("ones_b", [128, 1], BF16)
    ones_f1 = b.sb("ones_f1", [1, 128])

    xw = [b.sb(f"xw{i}", [128, TC2 + 1]) for i in range(NDT)]
    xs_b = [b.sb(f"xsb{i}", [128, TC2], BF16) for i in range(NDT)]
    sq_b = [b.sb(f"sqb{i}", [128, TC2], BF16) for i in range(NDT)]
    xn = [b.sb(f"xn{i}", [128, TC2], BF16) for i in range(NDT)]
    hjs = [b.sb(f"hj{j}", [128, TC2], BF16) for j in range(NJ)]
    gel = b.sb("gel", [128, TC2])
    row_m = b.sb("row_m", [1, TC2])
    row_v = b.sb("row_v", [1, TC2])
    row_r = b.sb("row_r", [1, TC2])
    row_mr = b.sb("row_mr", [1, TC2])
    t1 = b.sb("t1", [128, TC2])

    PY4 = [b.ps(f"PY{i}", [128, 2 * TC2]) for i in range(4)]
    PY = [PY4[i // 2][:, (i % 2) * TC2:(i % 2 + 1) * TC2] for i in range(NDT)]
    PA = b.ps("PA", [128, TC2])
    PG = b.ps("PG", [128, TC2])
    PB1 = b.ps("PB1", [128, TC2])
    PB2 = b.ps("PB2", [128, TC2])
    PS = PB1[0:1, :]
    PQ = PB2[0:1, :]
    PR = PB1[:]
    PMR = PB2[:]

    for i in range(NDT):
        b.dma("gpsimd", W1[i][:], w1_d[i * 128:(i + 1) * 128, :])
    for j in range(NJ):
        r0 = j * 128
        r1 = min(DFF, r0 + 128)
        b.dma("gpsimd", W2[j][:], w2_d[r0:r1, :])
    b.dma("sync", g_s[:], g_d[:])
    b.dma("sync", bb_s[:], bb_d[:])
    b.dma("sync", b1a_s[:], b1a_d[:])
    b.dma("sync", b1g_s[:], b1g_d[:])
    b.dma("sync", b2_s[:], b2_d[:])
    b.auto("vector", lambda e: e.memset(ones_b[:], 1.0), wr=[ones_b])
    b.auto("vector", lambda e: e.memset(ones_f1[:], 1.0), wr=[ones_f1])

    for bi in range(B_PER_CORE):
        for c in range(T // TC2):
            t0 = c * TC2
            for i in range(NDT):
                rsl = slice(i * 128, (i + 1) * 128)
                if c == 0:
                    b.auto("vector", lambda e, i=i: e.memset(xw[i][:, 0:1], 0.0),
                           wr=[(xw[i], "m")])
                    b.dma("sync", (xw[i][:, 1:TC2 + 1], "b"), x_in[bi, rsl, 0:TC2])
                else:
                    b.dma("sync", xw[i][:], x_in[bi, rsl, t0 - 1:t0 + TC2])
            for i in range(NDT):
                src = (xw[i][:, 1:TC2 + 1]) if i < 4 else (xw[i][:, 0:TC2])
                b.auto("scalar", lambda e, i=i, src=src: e.copy(xs_b[i][:], src),
                       rd=[xw[i], (xw[i], "m"), (xw[i], "b")], wr=[xs_b[i]])
                b.auto("scalar", lambda e, i=i, src=src: e.activation(sq_b[i][:], src, AF.Square),
                       rd=[xw[i], (xw[i], "m"), (xw[i], "b")], wr=[sq_b[i]])
            b.mm_group(PS, [(ones_b[:], xs_b[i][:]) for i in range(NDT)])
            b.mm_group(PQ, [(ones_b[:], sq_b[i][:]) for i in range(NDT)])
            b.auto("vector", lambda e: e.tensor_single_scalar(row_m[:], PS, 1.0 / D, ALU.mult),
                   rd=[PS], wr=[row_m])
            b.auto("vector", lambda e: e.tensor_single_scalar(row_v[:], PQ, 1.0 / D, ALU.mult),
                   rd=[PQ], wr=[row_v])
            b.auto("vector", lambda e: e.tensor_mul(row_r[:], row_m[:], row_m[:]),
                   rd=[row_m], wr=[row_r])
            b.auto("vector", lambda e: e.scalar_tensor_tensor(
                row_v[:], row_v[:], 1e-5, row_r[:], ALU.add, ALU.subtract),
                rd=[row_v, row_r], wr=[row_v])
            b.auto("scalar", lambda e: e.activation(row_r[:], row_v[:], AF.Ln),
                   rd=[row_v], wr=[row_r])
            b.auto("scalar", lambda e: e.activation(row_r[:], row_r[:], AF.Exp, scale=-0.5),
                   rd=[row_r], wr=[row_r])
            b.auto("vector", lambda e: e.tensor_mul(row_mr[:], row_m[:], row_r[:]),
                   rd=[row_m, row_r], wr=[row_mr])
            b.mm_group(PR, [(ones_f1[:], row_r[:])])
            b.mm_group(PMR, [(ones_f1[:], row_mr[:])])
            for i in range(NDT):
                src = (xw[i][:, 1:TC2 + 1]) if i < 4 else (xw[i][:, 0:TC2])
                b.auto("vector", lambda e, i=i, src=src: e.scalar_tensor_tensor(
                    t1[:], src, 1.0, PR, ALU.mult, ALU.mult),
                    rd=[xw[i], PR], wr=[t1])
                b.auto("vector", lambda e, i=i: e.tensor_sub(t1[:], t1[:], PMR),
                       rd=[t1, PMR], wr=[t1])
                b.auto("vector", lambda e, i=i: e.tensor_scalar(
                    xn[i][:], t1[:], g_s[:, i:i + 1], bb_s[:, i:i + 1], ALU.mult, ALU.add),
                    rd=[t1, g_s, bb_s], wr=[xn[i]])
            for j in range(NJ):
                rows = 128 if j < NJ - 1 else 42
                a0 = j * 128
                g0 = DFF + j * 128
                b.mm_group(PA[0:rows, :], [(W1[k][:, a0:a0 + rows], xn[k][:]) for k in range(NDT)])
                b.mm_group(PG[0:rows, :], [(W1[k][:, g0:g0 + rows], xn[k][:]) for k in range(NDT)])
                b.auto("scalar", lambda e, rows=rows, j=j: e.activation(
                    gel[0:rows, :], PG[0:rows, :], AF.Gelu, bias=b1g_s[0:rows, j:j + 1]),
                    rd=[PG, b1g_s], wr=[gel])
                b.auto("vector", lambda e, rows=rows, j=j: e.scalar_tensor_tensor(
                    hjs[j][0:rows, :], PA[0:rows, :], b1a_s[0:rows, j:j + 1], gel[0:rows, :],
                    ALU.add, ALU.mult),
                    rd=[PA, b1a_s, gel], wr=[hjs[j]])
            for mo in range(NDT):
                b.mm_group(PY[mo],
                           [(W2[j][0:(128 if j < NJ - 1 else 42), mo * 128:(mo + 1) * 128],
                             hjs[j][0:(128 if j < NJ - 1 else 42), :]) for j in range(NJ)],)
            for mo in range(NDT):
                b.auto("vector", lambda e, mo=mo: e.scalar_tensor_tensor(
                    xw[mo][:, 1:TC2 + 1], PY[mo], b2_s[:, mo:mo + 1],
                    xw[mo][:, 1:TC2 + 1], ALU.add, ALU.add),
                    rd=[PY4[mo // 2], b2_s, xw[mo], (xw[mo], "b")],
                    wr=[(xw[mo], "b"), xw[mo]])
                b.dma("sync", (y_out[bi, mo * 128:(mo + 1) * 128, t0:t0 + TC2],
                               f"{bi}.{c}.{mo}"), xw[mo][:, 1:TC2 + 1])


def build_phase_B_ffn():
    nc = bass.Bass("TRN2", target_bir_lowering=False)
    b = Bld(nc)
    emit_B_ffn(b, _mk_ext(nc, BF_EXT))
    return b.finalize(_st_waits(b))



LBLK = 256
LW = 128
LBAND = 384
NLAT = 64
LHD = 64
NLH = 4


C_EXT = {"qkvw": [D, 3 * D], "qkvb": [128, 3 * NDT], "vbrow": [1, D], "rqw": [D, 256],
         "rqb": [128, 2], "rkvw": [D, 512], "rkb": [128, 2], "rvbrow": [1, 256],
         "routw": [256, D], "routb": [128, NDT], "outw": [D, D], "outb": [128, NDT],
         "glrw": [128, 2], "maskl": [128, 3 * LBAND], "masklat": [128, 16 * NLAT],
         "identc": [128, 128]}


def emit_C(b, ext):
    x_in = ext["x"]
    qkv_w = ext["qkvw"]
    qkv_b = ext["qkvb"]
    vb_row = ext["vbrow"]
    rq_w = ext["rqw"]
    rq_b = ext["rqb"]
    rkv_w = ext["rkvw"]
    rk_b = ext["rkb"]
    rvb_row = ext["rvbrow"]
    rout_w = ext["routw"]
    rout_b = ext["routb"]
    out_w = ext["outw"]
    out_b = ext["outb"]
    glr_w = ext["glrw"]
    maskl_d = ext["maskl"]
    masklat_d = ext["masklat"]
    ident_d = ext["identc"]
    y_out = ext["y"]

    KVW = [b.sb(f"KVW{i}", [128, 2 * D], BF16) for i in range(NDT)]
    WqT = [b.sb(f"WqT{i}", [128, D], BF16) for i in range(NDT)]
    RqW = [b.sb(f"RqW{i}", [128, 256], BF16) for i in range(NDT)]
    RkvW = [b.sb(f"RkvW{i}", [128, 512], BF16) for i in range(NDT)]
    RoW = [b.sb(f"RoW{i}", [128, D], BF16) for i in range(2)]
    OW = [b.sb(f"OW{i}", [128, D], BF16) for i in range(NDT)]
    qkvb_s = b.sb("qkvb_s", [128, 3 * NDT])
    vbr = b.sb("vbr", [1, D], BF16)
    rqb_s = b.sb("rqb_s", [128, 2])
    rkb_s = b.sb("rkb_s", [128, 2])
    rvbr = b.sb("rvbr", [1, 256], BF16)
    routb_s = b.sb("routb_s", [128, NDT])
    outb_s = b.sb("outb_s", [128, NDT])
    glr_s = b.sb("glr_s", [128, 2], BF16)
    maskl = b.sb("maskl_s", [128, 3 * LBAND], BF16)
    masklat = b.sb("masklat_s", [128, 16 * NLAT], BF16)
    ident = b.sb("ident_s", [128, 128], BF16)
    ones_b = b.sb("ones_b", [128, 1], BF16)
    ones_f1 = b.sb("ones_f1", [1, 128])
    ones_r128 = b.sb("ones_r128", [1, 128], BF16)

    KT = [b.sb(f"KT{i}", [128, 640], BF16) for i in range(NDT)]
    VT = [b.sb(f"VT{i}", [128, D], BF16) for i in range(5)]
    localT = [b.sb(f"localT{i}", [128, T], BF16) for i in range(NDT)]
    rqT = [b.sb(f"rqT{i}", [128, T], BF16) for i in range(2)]
    latoT = [b.sb(f"latoT{i}", [128, T], BF16) for i in range(2)]
    pooled = [b.sb(f"pooled{i}", [128, NLAT], BF16) for i in range(NDT)]
    pooled_f = [b.sb(f"pooledf{i}", [128, NLAT]) for i in range(NDT)]
    rkT = [b.sb(f"rkT{i}", [128, NLAT], BF16) for i in range(2)]
    rv_sb = b.sb("rv_sb", [NLAT, 256], BF16)
    x_t = [b.sb(f"x_t{i}", [128, TC]) for i in range(NDT)]
    xb_t = [b.sb(f"xb_t{i}", [128, TC], BF16) for i in range(NDT)]
    qblk = [b.sb(f"qblk{i}", [128, LBLK], BF16) for i in range(NDT)]
    atb = b.sb("atb", [128, LBAND], BF16)
    esb = atb
    attnT = b.sb("attnT", [128, LBAND], BF16)
    zrow = b.sb("zrow", [128, 1])
    rz = b.sb("rz", [128, 1])
    ngm = b.sb("ngm", [128, 1])
    remb = [b.sb(f"remb{i}", [128, TC], BF16) for i in range(NDT)]
    xq_t = remb
    mx = xb_t
    g_buf = b.sb("g_buf", [1, 2 * TC], BF16)
    t1 = b.sb("t1", [128, TC])
    row_m = b.sb("row_m", [1, TC])
    row_v = b.sb("row_v", [1, TC])
    row_r = b.sb("row_r", [1, TC])
    row_mr = row_v

    P = [b.ps(f"P{i}", [128, TC]) for i in range(6)]
    PT = b.ps("PT", [128, LBAND], BF16)
    PV = b.ps("PV", [128, 128])

    for i in range(NDT):
        rsl = slice(i * 128, (i + 1) * 128)
        b.dma("gpsimd", WqT[i][:], qkv_w[rsl, 0:D])
        b.dma("gpsimd", (KVW[i][:, 0:D], "k"), qkv_w[rsl, D:2 * D], f"q_gp_KVW{i}")
        b.dma("gpsimd", (KVW[i][:, D:2 * D], "v"), qkv_w[rsl, 2 * D:3 * D], f"q_gp_KVW{i}")
        b.dma("gpsimd", RqW[i][:], rq_w[rsl, :])
        b.dma("gpsimd", RkvW[i][:], rkv_w[rsl, :])
        b.dma("gpsimd", OW[i][:], out_w[rsl, :])
    for i in range(2):
        b.dma("gpsimd", RoW[i][:], rout_w[i * 128:(i + 1) * 128, :])
    b.dma("sync", qkvb_s[:], qkv_b[:])
    b.dma("gpsimd", vbr[:], vb_row[:])
    b.dma("sync", rqb_s[:], rq_b[:])
    b.dma("sync", rkb_s[:], rk_b[:])
    b.dma("gpsimd", rvbr[:], rvb_row[:])
    b.dma("sync", routb_s[:], rout_b[:])
    b.dma("sync", outb_s[:], out_b[:])
    b.dma("gpsimd", glr_s[:], glr_w[:])
    b.dma("gpsimd", maskl[:], maskl_d[:])
    b.dma("gpsimd", masklat[:], masklat_d[:])
    b.dma("gpsimd", ident[:], ident_d[:])
    b.auto("vector", lambda e: e.memset(ones_b[:], 1.0), wr=[ones_b])
    b.auto("vector", lambda e: e.memset(ones_f1[:], 1.0), wr=[ones_f1])
    b.auto("vector", lambda e: e.memset(ones_r128[:], 1.0), wr=[ones_r128])

    for bi in range(B_PER_CORE):
        for c in range(NCH):
            t0 = c * TC
            xsl = slice(t0, t0 + TC)
            for i in range(NDT):
                b.dma("sync", x_t[i][:], x_in[bi, i * 128:(i + 1) * 128, xsl])
            for i in range(NDT):
                b.auto("scalar", lambda e, i=i: e.copy(xb_t[i][:], x_t[i][:]),
                       rd=[x_t[i]], wr=[xb_t[i]])
                b.auto("scalar", lambda e, i=i: e.activation(xq_t[i][:, 0:TC], x_t[i][:], AF.Square),
                       rd=[x_t[i]], wr=[xq_t[i]])
            b.mm_group(P[4][0:1, :], [(ones_b[:], xb_t[i][:]) for i in range(NDT)])
            b.mm_group(P[5][0:1, :], [(ones_b[:], xq_t[i][:, 0:TC]) for i in range(NDT)])
            b.auto("vector", lambda e: e.tensor_single_scalar(row_m[:], P[4][0:1, :], 1.0 / D, ALU.mult),
                   rd=[P[4]], wr=[row_m])
            b.auto("vector", lambda e: e.tensor_single_scalar(row_v[:], P[5][0:1, :], 1.0 / D, ALU.mult),
                   rd=[P[5]], wr=[row_v])
            b.auto("vector", lambda e: e.tensor_mul(row_r[:], row_m[:], row_m[:]),
                   rd=[row_m], wr=[row_r])
            b.auto("vector", lambda e: e.scalar_tensor_tensor(
                row_v[:], row_v[:], 1e-5, row_r[:], ALU.add, ALU.subtract),
                rd=[row_v, row_r], wr=[row_v])
            b.auto("scalar", lambda e: e.activation(row_r[:], row_v[:], AF.Ln),
                   rd=[row_v], wr=[row_r])
            b.auto("scalar", lambda e: e.activation(row_r[:], row_r[:], AF.Exp, scale=-0.5),
                   rd=[row_r], wr=[row_r])
            b.auto("vector", lambda e: e.tensor_mul(row_mr[:], row_m[:], row_r[:]),
                   rd=[row_m, row_r], wr=[row_mr])
            b.mm_group(P[4][:], [(ones_f1[:], row_r[:])])
            b.mm_group(P[5][:], [(ones_f1[:], row_mr[:])])
            for i in range(NDT):
                b.auto("vector", lambda e, i=i: e.scalar_tensor_tensor(
                    t1[:], x_t[i][:], 1.0, P[4][:], ALU.mult, ALU.mult),
                    rd=[x_t[i], P[4]], wr=[t1])
                b.auto("vector", lambda e, i=i: e.tensor_sub(xb_t[i][:], t1[:], P[5][:]),
                       rd=[t1, P[5]], wr=[xb_t[i]])
            for i in range(NDT):
                b.auto("vector", lambda e, i=i, c=c: e.reduce_sum(
                    pooled_f[i][:, c * 16:(c + 1) * 16],
                    xb_t[i][:].rearrange("p (a w) -> p a w", w=32), axis=AX.X),
                    rd=[xb_t[i]], wr=[(pooled_f[i], c)])
            for m in range(2):
                b.mm_group(P[4][:], [(RqW[k][:, m * 128:(m + 1) * 128], xb_t[k][:])
                                     for k in range(NDT)])
                b.auto("vector", lambda e, o=rqT[m][:, xsl], m=m: e.tensor_single_scalar(
                    o, P[4][:], rqb_s[:, m:m + 1], ALU.add),
                    rd=[P[4], rqb_s], wr=[(rqT[m], c)])
            for i in range(NDT):
                if c == 0:
                    b.auto("vector", lambda e, i=i: e.memset(KT[i][:, 0:128], 0.0),
                           wr=[(KT[i], "prev")])
                else:
                    b.auto("vector", lambda e, i=i: e.tensor_copy(
                        KT[i][:, 0:128], KT[i][:, 512:640]),
                        rd=[(KT[i], "cur")], wr=[(KT[i], "prev")])
            if c == 0:
                b.auto("vector", lambda e: e.memset(VT[0][:], 0.0), wr=[VT[0]])
            else:
                b.auto("vector", lambda e: e.tensor_copy(VT[0][:], VT[4][:]),
                       rd=[VT[4]], wr=[VT[0]])
            for m in range(NDT):
                b.mm_group(P[m % 2][:], [(KVW[k][:, m * 128:(m + 1) * 128], xb_t[k][:])
                                         for k in range(NDT)])
                b.auto("vector", lambda e, m=m: e.tensor_single_scalar(
                    KT[m][:, 128:640], P[m % 2][:], qkvb_s[:, NDT + m:NDT + m + 1], ALU.add),
                    rd=[P[m % 2], qkvb_s], wr=[(KT[m], "cur")])
            for tt in range(4):
                tsl = slice(tt * 128, (tt + 1) * 128)
                for ch in range(2):
                    b.mm_group(P[2 + ch][:],
                               [(xb_t[k][:, tsl], KVW[k][:, D + ch * 512:D + (ch + 1) * 512])
                                for k in range(NDT)]
                               + [(ones_r128[0:1, 0:128], vbr[0:1, ch * 512:(ch + 1) * 512])])
                    b.auto("vector", lambda e, tt=tt, ch=ch: e.tensor_copy(
                        VT[1 + tt][:, ch * 512:(ch + 1) * 512], P[2 + ch][:]),
                        rd=[P[2 + ch]], wr=[(VT[1 + tt], ch)])
            for wloc in range(2):
                bb = 2 * c + wloc
                bq0 = wloc * LBLK
                for m in range(NDT):
                    b.mm_group(P[4][:, 0:LBLK],
                               [(WqT[k][:, m * 128:(m + 1) * 128], xb_t[k][:, bq0:bq0 + LBLK])
                                for k in range(NDT)])
                    b.auto("vector", lambda e, m=m: e.tensor_single_scalar(
                        qblk[m][:], P[4][:, 0:LBLK], qkvb_s[:, m:m + 1], ALU.add),
                        rd=[P[4], qkvb_s], wr=[qblk[m]])
                kroll0 = 256 * wloc
                for qt in range(2):
                    mv = qt if bb > 0 else (2 if qt == 0 else 1)
                    msl = slice(mv * LBAND, (mv + 1) * LBAND)
                    for h in range(HEADS):
                        m = h // 2
                        hr = slice((h % 2) * 64, (h % 2) * 64 + 64)
                        b.mm_group(P[5][:, 0:LBAND],
                                   [(qblk[m][hr, qt * 128:(qt + 1) * 128],
                                     KT[m][hr, kroll0:kroll0 + LBAND])])
                        b.auto("vector", lambda e: e.reduce_max(
                            ngm[:], P[5][:, 0:LBAND], axis=AX.X, negate=True),
                            rd=[P[5]], wr=[ngm])
                        b.auto("vector", lambda e: e.tensor_single_scalar(
                            ngm[:], ngm[:], 0.125, ALU.mult), rd=[ngm], wr=[ngm])
                        b.auto("scalar", lambda e: e.activation(
                            atb[:], P[5][:, 0:LBAND], AF.Exp, bias=ngm[:], scale=0.125),
                            rd=[P[5], ngm], wr=[atb])
                        b.auto("vector", lambda e, msl=msl: e.scalar_tensor_tensor(
                            atb[:], atb[:], 1.0, maskl[:, msl], ALU.mult, ALU.mult,
                            accum_out=zrow[:]),
                            rd=[atb, maskl], wr=[atb, zrow])
                        b.auto("vector", lambda e: e.tensor_single_scalar(
                            zrow[:], zrow[:], 1e-30, ALU.max), rd=[zrow], wr=[zrow])
                        b.auto("vector", lambda e: e.reciprocal(rz[:], zrow[:]),
                               rd=[zrow], wr=[rz])
                        b.auto("vector", lambda e: e.tensor_single_scalar(
                            atb[:], atb[:], rz[:], ALU.mult), rd=[atb, rz], wr=[atb])
                        for j in range(3):
                            b.auto("tensor", lambda e, j=j: e.transpose(
                                PT[:, j * 128:(j + 1) * 128], atb[:, j * 128:(j + 1) * 128],
                                ident[:]), rd=[atb, ident], wr=[(PT, j)])
                        b.auto("vector", lambda e: e.tensor_copy(attnT[:], PT[:]),
                               rd=[PT, (PT, 0), (PT, 1), (PT, 2)], wr=[attnT])
                        vr0 = 2 * wloc
                        b.mm_group(PV[hr, :],
                                   [(VT[vr0 + j][:, h * 64:(h + 1) * 64],
                                     attnT[:, j * 128:(j + 1) * 128]) for j in range(3)])
                        if h % 2 == 1:
                            b.auto("vector", lambda e, m=m, bb=bb, qt=qt: e.tensor_copy(
                                localT[m][:, bb * LBLK + qt * 128:bb * LBLK + (qt + 1) * 128],
                                PV[:]), rd=[PV], wr=[(localT[m], f"{bb}.{qt}")])
        for i in range(NDT):
            b.auto("vector", lambda e, i=i: e.tensor_single_scalar(
                pooled[i][:], pooled_f[i][:], 1.0 / 32, ALU.mult),
                rd=[pooled_f[i]] + [(pooled_f[i], c) for c in range(NCH)], wr=[pooled[i]])
        for m in range(2):
            b.mm_group(P[0][:, 0:NLAT], [(RkvW[k][:, m * 128:(m + 1) * 128], pooled[k][:])
                                         for k in range(NDT)])
            b.auto("vector", lambda e, m=m: e.tensor_single_scalar(
                rkT[m][:, :], P[0][:, 0:NLAT], rkb_s[:, m:m + 1], ALU.add),
                rd=[P[0], rkb_s], wr=[rkT[m]])
        b.mm_group(P[1][0:NLAT, 0:256],
                   [(pooled[k][:], RkvW[k][:, 256:512]) for k in range(NDT)]
                   + [(ones_r128[0:1, 0:NLAT], rvbr[:])])
        b.auto("vector", lambda e: e.tensor_copy(rv_sb[:], P[1][0:NLAT, 0:256]),
               rd=[P[1]], wr=[rv_sb])
        for tt in range(16):
            q0 = tt * 128
            for lh in range(NLH):
                mt = lh // 2
                hr = slice((lh % 2) * 64, (lh % 2) * 64 + 64)
                b.mm_group(P[4][:, 0:NLAT], [(rqT[mt][hr, q0:q0 + 128], rkT[mt][hr, :])])
                b.auto("vector", lambda e: e.reduce_max(
                    ngm[:], P[4][:, 0:NLAT], axis=AX.X, negate=True),
                    rd=[P[4]], wr=[ngm])
                b.auto("vector", lambda e: e.tensor_single_scalar(
                    ngm[:], ngm[:], 0.125, ALU.mult), rd=[ngm], wr=[ngm])
                b.auto("scalar", lambda e: e.activation(
                    atb[:, 0:NLAT], P[4][:, 0:NLAT], AF.Exp, bias=ngm[:], scale=0.125),
                    rd=[P[4], ngm], wr=[atb])
                b.auto("vector", lambda e, tt=tt: e.scalar_tensor_tensor(
                    atb[:, 0:NLAT], atb[:, 0:NLAT], 1.0, masklat[:, tt * NLAT:(tt + 1) * NLAT],
                    ALU.mult, ALU.mult, accum_out=zrow[:]),
                    rd=[atb, masklat], wr=[atb, zrow])
                b.auto("vector", lambda e: e.tensor_single_scalar(
                    zrow[:], zrow[:], 1e-30, ALU.max), rd=[zrow], wr=[zrow])
                b.auto("vector", lambda e: e.reciprocal(rz[:], zrow[:]), rd=[zrow], wr=[rz])
                b.auto("vector", lambda e: e.tensor_single_scalar(
                    atb[:, 0:NLAT], atb[:, 0:NLAT], rz[:], ALU.mult), rd=[atb, rz], wr=[atb])
                b.auto("tensor", lambda e: e.transpose(
                    PT[0:NLAT, 0:128], atb[:, 0:NLAT], ident[:]),
                    rd=[atb, ident], wr=[PT])
                b.auto("vector", lambda e: e.tensor_copy(attnT[0:NLAT, 0:128], PT[0:NLAT, 0:128]),
                       rd=[PT], wr=[attnT])
                b.mm_group(PV[hr, :], [(rv_sb[:, lh * 64:(lh + 1) * 64], attnT[0:NLAT, 0:128])])
                if lh % 2 == 1:
                    b.auto("vector", lambda e, mt=mt, q0=q0: e.tensor_copy(
                        latoT[mt][:, q0:q0 + 128], PV[:]), rd=[PV], wr=[(latoT[mt], tt)])
        for c in range(NCH):
            t0 = c * TC
            xsl = slice(t0, t0 + TC)
            for m in range(NDT):
                b.mm_group(P[m % 2][:], [(RoW[k][:, m * 128:(m + 1) * 128],
                                          latoT[k][:, xsl]) for k in range(2)])
                b.auto("vector", lambda e, m=m: e.tensor_single_scalar(
                    remb[m][:], P[m % 2][:], routb_s[:, m:m + 1], ALU.add),
                    rd=[P[m % 2], routb_s], wr=[remb[m]])
            for m in range(NDT):
                for hh in range(2):
                    hr = slice(hh * 64, hh * 64 + 64)
                    b.mm_group(P[2][0:1, :],
                               [(glr_s[hr, 0:1], localT[m][hr, xsl]),
                                (glr_s[hr, 1:2], remb[m][hr, :])])
                    b.auto("scalar", lambda e, hh=hh: e.activation(
                        g_buf[0:1, hh * TC:(hh + 1) * TC], P[2][0:1, :], AF.Sigmoid),
                        rd=[P[2]], wr=[(g_buf, hh)])
                for hh in range(2):
                    b.mm_group(P[3][hh * 64:hh * 64 + 64, :],
                               [(ones_r128[0:1, 0:64], g_buf[0:1, hh * TC:(hh + 1) * TC])])
                b.auto("vector", lambda e, src=localT[m][:, xsl], m=m: e.tensor_tensor(
                    t1[:], src, remb[m][:], ALU.subtract),
                    rd=[localT[m], remb[m]], wr=[t1])
                b.auto("vector", lambda e, m=m: e.tensor_tensor(
                    t1[:], t1[:], P[3][:], ALU.mult), rd=[t1, P[3]], wr=[t1])
                b.auto("vector", lambda e, m=m: e.tensor_tensor(
                    mx[m][:], t1[:], remb[m][:], ALU.add), rd=[t1, remb[m]], wr=[mx[m]])
            for mo in range(NDT):
                b.mm_group(P[4 + mo % 2][:], [(OW[k][:, mo * 128:(mo + 1) * 128], mx[k][:])
                                              for k in range(NDT)])
                b.dma("sync", x_t[mo][:], x_in[bi, mo * 128:(mo + 1) * 128, xsl],
                      sem=f"q_sy_x2_{mo}")
                b.auto("vector", lambda e, mo=mo: e.scalar_tensor_tensor(
                    x_t[mo][:], P[4 + mo % 2][:], outb_s[:, mo:mo + 1], x_t[mo][:],
                    ALU.add, ALU.add),
                    rd=[P[4 + mo % 2], outb_s, x_t[mo]], wr=[x_t[mo]])
                b.dma("sync", (y_out[bi, mo * 128:(mo + 1) * 128, xsl], f"{bi}.{c}.{mo}"),
                      x_t[mo][:])


def build_phase_C():
    nc = bass.Bass("TRN2", target_bir_lowering=False)
    b = Bld(nc)
    emit_C(b, _mk_ext(nc, C_EXT))
    return b.finalize(_st_waits(b))


def _prep_phase_C(inputs):
    d = {}
    d["qkvw"] = np.ascontiguousarray(np.asarray(inputs["L1_attn_qkv_w"]))
    qb = np.asarray(inputs["L1_attn_qkv_b"])
    d["qkvb"] = np.ascontiguousarray(qb[:2 * D].reshape(2 * NDT, 128).T)
    d["qkvb"] = np.concatenate(
        [d["qkvb"], qb[2 * D:].reshape(NDT, 128).T], axis=1).astype(np.float32)
    d["qkvb"] = np.ascontiguousarray(d["qkvb"])
    d["vbrow"] = np.ascontiguousarray(qb[2 * D:][None, :])
    d["rqw"] = np.ascontiguousarray(np.asarray(inputs["L1_attn_rq_w"]))
    d["rqb"] = np.ascontiguousarray(np.asarray(inputs["L1_attn_rq_b"]).reshape(2, 128).T)
    d["rkvw"] = np.ascontiguousarray(np.asarray(inputs["L1_attn_rkv_w"]))
    rkvb = np.asarray(inputs["L1_attn_rkv_b"])
    d["rkb"] = np.ascontiguousarray(rkvb[:256].reshape(2, 128).T)
    d["rvbrow"] = np.ascontiguousarray(rkvb[256:][None, :])
    d["routw"] = np.ascontiguousarray(np.asarray(inputs["L1_attn_rout_w"]))
    d["routb"] = np.ascontiguousarray(np.asarray(inputs["L1_attn_rout_b"]).reshape(NDT, 128).T)
    d["outw"] = np.ascontiguousarray(np.asarray(inputs["L1_attn_out_w"]))
    d["outb"] = np.ascontiguousarray(np.asarray(inputs["L1_attn_out_b"]).reshape(NDT, 128).T)
    glr = np.concatenate(
        [np.asarray(inputs["L1_attn_gl_w"]), np.asarray(inputs["L1_attn_gr_w"])], axis=1)
    d["glrw"] = np.ascontiguousarray(np.tile(glr, (2, 1)))
    i = np.arange(128)[:, None]
    jj = np.arange(LBAND)[None, :]
    m0 = ((jj >= i) & (jj <= i + 127)).astype(np.float32)
    m1 = ((jj >= 128 + i) & (jj <= 255 + i)).astype(np.float32)
    m2 = m0 * (jj >= LW)
    d["maskl"] = np.ascontiguousarray(np.concatenate([m0, m1, m2], axis=1))
    mm = np.zeros((128, 16 * NLAT), dtype=np.float32)
    for tt in range(16):
        lim = 4 * tt + (np.arange(128) // 32)
        mm[:, tt * NLAT:(tt + 1) * NLAT] = (np.arange(NLAT)[None, :] < lim[:, None])
    d["masklat"] = np.ascontiguousarray(mm)
    d["identc"] = np.eye(128, dtype=np.float32)
    return d



NEXP = 8
DFF2 = 4096
NFF = DFF2 // 128


D_EXT = {"rw": [D, NEXP], "w1": [NEXP, D, DFF2], "b1": [NEXP, 128, NFF],
         "w2": [NEXP, DFF2, D], "b2": [NEXP, 128, NDT], "gam": [NEXP, 128, NDT],
         "bet": [NEXP, 128, NDT], "identf": [128, 128]}


def emit_D(b, ext):
    nc = b.nc
    x_in = ext["x"]
    rw_d = ext["rw"]
    w1_d = ext["w1"]
    b1_d = ext["b1"]
    w2_d = ext["w2"]
    b2_d = ext["b2"]
    gam_d = ext["gam"]
    bet_d = ext["bet"]
    identf_d = ext["identf"]
    y_out = ext["y"]
    xn_scr = ext["xnscr"]
    g_scr = ext["gscr"]

    W1 = [b.sb(f"W1_{k}", [128, DFF2], BF16) for k in range(NDT)]
    W2 = [b.sb(f"W2_{j}", [128, D], BF16) for j in range(NFF)]
    rw_s = b.sb("rw_s", [128, NEXP * NDT])
    b1_s = b.sb("b1_s", [128, NFF])
    b2_s = b.sb("b2_s", [128, NDT])
    gam_s = b.sb("gam_s", [128, NDT])
    bet_s = b.sb("bet_s", [128, NDT])
    identf = b.sb("identf_s", [128, 128])
    ones_b = b.sb("ones_b", [128, 1], BF16)

    x_t = [b.sb(f"x_t{i}", [128, TC]) for i in range(NDT)]
    xnf = [b.sb(f"xnf{i}", [128, TC]) for i in range(NDT)]
    xb_t = [b.sb(f"xb_t{i}", [128, TC], BF16) for i in range(NDT)]
    hbuf = [b.sb(f"hb{j}", [128, TC], BF16) for j in range(NFF)]
    r_f = xnf
    sqb = b.sb("sqb", [128, TC], BF16)
    rb = sqb
    xnc = xb_t
    gt_t = b.sb("gt_t", [NEXP, 128], BF16)
    gfin = b.sb("gfin", [128, NEXP])
    g_sb = b.sb("g_sb", [1, TC], BF16)
    ones_bb = b.sb("ones_bb", [1, 128], BF16)
    gate_t = b.sb("gate_t", [128, NEXP])
    m1c = b.sb("m1c", [128, 1])
    m2c = b.sb("m2c", [128, 1])
    w1c = b.sb("w1c", [128, 1])
    w2c = b.sb("w2c", [128, 1])
    msk1 = b.sb("msk1", [128, NEXP])
    msk2 = b.sb("msk2", [128, NEXP])
    t1 = b.sb("t1", [128, TC])
    row_m = b.sb("row_m", [1, TC], BF16)
    row_v = b.sb("row_v", [1, TC], BF16)

    PH = [b.ps(f"PH{i}", [128, TC]) for i in range(2)]
    PW = [b.ps(f"PW{i}", [128, TC]) for i in range(2)]
    PB1 = b.ps("PB1", [128, TC])
    PB2 = b.ps("PB2", [128, TC])
    PGB = b.ps("PGB", [128, TC])
    PR8 = b.ps("PR8", [128, 128])

    for i in range(NDT):
        b.dma("sync", rw_s[:, i * NEXP:(i + 1) * NEXP], rw_d[i * 128:(i + 1) * 128, :],
              sem="q_sy_rw")
    b.dma("sync", identf[:], identf_d[:])
    b.auto("vector", lambda e: e.memset(ones_b[:], 1.0), wr=[ones_b])
    b.auto("vector", lambda e: e.memset(ones_bb[:], 1.0), wr=[ones_bb])

    for bi in range(B_PER_CORE):
        for c in range(NCH):
            t0 = c * TC
            xsl = slice(t0, t0 + TC)
            for i in range(NDT):
                b.dma("sync", x_t[i][:], x_in[bi, i * 128:(i + 1) * 128, xsl])
            for i in range(NDT):
                b.auto("scalar", lambda e, i=i: e.copy(xb_t[i][:], x_t[i][:]),
                       rd=[x_t[i]], wr=[xb_t[i]])
                b.auto("scalar", lambda e, i=i: e.activation(sqb[:], x_t[i][:], AF.Square),
                       rd=[x_t[i]], wr=[sqb])
                b.auto("tensor", lambda e, i=i: e.matmul(
                    PB2[0:1, :], ones_b[:], sqb[:], start=(i == 0), stop=(i == NDT - 1)),
                    rd=[ones_b, sqb], wr=([PB2] if i in (0, NDT - 1) else []))
            b.mm_group(PB1[0:1, :], [(ones_b[:], xb_t[i][:]) for i in range(NDT)])
            b.auto("vector", lambda e: e.tensor_single_scalar(row_m[:], PB1[0:1, :], 1.0 / D, ALU.mult),
                   rd=[PB1], wr=[row_m])
            b.auto("vector", lambda e: e.tensor_single_scalar(row_v[:], PB2[0:1, :], 1.0 / D, ALU.mult),
                   rd=[PB2], wr=[row_v])
            b.auto("vector", lambda e: e.tensor_mul(t1[0:1, :], row_m[:], row_m[:]),
                   rd=[row_m], wr=[(t1, "row")])
            b.auto("vector", lambda e: e.scalar_tensor_tensor(
                row_v[:], row_v[:], 1e-5, t1[0:1, :], ALU.add, ALU.subtract),
                rd=[row_v, (t1, "row")], wr=[row_v])
            b.auto("scalar", lambda e: e.activation(row_v[:], row_v[:], AF.Ln),
                   rd=[row_v], wr=[row_v])
            b.auto("scalar", lambda e: e.activation(row_v[:], row_v[:], AF.Exp, scale=-0.5),
                   rd=[row_v], wr=[row_v])
            b.auto("vector", lambda e: e.tensor_mul(row_m[:], row_m[:], row_v[:]),
                   rd=[row_m, row_v], wr=[row_m])
            b.mm_group(PB1[:], [(ones_bb[:], row_v[:])])
            b.mm_group(PB2[:], [(ones_bb[:], row_m[:])])
            for i in range(NDT):
                b.auto("vector", lambda e, i=i: e.scalar_tensor_tensor(
                    t1[:], x_t[i][:], 1.0, PB1[:], ALU.mult, ALU.mult),
                    rd=[x_t[i], PB1], wr=[t1])
                b.auto("vector", lambda e, i=i: e.tensor_sub(xnf[i][:], t1[:], PB2[:]),
                       rd=[t1, PB2], wr=[xnf[i]])
                b.auto("scalar", lambda e, i=i: e.copy(xb_t[i][:], xnf[i][:]),
                       rd=[xnf[i]], wr=[xb_t[i]])
                b.dma("sync", (xn_scr[bi, i * 128:(i + 1) * 128, xsl], f"{bi}.{c}.{i}"),
                      xb_t[i][:])
            for tt in range(4):
                tloc = slice(tt * 128, (tt + 1) * 128)
                b.mm_group(PR8[:, 0:NEXP],
                           [(xnf[k][:, tloc], rw_s[:, k * NEXP:(k + 1) * NEXP])
                            for k in range(NDT)])
                b.auto("vector", lambda e: e.reduce_max(m1c[:], PR8[:, 0:NEXP], axis=AX.X),
                       rd=[PR8], wr=[m1c])
                b.auto("vector", lambda e: e.tensor_single_scalar(
                    gate_t[:], PR8[:, 0:NEXP], m1c[:], ALU.is_ge),
                    rd=[PR8, m1c], wr=[gate_t])
                b.auto("vector", lambda e: e.tensor_copy(msk1[:], gate_t[:]),
                       rd=[gate_t], wr=[msk1])
                b.auto("vector", lambda e: e.scalar_tensor_tensor(
                    gate_t[:], msk1[:], -1e30, PR8[:, 0:NEXP], ALU.mult, ALU.add),
                    rd=[msk1, PR8], wr=[gate_t])
                b.auto("vector", lambda e: e.reduce_max(m2c[:], gate_t[:], axis=AX.X),
                       rd=[gate_t], wr=[m2c])
                b.auto("vector", lambda e: e.tensor_single_scalar(
                    msk2[:], gate_t[:], m2c[:], ALU.is_ge),
                    rd=[gate_t, m2c], wr=[msk2])
                b.auto("vector", lambda e: e.tensor_sub(w1c[:], m1c[:], m2c[:]),
                       rd=[m1c, m2c], wr=[w1c])
                b.auto("scalar", lambda e: e.activation(w1c[:], w1c[:], AF.Sigmoid),
                       rd=[w1c], wr=[w1c])
                b.auto("vector", lambda e: e.tensor_scalar(
                    w2c[:], w1c[:], -1.0, 1.0, ALU.mult, ALU.add),
                    rd=[w1c], wr=[w2c])
                b.auto("vector", lambda e: e.tensor_single_scalar(
                    msk1[:], msk1[:], w1c[:], ALU.mult), rd=[msk1, w1c], wr=[msk1])
                b.auto("vector", lambda e: e.scalar_tensor_tensor(
                    gfin[:], msk2[:], w2c[:], msk1[:], ALU.mult, ALU.add),
                    rd=[msk2, w2c, msk1], wr=[gfin])
                b.auto("tensor", lambda e: e.transpose(
                    PR8[0:NEXP, 0:128], gfin[:, 0:NEXP], identf[:]),
                    rd=[gfin, identf], wr=[PR8])
                b.auto("scalar", lambda e: e.copy(gt_t[:], PR8[0:NEXP, 0:128]),
                       rd=[PR8], wr=[gt_t])
                b.dma("gpsimd", (g_scr[bi, :, t0 + tt * 128:t0 + (tt + 1) * 128],
                                 f"{bi}.{c}.{tt}"), gt_t[:], sem="q_gp_st_gt")
    for e_i in range(NEXP):
        for k in range(NDT):
            b.dma("gpsimd", W1[k][:], w1_d[e_i, k * 128:(k + 1) * 128, :])
        for j in range(NFF):
            b.dma("gpsimd", W2[j][:], w2_d[e_i, j * 128:(j + 1) * 128, :])
        b.dma("sync", b1_s[:], b1_d[e_i], sem="q_sy_b1")
        b.dma("sync", b2_s[:], b2_d[e_i], sem="q_sy_b2")
        b.dma("sync", gam_s[:], gam_d[e_i], sem="q_sy_gam")
        b.dma("sync", bet_s[:], bet_d[e_i], sem="q_sy_bet")
        for bi in range(B_PER_CORE):
            for c in range(NCH):
                t0 = c * TC
                xsl = slice(t0, t0 + TC)
                for i in range(NDT):
                    rsl = slice(i * 128, (i + 1) * 128)
                    b.dma("sync", xnc[i][:], xn_scr[bi, rsl, xsl])
                    b.dma("sync", x_t[i][:], x_in[bi, rsl, xsl])
                b.dma("gpsimd", g_sb[:], g_scr[bi, e_i, xsl])
                b.mm_group(PGB[:], [(ones_bb[:], g_sb[:])])
                for j in range(NFF):
                    b.mm_group(PH[j % 2][:], [(W1[k][:, j * 128:(j + 1) * 128], xnc[k][:])
                                              for k in range(NDT)])
                    b.auto("scalar", lambda e, j=j: e.activation(
                        hbuf[j][:], PH[j % 2][:], AF.Gelu, bias=b1_s[:, j:j + 1]),
                        rd=[PH[j % 2], b1_s], wr=[hbuf[j]])
                for mo in range(NDT):
                    b.mm_group(PW[mo % 2][:], [(W2[j][:, mo * 128:(mo + 1) * 128], hbuf[j][:])
                                               for j in range(NFF)])
                    b.auto("vector", lambda e, mo=mo: e.scalar_tensor_tensor(
                        r_f[mo][:], PW[mo % 2][:], b2_s[:, mo:mo + 1], xnc[mo][:],
                        ALU.add, ALU.add),
                        rd=[PW[mo % 2], b2_s, xnc[mo]], wr=[r_f[mo]])
                    b.auto("scalar", lambda e, mo=mo: e.activation(sqb[:], r_f[mo][:], AF.Square),
                           rd=[r_f[mo]], wr=[sqb])
                    b.auto("tensor", lambda e, mo=mo: e.matmul(
                        PB2[0:1, :], ones_b[:], sqb[:], start=(mo == 0), stop=(mo == NDT - 1)),
                        rd=[ones_b, sqb], wr=([PB2] if mo in (0, NDT - 1) else []))
                    b.auto("scalar", lambda e, mo=mo: e.copy(rb[:], r_f[mo][:]),
                           rd=[r_f[mo]], wr=[rb])
                    b.auto("tensor", lambda e, mo=mo: e.matmul(
                        PB1[0:1, :], ones_b[:], rb[:], start=(mo == 0), stop=(mo == NDT - 1)),
                        rd=[ones_b, rb], wr=([PB1] if mo in (0, NDT - 1) else []))
                b.auto("vector", lambda e: e.tensor_single_scalar(row_m[:], PB1[0:1, :], 1.0 / D, ALU.mult),
                       rd=[PB1], wr=[row_m])
                b.auto("vector", lambda e: e.tensor_single_scalar(row_v[:], PB2[0:1, :], 1.0 / D, ALU.mult),
                       rd=[PB2], wr=[row_v])
                b.auto("vector", lambda e: e.tensor_mul(t1[0:1, :], row_m[:], row_m[:]),
                       rd=[row_m], wr=[(t1, "row")])
                b.auto("vector", lambda e: e.scalar_tensor_tensor(
                    row_v[:], row_v[:], 1e-6, t1[0:1, :], ALU.add, ALU.subtract),
                    rd=[row_v, (t1, "row")], wr=[row_v])
                b.auto("scalar", lambda e: e.activation(row_v[:], row_v[:], AF.Ln),
                       rd=[row_v], wr=[row_v])
                b.auto("scalar", lambda e: e.activation(row_v[:], row_v[:], AF.Exp, scale=-0.5),
                       rd=[row_v], wr=[row_v])
                b.auto("vector", lambda e: e.tensor_mul(row_m[:], row_m[:], row_v[:]),
                       rd=[row_m, row_v], wr=[row_m])
                b.mm_group(PB1[:], [(ones_bb[:], row_v[:])])
                b.mm_group(PB2[:], [(ones_bb[:], row_m[:])])
                for mo in range(NDT):
                    b.auto("vector", lambda e, mo=mo: e.scalar_tensor_tensor(
                        t1[:], r_f[mo][:], 1.0, PB1[:], ALU.mult, ALU.mult),
                        rd=[r_f[mo], PB1], wr=[t1])
                    b.auto("vector", lambda e: e.tensor_sub(t1[:], t1[:], PB2[:]),
                           rd=[t1, PB2], wr=[t1])
                    b.auto("vector", lambda e, mo=mo: e.tensor_scalar(
                        t1[:], t1[:], gam_s[:, mo:mo + 1], bet_s[:, mo:mo + 1],
                        ALU.mult, ALU.add),
                        rd=[t1, gam_s, bet_s], wr=[t1])
                    b.auto("vector", lambda e: e.tensor_tensor(
                        t1[:], t1[:], PGB[:], ALU.mult), rd=[t1, PGB], wr=[t1])
                    rsl = slice(mo * 128, (mo + 1) * 128)
                    if e_i == 0:
                        b.auto("vector", lambda e, mo=mo: e.tensor_tensor(
                            t1[:], t1[:], x_t[mo][:], ALU.add),
                            rd=[t1, x_t[mo]], wr=[t1])
                        b.dma("sync", (y_out[bi, rsl, xsl], f"{bi}.{c}.{mo}"), t1[:],
                              sem="q_sy_st_y")
                    else:
                        b.auto("gpsimd",
                               lambda e, o=y_out[bi, rsl, xsl]: e.dma_start(
                                   out=o, in_=t1[:], accum_op=ALU.add),
                               rd=[t1, (y_out, f"{bi}.{c}.{mo}")],
                               wr=[(y_out, f"{bi}.{c}.{mo}")],
                               sem="q_gp_st_y", incn=16)


def build_phase_D():
    nc = bass.Bass("TRN2", target_bir_lowering=False)
    b = Bld(nc)
    ext = _mk_ext(nc, D_EXT)
    ext["xnscr"] = nc.dram_tensor("xnscr", [B_PER_CORE, D, T], BF16, kind="Internal")
    ext["gscr"] = nc.dram_tensor("gscr", [B_PER_CORE, NEXP, T], F32, kind="Internal")
    emit_D(b, ext)
    return b.finalize(_st_waits(b))


def _prep_phase_D(inputs):
    d = {}
    d["rw"] = np.ascontiguousarray(np.asarray(inputs["L1_ffn_router_w"]))
    d["w1"] = np.ascontiguousarray(np.asarray(inputs["L1_ffn_w1"]))
    d["b1"] = np.ascontiguousarray(
        np.asarray(inputs["L1_ffn_b1"]).reshape(NEXP, NFF, 128).transpose(0, 2, 1))
    d["w2"] = np.ascontiguousarray(np.asarray(inputs["L1_ffn_w2"]))
    d["b2"] = np.ascontiguousarray(
        np.asarray(inputs["L1_ffn_b2"]).reshape(NEXP, NDT, 128).transpose(0, 2, 1))
    d["gam"] = np.ascontiguousarray(
        np.asarray(inputs["L1_ffn_gamma"]).reshape(NEXP, NDT, 128).transpose(0, 2, 1))
    d["bet"] = np.ascontiguousarray(
        np.asarray(inputs["L1_ffn_beta"]).reshape(NEXP, NDT, 128).transpose(0, 2, 1))
    d["identf"] = np.eye(128, dtype=np.float32)
    return d



def _sel_matrices():
    R = np.zeros((128, 16, 128), dtype=np.float32)
    for k in range(16):
        for p in range(128):
            R[8 * k + p // 16, k, p] = 1.0
    S = np.zeros((128, 16, 128), dtype=np.float32)
    for k in range(16):
        for p in range(128):
            S[p, k, 8 * k + p // 16] = 1.0
    RB = np.zeros((16, 128), dtype=np.float32)
    for p in range(128):
        RB[p % 16, p] = 1.0
    return R.reshape(128, 16 * 128), S.reshape(128, 16 * 128), RB


def _rotary_tables():
    half = np.arange(0, 64, 2, dtype=np.float64)
    inv_freq = 1.0 / 10000.0 ** (half / 64.0)
    t = np.arange(T, dtype=np.float64)
    freqs = t[:, None] * inv_freq[None, :]
    base_scale = (half + 0.4 * 64) / (1.4 * 64)
    power = (t - T // 2) / (WIN // 2)
    scale = base_scale[None, :] ** power[:, None]
    j = np.arange(64)
    fr = freqs[:, j % 32]
    sc = scale[:, j % 32]
    sign = np.where(j < 32, -1.0, 1.0)
    cs_q = (np.cos(fr) * sc).T
    ss_q = (np.sin(fr) * sc).T * sign[:, None]
    cs_k = (np.cos(fr) / sc).T
    ss_k = (np.sin(fr) / sc).T * sign[:, None]
    tile2 = lambda a: np.concatenate([a, a], axis=0).astype(np.float32)
    return tile2(cs_q), tile2(ss_q), tile2(cs_k), tile2(ss_k)


def _masks3():
    i = np.arange(128)[:, None]
    jj = np.arange(BAND)[None, :]
    gen = ((jj >= i) & (jj <= i + 256)).astype(np.float32)
    m1 = gen * (jj >= 256)
    m2 = gen * (jj >= 128)
    mul = np.concatenate([gen, m1, m2], axis=1).astype(np.float32)
    return (mul - 1.0) * 1e9


def _prep_phase_B_shared():
    d = {}
    d["csq"], d["ssq"], d["csk"], d["ssk"] = _rotary_tables()
    d["mask3"] = _masks3()
    bsel = np.zeros((128, 128), dtype=np.float32)
    for m in range(8):
        for p in range(128):
            bsel[p, m * 16 + 2 * m + p // 64] = 1.0
    d["bsel"] = bsel
    psel = np.zeros((128, 2), dtype=np.float32)
    psel[0:64, 0] = 1.0
    psel[64:128, 1] = 1.0
    d["psel"] = psel
    d["pselT"] = np.ascontiguousarray(psel.T)
    d["ident"] = np.eye(128, dtype=np.float32)
    return d


def _prep_phase_B_attn(inputs, l):
    d = {}
    Wqkv = np.asarray(inputs["L0_ffn_Wqkv"])[l]
    wq = Wqkv[:, 0:D]
    wk = Wqkv[:, D:2 * D]
    d["wv"] = np.ascontiguousarray(Wqkv[:, 2 * D:3 * D])
    c = np.arange(D)
    perm = (c // 64) * 64 + ((c % 64) + 32) % 64
    d["wq"] = np.ascontiguousarray(wq)
    d["wk"] = np.ascontiguousarray(wk)
    d["wqr"] = np.ascontiguousarray(wq[:, perm])
    d["wkr"] = np.ascontiguousarray(wk[:, perm])
    d["wo"] = np.ascontiguousarray(np.asarray(inputs["L0_ffn_Wo"])[l])
    qs = np.asarray(inputs["L0_ffn_q_scale"])[l]
    ks = np.asarray(inputs["L0_ffn_k_scale"])[l]
    p64 = ((np.arange(64) + 32) % 64)
    d["qsc"] = np.ascontiguousarray(np.stack(
        [np.tile(qs, 2), np.tile(qs[p64], 2)], axis=1))
    d["ksc"] = np.ascontiguousarray(np.stack(
        [np.tile(ks, 2), np.tile(ks[p64], 2)], axis=1))
    return d


def _prep_phase_B_ffn(inputs, l):
    d = {}
    d["w1"] = np.ascontiguousarray(np.asarray(inputs["L0_ffn_W1"])[l])
    d["w2"] = np.ascontiguousarray(np.asarray(inputs["L0_ffn_W2"])[l])
    d["lng"] = np.ascontiguousarray(np.asarray(inputs["L0_ffn_ln_g"])[l].reshape(NDT, 128).T)
    d["lnb"] = np.ascontiguousarray(np.asarray(inputs["L0_ffn_ln_b"])[l].reshape(NDT, 128).T)
    b1 = np.asarray(inputs["L0_ffn_b1"])[l]
    b1a = np.zeros((NJ * 128,), dtype=np.float32)
    b1a[:DFF] = b1[:DFF]
    b1g = np.zeros((NJ * 128,), dtype=np.float32)
    b1g[:DFF] = b1[DFF:]
    d["b1a"] = np.ascontiguousarray(b1a.reshape(NJ, 128).T)
    d["b1g"] = np.ascontiguousarray(b1g.reshape(NJ, 128).T)
    d["b2"] = np.ascontiguousarray(np.asarray(inputs["L0_ffn_b2"])[l].reshape(NDT, 128).T)
    return d


def _prep_phase_A(inputs):
    d = {}
    d["w_inT"] = np.ascontiguousarray(np.asarray(inputs["L0_attn_in_proj_w"]).T)
    d["conv_w"] = np.ascontiguousarray(np.asarray(inputs["L0_attn_conv_w"])[:, 0, :])
    d["conv_b"] = np.ascontiguousarray(np.asarray(inputs["L0_attn_conv_b"])[:, None])
    d["w_xpT"] = np.ascontiguousarray(np.asarray(inputs["L0_attn_x_proj_w"]).T)
    d["dtwb"] = np.ascontiguousarray(
        np.stack([np.asarray(inputs["L0_attn_dt_proj_w"])[:, 0],
                  np.asarray(inputs["L0_attn_dt_proj_b"])]))
    alog = np.asarray(inputs["L0_attn_A_log"])
    d["a_rep"] = np.ascontiguousarray(
        alog.reshape(8, 16, 8, 16).transpose(2, 3, 0, 1).reshape(128, 128))
    R, S, RB = _sel_matrices()
    d["r_sel"], d["s_sel"], d["rb16"] = R, S, RB
    d["w_outT"] = np.ascontiguousarray(np.asarray(inputs["L0_attn_out_proj_w"]).T)
    return d



def build_fused():
    nc = bass.Bass("TRN2", target_bir_lowering=False)
    b = Bld(nc, coarse_dma_sems=True)
    h = {}

    def dt(name, shape, dtype=F32, kind="ExternalInput"):
        h[name] = nc.dram_tensor(name, shape, dtype, kind=kind)
        return h[name]

    x_in = dt("x", [B_PER_CORE, D, T])
    y_out = dt("y", [B_PER_CORE, D, T], kind="ExternalOutput")
    s0 = dt("s0", [B_PER_CORE, D, T], kind="Internal")
    s1 = dt("s1", [B_PER_CORE, D, T], kind="Internal")
    s2 = dt("s2", [B_PER_CORE, D, T], kind="Internal")
    xnscr = dt("xnscr", [B_PER_CORE, D, T], BF16, kind="Internal")
    gscr = dt("gscr", [B_PER_CORE, NEXP, T], kind="Internal")
    for nm, shape in A_EXT.items():
        dt("A_" + nm, shape)
    for nm, shape in BSH_EXT.items():
        dt("Bs_" + nm, shape)
    for l in range(6):
        for nm, shape in BA_EXT.items():
            dt(f"Ba{l}_" + nm, shape)
        for nm, shape in BF_EXT.items():
            dt(f"Bf{l}_" + nm, shape)
    for nm, shape in C_EXT.items():
        dt("C_" + nm, shape)
    for nm, shape in D_EXT.items():
        dt("D_" + nm, shape)

    def ext_for(prefix, spec, x, y, extra=None):
        e = {nm: h[prefix + nm] for nm in spec}
        e["x"] = x
        e["y"] = y
        if extra:
            e.update(extra)
        return e

    bsh = {nm: h["Bs_" + nm] for nm in BSH_EXT}
    with b.phase("A"):
        emit_A(b, ext_for("A_", A_EXT, x_in, s0))
    with b.phase("P"):
        emit_preln(b, {"x": s0, "y": s1})
    cur, alt = s1, s2
    for l in range(6):
        with b.phase(f"ba{l}"):
            emit_B_attn(b, ext_for(f"Ba{l}_", BA_EXT, cur, alt, extra=bsh))
        cur, alt = alt, cur
        with b.phase(f"bf{l}"):
            emit_B_ffn(b, ext_for(f"Bf{l}_", BF_EXT, cur, alt))
        cur, alt = alt, cur
    with b.phase("R"):
        emit_addres(b, {"x": cur, "xr": s0, "y": alt})
    cur, alt = alt, cur
    with b.phase("C"):
        emit_C(b, ext_for("C_", C_EXT, cur, alt))
    cur, alt = alt, cur
    with b.phase("Dx"):
        emit_D(b, ext_for("D_", D_EXT, cur, y_out,
                          extra={"xnscr": xnscr, "gscr": gscr}))
    return b.finalize([(nm, v) for nm, v in b.cnt.items() if v > 0])


def _prep_fused(inputs):
    shared = {}
    for nm, v in _prep_phase_A(inputs).items():
        shared["A_" + nm] = v
    for nm, v in _prep_phase_B_shared().items():
        shared["Bs_" + nm] = v
    for l in range(6):
        for nm, v in _prep_phase_B_attn(inputs, l).items():
            shared[f"Ba{l}_" + nm] = v
        for nm, v in _prep_phase_B_ffn(inputs, l).items():
            shared[f"Bf{l}_" + nm] = v
    for nm, v in _prep_phase_C(inputs).items():
        shared["C_" + nm] = v
    for nm, v in _prep_phase_D(inputs).items():
        shared["D_" + nm] = v
    return shared


def kernel(**inputs):
    x = np.asarray(inputs["L0_attn_x"])
    xT = np.ascontiguousarray(x.transpose(0, 2, 1))
    shared = _prep_fused(inputs)
    nc = build_fused()
    in_maps = []
    for c in range(NCORES):
        m = dict(shared)
        m["x"] = np.ascontiguousarray(xT[c * B_PER_CORE:(c + 1) * B_PER_CORE])
        in_maps.append(m)
    res = run_bass_kernel_spmd(nc, in_maps, core_ids=list(range(NCORES)))
    yT = np.concatenate([r["y"] for r in res.results], axis=0)
    return np.ascontiguousarray(yT.transpose(0, 2, 1))


# version 07 (final)
# speedup vs baseline: 1.1406x; 1.0446x over previous
import numpy as np
from contextlib import ExitStack

import concourse.bass as bass
import concourse.mybir as mybir
from concourse.bass_utils import run_bass_kernel_spmd

F32 = mybir.dt.float32
BF16 = mybir.dt.bfloat16
AF = mybir.ActivationFunctionType
ALU = mybir.AluOpType
AX = mybir.AxisListType

NCORES = 8
B_PER_CORE = 2
T = 2048
D = 1024
NDT = D // 128
TC = 512
NCH = T // TC

ENGS = ("sync", "tensor", "vector", "scalar", "gpsimd")
ESEM = {"sync": "e_sy", "tensor": "e_pe", "vector": "e_dv",
        "scalar": "e_sc", "gpsimd": "e_po"}


def _tname(x):
    if isinstance(x, tuple):
        return _tname(x[0]) + "#" + str(x[1])
    t = getattr(x, "tensor", None)
    if t is not None:
        return t.name
    return x.name


class Bld:

    def __init__(self, nc, coarse_dma_sems=False):
        self.nc = nc
        self.es = ExitStack()
        self.ops = {e: [] for e in ENGS}
        self.sems = {}
        self.cnt = {}
        self.lastw = {}
        self.readers = {}
        self.regions = {}
        self.clock = {e: {} for e in ENGS}
        self.coarse = coarse_dma_sems
        self.prefix = ""
        self.phase_es = None
        self.sem_alias = {}
        self.pool_used = 0

    def _alias(self, logical):
        nm = self.sem_alias.get(logical)
        if nm is None:
            nm = f"qp{self.pool_used}"
            self.pool_used += 1
            self.sem_alias[logical] = nm
        return nm

    def sem(self, name):
        if name not in self.sems:
            self.sems[name] = self.es.enter_context(self.nc.semaphore(name))
            self.cnt[name] = 0
        return self.sems[name]

    def sb(self, name, shape, dt=F32):
        es = self.phase_es if self.phase_es is not None else self.es
        return es.enter_context(self.nc.sbuf_tensor(self.prefix + name, shape, dt))

    def ps(self, name, shape, dt=F32):
        es = self.phase_es if self.phase_es is not None else self.es
        return es.enter_context(self.nc.psum_tensor(self.prefix + name, shape, dt))

    def phase(self, prefix):
        from contextlib import contextmanager

        @contextmanager
        def _cm():
            self.prefix = prefix + "_"
            self.phase_es = ExitStack()
            try:
                yield
            finally:
                self.barrier()
                self.phase_es.close()
                self.phase_es = None
                self.prefix = ""
        return _cm()

    def barrier(self):
        waits = [(n, v) for n, v in self.cnt.items() if v > 0]
        sems = self.sems
        for eng in ENGS:
            ck = self.clock[eng]
            todo = tuple((n, v) for n, v in waits if ck.get(n, 0) < v)
            for n, v in todo:
                ck[n] = v

            def fn(e, _t=todo):
                for n, v in _t:
                    e.wait_ge(sems[n], v)
            self.ops[eng].append(fn)
        self.sem_alias.clear()
        self.pool_used = 0

    def _conflicts(self, name):
        root = name.split("#")[0]
        if name == root:
            return [root] + sorted(self.regions.get(root, ()))
        return [name, root]

    def _deps(self, eng, rd, wr, extra):
        deps = {}

        def add(rec):
            if rec is None:
                return
            s, tok, peng = rec
            if eng == "tensor" and peng == "tensor":
                return
            if deps.get(s, 0) < tok:
                deps[s] = tok
        for x in rd:
            for n in self._conflicts(_tname(x)):
                add(self.lastw.get(n))
        for x in wr:
            for n in self._conflicts(_tname(x)):
                add(self.lastw.get(n))
                for rec in self.readers.get(n, ()):
                    add(rec)
        for s, tok in extra:
            if deps.get(s, 0) < tok:
                deps[s] = tok
        ck = self.clock[eng]
        out = [(s, v) for s, v in deps.items() if ck.get(s, 0) < v]
        for s, v in out:
            ck[s] = v
            self.sem(s)
        return out

    def _note_region(self, name):
        root = name.split("#")[0]
        if name != root:
            self.regions.setdefault(root, set()).add(name)

    def auto(self, eng, make, rd=(), wr=(), sem=None, incn=1, extra=()):
        sem = sem or ESEM[eng]
        if self.coarse and sem.startswith("q_"):
            sem = self._alias(sem)
        self.sem(sem)
        wait = self._deps(eng, rd, wr, extra)
        self.cnt[sem] += incn
        tok = self.cnt[sem]
        sems = self.sems

        def fn(e, _make=make, _wait=tuple(wait), _sem=sem, _incn=incn):
            for n, v in _wait:
                e.wait_ge(sems[n], v)
            _make(e).then_inc(sems[_sem], _incn)
        self.ops[eng].append(fn)
        rec = (sem, tok, eng)
        for x in rd:
            n = _tname(x)
            self._note_region(n)
            self.readers.setdefault(n, []).append(rec)
        for x in wr:
            n = _tname(x)
            self._note_region(n)
            self.lastw[n] = rec
            self.readers[n] = []
        return tok

    def dma(self, eng, out, in_, sem=None):
        ap_out = out[0] if isinstance(out, tuple) else out
        ap_in = in_[0] if isinstance(in_, tuple) else in_
        if sem is None:
            is_sb = getattr(ap_out, "tensor", ap_out).__class__.__name__.startswith("SB")
            root = (_tname(out).split("#")[0] if is_sb
                    else "st_" + _tname(in_).split("#")[0])
            sem = f"q_{eng[:2]}_{root}"
        return self.auto(eng, lambda e, o=ap_out, i=ap_in: e.dma_start(out=o, in_=i),
                         rd=[in_], wr=[out], sem=sem, incn=16)

    def mm_group(self, psum, tiles, extra=()):
        ps_ap = psum[0] if isinstance(psum, tuple) else psum
        n = len(tiles)
        rd = []
        for lh, rh in tiles:
            rd.append(lh)
            rd.append(rh)
        wait = self._deps("tensor", rd, [psum], extra)
        sems = self.sems
        self.sem("e_pe")
        for idx, (lh, rh) in enumerate(tiles):
            last = idx == n - 1
            if last:
                self.cnt["e_pe"] += 1
            tok = self.cnt["e_pe"]

            def fn(e, _lh=lh, _rh=rh, _st=(idx == 0), _sp=last,
                   _wait=tuple(wait if idx == 0 else ()), _ps=ps_ap, _inc=last, _tok=tok):
                for nm, v in _wait:
                    e.wait_ge(sems[nm], v)
                ins = e.matmul(_ps, _lh, _rh, start=_st, stop=_sp)
                if _inc:
                    ins.then_inc(sems["e_pe"], 1)
            self.ops["tensor"].append(fn)
        rec = ("e_pe", self.cnt["e_pe"], "tensor")
        for x in rd:
            n = _tname(x)
            self._note_region(n)
            self.readers.setdefault(n, []).append(rec)
        n_ = _tname(psum)
        self._note_region(n_)
        self.lastw[n_] = rec
        self.readers[n_] = []
        return self.cnt["e_pe"]

    def finalize(self, final_waits):
        nc = self.nc
        with self.es:
            for n, v in final_waits:
                self.sem(n)
            with nc.Block() as block:
                ops, sems = self.ops, self.sems

                @block.sync
                def _(e):
                    for fn in ops["sync"]:
                        fn(e)
                    for n, v in final_waits:
                        e.wait_ge(sems[n], v)

                @block.tensor
                def _(e):
                    for fn in ops["tensor"]:
                        fn(e)

                @block.vector
                def _(e):
                    for fn in ops["vector"]:
                        fn(e)

                @block.scalar
                def _(e):
                    for fn in ops["scalar"]:
                        fn(e)

                @block.gpsimd
                def _(e):
                    for fn in ops["gpsimd"]:
                        fn(e)
        return nc



A_EXT = {"w_inT": [D, 2 * D], "conv_w": [D, 4], "conv_b": [D, 1], "w_xpT": [D, 33],
         "dtwb": [2, D], "a_rep": [128, 128], "r_sel": [128, 16 * 128],
         "s_sel": [128, 16 * 128], "rb16": [16, 128], "w_outT": [D, D]}


def _mk_ext(nc, spec):
    ext = {"x": nc.dram_tensor("x", [B_PER_CORE, D, T], F32, kind="ExternalInput"),
           "y": nc.dram_tensor("y", [B_PER_CORE, D, T], F32, kind="ExternalOutput")}
    for name, shape in spec.items():
        ext[name] = nc.dram_tensor(name, shape, F32, kind="ExternalInput")
    return ext


def _st_waits(b):
    return [(nm, b.cnt[nm]) for nm in list(b.cnt) if nm.startswith("q_sy_st_")]


def emit_A(b, ext):
    nc = b.nc
    x_in = ext["x"]
    w_inT = ext["w_inT"]
    conv_w = ext["conv_w"]
    conv_b = ext["conv_b"]
    w_xpT = ext["w_xpT"]
    dtwb = ext["dtwb"]
    a_rep = ext["a_rep"]
    r_sel = ext["r_sel"]
    s_sel = ext["s_sel"]
    rb16 = ext["rb16"]
    w_outT = ext["w_outT"]
    y_out = ext["y"]

    ones_b = b.sb("ones_b", [128, 1], BF16)
    ones_f1 = b.sb("ones_f1", [1, 128], F32)
    WinT = [b.sb(f"WinT{i}", [128, 2 * D], BF16) for i in range(NDT)]
    XpT = [b.sb(f"XpT{i}", [128, 33], BF16) for i in range(NDT)]
    WoT = [b.sb(f"WoT{i}", [128, D], BF16) for i in range(NDT)]
    cw = b.sb("cw", [128, 4 * NDT], F32)
    cb = b.sb("cb", [128, NDT], F32)
    dtwb_s = b.sb("dtwb_s", [2, D], F32)
    a_rep_s = b.sb("a_rep_s", [128, 128], F32)
    Rsel = b.sb("Rsel", [128, 16 * 128], F32)
    Ssel = b.sb("Ssel", [128, 16 * 128], BF16)
    Rb16 = b.sb("Rb16", [16, 128], F32)

    x_t = [b.sb(f"x_t{i}", [128, TC]) for i in range(NDT)]
    xb_t = [b.sb(f"xb_t{i}", [128, TC], BF16) for i in range(NDT)]
    xn_t = [b.sb(f"xn_t{i}", [128, TC], BF16) for i in range(NDT)]
    xin_t = [b.sb(f"xin_t{i}", [128, TC + 3]) for i in range(NDT)]
    marg = b.sb("marg", [128, 3 * NDT])
    xc_t = [b.sb(f"xc_t{i}", [128, TC]) for i in range(NDT)]
    xcb_t = [b.sb(f"xcb_t{i}", [128, TC], BF16) for i in range(NDT)]
    xq_t = xcb_t
    sz_t = [b.sb(f"sz_t{i}", [128, TC]) for i in range(NDT)]
    dt_t = [b.sb(f"dt_t{i}", [128, TC]) for i in range(NDT)]
    ym_t = xn_t
    row_m = b.sb("row_m", [1, TC])
    row_v = b.sb("row_v", [1, TC])
    row_r = b.sb("row_r", [1, TC])
    row_mr = b.sb("row_mr", [1, TC])
    brow = b.sb("brow", [16, TC])
    crow = b.sb("crow", [16, TC])
    s1row = b.sb("s1row", [2, TC])
    sp_a = b.sb("sp_a", [128, TC])
    sp_b = b.sb("sp_b", [128, TC])
    brep = b.sb("brep", [128, TC])
    crep = b.sb("crep", [128, TC])
    dA = [b.sb(f"dA{j}", [128, TC]) for j in range(2)]
    dBx = [b.sb(f"dBx{j}", [128, TC]) for j in range(2)]
    hbuf = [b.sb(f"hbuf{j}", [128, TC]) for j in range(2)]
    hC = [b.sb(f"hC{j}", [128, TC], BF16) for j in range(2)]
    carry = b.sb("carry", [128, 128])

    P = [b.ps(f"P{i}", [128, TC]) for i in range(8)]

    for i in range(NDT):
        b.dma("gpsimd", WinT[i][:], w_inT[i * 128:(i + 1) * 128, :])
        b.dma("gpsimd", XpT[i][:], w_xpT[i * 128:(i + 1) * 128, :])
        b.dma("gpsimd", WoT[i][:], w_outT[i * 128:(i + 1) * 128, :])
        b.dma("sync", (cw[:, 4 * i:4 * i + 4], i), conv_w[i * 128:(i + 1) * 128, :], "q_cw")
        b.dma("sync", (cb[:, i:i + 1], i), conv_b[i * 128:(i + 1) * 128, :], "q_cb")
    b.dma("sync", dtwb_s[:], dtwb[:])
    b.dma("sync", a_rep_s[:], a_rep[:])
    b.dma("sync", Rsel[:], r_sel[:])
    b.dma("gpsimd", Ssel[:], s_sel[:])
    b.dma("sync", Rb16[:], rb16[:])

    b.auto("vector", lambda e: e.memset(ones_b[:], 1.0), wr=[ones_b])
    b.auto("vector", lambda e: e.memset(ones_f1[:], 1.0), wr=[ones_f1])
    b.auto("vector", lambda e: e.memset(s1row[:], 1.0), wr=[s1row])
    b.auto("scalar", lambda e: e.activation(a_rep_s[:], a_rep_s[:], AF.Exp),
           rd=[a_rep_s], wr=[a_rep_s])
    b.auto("vector", lambda e: e.tensor_single_scalar(a_rep_s[:], a_rep_s[:], -1.0, ALU.mult),
           rd=[a_rep_s], wr=[a_rep_s])

    for bi in range(B_PER_CORE):
        b.auto("vector", lambda e: e.memset(marg[:], 0.0), wr=[marg])
        b.auto("vector", lambda e: e.memset(carry[:], 0.0), wr=[carry])
        for c in range(NCH):
            t0 = c * TC
            xsl = slice(t0, t0 + TC)
            for i in range(NDT):
                b.dma("sync", x_t[i][:], x_in[bi, i * 128:(i + 1) * 128, xsl])

            for i in range(NDT):
                b.auto("scalar", lambda e, i=i: e.copy(xb_t[i][:], x_t[i][:]),
                       rd=[x_t[i]], wr=[xb_t[i]])
                b.auto("scalar", lambda e, i=i: e.activation(xq_t[i][:], x_t[i][:], AF.Square),
                       rd=[x_t[i]], wr=[xq_t[i]])

            b.mm_group(P[6][0:1, :], [(ones_b[:], xb_t[i][:]) for i in range(NDT)])
            b.mm_group(P[7][0:1, :], [(ones_b[:], xq_t[i][:]) for i in range(NDT)])

            b.auto("vector", lambda e: e.tensor_single_scalar(row_m[:], P[6][0:1, :], 1.0 / D, ALU.mult),
                   rd=[P[6]], wr=[row_m])
            b.auto("vector", lambda e: e.tensor_single_scalar(row_v[:], P[7][0:1, :], 1.0 / D, ALU.mult),
                   rd=[P[7]], wr=[row_v])
            b.auto("vector", lambda e: e.tensor_mul(row_r[:], row_m[:], row_m[:]),
                   rd=[row_m], wr=[row_r])
            b.auto("vector", lambda e: e.scalar_tensor_tensor(
                row_v[:], row_v[:], 1e-5, row_r[:], ALU.add, ALU.subtract),
                rd=[row_v, row_r], wr=[row_v])
            b.auto("scalar", lambda e: e.activation(row_r[:], row_v[:], AF.Ln),
                   rd=[row_v], wr=[row_r])
            b.auto("scalar", lambda e: e.activation(row_r[:], row_r[:], AF.Exp, scale=-0.5),
                   rd=[row_r], wr=[row_r])
            b.auto("vector", lambda e: e.tensor_mul(row_mr[:], row_m[:], row_r[:]),
                   rd=[row_m, row_r], wr=[row_mr])

            b.mm_group(P[6][:], [(ones_f1[:], row_r[:])])
            b.mm_group(P[7][:], [(ones_f1[:], row_mr[:])])

            for i in range(NDT):
                b.auto("vector", lambda e, i=i: e.scalar_tensor_tensor(
                    sp_a[:], x_t[i][:], 1.0, P[6][:], ALU.mult, ALU.mult),
                    rd=[x_t[i], P[6]], wr=[sp_a])
                b.auto("vector", lambda e, i=i: e.tensor_sub(xn_t[i][:], sp_a[:], P[7][:]),
                       rd=[sp_a, P[7]], wr=[xn_t[i]])

            for m in range(16):
                slot = m % 2
                b.mm_group(P[slot][:],
                           [(WinT[k][:, m * 128:(m + 1) * 128], xn_t[k][:]) for k in range(NDT)])
                if m < 8:
                    b.auto("scalar", lambda e, m=m, slot=slot: e.copy(
                        xin_t[m][:, 3:TC + 3], P[slot][:]), rd=[P[slot]], wr=[xin_t[m]])
                else:
                    b.auto("scalar", lambda e, m=m, slot=slot: e.activation(
                        sz_t[m - 8][:], P[slot][:], AF.Silu), rd=[P[slot]], wr=[sz_t[m - 8]])

            for i in range(NDT):
                b.auto("vector", lambda e, i=i: e.tensor_copy(
                    xin_t[i][:, 0:3], marg[:, 3 * i:3 * i + 3]), rd=[marg], wr=[xin_t[i]])
            for i in range(NDT):
                b.auto("vector", lambda e, i=i: e.tensor_single_scalar(
                    xc_t[i][:], xin_t[i][:, 0:TC], cw[:, 4 * i:4 * i + 1], ALU.mult),
                    rd=[xin_t[i], cw], wr=[xc_t[i]])
                for k in range(1, 4):
                    b.auto("vector", lambda e, i=i, k=k: e.scalar_tensor_tensor(
                        xc_t[i][:], xin_t[i][:, k:TC + k], cw[:, 4 * i + k:4 * i + k + 1],
                        xc_t[i][:], ALU.mult, ALU.add),
                        rd=[xin_t[i], cw, xc_t[i]], wr=[xc_t[i]])
            for i in range(NDT):
                b.auto("vector", lambda e, i=i: e.tensor_copy(
                    marg[:, 3 * i:3 * i + 3], xin_t[i][:, TC:TC + 3]),
                    rd=[xin_t[i]], wr=[marg])

            for i in range(NDT):
                b.auto("scalar", lambda e, i=i: e.activation(
                    xc_t[i][:], xc_t[i][:], AF.Silu, bias=cb[:, i:i + 1]),
                    rd=[xc_t[i], cb], wr=[xc_t[i]])
                b.auto("scalar", lambda e, i=i: e.copy(xcb_t[i][:], xc_t[i][:]),
                       rd=[xc_t[i]], wr=[xcb_t[i]])

            for (slot, c0, c1, dst) in ((0, 0, 16, brow), (1, 16, 32, crow), (2, 32, 33, None)):
                mm_rows = c1 - c0
                b.mm_group(P[slot][0:mm_rows, :],
                           [(XpT[k][:, c0:c1], xcb_t[k][:]) for k in range(NDT)])
                if dst is not None:
                    b.auto("scalar", lambda e, dst=dst, mm_rows=mm_rows, slot=slot:
                           e.copy(dst[:], P[slot][0:mm_rows, :]), rd=[P[slot]], wr=[dst])
                else:
                    b.auto("vector", lambda e: e.tensor_copy(s1row[0:1, :], P[2][0:1, :]),
                           rd=[P[2]], wr=[s1row])

            for i in range(NDT):
                slot = 3 + (i % 2)
                b.mm_group(P[slot][:], [(dtwb_s[:, i * 128:(i + 1) * 128], s1row[:])])
                b.auto("scalar", lambda e, slot=slot: e.activation(sp_a[:], P[slot][:], AF.Abs),
                       rd=[P[slot]], wr=[sp_a])
                b.auto("scalar", lambda e: e.activation(sp_b[:], sp_a[:], AF.Exp, scale=-1.0),
                       rd=[sp_a], wr=[sp_b])
                b.auto("scalar", lambda e: e.activation(sp_b[:], sp_b[:], AF.Ln, bias=1.0),
                       rd=[sp_b], wr=[sp_b])
                b.auto("scalar", lambda e, slot=slot, i=i: e.activation(
                    dt_t[i][:], P[slot][:], AF.Relu), rd=[P[slot]], wr=[dt_t[i]])
                b.auto("vector", lambda e, i=i: e.tensor_add(dt_t[i][:], dt_t[i][:], sp_b[:]),
                       rd=[dt_t[i], sp_b], wr=[dt_t[i]])
                b.auto("vector", lambda e, i=i: e.tensor_mul(xc_t[i][:], dt_t[i][:], xc_t[i][:]),
                       rd=[dt_t[i], xc_t[i]], wr=[xc_t[i]])

            b.mm_group(P[5][:], [(Rb16[:], brow[:])])
            b.auto("scalar", lambda e: e.copy(brep[:], P[5][:]), rd=[P[5]], wr=[brep])
            b.mm_group(P[6][:], [(Rb16[:], crow[:])])
            b.auto("scalar", lambda e: e.copy(crep[:], P[6][:]), rd=[P[6]], wr=[crep])

            for db in range(NDT):
                for k in range(16):
                    tau = db * 16 + k
                    j = k % 2
                    sA, sU = j, 2 + j
                    ksl = slice(k * 128, (k + 1) * 128)
                    b.mm_group(P[sA][:], [(Rsel[:, ksl], dt_t[db][:])])
                    b.mm_group(P[sU][:], [(Rsel[:, ksl], xc_t[db][:])])
                    b.auto("scalar", lambda e, j=j, sA=sA, tau=tau: e.activation(
                        dA[j][:], P[sA][:], AF.Exp, scale=a_rep_s[:, tau:tau + 1]),
                        rd=[P[sA], a_rep_s], wr=[dA[j]])
                    b.auto("vector", lambda e, j=j, sU=sU: e.tensor_tensor(
                        dBx[j][:], P[sU][:], brep[:], ALU.mult),
                        rd=[P[sU], brep], wr=[dBx[j]])
                    b.auto("vector", lambda e, j=j, tau=tau: e.tensor_tensor_scan(
                        hbuf[j][:], dA[j][:], dBx[j][:], carry[:, tau:tau + 1],
                        ALU.mult, ALU.add),
                        rd=[dA[j], dBx[j], carry], wr=[hbuf[j]])
                    b.auto("vector", lambda e, j=j, tau=tau: e.tensor_copy(
                        carry[:, tau:tau + 1], hbuf[j][:, TC - 1:TC]),
                        rd=[hbuf[j]], wr=[carry])
                    b.auto("vector", lambda e, j=j: e.tensor_tensor(
                        hC[j][:], hbuf[j][:], crep[:], ALU.mult),
                        rd=[hbuf[j], crep], wr=[hC[j]])
                    nfirst = (k == 0)
                    nlast = (k == 15)
                    b.auto(
                        "tensor",
                        (lambda e, _ksl=ksl, _j=j, _st=nfirst, _sp=nlast:
                         e.matmul(P[4][:], Ssel[:, _ksl], hC[_j][:], start=_st, stop=_sp)),
                        rd=[Ssel, hC[j]], wr=([P[4]] if (nfirst or nlast) else []))
                b.auto("vector", lambda e, db=db: e.tensor_tensor(
                    ym_t[db][:], P[4][:], sz_t[db][:], ALU.mult),
                    rd=[P[4], sz_t[db]], wr=[ym_t[db]])

            for m in range(NDT):
                slot = 5 + (m % 2)
                b.mm_group(P[slot][:],
                           [(WoT[k][:, m * 128:(m + 1) * 128], ym_t[k][:]) for k in range(NDT)])
                b.auto("vector", lambda e, m=m, slot=slot: e.tensor_tensor(
                    x_t[m][:], P[slot][:], x_t[m][:], ALU.add),
                    rd=[P[slot], x_t[m]], wr=[x_t[m]])
                b.dma("sync", (y_out[bi, m * 128:(m + 1) * 128, xsl], f"{bi}.{c}.{m}"),
                      x_t[m][:])


def build_phase_A():
    nc = bass.Bass("TRN2", target_bir_lowering=False)
    b = Bld(nc)
    emit_A(b, _mk_ext(nc, A_EXT))
    return b.finalize(_st_waits(b))



def emit_preln(b, ext):
    x_in = ext["x"]
    y_out = ext["y"]
    ones_b = b.sb("ones_b", [128, 1], BF16)
    ones_f1 = b.sb("ones_f1", [1, 128], F32)
    x_t = [b.sb(f"x_t{i}", [128, TC]) for i in range(NDT)]
    xb_t = [b.sb(f"xb_t{i}", [128, TC], BF16) for i in range(NDT)]
    sq_t = [b.sb(f"sq_t{i}", [128, TC], BF16) for i in range(NDT)]
    t1 = b.sb("t1", [128, TC])
    row_m = b.sb("row_m", [1, TC])
    row_v = b.sb("row_v", [1, TC])
    row_r = b.sb("row_r", [1, TC])
    row_mr = b.sb("row_mr", [1, TC])
    P = [b.ps(f"P{i}", [128, TC]) for i in range(2)]
    b.auto("vector", lambda e: e.memset(ones_b[:], 1.0), wr=[ones_b])
    b.auto("vector", lambda e: e.memset(ones_f1[:], 1.0), wr=[ones_f1])
    for bi in range(B_PER_CORE):
        for c in range(NCH):
            xsl = slice(c * TC, (c + 1) * TC)
            for i in range(NDT):
                b.dma("sync", x_t[i][:], x_in[bi, i * 128:(i + 1) * 128, xsl])
            for i in range(NDT):
                b.auto("scalar", lambda e, i=i: e.copy(xb_t[i][:], x_t[i][:]),
                       rd=[x_t[i]], wr=[xb_t[i]])
                b.auto("scalar", lambda e, i=i: e.activation(sq_t[i][:], x_t[i][:], AF.Square),
                       rd=[x_t[i]], wr=[sq_t[i]])
            b.mm_group(P[0][0:1, :], [(ones_b[:], xb_t[i][:]) for i in range(NDT)])
            b.mm_group(P[1][0:1, :], [(ones_b[:], sq_t[i][:]) for i in range(NDT)])
            b.auto("vector", lambda e: e.tensor_single_scalar(row_m[:], P[0][0:1, :], 1.0 / D, ALU.mult),
                   rd=[P[0]], wr=[row_m])
            b.auto("vector", lambda e: e.tensor_single_scalar(row_v[:], P[1][0:1, :], 1.0 / D, ALU.mult),
                   rd=[P[1]], wr=[row_v])
            b.auto("vector", lambda e: e.tensor_mul(row_r[:], row_m[:], row_m[:]),
                   rd=[row_m], wr=[row_r])
            b.auto("vector", lambda e: e.scalar_tensor_tensor(
                row_v[:], row_v[:], 1e-5, row_r[:], ALU.add, ALU.subtract),
                rd=[row_v, row_r], wr=[row_v])
            b.auto("scalar", lambda e: e.activation(row_r[:], row_v[:], AF.Ln),
                   rd=[row_v], wr=[row_r])
            b.auto("scalar", lambda e: e.activation(row_r[:], row_r[:], AF.Exp, scale=-0.5),
                   rd=[row_r], wr=[row_r])
            b.auto("vector", lambda e: e.tensor_mul(row_mr[:], row_m[:], row_r[:]),
                   rd=[row_m, row_r], wr=[row_mr])
            b.mm_group(P[0][:], [(ones_f1[:], row_r[:])])
            b.mm_group(P[1][:], [(ones_f1[:], row_mr[:])])
            for i in range(NDT):
                b.auto("vector", lambda e, i=i: e.scalar_tensor_tensor(
                    t1[:], x_t[i][:], 1.0, P[0][:], ALU.mult, ALU.mult),
                    rd=[x_t[i], P[0]], wr=[t1])
                b.auto("vector", lambda e, i=i: e.tensor_sub(x_t[i][:], t1[:], P[1][:]),
                       rd=[t1, P[1]], wr=[x_t[i]])
                b.dma("sync", (y_out[bi, i * 128:(i + 1) * 128, xsl], f"{bi}.{c}.{i}"),
                      x_t[i][:])


def build_phase_preln():
    nc = bass.Bass("TRN2", target_bir_lowering=False)
    b = Bld(nc)
    emit_preln(b, _mk_ext(nc, {}))
    return b.finalize(_st_waits(b))



def emit_addres(b, ext):
    x_in = ext["x"]
    xr_in = ext["xr"]
    y_out = ext["y"]
    a_t = [b.sb(f"a_t{i}", [128, T]) for i in range(NDT)]
    r_t = [b.sb(f"r_t{i}", [128, T]) for i in range(NDT)]
    for bi in range(B_PER_CORE):
        for i in range(NDT):
            rsl = slice(i * 128, (i + 1) * 128)
            b.dma("sync", (a_t[i][:], bi), x_in[bi, rsl, :])
            b.dma("gpsimd", (r_t[i][:], bi), xr_in[bi, rsl, :])
            b.auto("vector", lambda e, i=i: e.tensor_add(a_t[i][:], a_t[i][:], r_t[i][:]),
                   rd=[(a_t[i], bi), (r_t[i], bi), a_t[i], r_t[i]], wr=[a_t[i]])
            b.dma("sync", (y_out[bi, rsl, :], f"{bi}.{i}"), a_t[i][:])


def build_phase_addres():
    nc = bass.Bass("TRN2", target_bir_lowering=False)
    b = Bld(nc)
    ext = _mk_ext(nc, {})
    ext["xr"] = nc.dram_tensor("xr", [B_PER_CORE, D, T], F32, kind="ExternalInput")
    emit_addres(b, ext)
    return b.finalize(_st_waits(b))



HEADS = 16
DH = 64
WIN = 256
PAIR = 512
NPAIR = T // PAIR
BAND = 384


BA_EXT = {"wq": [D, D], "wqr": [D, D], "wk": [D, D], "wkr": [D, D], "wv": [D, D],
          "wo": [D, D], "qsc": [128, 2], "ksc": [128, 2]}
BSH_EXT = {"csq": [128, T], "ssq": [128, T], "csk": [128, T], "ssk": [128, T],
           "mask3": [128, 3 * BAND], "bsel": [128, 128], "psel": [128, 2],
           "pselT": [2, 128], "ident": [128, 128]}


def emit_B_attn(b, ext):
    x_in = ext["x"]
    wq = ext["wq"]
    wqr = ext["wqr"]
    wk = ext["wk"]
    wkr = ext["wkr"]
    wv = ext["wv"]
    wo = ext["wo"]
    qsc = ext["qsc"]
    ksc = ext["ksc"]
    csq_d = ext["csq"]
    ssq_d = ext["ssq"]
    csk_d = ext["csk"]
    ssk_d = ext["ssk"]
    mask_d = ext["mask3"]
    bsel_d = ext["bsel"]
    psel_d = ext["psel"]
    pselT_d = ext["pselT"]
    ident_d = ext["ident"]
    y_out = ext["y"]

    WqT = [b.sb(f"WqT{i}", [128, D], BF16) for i in range(NDT)]
    WqrT = [b.sb(f"WqrT{i}", [128, D], BF16) for i in range(NDT)]
    WkT = [b.sb(f"WkT{i}", [128, D], BF16) for i in range(NDT)]
    WkrT = [b.sb(f"WkrT{i}", [128, D], BF16) for i in range(NDT)]
    WvT = [b.sb(f"WvT{i}", [128, D], BF16) for i in range(NDT)]
    WoT = [b.sb(f"WoT{i}", [128, D], BF16) for i in range(NDT)]
    qs_s = b.sb("qs_s", [128, 2])
    ks_s = b.sb("ks_s", [128, 2])
    csq = b.sb("csq_s", [128, T], BF16)
    ssq = b.sb("ssq_s", [128, T], BF16)
    csk = b.sb("csk_s", [128, T], BF16)
    ssk = b.sb("ssk_s", [128, T], BF16)
    mask3 = b.sb("mask3_s", [128, 3 * BAND], BF16)
    bsel = b.sb("bsel_s", [128, 128], BF16)
    psel = b.sb("psel_s", [128, 2], BF16)
    pselT = b.sb("pselT_s", [2, 128])
    ident = b.sb("ident_s", [128, 128], BF16)

    xw = [b.sb(f"xw{i}", [128, PAIR + 1]) for i in range(NDT)]
    xs = [b.sb(f"xs{i}", [128, PAIR], BF16) for i in range(NDT)]
    Kc = [b.sb(f"Kc{i}", [128, 768], BF16) for i in range(NDT)]
    Vc = [b.sb(f"Vc{i}", [128, D], BF16) for i in range(6)]
    Qp = [b.sb(f"Qp{i}", [128, PAIR], BF16) for i in range(NDT)]
    sqq = [b.sb(f"sqq{i}", [128, PAIR], BF16) for i in range(NDT)]
    sqk = b.sb("sqk", [128, PAIR], BF16)
    ao = xs
    rt1 = b.sb("rt1", [128, PAIR])
    rt2 = b.sb("rt2", [128, PAIR])
    rnk = b.sb("rnk", [2, PAIR])
    rsq8 = [b.sb(f"rsq8{i}", [128, HEADS]) for i in range(4)]
    nrm_a = b.sb("nrm_a", [2, PAIR])
    atb2 = [b.sb(f"atb{j}", [128, BAND], BF16) for j in range(2)]
    attnT2 = [b.sb(f"attnT{j}", [128, BAND], BF16) for j in range(2)]
    zrow2 = [b.sb(f"zrow{j}", [128, 1]) for j in range(2)]
    rz2 = [b.sb(f"rz{j}", [128, 1]) for j in range(2)]
    ngm2 = [b.sb(f"ngm{j}", [128, 1]) for j in range(2)]
    bia2 = [b.sb(f"bia{j}", [128, 1]) for j in range(2)]

    P = [b.ps(f"P{i}", [128, PAIR]) for i in range(6)]
    PTX = b.ps("PTX", [128, 2 * BAND], BF16)
    PVX = b.ps("PVX", [128, 256])
    Psc = [P[5], P[4]]

    for i in range(NDT):
        rsl = slice(i * 128, (i + 1) * 128)
        b.dma("gpsimd", WqT[i][:], wq[rsl, :])
        b.dma("gpsimd", WqrT[i][:], wqr[rsl, :])
        b.dma("gpsimd", WkT[i][:], wk[rsl, :])
        b.dma("gpsimd", WkrT[i][:], wkr[rsl, :])
        b.dma("gpsimd", WvT[i][:], wv[rsl, :])
        b.dma("gpsimd", WoT[i][:], wo[rsl, :])
    b.dma("sync", qs_s[:], qsc[:])
    b.dma("sync", ks_s[:], ksc[:])
    b.dma("gpsimd", csq[:], csq_d[:])
    b.dma("gpsimd", ssq[:], ssq_d[:])
    b.dma("gpsimd", csk[:], csk_d[:])
    b.dma("gpsimd", ssk[:], ssk_d[:])
    b.dma("gpsimd", mask3[:], mask_d[:])
    b.dma("gpsimd", bsel[:], bsel_d[:])
    b.dma("gpsimd", psel[:], psel_d[:])
    b.dma("sync", pselT[:], pselT_d[:])
    b.dma("gpsimd", ident[:], ident_d[:])

    for bi in range(B_PER_CORE):
        for p in range(NPAIR):
            t0 = p * PAIR
            for i in range(NDT):
                rsl = slice(i * 128, (i + 1) * 128)
                if p == 0:
                    b.auto("vector", lambda e, i=i: e.memset(xw[i][:, 0:1], 0.0),
                           wr=[(xw[i], "m")])
                    b.dma("sync", (xw[i][:, 1:PAIR + 1], "b"), x_in[bi, rsl, 0:PAIR])
                else:
                    b.dma("sync", xw[i][:], x_in[bi, rsl, t0 - 1:t0 + PAIR])
            for i in range(NDT):
                src = (xw[i][:, 1:PAIR + 1]) if i < 4 else (xw[i][:, 0:PAIR])
                b.auto("scalar", lambda e, i=i, src=src: e.copy(xs[i][:], src),
                       rd=[xw[i], (xw[i], "m"), (xw[i], "b")], wr=[xs[i]])
            if p == 0:
                for i in range(NDT):
                    b.auto("vector", lambda e, i=i: e.memset(Kc[i][:, 0:256], 0.0),
                           rd=[], wr=[(Kc[i], "prev")])
                for i in range(2):
                    b.auto("vector", lambda e, i=i: e.memset(Vc[i][:], 0.0), wr=[Vc[i]])
            else:
                for i in range(NDT):
                    b.auto("vector", lambda e, i=i: e.tensor_copy(
                        Kc[i][:, 0:256], Kc[i][:, 512:768]),
                        rd=[(Kc[i], "cur")], wr=[(Kc[i], "prev")])
                for i in range(2):
                    b.auto("vector", lambda e, i=i: e.tensor_copy(Vc[i][:], Vc[i + 4][:]),
                           rd=[Vc[i + 4]], wr=[Vc[i]])
            for tt in range(4):
                tsl = slice(tt * 128, (tt + 1) * 128)
                for ch in range(2):
                    b.mm_group(P[ch][:, 0:512],
                               [(xs[k][:, tsl], WvT[k][:, ch * 512:(ch + 1) * 512])
                                for k in range(NDT)])
                    b.auto("vector", lambda e, tt=tt, ch=ch: e.tensor_copy(
                        Vc[2 + tt][:, ch * 512:(ch + 1) * 512], P[ch][:, 0:512]),
                        rd=[P[ch]], wr=[(Vc[2 + tt], ch)])
            for m in range(NDT):
                msl = slice(m * 128, (m + 1) * 128)
                b.mm_group(P[0][:], [(WqT[k][:, msl], xs[k][:]) for k in range(NDT)])
                b.mm_group(P[1][:], [(WqrT[k][:, msl], xs[k][:]) for k in range(NDT)])
                b.mm_group(P[2][:], [(WkT[k][:, msl], xs[k][:]) for k in range(NDT)])
                b.mm_group(P[3][:], [(WkrT[k][:, msl], xs[k][:]) for k in range(NDT)])
                b.auto("scalar", lambda e, m=m: e.activation(sqq[m][:], P[0][:], AF.Square),
                       rd=[P[0]], wr=[sqq[m]])
                b.auto("scalar", lambda e: e.activation(sqk[:], P[2][:], AF.Square),
                       rd=[P[2]], wr=[sqk])
                b.mm_group(P[4][0:2, :], [(psel[:], sqk[:])])
                b.auto("scalar", lambda e: e.activation(nrm_a[:], P[4][0:2, :], AF.Ln),
                       rd=[P[4]], wr=[nrm_a])
                b.auto("scalar", lambda e: e.activation(nrm_a[:], nrm_a[:], AF.Exp, scale=0.5),
                       rd=[nrm_a], wr=[nrm_a])
                b.auto("vector", lambda e: e.tensor_single_scalar(nrm_a[:], nrm_a[:], 1e-12, ALU.max),
                       rd=[nrm_a], wr=[nrm_a])
                b.auto("vector", lambda e: e.reciprocal(rnk[:], nrm_a[:]),
                       rd=[nrm_a], wr=[rnk])
                b.mm_group(P[4][:], [(pselT[:], rnk[:])])
                tq = slice(t0, t0 + PAIR)
                b.auto("vector", lambda e, m=m, tq=tq: e.scalar_tensor_tensor(
                    rt1[:], P[0][:], qs_s[:, 0:1], csq[:, tq], ALU.mult, ALU.mult),
                    rd=[P[0], qs_s, csq], wr=[rt1])
                b.auto("vector", lambda e, m=m, tq=tq: e.scalar_tensor_tensor(
                    rt2[:], P[1][:], qs_s[:, 1:2], ssq[:, tq], ALU.mult, ALU.mult),
                    rd=[P[1], qs_s, ssq], wr=[rt2])
                b.auto("vector", lambda e, m=m: e.tensor_add(Qp[m][:], rt1[:], rt2[:]),
                       rd=[rt1, rt2], wr=[Qp[m]])
                b.auto("vector", lambda e, m=m, tq=tq: e.scalar_tensor_tensor(
                    rt1[:], P[2][:], ks_s[:, 0:1], csk[:, tq], ALU.mult, ALU.mult),
                    rd=[P[2], ks_s, csk], wr=[rt1])
                b.auto("vector", lambda e, m=m, tq=tq: e.scalar_tensor_tensor(
                    rt2[:], P[3][:], ks_s[:, 1:2], ssk[:, tq], ALU.mult, ALU.mult),
                    rd=[P[3], ks_s, ssk], wr=[rt2])
                b.auto("vector", lambda e: e.tensor_add(rt1[:], rt1[:], rt2[:]),
                       rd=[rt1, rt2], wr=[rt1])
                b.auto("vector", lambda e, m=m: e.tensor_tensor(
                    Kc[m][:, 256:768], rt1[:], P[4][:], ALU.mult),
                    rd=[rt1, P[4]], wr=[(Kc[m], "cur")])
            for tt in range(4):
                tsl = slice(tt * 128, (tt + 1) * 128)
                b.mm_group(P[5][:, 0:16],
                           [(sqq[m][:, tsl], bsel[:, m * 16:(m + 1) * 16]) for m in range(NDT)])
                b.auto("scalar", lambda e, tt=tt: e.activation(
                    rsq8[tt][:], P[5][:, 0:16], AF.Ln), rd=[P[5]], wr=[rsq8[tt]])
                b.auto("scalar", lambda e, tt=tt: e.activation(
                    rsq8[tt][:], rsq8[tt][:], AF.Exp, scale=0.5), rd=[rsq8[tt]], wr=[rsq8[tt]])
                b.auto("vector", lambda e, tt=tt: e.tensor_single_scalar(
                    rsq8[tt][:], rsq8[tt][:], 1e-12, ALU.max), rd=[rsq8[tt]], wr=[rsq8[tt]])
                b.auto("vector", lambda e, tt=tt: e.reciprocal(rsq8[tt][:], rsq8[tt][:]),
                       rd=[rsq8[tt]], wr=[rsq8[tt]])
                b.auto("vector", lambda e, tt=tt: e.tensor_single_scalar(
                    rsq8[tt][:], rsq8[tt][:], 8.0, ALU.mult), rd=[rsq8[tt]], wr=[rsq8[tt]])
            iters = [(wloc, qt, h)
                     for wloc in range(2) for qt in range(2) for h in range(HEADS)]

            def score_mm(idx):
                wloc, qt, h = iters[idx]
                m = h // 2
                hr = slice((h % 2) * 64, (h % 2) * 64 + 64)
                tt = 2 * wloc + qt
                qs_l = slice(tt * 128, (tt + 1) * 128)
                koff = 256 * wloc + 128 * qt
                w_g = 2 * p + wloc
                mv = 0 if w_g > 0 else (1 + qt)
                msl = slice(mv * BAND, (mv + 1) * BAND)
                b.mm_group(Psc[idx % 2][:, 0:BAND],
                           [(Qp[m][hr, qs_l], Kc[m][hr, koff:koff + BAND]),
                            (ident[:], mask3[:, msl])])

            score_mm(0)
            for idx in range(len(iters)):
                wloc, qt, h = iters[idx]
                m = h // 2
                hr = slice((h % 2) * 64, (h % 2) * 64 + 64)
                tt = 2 * wloc + qt
                sl = idx % 2
                psl = (idx // 2) % 2
                if idx + 1 < len(iters):
                    score_mm(idx + 1)
                b.auto("vector", lambda e, sl=sl: e.reduce_max(
                    ngm2[sl][:], Psc[sl][:, 0:BAND], axis=AX.X, negate=True),
                    rd=[Psc[sl]], wr=[ngm2[sl]])
                b.auto("vector", lambda e, sl=sl, tt=tt, h=h: e.tensor_mul(
                    bia2[sl][:], ngm2[sl][:], rsq8[tt][:, h:h + 1]),
                    rd=[ngm2[sl], rsq8[tt]], wr=[bia2[sl]])
                b.auto("scalar", lambda e, sl=sl, tt=tt, h=h: e.activation(
                    atb2[sl][:], Psc[sl][:, 0:BAND], AF.Exp,
                    bias=bia2[sl][:], scale=rsq8[tt][:, h:h + 1],
                    accum_out=zrow2[sl][:]),
                    rd=[Psc[sl], bia2[sl], rsq8[tt]], wr=[atb2[sl], zrow2[sl]])
                b.auto("vector", lambda e, sl=sl: e.reciprocal(rz2[sl][:], zrow2[sl][:]),
                       rd=[zrow2[sl]], wr=[rz2[sl]])
                b.auto("vector", lambda e, sl=sl: e.tensor_single_scalar(
                    atb2[sl][:], atb2[sl][:], rz2[sl][:], ALU.mult),
                    rd=[atb2[sl], rz2[sl]], wr=[atb2[sl]])
                for j in range(3):
                    b.auto("tensor", lambda e, sl=sl, j=j: e.transpose(
                        PTX[:, sl * BAND + j * 128:sl * BAND + (j + 1) * 128],
                        atb2[sl][:, j * 128:(j + 1) * 128], ident[:]),
                        rd=[atb2[sl], ident], wr=[(PTX, f"{sl}.{j}")])
                b.auto("scalar", lambda e, sl=sl: e.copy(
                    attnT2[sl][:], PTX[:, sl * BAND:(sl + 1) * BAND]),
                    rd=[(PTX, f"{sl}.0"), (PTX, f"{sl}.1"), (PTX, f"{sl}.2")],
                    wr=[attnT2[sl]])
                vbase = 2 * wloc + qt
                b.mm_group((PVX[hr, psl * 128:(psl + 1) * 128], psl),
                           [(Vc[vbase + j][:, h * 64:(h + 1) * 64],
                             attnT2[sl][:, j * 128:(j + 1) * 128]) for j in range(3)])
                if h % 2 == 1:
                    b.auto("scalar", lambda e, m=m, tt=tt, psl=psl: e.copy(
                        ao[m][:, tt * 128:(tt + 1) * 128],
                        PVX[:, psl * 128:(psl + 1) * 128]),
                        rd=[(PVX, psl)], wr=[(ao[m], tt)])
            for mo in range(NDT):
                msl = slice(mo * 128, (mo + 1) * 128)
                b.mm_group(P[mo % 2][:],
                           [(WoT[k][:, msl], ao[k][:]) for k in range(NDT)])
                b.auto("vector", lambda e, mo=mo: e.tensor_tensor(
                    xw[mo][:, 1:PAIR + 1], P[mo % 2][:], xw[mo][:, 1:PAIR + 1], ALU.add),
                    rd=[P[mo % 2], xw[mo], (xw[mo], "b")], wr=[(xw[mo], "b"), xw[mo]])
                b.dma("sync", (y_out[bi, msl, t0:t0 + PAIR], f"{bi}.{p}.{mo}"),
                      xw[mo][:, 1:PAIR + 1])


def build_phase_B_attn():
    nc = bass.Bass("TRN2", target_bir_lowering=False)
    b = Bld(nc)
    emit_B_attn(b, _mk_ext(nc, {**BA_EXT, **BSH_EXT}))
    return b.finalize(_st_waits(b))



TC2 = 512
DFF = 2730
NJ = 22


BF_EXT = {"w1": [D, 2 * DFF], "w2": [DFF, D], "lng": [128, NDT], "lnb": [128, NDT],
          "b1a": [128, NJ], "b1g": [128, NJ], "b2": [128, NDT]}


def emit_B_ffn(b, ext):
    x_in = ext["x"]
    w1_d = ext["w1"]
    w2_d = ext["w2"]
    g_d = ext["lng"]
    bb_d = ext["lnb"]
    b1a_d = ext["b1a"]
    b1g_d = ext["b1g"]
    b2_d = ext["b2"]
    y_out = ext["y"]

    W1 = [b.sb(f"W1_{i}", [128, 2 * DFF], BF16) for i in range(NDT)]
    W2 = [b.sb(f"W2_{j}", [128 if j < NJ - 1 else 42, D], BF16) for j in range(NJ)]
    g_s = b.sb("g_s", [128, NDT])
    bb_s = b.sb("bb_s", [128, NDT])
    b1a_s = b.sb("b1a_s", [128, NJ])
    b1g_s = b.sb("b1g_s", [128, NJ])
    b2_s = b.sb("b2_s", [128, NDT])
    ones_b = b.sb("ones_b", [128, 1], BF16)
    ones_f1 = b.sb("ones_f1", [1, 128])

    xw = [b.sb(f"xw{i}", [128, TC2 + 1]) for i in range(NDT)]
    xs_b = [b.sb(f"xsb{i}", [128, TC2], BF16) for i in range(NDT)]
    sq_b = [b.sb(f"sqb{i}", [128, TC2], BF16) for i in range(NDT)]
    xn = [b.sb(f"xn{i}", [128, TC2], BF16) for i in range(NDT)]
    hjs = [b.sb(f"hj{j}", [128, TC2], BF16) for j in range(NJ)]
    gel = b.sb("gel", [128, TC2])
    row_m = b.sb("row_m", [1, TC2])
    row_v = b.sb("row_v", [1, TC2])
    row_r = b.sb("row_r", [1, TC2])
    row_mr = b.sb("row_mr", [1, TC2])
    t1 = b.sb("t1", [128, TC2])

    PW = [b.ps(f"PW{i}", [128, TC2]) for i in range(2)]
    PA = b.ps("PA", [128, TC2])
    PG = b.ps("PG", [128, TC2])
    PB1 = b.ps("PB1", [128, TC2])
    PB2 = b.ps("PB2", [128, TC2])
    PS = PB1[0:1, :]
    PQ = PB2[0:1, :]
    PR = PB1[:]
    PMR = PB2[:]

    for i in range(NDT):
        b.dma("gpsimd", W1[i][:], w1_d[i * 128:(i + 1) * 128, :])
    for j in range(NJ):
        r0 = j * 128
        r1 = min(DFF, r0 + 128)
        b.dma("gpsimd", W2[j][:], w2_d[r0:r1, :])
    b.dma("sync", g_s[:], g_d[:])
    b.dma("sync", bb_s[:], bb_d[:])
    b.dma("sync", b1a_s[:], b1a_d[:])
    b.dma("sync", b1g_s[:], b1g_d[:])
    b.dma("sync", b2_s[:], b2_d[:])
    b.auto("vector", lambda e: e.memset(ones_b[:], 1.0), wr=[ones_b])
    b.auto("vector", lambda e: e.memset(ones_f1[:], 1.0), wr=[ones_f1])

    for bi in range(B_PER_CORE):
        for c in range(T // TC2):
            t0 = c * TC2
            for i in range(NDT):
                rsl = slice(i * 128, (i + 1) * 128)
                if c == 0:
                    b.auto("vector", lambda e, i=i: e.memset(xw[i][:, 0:1], 0.0),
                           wr=[(xw[i], "m")])
                    b.dma("sync", (xw[i][:, 1:TC2 + 1], "b"), x_in[bi, rsl, 0:TC2])
                else:
                    b.dma("sync", xw[i][:], x_in[bi, rsl, t0 - 1:t0 + TC2])
            for i in range(NDT):
                src = (xw[i][:, 1:TC2 + 1]) if i < 4 else (xw[i][:, 0:TC2])
                b.auto("scalar", lambda e, i=i, src=src: e.copy(xs_b[i][:], src),
                       rd=[xw[i], (xw[i], "m"), (xw[i], "b")], wr=[xs_b[i]])
                b.auto("scalar", lambda e, i=i, src=src: e.activation(sq_b[i][:], src, AF.Square),
                       rd=[xw[i], (xw[i], "m"), (xw[i], "b")], wr=[sq_b[i]])
            b.mm_group(PS, [(ones_b[:], xs_b[i][:]) for i in range(NDT)])
            b.mm_group(PQ, [(ones_b[:], sq_b[i][:]) for i in range(NDT)])
            b.auto("vector", lambda e: e.tensor_single_scalar(row_m[:], PS, 1.0 / D, ALU.mult),
                   rd=[PS], wr=[row_m])
            b.auto("vector", lambda e: e.tensor_single_scalar(row_v[:], PQ, 1.0 / D, ALU.mult),
                   rd=[PQ], wr=[row_v])
            b.auto("vector", lambda e: e.tensor_mul(row_r[:], row_m[:], row_m[:]),
                   rd=[row_m], wr=[row_r])
            b.auto("vector", lambda e: e.scalar_tensor_tensor(
                row_v[:], row_v[:], 1e-5, row_r[:], ALU.add, ALU.subtract),
                rd=[row_v, row_r], wr=[row_v])
            b.auto("scalar", lambda e: e.activation(row_r[:], row_v[:], AF.Ln),
                   rd=[row_v], wr=[row_r])
            b.auto("scalar", lambda e: e.activation(row_r[:], row_r[:], AF.Exp, scale=-0.5),
                   rd=[row_r], wr=[row_r])
            b.auto("vector", lambda e: e.tensor_mul(row_mr[:], row_m[:], row_r[:]),
                   rd=[row_m, row_r], wr=[row_mr])
            b.mm_group(PR, [(ones_f1[:], row_r[:])])
            b.mm_group(PMR, [(ones_f1[:], row_mr[:])])
            for i in range(NDT):
                src = (xw[i][:, 1:TC2 + 1]) if i < 4 else (xw[i][:, 0:TC2])
                b.auto("vector", lambda e, i=i, src=src: e.scalar_tensor_tensor(
                    t1[:], src, 1.0, PR, ALU.mult, ALU.mult),
                    rd=[xw[i], PR], wr=[t1])
                b.auto("vector", lambda e, i=i: e.tensor_sub(t1[:], t1[:], PMR),
                       rd=[t1, PMR], wr=[t1])
                b.auto("vector", lambda e, i=i: e.tensor_scalar(
                    xn[i][:], t1[:], g_s[:, i:i + 1], bb_s[:, i:i + 1], ALU.mult, ALU.add),
                    rd=[t1, g_s, bb_s], wr=[xn[i]])
            for j in range(NJ):
                rows = 128 if j < NJ - 1 else 42
                a0 = j * 128
                g0 = DFF + j * 128
                b.mm_group(PA[0:rows, :], [(W1[k][:, a0:a0 + rows], xn[k][:]) for k in range(NDT)])
                b.mm_group(PG[0:rows, :], [(W1[k][:, g0:g0 + rows], xn[k][:]) for k in range(NDT)])
                b.auto("scalar", lambda e, rows=rows, j=j: e.activation(
                    gel[0:rows, :], PG[0:rows, :], AF.Gelu, bias=b1g_s[0:rows, j:j + 1]),
                    rd=[PG, b1g_s], wr=[gel])
                b.auto("vector", lambda e, rows=rows, j=j: e.scalar_tensor_tensor(
                    hjs[j][0:rows, :], PA[0:rows, :], b1a_s[0:rows, j:j + 1], gel[0:rows, :],
                    ALU.add, ALU.mult),
                    rd=[PA, b1a_s, gel], wr=[hjs[j]])
            for mo in range(NDT):
                b.mm_group(PW[mo % 2][:],
                           [(W2[j][0:(128 if j < NJ - 1 else 42), mo * 128:(mo + 1) * 128],
                             hjs[j][0:(128 if j < NJ - 1 else 42), :]) for j in range(NJ)],)
                b.auto("vector", lambda e, mo=mo: e.scalar_tensor_tensor(
                    xw[mo][:, 1:TC2 + 1], PW[mo % 2][:], b2_s[:, mo:mo + 1],
                    xw[mo][:, 1:TC2 + 1], ALU.add, ALU.add),
                    rd=[PW[mo % 2], b2_s, xw[mo], (xw[mo], "b")],
                    wr=[(xw[mo], "b"), xw[mo]])
                b.dma("sync", (y_out[bi, mo * 128:(mo + 1) * 128, t0:t0 + TC2],
                               f"{bi}.{c}.{mo}"), xw[mo][:, 1:TC2 + 1])


def build_phase_B_ffn():
    nc = bass.Bass("TRN2", target_bir_lowering=False)
    b = Bld(nc)
    emit_B_ffn(b, _mk_ext(nc, BF_EXT))
    return b.finalize(_st_waits(b))



LBLK = 256
LW = 128
LBAND = 384
NLAT = 64
LHD = 64
NLH = 4


C_EXT = {"qkvw": [D, 3 * D], "qkvb": [128, 3 * NDT], "vbrow": [1, D], "rqw": [D, 256],
         "rqb": [128, 2], "rkvw": [D, 512], "rkb": [128, 2], "rvbrow": [1, 256],
         "routw": [256, D], "routb": [128, NDT], "outw": [D, D], "outb": [128, NDT],
         "glrw": [128, 2], "maskl": [128, 3 * LBAND], "masklat": [128, 16 * NLAT],
         "identc": [128, 128]}


def emit_C(b, ext):
    x_in = ext["x"]
    qkv_w = ext["qkvw"]
    qkv_b = ext["qkvb"]
    vb_row = ext["vbrow"]
    rq_w = ext["rqw"]
    rq_b = ext["rqb"]
    rkv_w = ext["rkvw"]
    rk_b = ext["rkb"]
    rvb_row = ext["rvbrow"]
    rout_w = ext["routw"]
    rout_b = ext["routb"]
    out_w = ext["outw"]
    out_b = ext["outb"]
    glr_w = ext["glrw"]
    maskl_d = ext["maskl"]
    masklat_d = ext["masklat"]
    ident_d = ext["identc"]
    y_out = ext["y"]

    KVW = [b.sb(f"KVW{i}", [128, 2 * D], BF16) for i in range(NDT)]
    WqT = [b.sb(f"WqT{i}", [128, D], BF16) for i in range(NDT)]
    RqW = [b.sb(f"RqW{i}", [128, 256], BF16) for i in range(NDT)]
    RkvW = [b.sb(f"RkvW{i}", [128, 512], BF16) for i in range(NDT)]
    RoW = [b.sb(f"RoW{i}", [128, D], BF16) for i in range(2)]
    OW = [b.sb(f"OW{i}", [128, D], BF16) for i in range(NDT)]
    qkvb_s = b.sb("qkvb_s", [128, 3 * NDT])
    vbr = b.sb("vbr", [1, D], BF16)
    rqb_s = b.sb("rqb_s", [128, 2])
    rkb_s = b.sb("rkb_s", [128, 2])
    rvbr = b.sb("rvbr", [1, 256], BF16)
    routb_s = b.sb("routb_s", [128, NDT])
    outb_s = b.sb("outb_s", [128, NDT])
    glr_s = b.sb("glr_s", [128, 2], BF16)
    maskl = b.sb("maskl_s", [128, 3 * LBAND], BF16)
    masklat = b.sb("masklat_s", [128, 16 * NLAT], BF16)
    ident = b.sb("ident_s", [128, 128], BF16)
    ones_b = b.sb("ones_b", [128, 1], BF16)
    ones_f1 = b.sb("ones_f1", [1, 128])
    ones_r128 = b.sb("ones_r128", [1, 128], BF16)

    KT = [b.sb(f"KT{i}", [128, 640], BF16) for i in range(NDT)]
    VT = [b.sb(f"VT{i}", [128, D], BF16) for i in range(5)]
    localT = [b.sb(f"localT{i}", [128, T], BF16) for i in range(NDT)]
    rqT = [b.sb(f"rqT{i}", [128, T], BF16) for i in range(2)]
    latoT = [b.sb(f"latoT{i}", [128, T], BF16) for i in range(2)]
    pooled = [b.sb(f"pooled{i}", [128, NLAT], BF16) for i in range(NDT)]
    pooled_f = [b.sb(f"pooledf{i}", [128, NLAT]) for i in range(NDT)]
    rkT = [b.sb(f"rkT{i}", [128, NLAT], BF16) for i in range(2)]
    rv_sb = b.sb("rv_sb", [NLAT, 256], BF16)
    x_t = [b.sb(f"x_t{i}", [128, TC]) for i in range(NDT)]
    xb_t = [b.sb(f"xb_t{i}", [128, TC], BF16) for i in range(NDT)]
    qblk = [b.sb(f"qblk{i}", [128, LBLK], BF16) for i in range(NDT)]
    atb2 = [b.sb(f"atb{j}", [128, LBAND], BF16) for j in range(2)]
    attnT2 = [b.sb(f"attnT{j}", [128, LBAND], BF16) for j in range(2)]
    zrow2 = [b.sb(f"zrow{j}", [128, 1]) for j in range(2)]
    rz2 = [b.sb(f"rz{j}", [128, 1]) for j in range(2)]
    ngm2 = [b.sb(f"ngm{j}", [128, 1]) for j in range(2)]
    remb = [b.sb(f"remb{i}", [128, TC], BF16) for i in range(NDT)]
    xq_t = remb
    mx = xb_t
    g_buf = b.sb("g_buf", [1, 2 * TC], BF16)
    t1 = b.sb("t1", [128, TC])
    row_m = b.sb("row_m", [1, TC], BF16)
    row_v = b.sb("row_v", [1, TC], BF16)
    row_r = b.sb("row_r", [1, TC], BF16)
    row_mr = row_v

    P = [b.ps(f"P{i}", [128, TC]) for i in range(6)]
    PTX = b.ps("PTX", [128, 2 * LBAND], BF16)
    PVX = b.ps("PVX", [128, 256])
    Psc = [P[5], P[4]]

    for i in range(NDT):
        rsl = slice(i * 128, (i + 1) * 128)
        b.dma("gpsimd", WqT[i][:], qkv_w[rsl, 0:D])
        b.dma("gpsimd", (KVW[i][:, 0:D], "k"), qkv_w[rsl, D:2 * D], f"q_gp_KVW{i}")
        b.dma("gpsimd", (KVW[i][:, D:2 * D], "v"), qkv_w[rsl, 2 * D:3 * D], f"q_gp_KVW{i}")
        b.dma("gpsimd", RqW[i][:], rq_w[rsl, :])
        b.dma("gpsimd", RkvW[i][:], rkv_w[rsl, :])
        b.dma("gpsimd", OW[i][:], out_w[rsl, :])
    for i in range(2):
        b.dma("gpsimd", RoW[i][:], rout_w[i * 128:(i + 1) * 128, :])
    b.dma("sync", qkvb_s[:], qkv_b[:])
    b.dma("gpsimd", vbr[:], vb_row[:])
    b.dma("sync", rqb_s[:], rq_b[:])
    b.dma("sync", rkb_s[:], rk_b[:])
    b.dma("gpsimd", rvbr[:], rvb_row[:])
    b.dma("sync", routb_s[:], rout_b[:])
    b.dma("sync", outb_s[:], out_b[:])
    b.dma("gpsimd", glr_s[:], glr_w[:])
    b.dma("gpsimd", maskl[:], maskl_d[:])
    b.dma("gpsimd", masklat[:], masklat_d[:])
    b.dma("gpsimd", ident[:], ident_d[:])
    b.auto("vector", lambda e: e.memset(ones_b[:], 1.0), wr=[ones_b])
    b.auto("vector", lambda e: e.memset(ones_f1[:], 1.0), wr=[ones_f1])
    b.auto("vector", lambda e: e.memset(ones_r128[:], 1.0), wr=[ones_r128])

    for bi in range(B_PER_CORE):
        for c in range(NCH):
            t0 = c * TC
            xsl = slice(t0, t0 + TC)
            for i in range(NDT):
                b.dma("sync", x_t[i][:], x_in[bi, i * 128:(i + 1) * 128, xsl])
            for i in range(NDT):
                b.auto("scalar", lambda e, i=i: e.copy(xb_t[i][:], x_t[i][:]),
                       rd=[x_t[i]], wr=[xb_t[i]])
                b.auto("scalar", lambda e, i=i: e.activation(xq_t[i][:, 0:TC], x_t[i][:], AF.Square),
                       rd=[x_t[i]], wr=[xq_t[i]])
            b.mm_group(P[4][0:1, :], [(ones_b[:], xb_t[i][:]) for i in range(NDT)])
            b.mm_group(P[5][0:1, :], [(ones_b[:], xq_t[i][:, 0:TC]) for i in range(NDT)])
            b.auto("vector", lambda e: e.tensor_single_scalar(row_m[:], P[4][0:1, :], 1.0 / D, ALU.mult),
                   rd=[P[4]], wr=[row_m])
            b.auto("vector", lambda e: e.tensor_single_scalar(row_v[:], P[5][0:1, :], 1.0 / D, ALU.mult),
                   rd=[P[5]], wr=[row_v])
            b.auto("vector", lambda e: e.tensor_mul(row_r[:], row_m[:], row_m[:]),
                   rd=[row_m], wr=[row_r])
            b.auto("vector", lambda e: e.scalar_tensor_tensor(
                row_v[:], row_v[:], 1e-5, row_r[:], ALU.add, ALU.subtract),
                rd=[row_v, row_r], wr=[row_v])
            b.auto("scalar", lambda e: e.activation(row_r[:], row_v[:], AF.Ln),
                   rd=[row_v], wr=[row_r])
            b.auto("scalar", lambda e: e.activation(row_r[:], row_r[:], AF.Exp, scale=-0.5),
                   rd=[row_r], wr=[row_r])
            b.auto("vector", lambda e: e.tensor_mul(row_mr[:], row_m[:], row_r[:]),
                   rd=[row_m, row_r], wr=[row_mr])
            b.mm_group(P[4][:], [(ones_r128[0:1, :], row_r[:])])
            b.mm_group(P[5][:], [(ones_r128[0:1, :], row_mr[:])])
            for i in range(NDT):
                b.auto("vector", lambda e, i=i: e.scalar_tensor_tensor(
                    t1[:], x_t[i][:], 1.0, P[4][:], ALU.mult, ALU.mult),
                    rd=[x_t[i], P[4]], wr=[t1])
                b.auto("vector", lambda e, i=i: e.tensor_sub(xb_t[i][:], t1[:], P[5][:]),
                       rd=[t1, P[5]], wr=[xb_t[i]])
            for i in range(NDT):
                b.auto("vector", lambda e, i=i, c=c: e.reduce_sum(
                    pooled_f[i][:, c * 16:(c + 1) * 16],
                    xb_t[i][:].rearrange("p (a w) -> p a w", w=32), axis=AX.X),
                    rd=[xb_t[i]], wr=[(pooled_f[i], c)])
            for m in range(2):
                b.mm_group(P[4][:], [(RqW[k][:, m * 128:(m + 1) * 128], xb_t[k][:])
                                     for k in range(NDT)])
                b.auto("vector", lambda e, o=rqT[m][:, xsl], m=m: e.tensor_single_scalar(
                    o, P[4][:], rqb_s[:, m:m + 1], ALU.add),
                    rd=[P[4], rqb_s], wr=[(rqT[m], c)])
            for i in range(NDT):
                if c == 0:
                    b.auto("vector", lambda e, i=i: e.memset(KT[i][:, 0:128], 0.0),
                           wr=[(KT[i], "prev")])
                else:
                    b.auto("vector", lambda e, i=i: e.tensor_copy(
                        KT[i][:, 0:128], KT[i][:, 512:640]),
                        rd=[(KT[i], "cur")], wr=[(KT[i], "prev")])
            if c == 0:
                b.auto("vector", lambda e: e.memset(VT[0][:], 0.0), wr=[VT[0]])
            else:
                b.auto("vector", lambda e: e.tensor_copy(VT[0][:], VT[4][:]),
                       rd=[VT[4]], wr=[VT[0]])
            for m in range(NDT):
                b.mm_group(P[m % 2][:], [(KVW[k][:, m * 128:(m + 1) * 128], xb_t[k][:])
                                         for k in range(NDT)])
                b.auto("vector", lambda e, m=m: e.tensor_single_scalar(
                    KT[m][:, 128:640], P[m % 2][:], qkvb_s[:, NDT + m:NDT + m + 1], ALU.add),
                    rd=[P[m % 2], qkvb_s], wr=[(KT[m], "cur")])
            for tt in range(4):
                tsl = slice(tt * 128, (tt + 1) * 128)
                for ch in range(2):
                    b.mm_group(P[2 + ch][:],
                               [(xb_t[k][:, tsl], KVW[k][:, D + ch * 512:D + (ch + 1) * 512])
                                for k in range(NDT)]
                               + [(ones_r128[0:1, 0:128], vbr[0:1, ch * 512:(ch + 1) * 512])])
                    b.auto("vector", lambda e, tt=tt, ch=ch: e.tensor_copy(
                        VT[1 + tt][:, ch * 512:(ch + 1) * 512], P[2 + ch][:]),
                        rd=[P[2 + ch]], wr=[(VT[1 + tt], ch)])
            for wloc in range(2):
                bb = 2 * c + wloc
                bq0 = wloc * LBLK
                for m in range(NDT):
                    b.mm_group(P[4][:, 0:LBLK],
                               [(WqT[k][:, m * 128:(m + 1) * 128], xb_t[k][:, bq0:bq0 + LBLK])
                                for k in range(NDT)])
                    b.auto("vector", lambda e, m=m: e.tensor_single_scalar(
                        qblk[m][:], P[4][:, 0:LBLK], qkvb_s[:, m:m + 1], ALU.add),
                        rd=[P[4], qkvb_s], wr=[qblk[m]])
                kroll0 = 256 * wloc
                iters = [(qt, h) for qt in range(2) for h in range(HEADS)]

                def score_mm(idx):
                    qt, h = iters[idx]
                    m = h // 2
                    hr = slice((h % 2) * 64, (h % 2) * 64 + 64)
                    b.mm_group(Psc[idx % 2][:, 0:LBAND],
                               [(qblk[m][hr, qt * 128:(qt + 1) * 128],
                                 KT[m][hr, kroll0:kroll0 + LBAND])])

                score_mm(0)
                for idx in range(len(iters)):
                    qt, h = iters[idx]
                    m = h // 2
                    hr = slice((h % 2) * 64, (h % 2) * 64 + 64)
                    sl = idx % 2
                    psl = (idx // 2) % 2
                    mv = qt if bb > 0 else (2 if qt == 0 else 1)
                    msl = slice(mv * LBAND, (mv + 1) * LBAND)
                    if idx + 1 < len(iters):
                        score_mm(idx + 1)
                    b.auto("vector", lambda e, sl=sl: e.reduce_max(
                        ngm2[sl][:], Psc[sl][:, 0:LBAND], axis=AX.X, negate=True),
                        rd=[Psc[sl]], wr=[ngm2[sl]])
                    b.auto("vector", lambda e, sl=sl: e.tensor_single_scalar(
                        ngm2[sl][:], ngm2[sl][:], 0.125, ALU.mult),
                        rd=[ngm2[sl]], wr=[ngm2[sl]])
                    b.auto("scalar", lambda e, sl=sl: e.activation(
                        atb2[sl][:], Psc[sl][:, 0:LBAND], AF.Exp,
                        bias=ngm2[sl][:], scale=0.125),
                        rd=[Psc[sl], ngm2[sl]], wr=[atb2[sl]])
                    b.auto("vector", lambda e, sl=sl, msl=msl: e.scalar_tensor_tensor(
                        atb2[sl][:], atb2[sl][:], 1.0, maskl[:, msl], ALU.mult, ALU.mult,
                        accum_out=zrow2[sl][:]),
                        rd=[atb2[sl], maskl], wr=[atb2[sl], zrow2[sl]])
                    b.auto("vector", lambda e, sl=sl: e.tensor_single_scalar(
                        zrow2[sl][:], zrow2[sl][:], 1e-30, ALU.max),
                        rd=[zrow2[sl]], wr=[zrow2[sl]])
                    b.auto("vector", lambda e, sl=sl: e.reciprocal(
                        rz2[sl][:], zrow2[sl][:]), rd=[zrow2[sl]], wr=[rz2[sl]])
                    b.auto("scalar", lambda e, sl=sl: e.activation(
                        atb2[sl][:], atb2[sl][:], AF.Copy, scale=rz2[sl][:]),
                        rd=[atb2[sl], rz2[sl]], wr=[atb2[sl]])
                    for j in range(3):
                        b.auto("tensor", lambda e, sl=sl, j=j: e.transpose(
                            PTX[:, sl * LBAND + j * 128:sl * LBAND + (j + 1) * 128],
                            atb2[sl][:, j * 128:(j + 1) * 128], ident[:]),
                            rd=[atb2[sl], ident], wr=[(PTX, f"{sl}.{j}")])
                    b.auto("scalar", lambda e, sl=sl: e.copy(
                        attnT2[sl][:], PTX[:, sl * LBAND:(sl + 1) * LBAND]),
                        rd=[(PTX, f"{sl}.0"), (PTX, f"{sl}.1"), (PTX, f"{sl}.2")],
                        wr=[attnT2[sl]])
                    vr0 = 2 * wloc
                    b.mm_group((PVX[hr, psl * 128:(psl + 1) * 128], psl),
                               [(VT[vr0 + j][:, h * 64:(h + 1) * 64],
                                 attnT2[sl][:, j * 128:(j + 1) * 128]) for j in range(3)])
                    if h % 2 == 1:
                        b.auto("scalar", lambda e, m=m, bb=bb, qt=qt, psl=psl: e.copy(
                            localT[m][:, bb * LBLK + qt * 128:bb * LBLK + (qt + 1) * 128],
                            PVX[:, psl * 128:(psl + 1) * 128]),
                            rd=[(PVX, psl)], wr=[(localT[m], f"{bb}.{qt}")])
        for i in range(NDT):
            b.auto("vector", lambda e, i=i: e.tensor_single_scalar(
                pooled[i][:], pooled_f[i][:], 1.0 / 32, ALU.mult),
                rd=[pooled_f[i]] + [(pooled_f[i], c) for c in range(NCH)], wr=[pooled[i]])
        for m in range(2):
            b.mm_group(P[0][:, 0:NLAT], [(RkvW[k][:, m * 128:(m + 1) * 128], pooled[k][:])
                                         for k in range(NDT)])
            b.auto("vector", lambda e, m=m: e.tensor_single_scalar(
                rkT[m][:, :], P[0][:, 0:NLAT], rkb_s[:, m:m + 1], ALU.add),
                rd=[P[0], rkb_s], wr=[rkT[m]])
        b.mm_group(P[1][0:NLAT, 0:256],
                   [(pooled[k][:], RkvW[k][:, 256:512]) for k in range(NDT)]
                   + [(ones_r128[0:1, 0:NLAT], rvbr[:])])
        b.auto("vector", lambda e: e.tensor_copy(rv_sb[:], P[1][0:NLAT, 0:256]),
               rd=[P[1]], wr=[rv_sb])
        for tt in range(16):
            q0 = tt * 128
            for lh in range(NLH):
                mt = lh // 2
                hr = slice((lh % 2) * 64, (lh % 2) * 64 + 64)
                b.mm_group(P[4][:, 0:NLAT], [(rqT[mt][hr, q0:q0 + 128], rkT[mt][hr, :])])
                b.auto("vector", lambda e: e.reduce_max(
                    ngm2[0][:], P[4][:, 0:NLAT], axis=AX.X, negate=True),
                    rd=[P[4]], wr=[ngm2[0]])
                b.auto("vector", lambda e: e.tensor_single_scalar(
                    ngm2[0][:], ngm2[0][:], 0.125, ALU.mult), rd=[ngm2[0]], wr=[ngm2[0]])
                b.auto("scalar", lambda e: e.activation(
                    atb2[0][:, 0:NLAT], P[4][:, 0:NLAT], AF.Exp, bias=ngm2[0][:], scale=0.125),
                    rd=[P[4], ngm2[0]], wr=[atb2[0]])
                b.auto("vector", lambda e, tt=tt: e.scalar_tensor_tensor(
                    atb2[0][:, 0:NLAT], atb2[0][:, 0:NLAT], 1.0,
                    masklat[:, tt * NLAT:(tt + 1) * NLAT],
                    ALU.mult, ALU.mult, accum_out=zrow2[0][:]),
                    rd=[atb2[0], masklat], wr=[atb2[0], zrow2[0]])
                b.auto("vector", lambda e: e.tensor_single_scalar(
                    zrow2[0][:], zrow2[0][:], 1e-30, ALU.max), rd=[zrow2[0]], wr=[zrow2[0]])
                b.auto("vector", lambda e: e.reciprocal(rz2[0][:], zrow2[0][:]),
                       rd=[zrow2[0]], wr=[rz2[0]])
                b.auto("vector", lambda e: e.tensor_single_scalar(
                    atb2[0][:, 0:NLAT], atb2[0][:, 0:NLAT], rz2[0][:], ALU.mult),
                    rd=[atb2[0], rz2[0]], wr=[atb2[0]])
                b.auto("tensor", lambda e: e.transpose(
                    PTX[0:NLAT, 0:128], atb2[0][:, 0:NLAT], ident[:]),
                    rd=[atb2[0], ident], wr=[(PTX, "lat")])
                b.auto("vector", lambda e: e.tensor_copy(attnT2[0][0:NLAT, 0:128],
                                                         PTX[0:NLAT, 0:128]),
                       rd=[(PTX, "lat")], wr=[attnT2[0]])
                b.mm_group((PVX[hr, 0:128], "lat"),
                           [(rv_sb[:, lh * 64:(lh + 1) * 64], attnT2[0][0:NLAT, 0:128])])
                if lh % 2 == 1:
                    b.auto("vector", lambda e, mt=mt, q0=q0: e.tensor_copy(
                        latoT[mt][:, q0:q0 + 128], PVX[:, 0:128]),
                        rd=[(PVX, "lat")], wr=[(latoT[mt], tt)])
        for c in range(NCH):
            t0 = c * TC
            xsl = slice(t0, t0 + TC)
            for m in range(NDT):
                b.mm_group(P[m % 2][:], [(RoW[k][:, m * 128:(m + 1) * 128],
                                          latoT[k][:, xsl]) for k in range(2)])
                b.auto("vector", lambda e, m=m: e.tensor_single_scalar(
                    remb[m][:], P[m % 2][:], routb_s[:, m:m + 1], ALU.add),
                    rd=[P[m % 2], routb_s], wr=[remb[m]])
            for m in range(NDT):
                for hh in range(2):
                    hr = slice(hh * 64, hh * 64 + 64)
                    b.mm_group(P[2][0:1, :],
                               [(glr_s[hr, 0:1], localT[m][hr, xsl]),
                                (glr_s[hr, 1:2], remb[m][hr, :])])
                    b.auto("scalar", lambda e, hh=hh: e.activation(
                        g_buf[0:1, hh * TC:(hh + 1) * TC], P[2][0:1, :], AF.Sigmoid),
                        rd=[P[2]], wr=[(g_buf, hh)])
                for hh in range(2):
                    b.mm_group(P[3][hh * 64:hh * 64 + 64, :],
                               [(ones_r128[0:1, 0:64], g_buf[0:1, hh * TC:(hh + 1) * TC])])
                b.auto("vector", lambda e, src=localT[m][:, xsl], m=m: e.tensor_tensor(
                    t1[:], src, remb[m][:], ALU.subtract),
                    rd=[localT[m], remb[m]], wr=[t1])
                b.auto("vector", lambda e, m=m: e.tensor_tensor(
                    t1[:], t1[:], P[3][:], ALU.mult), rd=[t1, P[3]], wr=[t1])
                b.auto("vector", lambda e, m=m: e.tensor_tensor(
                    mx[m][:], t1[:], remb[m][:], ALU.add), rd=[t1, remb[m]], wr=[mx[m]])
            for mo in range(NDT):
                b.mm_group(P[4 + mo % 2][:], [(OW[k][:, mo * 128:(mo + 1) * 128], mx[k][:])
                                              for k in range(NDT)])
                b.dma("sync", x_t[mo][:], x_in[bi, mo * 128:(mo + 1) * 128, xsl],
                      sem=f"q_sy_x2_{mo}")
                b.auto("vector", lambda e, mo=mo: e.scalar_tensor_tensor(
                    x_t[mo][:], P[4 + mo % 2][:], outb_s[:, mo:mo + 1], x_t[mo][:],
                    ALU.add, ALU.add),
                    rd=[P[4 + mo % 2], outb_s, x_t[mo]], wr=[x_t[mo]])
                b.dma("sync", (y_out[bi, mo * 128:(mo + 1) * 128, xsl], f"{bi}.{c}.{mo}"),
                      x_t[mo][:])


def build_phase_C():
    nc = bass.Bass("TRN2", target_bir_lowering=False)
    b = Bld(nc)
    emit_C(b, _mk_ext(nc, C_EXT))
    return b.finalize(_st_waits(b))


def _prep_phase_C(inputs):
    d = {}
    d["qkvw"] = np.ascontiguousarray(np.asarray(inputs["L1_attn_qkv_w"]))
    qb = np.asarray(inputs["L1_attn_qkv_b"])
    d["qkvb"] = np.ascontiguousarray(qb[:2 * D].reshape(2 * NDT, 128).T)
    d["qkvb"] = np.concatenate(
        [d["qkvb"], qb[2 * D:].reshape(NDT, 128).T], axis=1).astype(np.float32)
    d["qkvb"] = np.ascontiguousarray(d["qkvb"])
    d["vbrow"] = np.ascontiguousarray(qb[2 * D:][None, :])
    d["rqw"] = np.ascontiguousarray(np.asarray(inputs["L1_attn_rq_w"]))
    d["rqb"] = np.ascontiguousarray(np.asarray(inputs["L1_attn_rq_b"]).reshape(2, 128).T)
    d["rkvw"] = np.ascontiguousarray(np.asarray(inputs["L1_attn_rkv_w"]))
    rkvb = np.asarray(inputs["L1_attn_rkv_b"])
    d["rkb"] = np.ascontiguousarray(rkvb[:256].reshape(2, 128).T)
    d["rvbrow"] = np.ascontiguousarray(rkvb[256:][None, :])
    d["routw"] = np.ascontiguousarray(np.asarray(inputs["L1_attn_rout_w"]))
    d["routb"] = np.ascontiguousarray(np.asarray(inputs["L1_attn_rout_b"]).reshape(NDT, 128).T)
    d["outw"] = np.ascontiguousarray(np.asarray(inputs["L1_attn_out_w"]))
    d["outb"] = np.ascontiguousarray(np.asarray(inputs["L1_attn_out_b"]).reshape(NDT, 128).T)
    glr = np.concatenate(
        [np.asarray(inputs["L1_attn_gl_w"]), np.asarray(inputs["L1_attn_gr_w"])], axis=1)
    d["glrw"] = np.ascontiguousarray(np.tile(glr, (2, 1)))
    i = np.arange(128)[:, None]
    jj = np.arange(LBAND)[None, :]
    m0 = ((jj >= i) & (jj <= i + 127)).astype(np.float32)
    m1 = ((jj >= 128 + i) & (jj <= 255 + i)).astype(np.float32)
    m2 = m0 * (jj >= LW)
    d["maskl"] = np.ascontiguousarray(np.concatenate([m0, m1, m2], axis=1))
    mm = np.zeros((128, 16 * NLAT), dtype=np.float32)
    for tt in range(16):
        lim = 4 * tt + (np.arange(128) // 32)
        mm[:, tt * NLAT:(tt + 1) * NLAT] = (np.arange(NLAT)[None, :] < lim[:, None])
    d["masklat"] = np.ascontiguousarray(mm)
    d["identc"] = np.eye(128, dtype=np.float32)
    return d



NEXP = 8
DFF2 = 4096
NFF = DFF2 // 128


D_EXT = {"rw": [D, NEXP], "w1": [NEXP, D, DFF2], "b1": [NEXP, 128, NFF],
         "w2": [NEXP, DFF2, D], "b2": [NEXP, 128, NDT], "gam": [NEXP, 128, NDT],
         "bet": [NEXP, 128, NDT], "identf": [128, 128]}


def emit_D(b, ext):
    nc = b.nc
    x_in = ext["x"]
    rw_d = ext["rw"]
    w1_d = ext["w1"]
    b1_d = ext["b1"]
    w2_d = ext["w2"]
    b2_d = ext["b2"]
    gam_d = ext["gam"]
    bet_d = ext["bet"]
    identf_d = ext["identf"]
    y_out = ext["y"]
    xn_scr = ext["xnscr"]
    g_scr = ext["gscr"]

    W1 = [b.sb(f"W1_{k}", [128, DFF2], BF16) for k in range(NDT)]
    W2 = [b.sb(f"W2_{j}", [128, D], BF16) for j in range(NFF)]
    rw_s = b.sb("rw_s", [128, NEXP * NDT])
    b1_s = b.sb("b1_s", [128, NFF])
    b2_s = b.sb("b2_s", [128, NDT])
    gam_s = b.sb("gam_s", [128, NDT])
    bet_s = b.sb("bet_s", [128, NDT])
    identf = b.sb("identf_s", [128, 128])
    ones_b = b.sb("ones_b", [128, 1], BF16)

    x_t = [b.sb(f"x_t{i}", [128, TC]) for i in range(NDT)]
    xnf = [b.sb(f"xnf{i}", [128, TC]) for i in range(NDT)]
    xb_t = [b.sb(f"xb_t{i}", [128, TC], BF16) for i in range(NDT)]
    hbuf = [b.sb(f"hb{j}", [128, TC], BF16) for j in range(NFF)]
    r_f = xnf
    sqb = b.sb("sqb", [128, TC], BF16)
    rb = sqb
    xnc = xb_t
    gt_t = b.sb("gt_t", [NEXP, 128], BF16)
    gfin = b.sb("gfin", [128, NEXP])
    g_sb = b.sb("g_sb", [1, TC], BF16)
    ones_bb = b.sb("ones_bb", [1, 128], BF16)
    gate_t = b.sb("gate_t", [128, NEXP])
    m1c = b.sb("m1c", [128, 1])
    m2c = b.sb("m2c", [128, 1])
    w1c = b.sb("w1c", [128, 1])
    w2c = b.sb("w2c", [128, 1])
    msk1 = b.sb("msk1", [128, NEXP])
    msk2 = b.sb("msk2", [128, NEXP])
    t1 = b.sb("t1", [128, TC])
    row_m = b.sb("row_m", [1, TC], BF16)
    row_v = b.sb("row_v", [1, TC], BF16)

    PH = [b.ps(f"PH{i}", [128, TC]) for i in range(2)]
    PW = [b.ps(f"PW{i}", [128, TC]) for i in range(2)]
    PB1 = b.ps("PB1", [128, TC])
    PB2 = b.ps("PB2", [128, TC])
    PGB = b.ps("PGB", [128, TC])
    PR8 = b.ps("PR8", [128, 128])

    for i in range(NDT):
        b.dma("sync", rw_s[:, i * NEXP:(i + 1) * NEXP], rw_d[i * 128:(i + 1) * 128, :],
              sem="q_sy_rw")
    b.dma("sync", identf[:], identf_d[:])
    b.auto("vector", lambda e: e.memset(ones_b[:], 1.0), wr=[ones_b])
    b.auto("vector", lambda e: e.memset(ones_bb[:], 1.0), wr=[ones_bb])

    for bi in range(B_PER_CORE):
        for c in range(NCH):
            t0 = c * TC
            xsl = slice(t0, t0 + TC)
            for i in range(NDT):
                b.dma("sync", x_t[i][:], x_in[bi, i * 128:(i + 1) * 128, xsl])
            for i in range(NDT):
                b.auto("scalar", lambda e, i=i: e.copy(xb_t[i][:], x_t[i][:]),
                       rd=[x_t[i]], wr=[xb_t[i]])
                b.auto("scalar", lambda e, i=i: e.activation(sqb[:], x_t[i][:], AF.Square),
                       rd=[x_t[i]], wr=[sqb])
                b.auto("tensor", lambda e, i=i: e.matmul(
                    PB2[0:1, :], ones_b[:], sqb[:], start=(i == 0), stop=(i == NDT - 1)),
                    rd=[ones_b, sqb], wr=([PB2] if i in (0, NDT - 1) else []))
            b.mm_group(PB1[0:1, :], [(ones_b[:], xb_t[i][:]) for i in range(NDT)])
            b.auto("vector", lambda e: e.tensor_single_scalar(row_m[:], PB1[0:1, :], 1.0 / D, ALU.mult),
                   rd=[PB1], wr=[row_m])
            b.auto("vector", lambda e: e.tensor_single_scalar(row_v[:], PB2[0:1, :], 1.0 / D, ALU.mult),
                   rd=[PB2], wr=[row_v])
            b.auto("vector", lambda e: e.tensor_mul(t1[0:1, :], row_m[:], row_m[:]),
                   rd=[row_m], wr=[(t1, "row")])
            b.auto("vector", lambda e: e.scalar_tensor_tensor(
                row_v[:], row_v[:], 1e-5, t1[0:1, :], ALU.add, ALU.subtract),
                rd=[row_v, (t1, "row")], wr=[row_v])
            b.auto("scalar", lambda e: e.activation(row_v[:], row_v[:], AF.Ln),
                   rd=[row_v], wr=[row_v])
            b.auto("scalar", lambda e: e.activation(row_v[:], row_v[:], AF.Exp, scale=-0.5),
                   rd=[row_v], wr=[row_v])
            b.auto("vector", lambda e: e.tensor_mul(row_m[:], row_m[:], row_v[:]),
                   rd=[row_m, row_v], wr=[row_m])
            b.mm_group(PB1[:], [(ones_bb[:], row_v[:])])
            b.mm_group(PB2[:], [(ones_bb[:], row_m[:])])
            for i in range(NDT):
                b.auto("vector", lambda e, i=i: e.scalar_tensor_tensor(
                    t1[:], x_t[i][:], 1.0, PB1[:], ALU.mult, ALU.mult),
                    rd=[x_t[i], PB1], wr=[t1])
                b.auto("vector", lambda e, i=i: e.tensor_sub(xnf[i][:], t1[:], PB2[:]),
                       rd=[t1, PB2], wr=[xnf[i]])
                b.auto("scalar", lambda e, i=i: e.copy(xb_t[i][:], xnf[i][:]),
                       rd=[xnf[i]], wr=[xb_t[i]])
                b.dma("sync", (xn_scr[bi, i * 128:(i + 1) * 128, xsl], f"{bi}.{c}.{i}"),
                      xb_t[i][:])
            for tt in range(4):
                tloc = slice(tt * 128, (tt + 1) * 128)
                b.mm_group(PR8[:, 0:NEXP],
                           [(xnf[k][:, tloc], rw_s[:, k * NEXP:(k + 1) * NEXP])
                            for k in range(NDT)])
                b.auto("vector", lambda e: e.reduce_max(m1c[:], PR8[:, 0:NEXP], axis=AX.X),
                       rd=[PR8], wr=[m1c])
                b.auto("vector", lambda e: e.tensor_single_scalar(
                    gate_t[:], PR8[:, 0:NEXP], m1c[:], ALU.is_ge),
                    rd=[PR8, m1c], wr=[gate_t])
                b.auto("vector", lambda e: e.tensor_copy(msk1[:], gate_t[:]),
                       rd=[gate_t], wr=[msk1])
                b.auto("vector", lambda e: e.scalar_tensor_tensor(
                    gate_t[:], msk1[:], -1e30, PR8[:, 0:NEXP], ALU.mult, ALU.add),
                    rd=[msk1, PR8], wr=[gate_t])
                b.auto("vector", lambda e: e.reduce_max(m2c[:], gate_t[:], axis=AX.X),
                       rd=[gate_t], wr=[m2c])
                b.auto("vector", lambda e: e.tensor_single_scalar(
                    msk2[:], gate_t[:], m2c[:], ALU.is_ge),
                    rd=[gate_t, m2c], wr=[msk2])
                b.auto("vector", lambda e: e.tensor_sub(w1c[:], m1c[:], m2c[:]),
                       rd=[m1c, m2c], wr=[w1c])
                b.auto("scalar", lambda e: e.activation(w1c[:], w1c[:], AF.Sigmoid),
                       rd=[w1c], wr=[w1c])
                b.auto("vector", lambda e: e.tensor_scalar(
                    w2c[:], w1c[:], -1.0, 1.0, ALU.mult, ALU.add),
                    rd=[w1c], wr=[w2c])
                b.auto("vector", lambda e: e.tensor_single_scalar(
                    msk1[:], msk1[:], w1c[:], ALU.mult), rd=[msk1, w1c], wr=[msk1])
                b.auto("vector", lambda e: e.scalar_tensor_tensor(
                    gfin[:], msk2[:], w2c[:], msk1[:], ALU.mult, ALU.add),
                    rd=[msk2, w2c, msk1], wr=[gfin])
                b.auto("tensor", lambda e: e.transpose(
                    PR8[0:NEXP, 0:128], gfin[:, 0:NEXP], identf[:]),
                    rd=[gfin, identf], wr=[PR8])
                b.auto("scalar", lambda e: e.copy(gt_t[:], PR8[0:NEXP, 0:128]),
                       rd=[PR8], wr=[gt_t])
                b.dma("gpsimd", (g_scr[bi, :, t0 + tt * 128:t0 + (tt + 1) * 128],
                                 f"{bi}.{c}.{tt}"), gt_t[:], sem="q_gp_st_gt")
    for e_i in range(NEXP):
        for k in range(NDT):
            b.dma("gpsimd", W1[k][:], w1_d[e_i, k * 128:(k + 1) * 128, :])
        for j in range(NFF):
            b.dma("gpsimd", W2[j][:], w2_d[e_i, j * 128:(j + 1) * 128, :])
        b.dma("sync", b1_s[:], b1_d[e_i], sem="q_sy_b1")
        b.dma("sync", b2_s[:], b2_d[e_i], sem="q_sy_b2")
        b.dma("sync", gam_s[:], gam_d[e_i], sem="q_sy_gam")
        b.dma("sync", bet_s[:], bet_d[e_i], sem="q_sy_bet")
        for bi in range(B_PER_CORE):
            for c in range(NCH):
                t0 = c * TC
                xsl = slice(t0, t0 + TC)
                for i in range(NDT):
                    rsl = slice(i * 128, (i + 1) * 128)
                    b.dma("sync", xnc[i][:], xn_scr[bi, rsl, xsl])
                    b.dma("sync", x_t[i][:], x_in[bi, rsl, xsl])
                b.dma("gpsimd", g_sb[:], g_scr[bi, e_i, xsl])
                b.mm_group(PGB[:], [(ones_bb[:], g_sb[:])])
                for j in range(NFF):
                    b.mm_group(PH[j % 2][:], [(W1[k][:, j * 128:(j + 1) * 128], xnc[k][:])
                                              for k in range(NDT)])
                    b.auto("scalar", lambda e, j=j: e.activation(
                        hbuf[j][:], PH[j % 2][:], AF.Gelu, bias=b1_s[:, j:j + 1]),
                        rd=[PH[j % 2], b1_s], wr=[hbuf[j]])
                for mo in range(NDT):
                    b.mm_group(PW[mo % 2][:], [(W2[j][:, mo * 128:(mo + 1) * 128], hbuf[j][:])
                                               for j in range(NFF)])
                    b.auto("vector", lambda e, mo=mo: e.scalar_tensor_tensor(
                        r_f[mo][:], PW[mo % 2][:], b2_s[:, mo:mo + 1], xnc[mo][:],
                        ALU.add, ALU.add),
                        rd=[PW[mo % 2], b2_s, xnc[mo]], wr=[r_f[mo]])
                    b.auto("scalar", lambda e, mo=mo: e.activation(sqb[:], r_f[mo][:], AF.Square),
                           rd=[r_f[mo]], wr=[sqb])
                    b.auto("tensor", lambda e, mo=mo: e.matmul(
                        PB2[0:1, :], ones_b[:], sqb[:], start=(mo == 0), stop=(mo == NDT - 1)),
                        rd=[ones_b, sqb], wr=([PB2] if mo in (0, NDT - 1) else []))
                    b.auto("scalar", lambda e, mo=mo: e.copy(rb[:], r_f[mo][:]),
                           rd=[r_f[mo]], wr=[rb])
                    b.auto("tensor", lambda e, mo=mo: e.matmul(
                        PB1[0:1, :], ones_b[:], rb[:], start=(mo == 0), stop=(mo == NDT - 1)),
                        rd=[ones_b, rb], wr=([PB1] if mo in (0, NDT - 1) else []))
                b.auto("vector", lambda e: e.tensor_single_scalar(row_m[:], PB1[0:1, :], 1.0 / D, ALU.mult),
                       rd=[PB1], wr=[row_m])
                b.auto("vector", lambda e: e.tensor_single_scalar(row_v[:], PB2[0:1, :], 1.0 / D, ALU.mult),
                       rd=[PB2], wr=[row_v])
                b.auto("vector", lambda e: e.tensor_mul(t1[0:1, :], row_m[:], row_m[:]),
                       rd=[row_m], wr=[(t1, "row")])
                b.auto("vector", lambda e: e.scalar_tensor_tensor(
                    row_v[:], row_v[:], 1e-6, t1[0:1, :], ALU.add, ALU.subtract),
                    rd=[row_v, (t1, "row")], wr=[row_v])
                b.auto("scalar", lambda e: e.activation(row_v[:], row_v[:], AF.Ln),
                       rd=[row_v], wr=[row_v])
                b.auto("scalar", lambda e: e.activation(row_v[:], row_v[:], AF.Exp, scale=-0.5),
                       rd=[row_v], wr=[row_v])
                b.auto("vector", lambda e: e.tensor_mul(row_m[:], row_m[:], row_v[:]),
                       rd=[row_m, row_v], wr=[row_m])
                b.mm_group(PB1[:], [(ones_bb[:], row_v[:])])
                b.mm_group(PB2[:], [(ones_bb[:], row_m[:])])
                for mo in range(NDT):
                    b.auto("vector", lambda e, mo=mo: e.scalar_tensor_tensor(
                        t1[:], r_f[mo][:], 1.0, PB1[:], ALU.mult, ALU.mult),
                        rd=[r_f[mo], PB1], wr=[t1])
                    b.auto("vector", lambda e: e.tensor_sub(t1[:], t1[:], PB2[:]),
                           rd=[t1, PB2], wr=[t1])
                    b.auto("vector", lambda e, mo=mo: e.tensor_scalar(
                        t1[:], t1[:], gam_s[:, mo:mo + 1], bet_s[:, mo:mo + 1],
                        ALU.mult, ALU.add),
                        rd=[t1, gam_s, bet_s], wr=[t1])
                    b.auto("vector", lambda e: e.tensor_tensor(
                        t1[:], t1[:], PGB[:], ALU.mult), rd=[t1, PGB], wr=[t1])
                    rsl = slice(mo * 128, (mo + 1) * 128)
                    if e_i == 0:
                        b.auto("vector", lambda e, mo=mo: e.tensor_tensor(
                            t1[:], t1[:], x_t[mo][:], ALU.add),
                            rd=[t1, x_t[mo]], wr=[t1])
                        b.dma("sync", (y_out[bi, rsl, xsl], f"{bi}.{c}.{mo}"), t1[:],
                              sem="q_sy_st_y")
                    else:
                        b.auto("gpsimd",
                               lambda e, o=y_out[bi, rsl, xsl]: e.dma_start(
                                   out=o, in_=t1[:], accum_op=ALU.add),
                               rd=[t1, (y_out, f"{bi}.{c}.{mo}")],
                               wr=[(y_out, f"{bi}.{c}.{mo}")],
                               sem="q_gp_st_y", incn=16)


def build_phase_D():
    nc = bass.Bass("TRN2", target_bir_lowering=False)
    b = Bld(nc)
    ext = _mk_ext(nc, D_EXT)
    ext["xnscr"] = nc.dram_tensor("xnscr", [B_PER_CORE, D, T], BF16, kind="Internal")
    ext["gscr"] = nc.dram_tensor("gscr", [B_PER_CORE, NEXP, T], F32, kind="Internal")
    emit_D(b, ext)
    return b.finalize(_st_waits(b))


def _prep_phase_D(inputs):
    d = {}
    d["rw"] = np.ascontiguousarray(np.asarray(inputs["L1_ffn_router_w"]))
    d["w1"] = np.ascontiguousarray(np.asarray(inputs["L1_ffn_w1"]))
    d["b1"] = np.ascontiguousarray(
        np.asarray(inputs["L1_ffn_b1"]).reshape(NEXP, NFF, 128).transpose(0, 2, 1))
    d["w2"] = np.ascontiguousarray(np.asarray(inputs["L1_ffn_w2"]))
    d["b2"] = np.ascontiguousarray(
        np.asarray(inputs["L1_ffn_b2"]).reshape(NEXP, NDT, 128).transpose(0, 2, 1))
    d["gam"] = np.ascontiguousarray(
        np.asarray(inputs["L1_ffn_gamma"]).reshape(NEXP, NDT, 128).transpose(0, 2, 1))
    d["bet"] = np.ascontiguousarray(
        np.asarray(inputs["L1_ffn_beta"]).reshape(NEXP, NDT, 128).transpose(0, 2, 1))
    d["identf"] = np.eye(128, dtype=np.float32)
    return d



def _sel_matrices():
    R = np.zeros((128, 16, 128), dtype=np.float32)
    for k in range(16):
        for p in range(128):
            R[8 * k + p // 16, k, p] = 1.0
    S = np.zeros((128, 16, 128), dtype=np.float32)
    for k in range(16):
        for p in range(128):
            S[p, k, 8 * k + p // 16] = 1.0
    RB = np.zeros((16, 128), dtype=np.float32)
    for p in range(128):
        RB[p % 16, p] = 1.0
    return R.reshape(128, 16 * 128), S.reshape(128, 16 * 128), RB


def _rotary_tables():
    half = np.arange(0, 64, 2, dtype=np.float64)
    inv_freq = 1.0 / 10000.0 ** (half / 64.0)
    t = np.arange(T, dtype=np.float64)
    freqs = t[:, None] * inv_freq[None, :]
    base_scale = (half + 0.4 * 64) / (1.4 * 64)
    power = (t - T // 2) / (WIN // 2)
    scale = base_scale[None, :] ** power[:, None]
    j = np.arange(64)
    fr = freqs[:, j % 32]
    sc = scale[:, j % 32]
    sign = np.where(j < 32, -1.0, 1.0)
    cs_q = (np.cos(fr) * sc).T
    ss_q = (np.sin(fr) * sc).T * sign[:, None]
    cs_k = (np.cos(fr) / sc).T
    ss_k = (np.sin(fr) / sc).T * sign[:, None]
    tile2 = lambda a: np.concatenate([a, a], axis=0).astype(np.float32)
    return tile2(cs_q), tile2(ss_q), tile2(cs_k), tile2(ss_k)


def _masks3():
    i = np.arange(128)[:, None]
    jj = np.arange(BAND)[None, :]
    gen = ((jj >= i) & (jj <= i + 256)).astype(np.float32)
    m1 = gen * (jj >= 256)
    m2 = gen * (jj >= 128)
    mul = np.concatenate([gen, m1, m2], axis=1).astype(np.float32)
    return (mul - 1.0) * 1e9


def _prep_phase_B_shared():
    d = {}
    d["csq"], d["ssq"], d["csk"], d["ssk"] = _rotary_tables()
    d["mask3"] = _masks3()
    bsel = np.zeros((128, 128), dtype=np.float32)
    for m in range(8):
        for p in range(128):
            bsel[p, m * 16 + 2 * m + p // 64] = 1.0
    d["bsel"] = bsel
    psel = np.zeros((128, 2), dtype=np.float32)
    psel[0:64, 0] = 1.0
    psel[64:128, 1] = 1.0
    d["psel"] = psel
    d["pselT"] = np.ascontiguousarray(psel.T)
    d["ident"] = np.eye(128, dtype=np.float32)
    return d


def _prep_phase_B_attn(inputs, l):
    d = {}
    Wqkv = np.asarray(inputs["L0_ffn_Wqkv"])[l]
    wq = Wqkv[:, 0:D]
    wk = Wqkv[:, D:2 * D]
    d["wv"] = np.ascontiguousarray(Wqkv[:, 2 * D:3 * D])
    c = np.arange(D)
    perm = (c // 64) * 64 + ((c % 64) + 32) % 64
    d["wq"] = np.ascontiguousarray(wq)
    d["wk"] = np.ascontiguousarray(wk)
    d["wqr"] = np.ascontiguousarray(wq[:, perm])
    d["wkr"] = np.ascontiguousarray(wk[:, perm])
    d["wo"] = np.ascontiguousarray(np.asarray(inputs["L0_ffn_Wo"])[l])
    qs = np.asarray(inputs["L0_ffn_q_scale"])[l]
    ks = np.asarray(inputs["L0_ffn_k_scale"])[l]
    p64 = ((np.arange(64) + 32) % 64)
    d["qsc"] = np.ascontiguousarray(np.stack(
        [np.tile(qs, 2), np.tile(qs[p64], 2)], axis=1))
    d["ksc"] = np.ascontiguousarray(np.stack(
        [np.tile(ks, 2), np.tile(ks[p64], 2)], axis=1))
    return d


def _prep_phase_B_ffn(inputs, l):
    d = {}
    d["w1"] = np.ascontiguousarray(np.asarray(inputs["L0_ffn_W1"])[l])
    d["w2"] = np.ascontiguousarray(np.asarray(inputs["L0_ffn_W2"])[l])
    d["lng"] = np.ascontiguousarray(np.asarray(inputs["L0_ffn_ln_g"])[l].reshape(NDT, 128).T)
    d["lnb"] = np.ascontiguousarray(np.asarray(inputs["L0_ffn_ln_b"])[l].reshape(NDT, 128).T)
    b1 = np.asarray(inputs["L0_ffn_b1"])[l]
    b1a = np.zeros((NJ * 128,), dtype=np.float32)
    b1a[:DFF] = b1[:DFF]
    b1g = np.zeros((NJ * 128,), dtype=np.float32)
    b1g[:DFF] = b1[DFF:]
    d["b1a"] = np.ascontiguousarray(b1a.reshape(NJ, 128).T)
    d["b1g"] = np.ascontiguousarray(b1g.reshape(NJ, 128).T)
    d["b2"] = np.ascontiguousarray(np.asarray(inputs["L0_ffn_b2"])[l].reshape(NDT, 128).T)
    return d


def _prep_phase_A(inputs):
    d = {}
    d["w_inT"] = np.ascontiguousarray(np.asarray(inputs["L0_attn_in_proj_w"]).T)
    d["conv_w"] = np.ascontiguousarray(np.asarray(inputs["L0_attn_conv_w"])[:, 0, :])
    d["conv_b"] = np.ascontiguousarray(np.asarray(inputs["L0_attn_conv_b"])[:, None])
    d["w_xpT"] = np.ascontiguousarray(np.asarray(inputs["L0_attn_x_proj_w"]).T)
    d["dtwb"] = np.ascontiguousarray(
        np.stack([np.asarray(inputs["L0_attn_dt_proj_w"])[:, 0],
                  np.asarray(inputs["L0_attn_dt_proj_b"])]))
    alog = np.asarray(inputs["L0_attn_A_log"])
    d["a_rep"] = np.ascontiguousarray(
        alog.reshape(8, 16, 8, 16).transpose(2, 3, 0, 1).reshape(128, 128))
    R, S, RB = _sel_matrices()
    d["r_sel"], d["s_sel"], d["rb16"] = R, S, RB
    d["w_outT"] = np.ascontiguousarray(np.asarray(inputs["L0_attn_out_proj_w"]).T)
    return d



def build_fused():
    nc = bass.Bass("TRN2", target_bir_lowering=False)
    b = Bld(nc, coarse_dma_sems=True)
    h = {}

    def dt(name, shape, dtype=F32, kind="ExternalInput"):
        h[name] = nc.dram_tensor(name, shape, dtype, kind=kind)
        return h[name]

    x_in = dt("x", [B_PER_CORE, D, T])
    y_out = dt("y", [B_PER_CORE, D, T], kind="ExternalOutput")
    s0 = dt("s0", [B_PER_CORE, D, T], kind="Internal")
    s1 = dt("s1", [B_PER_CORE, D, T], kind="Internal")
    s2 = dt("s2", [B_PER_CORE, D, T], kind="Internal")
    xnscr = dt("xnscr", [B_PER_CORE, D, T], BF16, kind="Internal")
    gscr = dt("gscr", [B_PER_CORE, NEXP, T], kind="Internal")
    for nm, shape in A_EXT.items():
        dt("A_" + nm, shape)
    for nm, shape in BSH_EXT.items():
        dt("Bs_" + nm, shape)
    for l in range(6):
        for nm, shape in BA_EXT.items():
            dt(f"Ba{l}_" + nm, shape)
        for nm, shape in BF_EXT.items():
            dt(f"Bf{l}_" + nm, shape)
    for nm, shape in C_EXT.items():
        dt("C_" + nm, shape)
    for nm, shape in D_EXT.items():
        dt("D_" + nm, shape)

    def ext_for(prefix, spec, x, y, extra=None):
        e = {nm: h[prefix + nm] for nm in spec}
        e["x"] = x
        e["y"] = y
        if extra:
            e.update(extra)
        return e

    bsh = {nm: h["Bs_" + nm] for nm in BSH_EXT}
    with b.phase("A"):
        emit_A(b, ext_for("A_", A_EXT, x_in, s0))
    with b.phase("P"):
        emit_preln(b, {"x": s0, "y": s1})
    cur, alt = s1, s2
    for l in range(6):
        with b.phase(f"ba{l}"):
            emit_B_attn(b, ext_for(f"Ba{l}_", BA_EXT, cur, alt, extra=bsh))
        cur, alt = alt, cur
        with b.phase(f"bf{l}"):
            emit_B_ffn(b, ext_for(f"Bf{l}_", BF_EXT, cur, alt))
        cur, alt = alt, cur
    with b.phase("R"):
        emit_addres(b, {"x": cur, "xr": s0, "y": alt})
    cur, alt = alt, cur
    with b.phase("C"):
        emit_C(b, ext_for("C_", C_EXT, cur, alt))
    cur, alt = alt, cur
    with b.phase("Dx"):
        emit_D(b, ext_for("D_", D_EXT, cur, y_out,
                          extra={"xnscr": xnscr, "gscr": gscr}))
    return b.finalize([(nm, v) for nm, v in b.cnt.items() if v > 0])


def _prep_fused(inputs):
    shared = {}
    for nm, v in _prep_phase_A(inputs).items():
        shared["A_" + nm] = v
    for nm, v in _prep_phase_B_shared().items():
        shared["Bs_" + nm] = v
    for l in range(6):
        for nm, v in _prep_phase_B_attn(inputs, l).items():
            shared[f"Ba{l}_" + nm] = v
        for nm, v in _prep_phase_B_ffn(inputs, l).items():
            shared[f"Bf{l}_" + nm] = v
    for nm, v in _prep_phase_C(inputs).items():
        shared["C_" + nm] = v
    for nm, v in _prep_phase_D(inputs).items():
        shared["D_" + nm] = v
    return shared


def kernel(**inputs):
    x = np.asarray(inputs["L0_attn_x"])
    xT = np.ascontiguousarray(x.transpose(0, 2, 1))
    shared = _prep_fused(inputs)
    nc = build_fused()
    in_maps = []
    for c in range(NCORES):
        m = dict(shared)
        m["x"] = np.ascontiguousarray(xT[c * B_PER_CORE:(c + 1) * B_PER_CORE])
        in_maps.append(m)
    res = run_bass_kernel_spmd(nc, in_maps, core_ids=list(range(NCORES)))
    yT = np.concatenate([r["y"] for r in res.results], axis=0)
    return np.ascontiguousarray(yT.transpose(0, 2, 1))
